# version 01
# baseline (speedup 1.0000x reference)
import math
import numpy as np
import concourse.bass as bass
import concourse.mybir as mybir
from concourse.bass_utils import run_bass_kernel_spmd

F32 = mybir.dt.float32
BF16 = mybir.dt.bfloat16
I32 = mybir.dt.int32
AF = mybir.ActivationFunctionType
ALU = mybir.AluOpType

NCORES = 8
T = 2048
C = 1024
CCH = 8
NH = 16
HD = 64
FF = 4096
SEG = 512
NSEG = T // SEG

ENGINES = ("sync", "tensor", "vector", "scalar", "gpsimd")


class SemC:

    def __init__(self, h):
        self.h = h
        self.count = 0


class Prog:

    SERIAL = ("vector", "scalar", "gpsimd")

    def __init__(self, nc):
        self.nc = nc
        self.ops = {e: [] for e in ENGINES}
        self._semid = 0
        self.esem = {}
        for e in self.SERIAL:
            self.esem[e] = self.sem(f"e_{e}")

    def sem(self, name=None):
        self._semid += 1
        h = self.nc.alloc_semaphore(name or f"s{self._semid}")
        return SemC(h)

    def emit(self, eng, fn):
        self.ops[eng].append(fn)

    def do(self, eng, fn, sem=None, n=1):
        if eng in self.SERIAL and n == 1 and (sem is None or sem is self.esem[eng]):
            s = self.esem[eng]
            if s.count > 0:
                self.ops[eng].append(lambda q, s=s, v=s.count: q.wait_ge(s.h, v))

            def wrapped(q, fn=fn, s=s):
                fn(q).then_inc(s.h, 1)
            self.ops[eng].append(wrapped)
            s.count += 1
            return s.count
        if sem is None:
            self.ops[eng].append(fn)
            return None
        def wrapped(q, fn=fn, s=sem, n=n):
            fn(q).then_inc(s.h, n)
        self.ops[eng].append(wrapped)
        sem.count += n
        return sem.count

    def wait(self, eng, sem, val=None):
        v = sem.count if val is None else val
        self.ops[eng].append(lambda q, s=sem, v=v: q.wait_ge(s.h, v))

    def barrier(self, name=None):
        s = self.sem(name)
        for e in ENGINES:
            self.do(e, lambda q: q.engine_nop() if hasattr(q, "engine_nop") else q.nop(), s, 1)
        for e in ENGINES:
            self.wait(e, s, len(ENGINES))
        return s

    def build(self):
        with self.nc.Block() as block:
            @block.sync
            def _(q):
                for fn in self.ops["sync"]:
                    fn(q)

            @block.tensor
            def _(q):
                for fn in self.ops["tensor"]:
                    fn(q)

            @block.vector
            def _(q):
                for fn in self.ops["vector"]:
                    fn(q)

            @block.scalar
            def _(q):
                for fn in self.ops["scalar"]:
                    fn(q)

            @block.gpsimd
            def _(q):
                for fn in self.ops["gpsimd"]:
                    fn(q)


class DmaStream:

    def __init__(self, P, name, rot=4):
        self.P = P
        self.sems = [P.sem(f"{name}{i}") for i in range(rot)]
        self.rot = rot
        self.idx = 0

    def start(self, eng, fn):
        s = self.sems[self.idx % self.rot]
        self.idx += 1
        if s.count > 0:
            self.P.wait(eng, s, s.count)
        self.P.do(eng, fn, s, 16)
        return (s, s.count)

    def wait(self, eng, tok):
        self.P.wait(eng, tok[0], tok[1])

    def wait_all(self, eng):
        for s in self.sems:
            if s.count > 0:
                self.P.wait(eng, s, s.count)


def _nop_inc(P, eng, sem):
    P.do(eng, lambda q: q.nop(), sem, 1)


def phase_barrier(P, name=None):
    s = P.sem(name)
    for e in ENGINES:
        _nop_inc(P, e, s)
    for e in ENGINES:
        P.wait(e, s, len(ENGINES))
    return s



class Ctx:
    pass


def alloc_common(nc, P):
    cx = Ctx()
    cx.nc = nc
    cx.P = P
    A = nc.alloc_sbuf_tensor
    cx.xcur = A("xcur", [128, CCH, T], F32)
    cx.ones1_f = A("ones1_f", [1, 128], F32)
    cx.ones128_b = A("ones128_b", [128, 1], BF16)
    cx.ident_f = A("ident_f", [128, 128], F32)
    cx.xbf = A("ln_xbf", [128, CCH, SEG], BF16)
    cx.sqbf = A("ln_sqbf", [128, CCH, SEG], BF16)
    cx.rstd = A("ln_rstd", [1, SEG], F32)
    cx.bneg = A("ln_bneg", [1, SEG], F32)
    cx.tstat = A("ln_tstat", [1, SEG], F32)
    cx.a_sb = A("ln_a_sb", [128, SEG], F32)
    cx.eps = A("ln_eps", [1, 1], F32)
    cx.ssum = A("ln_ssum", [1, SEG], F32)
    cx.b_sb = A("ln_b_sb", [128, SEG], F32)
    cx.lntmp = cx.sqbf
    cx.pool = nc.alloc_psum_tensor("pool_ps", [128, 4, 1024], F32)
    cx.s_ps = cx.pool[0:1, 0, 0:SEG]
    cx.ss_ps = cx.pool[0:1, 0, SEG:2 * SEG]
    cx.ab_ps = cx.pool[:, 1, :].rearrange("p (a s) -> p a s", a=2)

    cx.dma = DmaStream(P, "c_dma", rot=4)
    cx.gdma = DmaStream(P, "c_gdma", rot=4)
    cx.stst = DmaStream(P, "c_st", rot=2)
    cx.s_dv = P.esem["vector"]
    cx.s_act = P.esem["scalar"]
    cx.s_pool = P.esem["gpsimd"]
    cx.s_pe = P.sem("c_pe")
    return cx


def emit_consts(cx):
    P = cx.P
    P.do("vector", lambda q, c=cx: q.memset(c.ones1_f[:, :], 1.0))
    P.do("vector", lambda q, c=cx: q.memset(c.ones128_b[:, :], 1.0))
    P.do("vector", lambda q, c=cx: q.memset(c.eps[:, :], 1e-5))
    P.do("vector", lambda q, c=cx: q.memset(c.ident_f[:, :], 1.0), cx.s_dv, 1)
    P.wait("gpsimd", cx.s_dv)
    P.do("gpsimd",
         lambda q, c=cx: q.affine_select(c.ident_f[:, :], c.ident_f[:, :],
                                         [[-1, 128]], ALU.is_equal, 0.0,
                                         base=0, channel_multiplier=1),
         cx.s_pool, 1)
    cx.dv_consts = cx.s_dv.count



def xstore_view(xstore_ap, b):
    return xstore_ap[b].rearrange("(cc p) t -> p cc t", p=128)


def emit_load_xcur(cx, xstore_ap, b):
    return cx.dma.start("sync", lambda q, c=cx, x=xstore_ap, b=b:
                        q.dma_start(c.xcur[:, :, :], xstore_view(x, b)))


def emit_store_xcur(cx, xstore_ap, b, eng="sync"):
    return cx.stst.start(eng, lambda q, c=cx, x=xstore_ap, b=b:
                         q.dma_start(xstore_view(x, b), c.xcur[:, :, :]))



def emit_input_transpose(cx, x_ap, b, tp_ps, stage):
    P = cx.P
    pe_before = cx.s_pe.count
    for cc in range(CCH):
        for tg4 in range(T // 512):
            P.wait("sync", cx.s_pe, cx.s_pe.count)
            tok = cx.dma.start("sync", lambda q, c=cx, x=x_ap, b=b, cc=cc, tg4=tg4, st=stage:
                               q.dma_start(
                                   st[:, :].rearrange("p (i c) -> p i c", i=4),
                                   x[b, tg4 * 512:(tg4 + 1) * 512, cc * 128:(cc + 1) * 128]
                                   .rearrange("(i p) c -> p i c", p=128)))
            cx.dma.wait("tensor", tok)
            if cc == 0 and tg4 == 0:
                P.wait("tensor", cx.s_pool, 1)
            for i in range(4):
                P.wait("tensor", cx.s_dv, cx.s_dv.count)
                P.do("tensor", lambda q, c=cx, ps=tp_ps, st=stage, i=i:
                     q.transpose(ps[:, i * 128:(i + 1) * 128],
                                 st[:, i * 128:(i + 1) * 128], c.ident_f[:, :]),
                     cx.s_pe, 1)
            pe = cx.s_pe.count
            P.wait("vector", cx.s_pe, pe)
            P.do("vector", lambda q, c=cx, ps=tp_ps, cc=cc, tg4=tg4:
                 q.tensor_copy(c.xcur[:, cc, tg4 * 512:(tg4 + 1) * 512], ps[:, :]),
                 cx.s_dv, 1)
    return cx.s_dv.count


def emit_output_transpose(cx, y_ap, b, tp_ps, stage):
    P = cx.P
    for tg in range(T // 128):
        for half in range(2):
            for j in range(4):
                cc = half * 4 + j
                P.wait("tensor", cx.s_dv, cx.s_dv.count)
                P.do("tensor", lambda q, c=cx, ps=tp_ps, cc=cc, tg=tg, j=j:
                     q.transpose(ps[:, j * 128:(j + 1) * 128],
                                 c.xcur[:, cc, tg * 128:(tg + 1) * 128],
                                 c.ident_f[:, :]),
                     cx.s_pe, 1)
            pe = cx.s_pe.count
            P.wait("vector", cx.s_pe, pe)
            P.do("vector", lambda q, st=stage, half=half, ps=tp_ps:
                 q.tensor_copy(st[:, half * 512:(half + 1) * 512], ps[:, :]),
                 cx.s_dv, 1)
        dv = cx.s_dv.count
        P.wait("sync", cx.s_dv, dv)
        tok = cx.stst.start("sync", lambda q, y=y_ap, b=b, tg=tg, st=stage:
                            q.dma_start(y[b, tg * 128:(tg + 1) * 128, :], st[:, :]))
        cx.stst.wait("vector", tok)
    return None



def emit_ln_seg(cx, t0, dst_fn, sem_out=None):
    P = cx.P
    c = cx
    P.do("vector", lambda q, c=c, t0=t0:
         q.tensor_copy(c.xbf[:, :, :], c.xcur[:, :, t0:t0 + SEG]), c.s_dv, 1)
    dv_cast = c.s_dv.count
    P.wait("scalar", c.s_dv, dv_cast)
    P.do("scalar", lambda q, c=c:
         q.activation(c.sqbf[:, :, :], c.xbf[:, :, :], AF.Square), c.s_act, 1)
    act_sq = c.s_act.count
    P.wait("tensor", c.s_dv, dv_cast)
    for cc in range(CCH):
        P.do("tensor", lambda q, c=c, cc=cc:
             q.matmul(c.s_ps[:, :], c.ones128_b[:, :], c.xbf[:, cc, :],
                      start=(cc == 0), stop=(cc == CCH - 1)),
             c.s_pe, 0 if cc < CCH - 1 else 1)
    P.wait("tensor", c.s_act, act_sq)
    for cc in range(CCH):
        P.do("tensor", lambda q, c=c, cc=cc:
             q.matmul(c.ss_ps[:, :], c.ones128_b[:, :], c.sqbf[:, cc, :],
                      start=(cc == 0), stop=(cc == CCH - 1)),
             c.s_pe, 0 if cc < CCH - 1 else 1)
    pe_stats = c.s_pe.count
    P.wait("vector", c.s_pe, pe_stats)
    P.do("vector", lambda q, c=c: q.tensor_copy(c.ssum[:, :], c.s_ps[:, :]))
    P.do("vector", lambda q, c=c:
         q.scalar_tensor_tensor(c.tstat[:, :], c.ssum[:, :], 1.0 / (1024.0 * 1024.0),
                                c.ssum[:, :], ALU.mult, ALU.mult))
    P.do("vector", lambda q, c=c:
         q.scalar_tensor_tensor(c.rstd[:, :], c.ss_ps[:, :], 1.0 / 1024.0,
                                c.tstat[:, :], ALU.mult, ALU.subtract), c.s_dv, 1)
    dv_var = c.s_dv.count
    P.wait("scalar", c.s_dv, dv_var)
    P.do("scalar", lambda q, c=c:
         q.activation(c.tstat[:, :], c.rstd[:, :], AF.Sqrt, bias=c.eps[:, :]), c.s_act, 1)
    act_std = c.s_act.count
    P.wait("vector", c.s_act, act_std)
    P.do("vector", lambda q, c=c: q.reciprocal(c.rstd[:, :], c.tstat[:, :]))
    P.do("vector", lambda q, c=c:
         q.scalar_tensor_tensor(c.bneg[:, :], c.ssum[:, :], -1.0 / 1024.0,
                                c.rstd[:, :], ALU.mult, ALU.mult), c.s_dv, 1)
    dv_stats = c.s_dv.count
    P.wait("tensor", c.s_dv, dv_stats)
    P.do("tensor", lambda q, c=c:
         q.matmul(c.ab_ps[:, 0, :], c.ones1_f[:, :], c.rstd[:, :],
                  start=True, stop=True))
    P.do("tensor", lambda q, c=c:
         q.matmul(c.ab_ps[:, 1, :], c.ones1_f[:, :], c.bneg[:, :],
                  start=True, stop=True), c.s_pe, 1)
    pe_bc = c.s_pe.count
    P.wait("scalar", c.s_pe, pe_bc)
    P.do("scalar", lambda q, c=c: q.copy(c.a_sb[:, :], c.ab_ps[:, 0, :]))
    P.do("scalar", lambda q, c=c: q.copy(c.b_sb[:, :], c.ab_ps[:, 1, :]), c.s_act, 1)
    act_ab = c.s_act.count
    P.wait("vector", c.s_act, act_ab)
    P.do("vector", lambda q, c=c:
         q.tensor_tensor(c.lntmp[:, :, :], c.xbf[:, :, :],
                         c.a_sb[:, :].unsqueeze(1).broadcast_to([128, CCH, SEG]),
                         ALU.mult))
    for cc in range(CCH):
        P.do("vector", lambda q, c=c, cc=cc, dst_fn=dst_fn:
             q.tensor_tensor(dst_fn(cc), c.lntmp[:, cc, :], c.b_sb[:, :], ALU.add),
             c.s_dv if (cc == CCH - 1) else None, 1)
    if sem_out is not None:
        _nop_inc(P, "vector", sem_out)
    return c.s_dv.count



def alloc_ffn0(nc):
    f = Ctx()
    A = nc.alloc_sbuf_tensor
    f.w_r = A("f_w_r", [128, CCH, C], BF16)
    f.w_k = A("f_w_k", [128, CCH, C], BF16)
    f.w_v = A("f_w_v", [128, CCH, C], BF16)
    f.mu = A("f_mu", [128, 2, CCH], F32)
    f.diff = A("f_diff", [128, CCH, SEG], BF16)
    f.xr = A("f_xr", [128, CCH, SEG], BF16)
    f.xk = f.xr
    f.r = A("f_r", [128, CCH, SEG], F32)
    f.kact = f.diff
    f.vtmp = A("f_vtmp", [128, SEG], F32)
    return f


def emit_ffn0_weights(cx, f, wr_ap, wk_ap, wv_ap):
    P = cx.P
    toks = []
    for (w_sb, w_ap) in ((f.w_r, wr_ap), (f.w_k, wk_ap), (f.w_v, wv_ap)):
        toks.append(cx.gdma.start("gpsimd", lambda q, w_sb=w_sb, w_ap=w_ap:
                    q.dma_start(w_sb[:, :, :],
                                w_ap.rearrange("(cc p) d -> p cc d", p=128))))
    for t in toks:
        cx.gdma.wait("vector", t)
    P.do("vector", lambda q, f=f:
         q.tensor_scalar_mul(f.w_v[:, :, :], f.w_v[:, :, :], 0.5), cx.s_dv, 1)
    return cx.s_dv.count, toks


def emit_ffn0_batch(cx, f, w_dv, w_toks, ld_tok):
    P = cx.P
    c = cx
    if ld_tok is not None:
        cx.dma.wait("vector", ld_tok)
    P.do("vector", lambda q, c=c: q.memset(c.lnx[:, :, 0:1], 0.0))
    for t in w_toks:
        cx.gdma.wait("tensor", t)
    for s in range(NSEG):
        t0 = s * SEG
        emit_ln_seg(cx, t0, lambda cc, c=c: c.lnx[:, cc, 1:SEG + 1])
        dv_ln = c.s_dv.count
        P.do("vector", lambda q, c=c, f=f:
             q.tensor_tensor(f.diff[:, :, :], c.lnx[:, :, 1:SEG + 1],
                             c.lnx[:, :, 0:SEG], ALU.subtract))
        for cc in range(CCH):
            P.do("vector", lambda q, c=c, f=f, cc=cc:
                 q.scalar_tensor_tensor(f.xr[:, cc, :], f.diff[:, cc, :],
                                        f.mu[:, 0, cc:cc + 1], c.lnx[:, cc, 0:SEG],
                                        ALU.mult, ALU.add))
        P.do("vector", lambda q, c=c, f=f:
             q.engine_nop(), c.s_dv, 1)
        dv_lerp = c.s_dv.count

        P.wait("tensor", c.s_dv, dv_lerp)
        for dg in range(CCH):
            ps = f.p_ps if dg % 2 == 0 else f.p2_ps
            P.wait("tensor", c.s_act, c.s_act.count)
            for ccc in range(CCH):
                P.do("tensor", lambda q, f=f, ps=ps, dg=dg, ccc=ccc:
                     q.matmul(ps[:, :], f.w_r[:, ccc, dg * 128:(dg + 1) * 128],
                              f.xr[:, ccc, :], start=(ccc == 0), stop=(ccc == CCH - 1)),
                     c.s_pe, 0 if ccc < CCH - 1 else 1)
            pe = c.s_pe.count
            P.wait("scalar", c.s_pe, pe)
            P.do("scalar", lambda q, f=f, ps=ps, dg=dg:
                 q.activation(f.r[:, dg, :], ps[:, :], AF.Sigmoid), c.s_act, 1)
        pe_r_done = c.s_pe.count
        P.wait("vector", c.s_pe, pe_r_done)
        for cc in range(CCH):
            P.do("vector", lambda q, c=c, f=f, cc=cc:
                 q.scalar_tensor_tensor(f.xk[:, cc, :], f.diff[:, cc, :],
                                        f.mu[:, 1, cc:cc + 1], c.lnx[:, cc, 0:SEG],
                                        ALU.mult, ALU.add))
        P.do("vector", lambda q, c=c:
             q.tensor_copy(c.lnx[:, :, 0:1], c.lnx[:, :, SEG:SEG + 1]), c.s_dv, 1)
        dv_xk = c.s_dv.count
        P.wait("tensor", c.s_dv, dv_xk)
        for dg in range(CCH):
            ps = f.p_ps if dg % 2 == 0 else f.p2_ps
            P.wait("tensor", c.s_act, c.s_act.count)
            for ccc in range(CCH):
                P.do("tensor", lambda q, f=f, ps=ps, dg=dg, ccc=ccc:
                     q.matmul(ps[:, :], f.w_k[:, ccc, dg * 128:(dg + 1) * 128],
                              f.xk[:, ccc, :], start=(ccc == 0), stop=(ccc == CCH - 1)),
                     c.s_pe, 0 if ccc < CCH - 1 else 1)
            pe = c.s_pe.count
            P.wait("scalar", c.s_pe, pe)
            P.do("scalar", lambda q, f=f, ps=ps, dg=dg:
                 q.activation(f.kact[:, dg, :], ps[:, :], AF.Relu), c.s_act, 1)
        act_relu = c.s_act.count
        P.wait("scalar", c.s_act, act_relu)
        P.do("scalar", lambda q, f=f:
             q.activation(f.kact[:, :, :], f.kact[:, :, :], AF.Square), c.s_act, 1)
        act_sq = c.s_act.count
        P.wait("tensor", c.s_act, act_sq)
        for dg in range(CCH):
            ps = f.p_ps if dg % 2 == 0 else f.p2_ps
            P.wait("tensor", c.s_dv, c.s_dv.count)
            for ccc in range(CCH):
                P.do("tensor", lambda q, f=f, ps=ps, dg=dg, ccc=ccc:
                     q.matmul(ps[:, :], f.w_v[:, ccc, dg * 128:(dg + 1) * 128],
                              f.kact[:, ccc, :], start=(ccc == 0), stop=(ccc == CCH - 1)),
                     c.s_pe, 0 if ccc < CCH - 1 else 1)
            pe = c.s_pe.count
            P.wait("vector", c.s_pe, pe)
            P.do("vector", lambda q, f=f, ps=ps, dg=dg:
                 q.tensor_tensor(f.vtmp[:, :], ps[:, :], f.r[:, dg, :], ALU.mult))
            P.do("vector", lambda q, c=c, f=f, dg=dg, t0=t0:
                 q.tensor_tensor(c.xcur[:, dg, t0:t0 + SEG],
                                 c.xcur[:, dg, t0:t0 + SEG], f.vtmp[:, :], ALU.add),
                 c.s_dv, 1)
    return c.s_dv.count



NKC = T // 128
QS = 1024
NQS = T // QS
RT8 = 0.35355339059327373


def alloc_attn0(nc):
    a = Ctx()
    A = nc.alloc_sbuf_tensor
    a.w_v0 = A("a_wv", [128, CCH, C], BF16)
    a.wqk = A("a_wqk", [128, 2, CCH, 128], BF16)
    a.bq = A("a_bq", [64, NH], F32)
    a.bk = A("a_bk", [64, NH], F32)
    a.bvrow = A("a_bvrow", [1, C], F32)
    a.bvb = A("a_bvb", [128, C], F32)
    a.qp = A("a_qp", [65, T], BF16)
    a.kp = A("a_kp", [65, T], BF16)
    a.vp = A("a_vp", [128, NKC, 65], BF16)
    a.qsq = A("a_qsq", [64, T], BF16)
    a.ksq = A("a_ksq", [64, T], BF16)
    a.kmax = A("a_kmax", [64, 1], BF16)
    a.negmk2h = A("a_negmk2h", [1, 1], F32)
    a.pt = A("a_pt", [128, 2, QS], BF16)
    a.rl = A("a_rl", [1, QS], F32)
    a.rb = A("a_rb", [64, QS], F32)
    a.otmp = A("a_otmp", [128, QS], F32)
    return a


def emit_attn0_weights(cx, a, wv_ap, bq_ap, bk_ap, bv_ap):
    P = cx.P
    wv_tok = cx.gdma.start("gpsimd", lambda q, a=a, w=wv_ap:
                           q.dma_start(a.w_v0[:, :, :],
                                       w.rearrange("(cc p) d -> p cc d", p=128)))
    a.wv_tok = wv_tok
    t1 = cx.dma.start("sync", lambda q, a=a, b=bq_ap: q.dma_start(a.bq[:, :], b))
    t2 = cx.dma.start("sync", lambda q, a=a, b=bk_ap: q.dma_start(a.bk[:, :], b))
    t3 = cx.dma.start("sync", lambda q, a=a, b=bv_ap: q.dma_start(a.bvrow[:, :], b))
    for t in (t1, t2, t3):
        cx.dma.wait("vector", t)
    P.do("vector", lambda q, a=a: q.tensor_scalar_mul(a.bq[:, :], a.bq[:, :], RT8))
    P.do("vector", lambda q, a=a: q.tensor_scalar_mul(a.bk[:, :], a.bk[:, :], RT8),
         cx.s_dv, 1)
    bb = cx.pool[:, 2, :]
    cx.dma.wait("tensor", t3)
    P.wait("tensor", cx.s_dv, cx.dv_consts)
    for s in range(2):
        P.do("tensor", lambda q, c=cx, a=a, s=s, bb=bb:
             q.matmul(bb[:, s * 512:(s + 1) * 512], c.ones1_f[:, :],
                      a.bvrow[0:1, s * 512:(s + 1) * 512], start=True, stop=True),
             cx.s_pe, 1)
    pe = cx.s_pe.count
    P.wait("scalar", cx.s_pe, pe)
    P.do("scalar", lambda q, a=a, bb=bb: q.copy(a.bvb[:, :], bb[:, :]), cx.s_act, 1)
    return cx.s_act.count


def emit_wqk_load(cx, a, wqk_heads_ap, h):
    return cx.gdma.start("gpsimd", lambda q, a=a, w=wqk_heads_ap, h=h:
                         q.dma_start(a.wqk[:, h % 2, :, :],
                                     w[h].rearrange("(cc p) d -> p cc d", p=128)))


def emit_attn0_batch(cx, a, wqk_heads_ap, wg_first):
    P = cx.P
    c = cx
    pool = c.pool
    st_pair = [pool[:, 0, :], pool[:, 1, :]]
    o_ps = pool[0:65, 2, :]
    rb_ps = pool[0:64, 3, :]
    q_ps = pool[0:64, 2, :]
    k_ps = pool[0:64, 3, :]
    v_ps = pool[:, 2, 0:64]
    qs2_ps = pool[0:1, 0:2, :].rearrange("p a s -> p (a s)")
    mk2_ps = pool[0:1, 3, 512:513]

    P.do("vector", lambda q, a=a: q.memset(a.vp[:, :, 64:65], 1.0))
    P.do("vector", lambda q, a=a: q.memset(a.kp[64:65, :], 1.0), c.s_dv, 1)

    wg_prev = wg_first
    for h in range(NH):
        par = h % 2
        if h + 1 < NH:
            P.wait("gpsimd", c.s_pe, c.s_pe.count)
            wg_next = emit_wqk_load(cx, a, wqk_heads_ap, h + 1)
        c.gdma.wait("tensor", wg_prev)
        for half in range(2):
            P.wait("tensor", c.s_dv, c.s_dv.count)
            for s2 in range(2):
                t0 = half * 1024 + s2 * 512
                for ccc in range(CCH):
                    P.do("tensor", lambda q, a=a, par=par, ccc=ccc, s2=s2, t0=t0, q_ps=q_ps:
                         q.matmul(q_ps[:, s2 * 512:(s2 + 1) * 512],
                                  a.wqk[:, par, ccc, 0:64], c.lnx[:, ccc, t0:t0 + 512],
                                  start=(ccc == 0), stop=(ccc == CCH - 1)),
                         c.s_pe, 0 if ccc < CCH - 1 else 1)
            pe = c.s_pe.count
            P.wait("vector", c.s_pe, pe)
            P.do("vector", lambda q, a=a, h=h, half=half, q_ps=q_ps:
                 q.tensor_scalar(a.qp[0:64, half * 1024:(half + 1) * 1024], q_ps[:, :],
                                 RT8, a.bq[:, h:h + 1], ALU.mult, ALU.add), c.s_dv, 1)
        for half in range(2):
            P.wait("tensor", c.s_act, c.s_act.count)
            P.wait("tensor", c.s_dv, c.s_dv.count)
            for s2 in range(2):
                t0 = half * 1024 + s2 * 512
                for ccc in range(CCH):
                    P.do("tensor", lambda q, a=a, par=par, ccc=ccc, s2=s2, t0=t0, k_ps=k_ps:
                         q.matmul(k_ps[:, s2 * 512:(s2 + 1) * 512],
                                  a.wqk[:, par, ccc, 64:128], c.lnx[:, ccc, t0:t0 + 512],
                                  start=(ccc == 0), stop=(ccc == CCH - 1)),
                         c.s_pe, 0 if ccc < CCH - 1 else 1)
            pe = c.s_pe.count
            P.wait("vector", c.s_pe, pe)
            P.do("vector", lambda q, a=a, h=h, half=half, k_ps=k_ps:
                 q.tensor_scalar(a.kp[0:64, half * 1024:(half + 1) * 1024], k_ps[:, :],
                                 RT8, a.bk[:, h:h + 1], ALU.mult, ALU.add), c.s_dv, 1)
        dv_qk = c.s_dv.count
        P.wait("scalar", c.s_dv, dv_qk)
        P.do("scalar", lambda q, a=a:
             q.activation(a.qsq[:, :], a.qp[0:64, :], AF.Square))
        P.do("scalar", lambda q, a=a:
             q.activation(a.ksq[:, :], a.kp[0:64, :], AF.Square), c.s_act, 1)
        act_sq = c.s_act.count
        P.wait("tensor", c.s_act, act_sq)
        for s in range(4):
            P.do("tensor", lambda q, c=c, a=a, s=s, qs2_ps=qs2_ps:
                 q.matmul(qs2_ps[:, s * 512:(s + 1) * 512], c.ones128_b[0:64, :],
                          a.qsq[:, s * 512:(s + 1) * 512], start=True, stop=True),
                 c.s_pe, 0 if s < 3 else 1)
        P.wait("vector", c.s_act, act_sq)
        P.do("vector", lambda q, a=a:
             q.reduce_max(a.kmax[:, :], a.ksq[:, :], mybir.AxisListType.X), c.s_dv, 1)
        dv_kmax = c.s_dv.count
        P.wait("tensor", c.s_dv, dv_kmax)
        P.do("tensor", lambda q, c=c, a=a, mk2_ps=mk2_ps:
             q.matmul(mk2_ps, c.ones128_b[0:64, :], a.kmax[:, :],
                      start=True, stop=True), c.s_pe, 1)
        pe_bound = c.s_pe.count
        P.wait("scalar", c.s_pe, pe_bound)
        P.do("scalar", lambda q, a=a, mk2_ps=mk2_ps:
             q.mul(a.negmk2h[:, :], mk2_ps, -0.5), c.s_act, 1)
        act_mk2 = c.s_act.count
        P.wait("vector", c.s_act, act_mk2)
        P.do("vector", lambda q, a=a, qs2_ps=qs2_ps:
             q.tensor_scalar(a.qp[64:65, :], qs2_ps, -0.5, a.negmk2h[:, :],
                             ALU.mult, ALU.add), c.s_dv, 1)
        dv_bound = c.s_dv.count
        for tg in range(NKC):
            P.wait("tensor", c.s_dv, c.s_dv.count)
            for ccc in range(CCH):
                P.do("tensor", lambda q, c=c, a=a, tg=tg, ccc=ccc, h=h, v_ps=v_ps:
                     q.matmul(v_ps, c.lnx[:, ccc, tg * 128:(tg + 1) * 128],
                              a.w_v0[:, ccc, h * 64:(h + 1) * 64],
                              start=(ccc == 0), stop=(ccc == CCH - 1)),
                     c.s_pe, 0 if ccc < CCH - 1 else 1)
            pe = c.s_pe.count
            P.wait("vector", c.s_pe, pe)
            P.do("vector", lambda q, a=a, tg=tg, h=h, v_ps=v_ps:
                 q.tensor_tensor(a.vp[:, tg, 0:64], v_ps,
                                 a.bvb[:, h * 64:(h + 1) * 64], ALU.add), c.s_dv, 1)
        for qs in range(NQS):
            nkc = (qs + 1) * (QS // 128)
            for kc in range(nkc):
                sp = kc % 2
                stv = st_pair[sp]
                if kc < 2 and qs == 0:
                    P.wait("tensor", c.s_dv, dv_bound)
                P.wait("tensor", c.s_act, c.s_act.count)
                for s2 in range(2):
                    P.do("tensor", lambda q, a=a, kc=kc, qs=qs, s2=s2, stv=stv:
                         q.matmul(stv[:, s2 * 512:(s2 + 1) * 512],
                                  a.kp[:, kc * 128:(kc + 1) * 128],
                                  a.qp[:, qs * QS + s2 * 512:qs * QS + (s2 + 1) * 512],
                                  start=True, stop=True),
                         c.s_pe, 0 if s2 == 0 else 1)
                pe_st = c.s_pe.count
                P.wait("scalar", c.s_pe, pe_st)
                P.do("scalar", lambda q, a=a, sp=sp, stv=stv:
                     q.activation(a.pt[:, sp, :], stv[:, :], AF.Exp), c.s_act, 1)
                act_exp = c.s_act.count
                diag = (kc >= qs * (QS // 128))
                if diag:
                    P.wait("gpsimd", c.s_act, act_exp)
                    P.do("gpsimd", lambda q, a=a, sp=sp, qs=qs, kc=kc:
                         q.affine_select(a.pt[:, sp, :], a.pt[:, sp, :],
                                         [[1, QS]], ALU.is_ge, 0.0,
                                         base=qs * QS - kc * 128,
                                         channel_multiplier=-1), c.s_pool, 1)
                    P.wait("tensor", c.s_pool, c.s_pool.count)
                else:
                    P.wait("tensor", c.s_act, act_exp)
                if kc == 0:
                    P.wait("tensor", c.s_dv, c.s_dv.count)
                for s2 in range(2):
                    P.do("tensor", lambda q, a=a, kc=kc, sp=sp, s2=s2, nkc=nkc, o_ps=o_ps:
                         q.matmul(o_ps[:, s2 * 512:(s2 + 1) * 512],
                                  a.vp[:, kc, :], a.pt[:, sp, s2 * 512:(s2 + 1) * 512],
                                  start=(kc == 0), stop=(kc == nkc - 1)),
                         c.s_pe, 0 if (s2 == 0 or kc < nkc - 1) else 1)
            pe_pv = c.s_pe.count
            P.wait("vector", c.s_pe, pe_pv)
            P.do("vector", lambda q, a=a, o_ps=o_ps:
                 q.reciprocal(a.rl[:, :], o_ps[64:65, :]), c.s_dv, 1)
            dv_rl = c.s_dv.count
            P.wait("tensor", c.s_dv, dv_rl)
            for s2 in range(2):
                P.do("tensor", lambda q, c=c, a=a, s2=s2, rb_ps=rb_ps:
                     q.matmul(rb_ps[:, s2 * 512:(s2 + 1) * 512], c.ones1_f[:, 0:64],
                              a.rl[:, s2 * 512:(s2 + 1) * 512], start=True, stop=True),
                     c.s_pe, 0 if s2 == 0 else 1)
            pe_rb = c.s_pe.count
            P.wait("scalar", c.s_pe, pe_rb)
            P.do("scalar", lambda q, a=a, rb_ps=rb_ps:
                 q.copy(a.rb[:, :], rb_ps), c.s_act, 1)
            act_rb = c.s_act.count
            P.wait("vector", c.s_act, act_rb)
            base = (h % 2) * 64
            cc_t = h // 2
            P.do("vector", lambda q, a=a, base=base, o_ps=o_ps:
                 q.tensor_tensor(a.otmp[base:base + 64, :], o_ps[0:64, :],
                                 a.rb[:, :], ALU.mult))
            P.do("vector", lambda q, c=c, a=a, base=base, cc_t=cc_t, qs=qs:
                 q.tensor_tensor(c.xcur[base:base + 64, cc_t, qs * QS:(qs + 1) * QS],
                                 c.xcur[base:base + 64, cc_t, qs * QS:(qs + 1) * QS],
                                 a.otmp[base:base + 64, :], ALU.add), c.s_dv, 1)
        if h + 1 < NH:
            wg_prev = wg_next
    return c.s_dv.count


def build_p2():
    nc = bass.Bass("TRN2", target_bir_lowering=False, debug=False,
                   detect_race_conditions=False)
    x_in = nc.dram_tensor("x_in", [2, C, T], F32, kind="ExternalInput").ap()
    wqk_h = nc.dram_tensor("wqk_h", [NH, C, 128], F32, kind="ExternalInput").ap()
    wv = nc.dram_tensor("wv", [C, C], F32, kind="ExternalInput").ap()
    bq = nc.dram_tensor("bq", [64, NH], F32, kind="ExternalInput").ap()
    bk = nc.dram_tensor("bk", [64, NH], F32, kind="ExternalInput").ap()
    bv = nc.dram_tensor("bv", [1, C], F32, kind="ExternalInput").ap()
    x_out = nc.dram_tensor("x_out", [2, C, T], F32, kind="ExternalOutput").ap()

    P = Prog(nc)
    cx = alloc_common(nc, P)
    cx.lnx = nc.alloc_sbuf_tensor("lnx", [128, CCH, T + 1], BF16)
    a = alloc_attn0(nc)
    emit_consts(cx)
    emit_attn0_weights(cx, a, wv, bq, bk, bv)
    wg0 = emit_wqk_load(cx, a, wqk_h, 0)

    wg_for_batch = wg0
    for b in range(2):
        ld_tok = emit_load_xcur(cx, x_in, b)
        cx.dma.wait("vector", ld_tok)
        for s in range(NSEG):
            emit_ln_seg(cx, s * SEG, lambda cc, c=cx, s=s:
                        c.lnx[:, cc, s * SEG:(s + 1) * SEG])
        P.wait("tensor", cx.s_dv, cx.s_dv.count)
        cx.gdma.wait("tensor", a.wv_tok)
        emit_attn0_batch(cx, a, wqk_h, wg_for_batch)
        if b == 0:
            P.wait("gpsimd", cx.s_pe, cx.s_pe.count)
            wg_for_batch = emit_wqk_load(cx, a, wqk_h, 0)
        dv_done = cx.s_dv.count
        P.wait("sync", cx.s_dv, dv_done)
        st_tok = emit_store_xcur(cx, x_out, b)
        if b == 0:
            for e in ("sync", "vector", "tensor", "scalar"):
                cx.stst.wait(e, st_tok)
    cx.stst.wait_all("sync")
    P.build()
    return nc



MT = 4096
P4_DEBUG = False
TTS = 512
NTT = MT // TTS


def alloc_moe(nc):
    m = Ctx()
    A = nc.alloc_sbuf_tensor
    m.idx = A("m_idx", [128, 64], I32)
    m.idxf = A("m_idxf", [128, 64], F32)
    m.mcmp = A("m_mcmp", [128, 64], F32)
    m.ew = A("m_ew", [128, 64], F32)
    m.maskp = A("m_maskp", [128, 32], F32)
    m.eid = A("m_eid", [128, 1], F32)
    m.gate = A("m_gate", [128, 2], F32)
    m.bias = A("m_bias", [128, CCH], F32)
    m.mrow = A("m_mrow", [1, 2, TTS], F32)
    m.lns = A("m_lns", [128, CCH, SEG], BF16)
    m.wbb = A("m_wbb", [128, FF], BF16)
    m.hbf = m.wbb[:, :].rearrange("p (a t) -> p a t", a=CCH)
    m.w1f = A("m_w1f", [128, 2, CCH, 1024], BF16)
    m.w2f = A("m_w2f", [128, CCH, 1024], BF16)
    m.xm = A("m_xm", [128, CCH, TTS], BF16)
    m.ysb = A("m_ysb", [128, CCH, TTS], F32)
    m.rsb = m.ysb
    return m


def emit_moe_mask(cx, m):
    P = cx.P
    P.do("vector", lambda q, m=m: q.tensor_copy(m.idxf[:, :], m.idx[:, :]))
    P.do("vector", lambda q, m=m:
         q.tensor_scalar(m.mcmp[:, :], m.idxf[:, :], m.eid[:, :], None, ALU.is_equal))
    P.do("vector", lambda q, m=m:
         q.tensor_tensor(m.mcmp[:, :], m.mcmp[:, :], m.ew[:, :], ALU.mult))
    P.do("vector", lambda q, m=m:
         q.tensor_copy(m.maskp[:, :], m.mcmp[:, 32:64]))
    P.do("vector", lambda q, m=m:
         q.tensor_tensor(m.maskp[:, :], m.maskp[:, :], m.mcmp[:, 0:32],
                         ALU.add))
    P.do("vector", lambda q, m=m:
         q.tensor_scalar_mul(m.maskp[:, :], m.maskp[:, :], 0.5), cx.s_dv, 1)
    P.wait("sync", cx.s_dv, cx.s_dv.count)
    t = cx.stst.start("sync", lambda q, m=m, mask_d=m.mask_d:
                      q.dma_start(mask_d.rearrange("o (p f) -> (o p) f", p=128),
                                  m.maskp[:, :]))
    return t


def emit_weight_cache(cx, m, w_ap, wbf_ap, rows):
    toks = []
    for r0 in range(0, rows, 128):
        t1 = cx.gdma.start("gpsimd", lambda q, m=m, w=w_ap, r0=r0:
                           q.dma_start(m.wbb[:, 0:w.shape[1]], w[r0:r0 + 128, :]))
        cx.gdma.wait("sync", t1)
        t2 = cx.stst.start("sync", lambda q, m=m, w=wbf_ap, r0=r0:
                           q.dma_start(w[r0:r0 + 128, :], m.wbb[:, 0:w.shape[1]]))
        cx.stst.wait("gpsimd", t2)
        toks.append(t2)
    return toks


def build_p4():
    nc = bass.Bass("TRN2", target_bir_lowering=False, debug=False,
                   detect_race_conditions=False)
    x_in = nc.dram_tensor("x_in", [2, C, T], F32, kind="ExternalInput").ap()
    w1c = nc.dram_tensor("w1c", [C, FF], F32, kind="ExternalInput").ap()
    w2c = nc.dram_tensor("w2c", [FF, C], F32, kind="ExternalInput").ap()
    idx = nc.dram_tensor("idx", [128, 64], I32, kind="ExternalInput").ap()
    ew = nc.dram_tensor("ew", [128, 64], F32, kind="ExternalInput").ap()
    eid = nc.dram_tensor("eid", [128, 1], F32, kind="ExternalInput").ap()
    gate = nc.dram_tensor("gate", [128, 2], F32, kind="ExternalInput").ap()
    bias = nc.dram_tensor("bias", [128, CCH], F32, kind="ExternalInput").ap()
    x_out = nc.dram_tensor("x_out", [2, C, T], F32, kind="ExternalOutput").ap()

    w1bf = nc.dram_tensor("w1bf", [C, FF], BF16, kind="Internal").ap()
    w2bf = nc.dram_tensor("w2bf", [FF, C], BF16, kind="Internal").ap()
    cc_in = nc.dram_tensor("cc_in", [C, T], BF16, kind="Internal").ap()
    cc_out = nc.dram_tensor("cc_out", [NCORES * C, T], BF16, kind="Internal", addr_space="Shared").ap()
    arin = nc.dram_tensor("arin", [C, MT], F32, kind="Internal").ap()
    arout = nc.dram_tensor("arout", [C, MT], F32, kind="Internal", addr_space="Shared").ap()
    mask_d = nc.dram_tensor("mask_d", [1, MT], F32, kind="Internal").ap()

    P = Prog(nc)
    cx = alloc_common(nc, P)
    m = alloc_moe(nc)
    m.mask_d = mask_d
    h_ps = [cx.pool[:, 2, 0:512], cx.pool[:, 2, 512:1024]]
    y_ps = [cx.pool[:, 3, 0:512], cx.pool[:, 3, 512:1024]]
    mb_ps = cx.pool[:, 0, 0:512]

    emit_consts(cx)
    for (sb, ap) in ((m.idx, idx), (m.ew, ew), (m.eid, eid), (m.gate, gate),
                     (m.bias, bias)):
        t = cx.dma.start("sync", lambda q, sb=sb, ap=ap: q.dma_start(sb[:, :], ap))
        cx.dma.wait("vector", t)
    mask_tok = emit_moe_mask(cx, m)
    P.do("vector", lambda q, m=m:
         q.tensor_scalar_mul(m.bias[:, :], m.bias[:, :], 0.5), cx.s_dv, 1)
    dv_pre = cx.s_dv.count

    w1toks = emit_weight_cache(cx, m, w1c, w1bf, C)
    w2toks = emit_weight_cache(cx, m, w2c, w2bf, FF)

    ldA = emit_load_xcur(cx, x_in, 0)
    cx.dma.wait("vector", ldA)
    ctoks = []
    for s in range(NSEG):
        emit_ln_seg(cx, s * SEG, lambda cc, m=m: m.lns[:, cc, :])
        dv_ln = cx.s_dv.count
        P.wait("sync", cx.s_dv, dv_ln)
        ct = cx.stst.start("sync", lambda q, m=m, cc_in=cc_in, s=s:
                           q.dma_start(cc_in[:, s * SEG:(s + 1) * SEG]
                                       .rearrange("(cc p) t -> p cc t", p=128),
                                       m.lns[:, :, :]))
        ctoks.append(ct)
        cx.stst.wait("vector", ct)
    cc_sem = P.sem("cc")
    for ct in ctoks:
        cx.stst.wait("gpsimd", ct)
    P.do("gpsimd", lambda q, cc_in=cc_in, cc_out=cc_out:
         q.collective_compute("AllGather", ALU.bypass,
                              replica_groups=[list(range(NCORES))],
                              ins=[cc_in.opt()], outs=[cc_out.opt()]),
         cc_sem, 1)
    P.wait("sync", cc_sem, 1)

    for t in w1toks + w2toks:
        cx.stst.wait("gpsimd", t)
    cx.stst.wait("scalar", w1toks[-1])
    cx.stst.wait("scalar", w2toks[-1])
    ytoks = []
    for tt in range(NTT):
        rank = tt // (NTT // 2)
        col0 = (tt % (NTT // 2)) * TTS
        P.wait("sync", cx.s_pe, cx.s_pe.count)
        xt = cx.dma.start("sync", lambda q, m=m, cc_out=cc_out, rank=rank, col0=col0:
                          q.dma_start(m.xm[:, :, :],
                                      cc_out[rank * C:(rank + 1) * C,
                                             col0:col0 + TTS]
                                      .rearrange("(cc p) t -> p cc t", p=128)))
        cx.dma.wait("tensor", xt)
        mpar = tt % 2
        if tt == 0:
            cx.stst.wait("sync", mask_tok)
        P.wait("sync", cx.s_pe, cx.s_pe.count)
        mrt = cx.dma.start("sync", lambda q, m=m, tt=tt, mpar=mpar:
                           q.dma_start(m.mrow[:, mpar, :],
                                       m.mask_d[:, tt * TTS:(tt + 1) * TTS]))
        if tt == 0:
            P.wait("tensor", cx.s_dv, dv_pre)
        cx.dma.wait("tensor", mrt)
        P.wait("tensor", cx.s_dv, cx.s_dv.count)
        P.do("tensor", lambda q, c=cx, m=m, tt=tt, mpar=mpar:
             q.matmul(mb_ps, c.ones1_f[:, :], m.mrow[:, mpar, :],
                      start=True, stop=True), cx.s_pe, 1)
        for fc in range(4):
            par = fc % 2
            P.wait("gpsimd", cx.s_pe, cx.s_pe.count)
            w1t = cx.gdma.start("gpsimd", lambda q, m=m, w1bf=w1bf, fc=fc, par=par:
                                q.dma_start(m.w1f[:, par, :, :],
                                            w1bf[:, fc * 1024:(fc + 1) * 1024]
                                            .rearrange("(cc p) f -> p cc f", p=128)))
            cx.gdma.wait("tensor", w1t)
            if P4_DEBUG and tt == 0 and fc == 0:
                dbg_xm = nc.dram_tensor("dbg_xm", [128, CCH, TTS], F32,
                                        kind="ExternalOutput").ap()
                cx.dma.wait("gpsimd", xt)
                dxm = cx.stst.start("gpsimd", lambda q, m=m, o=dbg_xm:
                                    q.dma_start(o, m.xm[:, :, :]))
                cx.stst.wait("gpsimd", dxm)
            for mg in range(CCH):
                hp = h_ps[mg % 2]
                P.wait("tensor", cx.s_act, cx.s_act.count)
                for ccc in range(CCH):
                    P.do("tensor", lambda q, m=m, par=par, mg=mg, ccc=ccc, hp=hp:
                         q.matmul(hp, m.w1f[:, par, ccc, mg * 128:(mg + 1) * 128],
                                  m.xm[:, ccc, :],
                                  start=(ccc == 0), stop=(ccc == CCH - 1)),
                         cx.s_pe, 0 if ccc < CCH - 1 else 1)
                pe = cx.s_pe.count
                P.wait("scalar", cx.s_pe, pe)
                P.do("scalar", lambda q, m=m, mg=mg, hp=hp:
                     q.activation(m.hbf[:, mg, :], hp, AF.Gelu), cx.s_act, 1)
            act_h = cx.s_act.count
            if P4_DEBUG and tt == 0 and fc == 0:
                dbg_h = nc.dram_tensor("dbg_h", [128, CCH, TTS], F32,
                                       kind="ExternalOutput").ap()
                P.wait("gpsimd", cx.s_act, act_h)
                dh = cx.stst.start("gpsimd", lambda q, m=m, o=dbg_h:
                                   q.dma_start(o, m.hbf[:, :, :]))
                cx.stst.wait("gpsimd", dh)
                cx.stst.wait("scalar", dh)
                cx.stst.wait("tensor", dh)
            P.wait("gpsimd", cx.s_pe, cx.s_pe.count)
            w2t = cx.gdma.start("gpsimd", lambda q, m=m, w2bf=w2bf, fc=fc:
                                q.dma_start(m.w2f[:, :, :],
                                            w2bf[fc * 1024:(fc + 1) * 1024, :]
                                            .rearrange("(fm p) d -> p fm d", p=128)))
            cx.gdma.wait("tensor", w2t)
            P.wait("tensor", cx.s_act, act_h)
            for dg in range(CCH):
                yp = y_ps[dg % 2]
                P.wait("tensor", cx.s_dv, cx.s_dv.count)
                for fm in range(CCH):
                    P.do("tensor", lambda q, m=m, fm=fm, dg=dg, yp=yp:
                         q.matmul(yp,
                                  m.w2f[:, fm, dg * 128:(dg + 1) * 128],
                                  m.hbf[:, fm, :],
                                  start=(fm == 0), stop=(fm == CCH - 1)),
                         cx.s_pe, 0 if fm < CCH - 1 else 1)
                pe = cx.s_pe.count
                P.wait("vector", cx.s_pe, pe)
                if fc == 0:
                    P.do("vector", lambda q, m=m, dg=dg, yp=yp:
                         q.tensor_copy(m.ysb[:, dg, :], yp))
                else:
                    P.do("vector", lambda q, m=m, dg=dg, yp=yp:
                         q.tensor_tensor(m.ysb[:, dg, :], m.ysb[:, dg, :],
                                         yp, ALU.add))
        if P4_DEBUG and tt == 0:
            dbg_y0 = nc.dram_tensor("dbg_y0", [128, CCH, TTS], F32,
                                    kind="ExternalOutput").ap()
            P.wait("sync", cx.s_dv, cx.s_dv.count)
            dy0 = cx.stst.start("sync", lambda q, m=m, o=dbg_y0:
                                q.dma_start(o, m.ysb[:, :, :]))
            cx.stst.wait("sync", dy0)
            cx.stst.wait("vector", dy0)
        pe_mb = cx.s_pe.count
        P.wait("vector", cx.s_pe, pe_mb)
        P.do("vector", lambda q, m=m, mb_ps=mb_ps:
             q.tensor_tensor(m.ysb[:, :, :], m.ysb[:, :, :],
                             mb_ps[:, :].unsqueeze(1).broadcast_to([128, CCH, TTS]),
                             ALU.mult), cx.s_dv, 1)
        dv_y = cx.s_dv.count
        P.wait("sync", cx.s_dv, dv_y)
        yt = cx.stst.start("sync", lambda q, m=m, arin=arin, tt=tt:
                           q.dma_start(arin[:, tt * TTS:(tt + 1) * TTS]
                                       .rearrange("(cc p) t -> p cc t", p=128),
                                       m.ysb[:, :, :]))
        ytoks.append(yt)
        cx.stst.wait("vector", yt)

    if P4_DEBUG:
        dbg_maskp = nc.dram_tensor("dbg_maskp", [128, 32], F32, kind="ExternalOutput").ap()
        dbg_mcmp = nc.dram_tensor("dbg_mcmp", [128, 64], F32, kind="ExternalOutput").ap()
        dbg_idxf = nc.dram_tensor("dbg_idxf", [128, 64], F32, kind="ExternalOutput").ap()
        dbg_ew = nc.dram_tensor("dbg_ew", [128, 64], F32, kind="ExternalOutput").ap()
        dbg_mask = nc.dram_tensor("dbg_mask", [1, MT], F32, kind="ExternalOutput").ap()
        dbg_gather = nc.dram_tensor("dbg_gather", [C, T], F32, kind="ExternalOutput").ap()
        dbg_arin = nc.dram_tensor("dbg_arin", [C, MT], F32, kind="ExternalOutput").ap()
        cx.stst.wait("sync", mask_tok)
        dt0 = cx.stst.start("sync", lambda q: q.dma_start(dbg_maskp, m.maskp[:, :]))
        dta = cx.stst.start("sync", lambda q: q.dma_start(dbg_mcmp, m.mcmp[:, :]))
        dtb = cx.stst.start("sync", lambda q: q.dma_start(dbg_idxf, m.idxf[:, :]))
        dtc = cx.stst.start("sync", lambda q: q.dma_start(dbg_ew, m.ew[:, :]))
        for _t in (dta, dtb, dtc):
            cx.stst.wait("sync", _t)
        dt1 = cx.stst.start("sync", lambda q: q.dma_start(dbg_mask, m.mask_d))
        cx.stst.wait("sync", dt0)
        P.wait("gpsimd", cc_sem, 1)
        dt2 = cx.stst.start("gpsimd", lambda q: q.dma_start(dbg_gather, cc_out[0:C, :]))
        for t in ytoks:
            cx.stst.wait("sync", t)
        dt3 = cx.stst.start("sync", lambda q: q.dma_start(dbg_arin, arin))
        cx.stst.wait("sync", dt1)
        cx.stst.wait("sync", dt3)
        cx.stst.wait("gpsimd", dt2)
    ar_sem = P.sem("ar")
    for t in ytoks:
        cx.stst.wait("gpsimd", t)
    P.do("gpsimd", lambda q, arin=arin, arout=arout:
         q.collective_compute("AllReduce", ALU.add,
                              replica_groups=[list(range(NCORES))],
                              ins=[arin.opt()], outs=[arout.opt()]),
         ar_sem, 1)
    P.wait("sync", ar_sem, 1)

    if P4_DEBUG:
        dbg_arout = nc.dram_tensor("dbg_arout", [C, MT], F32, kind="ExternalOutput").ap()
        P.wait("sync", ar_sem, 1)
        dao = cx.stst.start("sync", lambda q: q.dma_start(dbg_arout, arout))
        cx.stst.wait("sync", dao)
    for half in range(2):
        for s in range(NSEG):
            P.wait("sync", cx.s_dv, cx.s_dv.count)
            rt = cx.dma.start("sync", lambda q, m=m, arout=arout, half=half, s=s:
                              q.dma_start(m.rsb[:, :, :],
                                          arout[:, half * T + s * SEG:
                                                half * T + (s + 1) * SEG]
                                          .rearrange("(cc p) t -> p cc t", p=128)))
            cx.dma.wait("vector", rt)
            for cc in range(CCH):
                P.do("vector", lambda q, c=cx, m=m, half=half, s=s, cc=cc:
                     q.scalar_tensor_tensor(
                         c.xcur[:, cc, s * SEG:(s + 1) * SEG],
                         m.rsb[:, cc, :], m.gate[:, half:half + 1],
                         c.xcur[:, cc, s * SEG:(s + 1) * SEG],
                         ALU.mult, ALU.add),
                     cx.s_dv if cc == CCH - 1 else None, 1)
    for cc in range(CCH):
        P.do("vector", lambda q, c=cx, m=m, cc=cc:
             q.tensor_scalar(c.xcur[:, cc, :], c.xcur[:, cc, :],
                             m.bias[:, cc:cc + 1], None, ALU.add),
             cx.s_dv if cc == CCH - 1 else None, 1)
    dvA = cx.s_dv.count
    P.wait("sync", cx.s_dv, dvA)
    stA = emit_store_xcur(cx, x_out, 0)
    for e in ("sync", "vector"):
        cx.stst.wait(e, stA)
    ldB = emit_load_xcur(cx, x_in, 1)
    cx.dma.wait("vector", ldB)
    for cc in range(CCH):
        P.do("vector", lambda q, c=cx, m=m, cc=cc:
             q.tensor_scalar(c.xcur[:, cc, :], c.xcur[:, cc, :],
                             m.bias[:, cc:cc + 1], None, ALU.add),
             cx.s_dv if cc == CCH - 1 else None, 1)
    dvB = cx.s_dv.count
    P.wait("sync", cx.s_dv, dvB)
    emit_store_xcur(cx, x_out, 1)
    cx.stst.wait_all("sync")
    P.build()
    return nc



PAIRS = [(0, 1), (0, 2), (0, 3), (1, 2), (1, 3), (2, 3)]
JSIGN = [1.0, -1.0, 1.0, 1.0, -1.0, 1.0]
LN6H = -0.5 * math.log(6.0)


def alloc_attn1(nc):
    a = Ctx()
    A = nc.alloc_sbuf_tensor
    a.wpq = A("b_wpq", [128, CCH, 128], BF16)
    a.wpk = A("b_wpk", [128, CCH, 128], BF16)
    a.w_vh = A("b_wvh", [128, 2, CCH, 64], BF16)
    a.w_out = A("b_wout", [128, CCH, C], BF16)
    a.bvb = A("b_bvb", [128, C], F32)
    a.bout = A("b_bout", [128, CCH], F32)
    a.AO = A("b_AO", [128, CCH, T], BF16)
    a.lqe = A("b_lqe", [128, 6, 256], BF16)
    a.jlke = A("b_jlke", [128, 6, 256], BF16)

    a.vp = A("b_vp", [128, NKC, 65], BF16)
    a.pt = A("b_pt", [128, 1, QS], BF16)
    a.rl = A("b_rl", [1, QS], F32)
    a.rb = A("b_rb", [64, QS], F32)
    a.ytmp = A("b_ytmp", [128, SEG], F32)
    a.ln6h = A("b_ln6h", [128, 1], F32)
    a.bvrow = a.rb[0:1, :]
    return a


def bind_attn1_aliases(cx, a):
    xb = cx.xbf[0:6, :, :].rearrange("p a t -> p (a t)")
    a.lqh = xb[:, 0:T]
    a.jlkh = xb[:, T:2 * T]
    a.pairq = a.AO[:, 0, :]
    a.pairk = a.AO[:, 1, :]
    a.q1e = a.AO[:, 2, 0:1024].rearrange("p (c t) -> p c t", c=4)
    a.k1e = a.AO[:, 2, 1024:2048].rearrange("p (c t) -> p c t", c=4)
    a.q2e = a.AO[:, 3, 0:1024].rearrange("p (c t) -> p c t", c=4)
    a.k2e = a.AO[:, 3, 1024:2048].rearrange("p (c t) -> p c t", c=4)
    arena = a.AO[:, 4:8, :].rearrange("p a t -> p (a t)").bitcast(F32)
    a.lx = arena[:, 0:1536].rearrange("p (c t) -> p c t", c=6)
    a.sq = arena[:, 1536:3072].rearrange("p (c t) -> p c t", c=6)
    a.acc = arena[:, 3072:3328]
    a.rn = arena[:, 3328:3584]
    a.rnn = arena[:, 3584:3840]
    a.tmp1 = arena[:, 3840:4096]
    return a


def emit_wv1_load(cx, a, wvh_ap, h):
    return cx.gdma.start("gpsimd", lambda q, a=a, w=wvh_ap, h=h:
                         q.dma_start(a.w_vh[:, h % 2, :, :],
                                     w[h].rearrange("(cc p) d -> p cc d", p=128)))


def emit_attn1_weights(cx, a, wpq_ap, wpk_ap, wout_ap, bv_ap, bout_ap):
    P = cx.P
    toks = []
    for (sb, ap) in ((a.wpq, wpq_ap), (a.wpk, wpk_ap)):
        toks.append(cx.gdma.start("gpsimd", lambda q, sb=sb, ap=ap:
                    q.dma_start(sb[:, :, :],
                                ap.rearrange("(cc p) d -> p cc d", p=128))))
    a.wout_tok = cx.gdma.start("gpsimd", lambda q, a=a, w=wout_ap:
                               q.dma_start(a.w_out[:, :, :],
                                           w.rearrange("(cc p) d -> p cc d", p=128)))
    t3 = cx.dma.start("sync", lambda q, a=a, b=bv_ap: q.dma_start(a.bvrow, b))
    t4 = cx.dma.start("sync", lambda q, a=a, b=bout_ap: q.dma_start(a.bout[:, :], b))
    a.bout_tok = t4
    bb = cx.pool[:, 2, :]
    cx.dma.wait("tensor", t3)
    P.wait("tensor", cx.s_dv, cx.dv_consts)
    for s in range(2):
        P.do("tensor", lambda q, c=cx, a=a, s=s, bb=bb:
             q.matmul(bb[:, s * 512:(s + 1) * 512], c.ones1_f[:, :],
                      a.bvrow[0:1, s * 512:(s + 1) * 512], start=True, stop=True),
             cx.s_pe, 1)
    pe = cx.s_pe.count
    P.wait("scalar", cx.s_pe, pe)
    P.do("scalar", lambda q, a=a, bb=bb: q.copy(a.bvb[:, :], bb[:, :]))
    P.do("vector", lambda q, a=a: q.memset(a.ln6h[:, :], LN6H))
    a.wpq_toks = toks
    return toks


def emit_attn1_batch(cx, a):
    P = cx.P
    c = cx
    pool = c.pool
    st_pair = [pool[:, 0, :], pool[:, 1, :]]
    o_ps = pool[0:65, 2, :]
    rb_ps = pool[0:64, 3, :]
    pk_ps = pool[:, 2, 0:512]
    v_ps = pool[:, 3, 0:64]
    y_ps = [pool[:, 2, 0:512], pool[:, 2, 512:1024]]

    for t in a.wpq_toks:
        cx.gdma.wait("tensor", t)
    for (wp, dst) in ((a.wpq, a.pairq), (a.wpk, a.pairk)):
        for s in range(NSEG):
            P.wait("tensor", c.s_dv, c.s_dv.count)
            for ccc in range(CCH):
                P.do("tensor", lambda q, c=c, wp=wp, ccc=ccc, s=s:
                     q.matmul(pk_ps, wp[:, ccc, :], c.lnx[:, ccc, s * SEG:(s + 1) * SEG],
                              start=(ccc == 0), stop=(ccc == CCH - 1)),
                     c.s_pe, 0 if ccc < CCH - 1 else 1)
            pe = c.s_pe.count
            P.wait("vector", c.s_pe, pe)
            P.do("vector", lambda q, dst=dst, s=s, pk_ps=pk_ps:
                 q.tensor_copy(dst[:, s * SEG:(s + 1) * SEG], pk_ps))
    dv_proj = c.s_dv.count

    rp = []
    P.wait("sync", c.s_dv, dv_proj)
    for (src, dst) in ((a.pairq, a.q1e), (a.pairk, a.k1e)):
        for d4 in range(4):
            rp.append(cx.dma.start("sync", lambda q, src=src, dst=dst, d4=d4:
                      q.dma_start(
                          dst[:, d4, :], src[d4:64:4, :])))
    for (src, dst) in ((a.pairq, a.q2e), (a.pairk, a.k2e)):
        for d4 in range(4):
            rp.append(cx.dma.start("sync", lambda q, src=src, dst=dst, d4=d4:
                      q.dma_start(
                          dst[:, d4, :], src[64 + d4:128:4, :])))
    for t in rp:
        cx.dma.wait("vector", t)

    for side in range(2):
        e1 = a.q1e if side == 0 else a.k1e
        e2 = a.q2e if side == 0 else a.k2e
        for ci, (i, j) in enumerate(PAIRS):
            P.do("vector", lambda q, a=a, e1=e1, e2=e2, i=i, j=j:
                 q.tensor_tensor(a.tmp1[:, :], e1[:, i, :], e2[:, j, :], ALU.mult))
            P.do("vector", lambda q, a=a, e1=e1, e2=e2, i=i, j=j, ci=ci:
                 q.tensor_tensor(a.sq[:, ci, :], e1[:, j, :], e2[:, i, :], ALU.mult))
            P.do("vector", lambda q, a=a, ci=ci:
                 q.tensor_tensor(a.lx[:, ci, :], a.tmp1[:, :], a.sq[:, ci, :],
                                 ALU.subtract))
        dv_l = c.s_dv.count
        P.wait("scalar", c.s_dv, dv_l)
        P.do("scalar", lambda q, a=a:
             q.activation(a.sq[:, :, :], a.lx[:, :, :], AF.Square), c.s_act, 1)
        act_sq = c.s_act.count
        P.wait("vector", c.s_act, act_sq)
        P.do("vector", lambda q, a=a: q.tensor_copy(a.acc[:, :], a.sq[:, 0, :]))
        for ci in range(1, 6):
            P.do("vector", lambda q, a=a, ci=ci:
                 q.tensor_tensor(a.acc[:, :], a.acc[:, :], a.sq[:, ci, :], ALU.add))
        P.do("vector", lambda q, a=a:
             q.tensor_scalar_max(a.acc[:, :], a.acc[:, :], 1e-24), c.s_dv, 1)
        dv_acc = c.s_dv.count
        P.wait("scalar", c.s_dv, dv_acc)
        P.do("scalar", lambda q, a=a: q.activation(a.tmp1[:, :], a.acc[:, :], AF.Ln))
        if side == 0:
            P.do("scalar", lambda q, a=a:
                 q.activation(a.rn[:, :], a.tmp1[:, :], AF.Exp, scale=-0.5,
                              bias=a.ln6h[:, :]), c.s_act, 1)
        else:
            P.do("scalar", lambda q, a=a:
                 q.activation(a.rn[:, :], a.tmp1[:, :], AF.Exp, scale=-0.5),
                 c.s_act, 1)
        act_rn = c.s_act.count
        P.wait("vector", c.s_act, act_rn)
        if side == 0:
            for ci in range(6):
                P.do("vector", lambda q, a=a, ci=ci:
                     q.tensor_tensor(a.lqe[:, ci, :], a.lx[:, ci, :], a.rn[:, :],
                                     ALU.mult))
        else:
            P.do("vector", lambda q, a=a:
                 q.tensor_scalar_mul(a.rnn[:, :], a.rn[:, :], -1.0))
            for aa in range(6):
                src_c = 5 - aa
                rn_t = "rn" if JSIGN[aa] > 0 else "rnn"
                P.do("vector", lambda q, a=a, aa=aa, src_c=src_c, rn_t=rn_t:
                     q.tensor_tensor(a.jlke[:, aa, :], a.lx[:, src_c, :],
                                     getattr(a, rn_t)[:, :], ALU.mult))
    dv_ext = c.s_dv.count

    P.do("vector", lambda q, a=a: q.memset(a.vp[:, :, 64:65], 1.0))

    wvt_prev = a.wv0_tok
    for h in range(NH):
        par = h % 2
        if h + 1 < NH:
            P.wait("gpsimd", c.s_pe, c.s_pe.count)
            wvt_next = emit_wv1_load(cx, a, a.wvh_ap, h + 1)
        P.wait("sync", c.s_pe, c.s_pe.count)
        if h == 0:
            P.wait("sync", c.s_dv, dv_ext)
        rp_toks = []
        for ci in range(6):
            qn = "sync" if ci % 2 == 0 else "scalar"
            rp_toks.append(cx.dma.start(qn, lambda q, a=a, h=h, ci=ci:
                           q.dma_start(a.lqh[ci:ci + 1, :],
                                       a.lqe[h * 8:(h + 1) * 8, ci, :])))
            rp_toks.append(cx.dma.start(qn, lambda q, a=a, h=h, ci=ci:
                           q.dma_start(a.jlkh[ci:ci + 1, :],
                                       a.jlke[h * 8:(h + 1) * 8, ci, :])))
        for t in rp_toks:
            cx.dma.wait("tensor", t)
        cx.gdma.wait("tensor", wvt_prev)
        for tg in range(NKC):
            P.wait("tensor", c.s_dv, c.s_dv.count)
            for ccc in range(CCH):
                P.do("tensor", lambda q, c=c, a=a, tg=tg, ccc=ccc, par=par, v_ps=v_ps:
                     q.matmul(v_ps, c.lnx[:, ccc, tg * 128:(tg + 1) * 128],
                              a.w_vh[:, par, ccc, :],
                              start=(ccc == 0), stop=(ccc == CCH - 1)),
                     c.s_pe, 0 if ccc < CCH - 1 else 1)
            pe = c.s_pe.count
            P.wait("vector", c.s_pe, pe)
            P.do("vector", lambda q, a=a, tg=tg, h=h, v_ps=v_ps:
                 q.tensor_tensor(a.vp[:, tg, 0:64], v_ps,
                                 a.bvb[:, h * 64:(h + 1) * 64], ALU.add))
        for qs in range(NQS):
            nkc = (qs + 1) * (QS // 128)
            for kc in range(nkc):
                sp = 0
                stv = st_pair[kc % 2]
                P.wait("tensor", c.s_act, c.s_act.count)
                P.wait("scalar", c.s_pe, c.s_pe.count)
                for s2 in range(2):
                    P.do("tensor", lambda q, a=a, kc=kc, qs=qs, s2=s2, stv=stv:
                         q.matmul(stv[:, s2 * 512:(s2 + 1) * 512],
                                  a.jlkh[:, kc * 128:(kc + 1) * 128],
                                  a.lqh[:, qs * QS + s2 * 512:qs * QS + (s2 + 1) * 512],
                                  start=True, stop=True),
                         c.s_pe, 0 if s2 == 0 else 1)
                pe_st = c.s_pe.count
                P.wait("scalar", c.s_pe, pe_st)
                P.do("scalar", lambda q, a=a, sp=sp, stv=stv:
                     q.activation(a.pt[:, sp, :], stv[:, :], AF.Exp), c.s_act, 1)
                act_exp = c.s_act.count
                diag = (kc >= qs * (QS // 128))
                if diag:
                    P.wait("gpsimd", c.s_act, act_exp)
                    P.do("gpsimd", lambda q, a=a, sp=sp, qs=qs, kc=kc:
                         q.affine_select(a.pt[:, sp, :], a.pt[:, sp, :],
                                         [[1, QS]], ALU.is_ge, 0.0,
                                         base=qs * QS - kc * 128,
                                         channel_multiplier=-1))
                    P.wait("tensor", c.s_pool, c.s_pool.count)
                else:
                    P.wait("tensor", c.s_act, act_exp)
                if kc == 0:
                    P.wait("tensor", c.s_dv, c.s_dv.count)
                for s2 in range(2):
                    P.do("tensor", lambda q, a=a, kc=kc, sp=sp, s2=s2, nkc=nkc, o_ps=o_ps:
                         q.matmul(o_ps[:, s2 * 512:(s2 + 1) * 512],
                                  a.vp[:, kc, :], a.pt[:, sp, s2 * 512:(s2 + 1) * 512],
                                  start=(kc == 0), stop=(kc == nkc - 1)),
                         c.s_pe, 0 if (s2 == 0 or kc < nkc - 1) else 1)
            pe_pv = c.s_pe.count
            P.wait("vector", c.s_pe, pe_pv)
            P.do("vector", lambda q, a=a, o_ps=o_ps:
                 q.reciprocal(a.rl[:, :], o_ps[64:65, :]), c.s_dv, 1)
            dv_rl = c.s_dv.count
            P.wait("tensor", c.s_dv, dv_rl)
            for s2 in range(2):
                P.do("tensor", lambda q, c=c, a=a, s2=s2, rb_ps=rb_ps:
                     q.matmul(rb_ps[:, s2 * 512:(s2 + 1) * 512], c.ones1_f[:, 0:64],
                              a.rl[:, s2 * 512:(s2 + 1) * 512], start=True, stop=True),
                     c.s_pe, 0 if s2 == 0 else 1)
            pe_rb = c.s_pe.count
            P.wait("scalar", c.s_pe, pe_rb)
            P.do("scalar", lambda q, a=a, rb_ps=rb_ps:
                 q.copy(a.rb[:, :], rb_ps), c.s_act, 1)
            act_rb = c.s_act.count
            P.wait("vector", c.s_act, act_rb)
            base = (h % 2) * 64
            cc_t = h // 2
            P.do("vector", lambda q, c=c, a=a, base=base, cc_t=cc_t, qs=qs, o_ps=o_ps:
                 q.tensor_tensor(a.AO[base:base + 64, cc_t, qs * QS:(qs + 1) * QS],
                                 o_ps[0:64, :], a.rb[:, :], ALU.mult), c.s_dv, 1)
        if h + 1 < NH:
            wvt_prev = wvt_next
    cx.gdma.wait("tensor", a.wout_tok)
    cx.dma.wait("vector", a.bout_tok)
    dv_ao = c.s_dv.count
    P.wait("tensor", c.s_dv, dv_ao)
    for s in range(NSEG):
        for dg in range(CCH):
            yp = y_ps[dg % 2]
            P.wait("tensor", c.s_dv, c.s_dv.count)
            for ccc in range(CCH):
                P.do("tensor", lambda q, c=c, a=a, s=s, dg=dg, ccc=ccc, yp=yp:
                     q.matmul(yp, a.w_out[:, ccc, dg * 128:(dg + 1) * 128],
                              a.AO[:, ccc, s * SEG:(s + 1) * SEG],
                              start=(ccc == 0), stop=(ccc == CCH - 1)),
                     c.s_pe, 0 if ccc < CCH - 1 else 1)
            pe = c.s_pe.count
            P.wait("vector", c.s_pe, pe)
            P.do("vector", lambda q, a=a, dg=dg, yp=yp:
                 q.tensor_scalar(a.ytmp[:, :], yp, a.bout[:, dg:dg + 1], None,
                                 ALU.add))
            P.do("vector", lambda q, c=c, a=a, dg=dg, s=s:
                 q.tensor_tensor(c.xcur[:, dg, s * SEG:(s + 1) * SEG],
                                 c.xcur[:, dg, s * SEG:(s + 1) * SEG],
                                 a.ytmp[:, :], ALU.add), c.s_dv, 1)
    return c.s_dv.count


def build_p5():
    nc = bass.Bass("TRN2", target_bir_lowering=False, debug=False,
                   detect_race_conditions=False)
    x_in = nc.dram_tensor("x_in", [2, C, T], F32, kind="ExternalInput").ap()
    wpq = nc.dram_tensor("wpq", [C, 128], F32, kind="ExternalInput").ap()
    wpk = nc.dram_tensor("wpk", [C, 128], F32, kind="ExternalInput").ap()
    wvh = nc.dram_tensor("wvh", [NH, C, 64], F32, kind="ExternalInput").ap()
    wout = nc.dram_tensor("wout", [C, C], F32, kind="ExternalInput").ap()
    bv = nc.dram_tensor("bv", [1, C], F32, kind="ExternalInput").ap()
    bout = nc.dram_tensor("bout", [128, CCH], F32, kind="ExternalInput").ap()
    x_out = nc.dram_tensor("x_out", [2, C, T], F32, kind="ExternalOutput").ap()

    P = Prog(nc)
    cx = alloc_common(nc, P)
    cx.lnx = nc.alloc_sbuf_tensor("lnx", [128, CCH, T + 1], BF16)
    a = alloc_attn1(nc)
    bind_attn1_aliases(cx, a)
    a.wvh_ap = wvh
    emit_consts(cx)
    emit_attn1_weights(cx, a, wpq, wpk, wout, bv, bout)
    a.wv0_tok = emit_wv1_load(cx, a, wvh, 0)

    for b in range(2):
        ld_tok = emit_load_xcur(cx, x_in, b)
        cx.dma.wait("vector", ld_tok)
        for s in range(NSEG):
            emit_ln_seg(cx, s * SEG, lambda cc, c=cx, s=s:
                        c.lnx[:, cc, s * SEG:(s + 1) * SEG])
        P.wait("tensor", cx.s_dv, cx.s_dv.count)
        emit_attn1_batch(cx, a)
        if b == 0:
            P.wait("gpsimd", cx.s_pe, cx.s_pe.count)
            a.wv0_tok = emit_wv1_load(cx, a, wvh, 0)
        dv_done = cx.s_dv.count
        P.wait("sync", cx.s_dv, dv_done)
        st_tok = emit_store_xcur(cx, x_out, b)
        if b == 0:
            for e in ("sync", "vector", "tensor", "scalar"):
                cx.stst.wait(e, st_tok)
    cx.stst.wait_all("sync")
    P.build()
    return nc



def build_p1():
    nc = bass.Bass("TRN2", target_bir_lowering=False, debug=False,
                   detect_race_conditions=False)
    x_in = nc.dram_tensor("x_in", [2, C, T], F32, kind="ExternalInput").ap()
    wr = nc.dram_tensor("wr", [C, C], F32, kind="ExternalInput").ap()
    wk = nc.dram_tensor("wk", [C, C], F32, kind="ExternalInput").ap()
    wv = nc.dram_tensor("wv", [C, C], F32, kind="ExternalInput").ap()
    mu = nc.dram_tensor("mu", [128, 2, CCH], F32, kind="ExternalInput").ap()
    x_out = nc.dram_tensor("x_out", [2, C, T], F32, kind="ExternalOutput").ap()

    P = Prog(nc)
    cx = alloc_common(nc, P)
    cx.lnx = nc.alloc_sbuf_tensor("lnx", [128, CCH, T + 1], BF16)
    f = alloc_ffn0(nc)
    f.p_ps = cx.pool[:, 2, 0:SEG]
    f.p2_ps = cx.pool[:, 2, SEG:2 * SEG]

    emit_consts(cx)
    mu_tok = cx.dma.start("sync", lambda q, f=f, mu=mu: q.dma_start(f.mu[:, :, :], mu))
    cx.dma.wait("vector", mu_tok)
    wdv, wtoks = emit_ffn0_weights(cx, f, wr, wk, wv)
    P.wait("tensor", cx.s_dv, wdv)

    for b in range(2):
        ld_for_batch = emit_load_xcur(cx, x_in, b)
        emit_ffn0_batch(cx, f, wdv, wtoks, ld_for_batch)
        dv_done = cx.s_dv.count
        P.wait("sync", cx.s_dv, dv_done)
        st_tok = emit_store_xcur(cx, x_out, b)
        if b == 0:
            for e in ("sync", "vector", "tensor", "scalar"):
                cx.stst.wait(e, st_tok)
    cx.stst.wait_all("sync")
    P.build()
    return nc



def _prep_mu(mu_r, mu_k):
    m = np.stack([mu_r.reshape(CCH, 128).T, mu_k.reshape(CCH, 128).T], axis=1)
    return np.ascontiguousarray(m.astype(np.float32))


def kernel(**inputs):
    f32 = lambda k: np.asarray(inputs[k]).astype(np.float32)
    xs = f32("L0_attn_hidden_states")
    wr_t = np.ascontiguousarray(f32("L0_ffn_W_r").T)
    wk_t = np.ascontiguousarray(f32("L0_ffn_W_k").T)
    wv_t = np.ascontiguousarray(f32("L0_ffn_W_v").T)
    mu = _prep_mu(f32("L0_ffn_mu_r"), f32("L0_ffn_mu_k"))
    Wqk = f32("L0_attn_Wqk")
    wqk_h = np.ascontiguousarray(np.stack(
        [np.concatenate([Wqk[:, h * 64:(h + 1) * 64],
                         Wqk[:, C + h * 64:C + (h + 1) * 64]], axis=1)
         for h in range(NH)]))
    bqk = f32("L0_attn_bqk")
    bq = np.ascontiguousarray(bqk[:C].reshape(NH, 64).T)
    bk = np.ascontiguousarray(bqk[C:].reshape(NH, 64).T)
    wv0 = np.ascontiguousarray(f32("L0_attn_Wv"))
    bv0 = np.ascontiguousarray(f32("L0_attn_bv").reshape(1, C))
    w1 = f32("L1_ffn_w1")
    w2 = f32("L1_ffn_w2")
    idx_h = np.ascontiguousarray(
        np.asarray(inputs["L1_ffn_expert_indices"]).astype(np.int32)
        .reshape(128, 32, 2).transpose(0, 2, 1).reshape(128, 64))
    ew_h = np.ascontiguousarray(f32("L1_ffn_expert_weights")
                                .reshape(128, 32, 2).transpose(0, 2, 1).reshape(128, 64))
    l1bias = np.ascontiguousarray(f32("L1_ffn_bias").reshape(CCH, 128).T)
    wpq = np.ascontiguousarray(np.concatenate(
        [f32("L1_attn_W1_q").T, f32("L1_attn_W2_q").T], axis=1))
    wpk = np.ascontiguousarray(np.concatenate(
        [f32("L1_attn_W1_k").T, f32("L1_attn_W2_k").T], axis=1))
    wv1T = f32("L1_attn_W_v").T
    wv1_h = np.ascontiguousarray(np.stack(
        [wv1T[:, h * 64:(h + 1) * 64] for h in range(NH)]))
    wout_t = np.ascontiguousarray(f32("L1_attn_W_out").T)
    bv1 = np.ascontiguousarray(f32("L1_attn_b_v").reshape(1, C))
    bout = np.ascontiguousarray(f32("L1_attn_b_out").reshape(CCH, 128).T)

    ncores = list(range(NCORES))
    xst = [np.ascontiguousarray(np.stack([xs[c].T, xs[c + 8].T]))
           for c in range(NCORES)]

    in4 = []
    for c in range(NCORES):
        gate = np.zeros((128, 2), np.float32)
        if c < 2:
            gate[:, c] = 1.0
        in4.append({"w1c": np.ascontiguousarray(w1[c]),
                    "w2c": np.ascontiguousarray(w2[c]), "idx": idx_h, "ew": ew_h,
                    "eid": np.full((128, 1), float(c), np.float32),
                    "gate": gate, "bias": l1bias})

    nc1 = build_p1()
    res = run_bass_kernel_spmd(
        nc1, [{"x_in": xst[c], "wr": wr_t, "wk": wk_t, "wv": wv_t, "mu": mu}
              for c in range(NCORES)], core_ids=ncores)
    xst = [res.results[c]["x_out"] for c in range(NCORES)]

    nc2 = build_p2()
    res = run_bass_kernel_spmd(
        nc2, [{"x_in": xst[c], "wqk_h": wqk_h, "wv": wv0, "bq": bq, "bk": bk,
               "bv": bv0} for c in range(NCORES)], core_ids=ncores)
    xst = [res.results[c]["x_out"] for c in range(NCORES)]

    res = run_bass_kernel_spmd(
        nc1, [{"x_in": xst[c], "wr": wr_t, "wk": wk_t, "wv": wv_t, "mu": mu}
              for c in range(NCORES)], core_ids=ncores)
    xst = [res.results[c]["x_out"] for c in range(NCORES)]

    nc4 = build_p4()
    res = run_bass_kernel_spmd(
        nc4, [dict(in4[c], x_in=xst[c]) for c in range(NCORES)], core_ids=ncores)
    xst = [res.results[c]["x_out"] for c in range(NCORES)]

    nc5 = build_p5()
    res = run_bass_kernel_spmd(
        nc5, [{"x_in": xst[c], "wpq": wpq, "wpk": wpk, "wvh": wv1_h,
               "wout": wout_t, "bv": bv1, "bout": bout}
              for c in range(NCORES)], core_ids=ncores)
    xst = [res.results[c]["x_out"] for c in range(NCORES)]

    res = run_bass_kernel_spmd(
        nc4, [dict(in4[c], x_in=xst[c]) for c in range(NCORES)], core_ids=ncores)
    xst = [res.results[c]["x_out"] for c in range(NCORES)]

    out = np.empty((16, T, C), dtype=np.float32)
    for c in range(NCORES):
        out[c] = xst[c][0].T
        out[c + 8] = xst[c][1].T
    return out


# version 02
# speedup vs baseline: 1.0160x; 1.0160x over previous
import math
import numpy as np
import concourse.bass as bass
import concourse.mybir as mybir
from concourse.bass_utils import run_bass_kernel_spmd

F32 = mybir.dt.float32
BF16 = mybir.dt.bfloat16
I32 = mybir.dt.int32
AF = mybir.ActivationFunctionType
ALU = mybir.AluOpType

NCORES = 8
T = 2048
C = 1024
CCH = 8
NH = 16
HD = 64
FF = 4096
SEG = 512
NSEG = T // SEG

ENGINES = ("sync", "tensor", "vector", "scalar", "gpsimd")


class SemC:

    def __init__(self, h):
        self.h = h
        self.count = 0


class Prog:

    SERIAL = ("vector", "scalar", "gpsimd")

    def __init__(self, nc):
        self.nc = nc
        self.ops = {e: [] for e in ENGINES}
        self._semid = 0
        self.esem = {}
        for e in self.SERIAL:
            self.esem[e] = self.sem(f"e_{e}")

    def sem(self, name=None):
        self._semid += 1
        h = self.nc.alloc_semaphore(name or f"s{self._semid}")
        return SemC(h)

    def emit(self, eng, fn):
        self.ops[eng].append(fn)

    def do(self, eng, fn, sem=None, n=1):
        if eng in self.SERIAL and n == 1 and (sem is None or sem is self.esem[eng]):
            s = self.esem[eng]
            if s.count > 0:
                self.ops[eng].append(lambda q, s=s, v=s.count: q.wait_ge(s.h, v))

            def wrapped(q, fn=fn, s=s):
                fn(q).then_inc(s.h, 1)
            self.ops[eng].append(wrapped)
            s.count += 1
            return s.count
        if sem is None:
            self.ops[eng].append(fn)
            return None
        def wrapped(q, fn=fn, s=sem, n=n):
            fn(q).then_inc(s.h, n)
        self.ops[eng].append(wrapped)
        sem.count += n
        return sem.count

    def wait(self, eng, sem, val=None):
        v = sem.count if val is None else val
        self.ops[eng].append(lambda q, s=sem, v=v: q.wait_ge(s.h, v))

    def barrier(self, name=None):
        s = self.sem(name)
        for e in ENGINES:
            self.do(e, lambda q: q.engine_nop() if hasattr(q, "engine_nop") else q.nop(), s, 1)
        for e in ENGINES:
            self.wait(e, s, len(ENGINES))
        return s

    def build(self):
        with self.nc.Block() as block:
            @block.sync
            def _(q):
                for fn in self.ops["sync"]:
                    fn(q)

            @block.tensor
            def _(q):
                for fn in self.ops["tensor"]:
                    fn(q)

            @block.vector
            def _(q):
                for fn in self.ops["vector"]:
                    fn(q)

            @block.scalar
            def _(q):
                for fn in self.ops["scalar"]:
                    fn(q)

            @block.gpsimd
            def _(q):
                for fn in self.ops["gpsimd"]:
                    fn(q)


class DmaStream:

    def __init__(self, P, name, rot=4):
        self.P = P
        self.sems = [P.sem(f"{name}{i}") for i in range(rot)]
        self.rot = rot
        self.idx = 0

    def start(self, eng, fn):
        s = self.sems[self.idx % self.rot]
        self.idx += 1
        if s.count > 0:
            self.P.wait(eng, s, s.count)
        self.P.do(eng, fn, s, 16)
        return (s, s.count)

    def wait(self, eng, tok):
        self.P.wait(eng, tok[0], tok[1])

    def wait_all(self, eng):
        for s in self.sems:
            if s.count > 0:
                self.P.wait(eng, s, s.count)


def _nop_inc(P, eng, sem):
    P.do(eng, lambda q: q.nop(), sem, 1)


def phase_barrier(P, name=None):
    s = P.sem(name)
    for e in ENGINES:
        _nop_inc(P, e, s)
    for e in ENGINES:
        P.wait(e, s, len(ENGINES))
    return s



class Ctx:
    pass


def alloc_common(nc, P):
    cx = Ctx()
    cx.nc = nc
    cx.P = P
    A = nc.alloc_sbuf_tensor
    cx.xcur = A("xcur", [128, CCH, T], F32)
    cx.ones1_f = A("ones1_f", [1, 128], F32)
    cx.ones128_b = A("ones128_b", [128, 1], BF16)
    cx.ident_f = A("ident_f", [128, 128], F32)
    cx.xbf = A("ln_xbf", [128, CCH, SEG], BF16)
    cx.sqbf = A("ln_sqbf", [128, CCH, SEG], BF16)
    cx.rstd = A("ln_rstd", [1, SEG], F32)
    cx.bneg = A("ln_bneg", [1, SEG], F32)
    cx.tstat = A("ln_tstat", [1, SEG], F32)
    cx.a_sb = A("ln_a_sb", [128, SEG], F32)
    cx.eps = A("ln_eps", [1, 1], F32)
    cx.ssum = A("ln_ssum", [1, SEG], F32)
    cx.b_sb = A("ln_b_sb", [128, SEG], F32)
    cx.lntmp = cx.sqbf
    cx.pool = nc.alloc_psum_tensor("pool_ps", [128, 4, 1024], F32)
    cx.s_ps = cx.pool[0:1, 0, 0:SEG]
    cx.ss_ps = cx.pool[0:1, 0, SEG:2 * SEG]
    cx.ab_ps = cx.pool[:, 1, :].rearrange("p (a s) -> p a s", a=2)

    cx.dma = DmaStream(P, "c_dma", rot=4)
    cx.gdma = DmaStream(P, "c_gdma", rot=4)
    cx.stst = DmaStream(P, "c_st", rot=2)
    cx.s_dv = P.esem["vector"]
    cx.s_act = P.esem["scalar"]
    cx.s_pool = P.esem["gpsimd"]
    cx.s_pe = P.sem("c_pe")
    return cx


def emit_consts(cx):
    P = cx.P
    P.do("vector", lambda q, c=cx: q.memset(c.ones1_f[:, :], 1.0))
    P.do("vector", lambda q, c=cx: q.memset(c.ones128_b[:, :], 1.0))
    P.do("vector", lambda q, c=cx: q.memset(c.eps[:, :], 1e-5))
    P.do("vector", lambda q, c=cx: q.memset(c.ident_f[:, :], 1.0), cx.s_dv, 1)
    P.wait("gpsimd", cx.s_dv)
    P.do("gpsimd",
         lambda q, c=cx: q.affine_select(c.ident_f[:, :], c.ident_f[:, :],
                                         [[-1, 128]], ALU.is_equal, 0.0,
                                         base=0, channel_multiplier=1),
         cx.s_pool, 1)
    cx.dv_consts = cx.s_dv.count



def xstore_view(xstore_ap, b):
    return xstore_ap[b].rearrange("(cc p) t -> p cc t", p=128)


def emit_load_xcur(cx, xstore_ap, b):
    prev = getattr(cx, "last_store", {}).get(b)
    if prev is not None:
        cx.stst.wait("sync", prev)
    return cx.dma.start("sync", lambda q, c=cx, x=xstore_ap, b=b:
                        q.dma_start(c.xcur[:, :, :], xstore_view(x, b)))


def note_store(cx, b, tok):
    if not hasattr(cx, "last_store"):
        cx.last_store = {}
    cx.last_store[b] = tok


def emit_store_xcur(cx, xstore_ap, b, eng="sync"):
    return cx.stst.start(eng, lambda q, c=cx, x=xstore_ap, b=b:
                         q.dma_start(xstore_view(x, b), c.xcur[:, :, :]))



def emit_input_transpose(cx, x_ap, b, tp_ps, stage):
    P = cx.P
    pe_before = cx.s_pe.count
    for cc in range(CCH):
        for tg4 in range(T // 512):
            P.wait("sync", cx.s_pe, cx.s_pe.count)
            tok = cx.dma.start("sync", lambda q, c=cx, x=x_ap, b=b, cc=cc, tg4=tg4, st=stage:
                               q.dma_start(
                                   st[:, :].rearrange("p (i c) -> p i c", i=4),
                                   x[b, tg4 * 512:(tg4 + 1) * 512, cc * 128:(cc + 1) * 128]
                                   .rearrange("(i p) c -> p i c", p=128)))
            cx.dma.wait("tensor", tok)
            if cc == 0 and tg4 == 0:
                P.wait("tensor", cx.s_pool, 1)
            for i in range(4):
                P.wait("tensor", cx.s_dv, cx.s_dv.count)
                P.do("tensor", lambda q, c=cx, ps=tp_ps, st=stage, i=i:
                     q.transpose(ps[:, i * 128:(i + 1) * 128],
                                 st[:, i * 128:(i + 1) * 128], c.ident_f[:, :]),
                     cx.s_pe, 1)
            pe = cx.s_pe.count
            P.wait("vector", cx.s_pe, pe)
            P.do("vector", lambda q, c=cx, ps=tp_ps, cc=cc, tg4=tg4:
                 q.tensor_copy(c.xcur[:, cc, tg4 * 512:(tg4 + 1) * 512], ps[:, :]),
                 cx.s_dv, 1)
    return cx.s_dv.count


def emit_output_transpose(cx, y_ap, b, tp_ps, stage):
    P = cx.P
    for tg in range(T // 128):
        for half in range(2):
            for j in range(4):
                cc = half * 4 + j
                P.wait("tensor", cx.s_dv, cx.s_dv.count)
                P.do("tensor", lambda q, c=cx, ps=tp_ps, cc=cc, tg=tg, j=j:
                     q.transpose(ps[:, j * 128:(j + 1) * 128],
                                 c.xcur[:, cc, tg * 128:(tg + 1) * 128],
                                 c.ident_f[:, :]),
                     cx.s_pe, 1)
            pe = cx.s_pe.count
            P.wait("vector", cx.s_pe, pe)
            P.do("vector", lambda q, st=stage, half=half, ps=tp_ps:
                 q.tensor_copy(st[:, half * 512:(half + 1) * 512], ps[:, :]),
                 cx.s_dv, 1)
        dv = cx.s_dv.count
        P.wait("sync", cx.s_dv, dv)
        tok = cx.stst.start("sync", lambda q, y=y_ap, b=b, tg=tg, st=stage:
                            q.dma_start(y[b, tg * 128:(tg + 1) * 128, :], st[:, :]))
        cx.stst.wait("vector", tok)
    return None



def emit_ln_seg(cx, t0, dst_fn, sem_out=None):
    P = cx.P
    c = cx
    P.do("vector", lambda q, c=c, t0=t0:
         q.tensor_copy(c.xbf[:, :, :], c.xcur[:, :, t0:t0 + SEG]), c.s_dv, 1)
    dv_cast = c.s_dv.count
    P.wait("scalar", c.s_dv, dv_cast)
    P.do("scalar", lambda q, c=c:
         q.activation(c.sqbf[:, :, :], c.xbf[:, :, :], AF.Square), c.s_act, 1)
    act_sq = c.s_act.count
    P.wait("tensor", c.s_dv, dv_cast)
    for cc in range(CCH):
        P.do("tensor", lambda q, c=c, cc=cc:
             q.matmul(c.s_ps[:, :], c.ones128_b[:, :], c.xbf[:, cc, :],
                      start=(cc == 0), stop=(cc == CCH - 1)),
             c.s_pe, 0 if cc < CCH - 1 else 1)
    P.wait("tensor", c.s_act, act_sq)
    for cc in range(CCH):
        P.do("tensor", lambda q, c=c, cc=cc:
             q.matmul(c.ss_ps[:, :], c.ones128_b[:, :], c.sqbf[:, cc, :],
                      start=(cc == 0), stop=(cc == CCH - 1)),
             c.s_pe, 0 if cc < CCH - 1 else 1)
    pe_stats = c.s_pe.count
    P.wait("vector", c.s_pe, pe_stats)
    P.do("vector", lambda q, c=c: q.tensor_copy(c.ssum[:, :], c.s_ps[:, :]))
    P.do("vector", lambda q, c=c:
         q.scalar_tensor_tensor(c.tstat[:, :], c.ssum[:, :], 1.0 / (1024.0 * 1024.0),
                                c.ssum[:, :], ALU.mult, ALU.mult))
    P.do("vector", lambda q, c=c:
         q.scalar_tensor_tensor(c.rstd[:, :], c.ss_ps[:, :], 1.0 / 1024.0,
                                c.tstat[:, :], ALU.mult, ALU.subtract), c.s_dv, 1)
    dv_var = c.s_dv.count
    P.wait("scalar", c.s_dv, dv_var)
    P.do("scalar", lambda q, c=c:
         q.activation(c.tstat[:, :], c.rstd[:, :], AF.Sqrt, bias=c.eps[:, :]), c.s_act, 1)
    act_std = c.s_act.count
    P.wait("vector", c.s_act, act_std)
    P.do("vector", lambda q, c=c: q.reciprocal(c.rstd[:, :], c.tstat[:, :]))
    P.do("vector", lambda q, c=c:
         q.scalar_tensor_tensor(c.bneg[:, :], c.ssum[:, :], -1.0 / 1024.0,
                                c.rstd[:, :], ALU.mult, ALU.mult), c.s_dv, 1)
    dv_stats = c.s_dv.count
    P.wait("tensor", c.s_dv, dv_stats)
    P.do("tensor", lambda q, c=c:
         q.matmul(c.ab_ps[:, 0, :], c.ones1_f[:, :], c.rstd[:, :],
                  start=True, stop=True))
    P.do("tensor", lambda q, c=c:
         q.matmul(c.ab_ps[:, 1, :], c.ones1_f[:, :], c.bneg[:, :],
                  start=True, stop=True), c.s_pe, 1)
    pe_bc = c.s_pe.count
    P.wait("scalar", c.s_pe, pe_bc)
    P.do("scalar", lambda q, c=c: q.copy(c.a_sb[:, :], c.ab_ps[:, 0, :]))
    P.do("scalar", lambda q, c=c: q.copy(c.b_sb[:, :], c.ab_ps[:, 1, :]), c.s_act, 1)
    act_ab = c.s_act.count
    P.wait("vector", c.s_act, act_ab)
    P.do("vector", lambda q, c=c:
         q.tensor_tensor(c.lntmp[:, :, :], c.xbf[:, :, :],
                         c.a_sb[:, :].unsqueeze(1).broadcast_to([128, CCH, SEG]),
                         ALU.mult))
    for cc in range(CCH):
        P.do("vector", lambda q, c=c, cc=cc, dst_fn=dst_fn:
             q.tensor_tensor(dst_fn(cc), c.lntmp[:, cc, :], c.b_sb[:, :], ALU.add),
             c.s_dv if (cc == CCH - 1) else None, 1)
    if sem_out is not None:
        _nop_inc(P, "vector", sem_out)
    return c.s_dv.count



def alloc_ffn0(nc, A=None):
    f = Ctx()
    A = A or nc.alloc_sbuf_tensor
    f.w_r = A("f_w_r", [128, CCH, C], BF16)
    f.w_k = A("f_w_k", [128, CCH, C], BF16)
    f.w_v = A("f_w_v", [128, CCH, C], BF16)
    f.mu = A("f_mu", [128, 2, CCH], F32)
    f.diff = A("f_diff", [128, CCH, SEG], BF16)
    f.xr = A("f_xr", [128, CCH, SEG], BF16)
    f.xk = f.xr
    f.r = A("f_r", [128, CCH, SEG], F32)
    f.kact = f.diff
    f.vtmp = A("f_vtmp", [128, SEG], F32)
    return f


def emit_ffn0_weights(cx, f, wr_ap, wk_ap, wv_ap):
    P = cx.P
    toks = []
    for (w_sb, w_ap) in ((f.w_r, wr_ap), (f.w_k, wk_ap), (f.w_v, wv_ap)):
        toks.append(cx.gdma.start("gpsimd", lambda q, w_sb=w_sb, w_ap=w_ap:
                    q.dma_start(w_sb[:, :, :],
                                w_ap.rearrange("(cc p) d -> p cc d", p=128))))
    for t in toks:
        cx.gdma.wait("vector", t)
    P.do("vector", lambda q, f=f:
         q.tensor_scalar_mul(f.w_v[:, :, :], f.w_v[:, :, :], 0.5), cx.s_dv, 1)
    return cx.s_dv.count, toks


def emit_ffn0_batch(cx, f, w_dv, w_toks, ld_tok):
    P = cx.P
    c = cx
    if ld_tok is not None:
        cx.dma.wait("vector", ld_tok)
    P.do("vector", lambda q, c=c: q.memset(c.lnx[:, :, 0:1], 0.0))
    for t in w_toks:
        cx.gdma.wait("tensor", t)
    for s in range(NSEG):
        t0 = s * SEG
        emit_ln_seg(cx, t0, lambda cc, c=c: c.lnx[:, cc, 1:SEG + 1])
        dv_ln = c.s_dv.count
        P.do("vector", lambda q, c=c, f=f:
             q.tensor_tensor(f.diff[:, :, :], c.lnx[:, :, 1:SEG + 1],
                             c.lnx[:, :, 0:SEG], ALU.subtract))
        for cc in range(CCH):
            P.do("vector", lambda q, c=c, f=f, cc=cc:
                 q.scalar_tensor_tensor(f.xr[:, cc, :], f.diff[:, cc, :],
                                        f.mu[:, 0, cc:cc + 1], c.lnx[:, cc, 0:SEG],
                                        ALU.mult, ALU.add))
        P.do("vector", lambda q, c=c, f=f:
             q.engine_nop(), c.s_dv, 1)
        dv_lerp = c.s_dv.count

        P.wait("tensor", c.s_dv, dv_lerp)
        for dg in range(CCH):
            ps = f.p_ps if dg % 2 == 0 else f.p2_ps
            P.wait("tensor", c.s_act, c.s_act.count)
            for ccc in range(CCH):
                P.do("tensor", lambda q, f=f, ps=ps, dg=dg, ccc=ccc:
                     q.matmul(ps[:, :], f.w_r[:, ccc, dg * 128:(dg + 1) * 128],
                              f.xr[:, ccc, :], start=(ccc == 0), stop=(ccc == CCH - 1)),
                     c.s_pe, 0 if ccc < CCH - 1 else 1)
            pe = c.s_pe.count
            P.wait("scalar", c.s_pe, pe)
            P.do("scalar", lambda q, f=f, ps=ps, dg=dg:
                 q.activation(f.r[:, dg, :], ps[:, :], AF.Sigmoid), c.s_act, 1)
        pe_r_done = c.s_pe.count
        P.wait("vector", c.s_pe, pe_r_done)
        for cc in range(CCH):
            P.do("vector", lambda q, c=c, f=f, cc=cc:
                 q.scalar_tensor_tensor(f.xk[:, cc, :], f.diff[:, cc, :],
                                        f.mu[:, 1, cc:cc + 1], c.lnx[:, cc, 0:SEG],
                                        ALU.mult, ALU.add))
        P.do("vector", lambda q, c=c:
             q.tensor_copy(c.lnx[:, :, 0:1], c.lnx[:, :, SEG:SEG + 1]), c.s_dv, 1)
        dv_xk = c.s_dv.count
        P.wait("tensor", c.s_dv, dv_xk)
        for dg in range(CCH):
            ps = f.p_ps if dg % 2 == 0 else f.p2_ps
            P.wait("tensor", c.s_act, c.s_act.count)
            for ccc in range(CCH):
                P.do("tensor", lambda q, f=f, ps=ps, dg=dg, ccc=ccc:
                     q.matmul(ps[:, :], f.w_k[:, ccc, dg * 128:(dg + 1) * 128],
                              f.xk[:, ccc, :], start=(ccc == 0), stop=(ccc == CCH - 1)),
                     c.s_pe, 0 if ccc < CCH - 1 else 1)
            pe = c.s_pe.count
            P.wait("scalar", c.s_pe, pe)
            P.do("scalar", lambda q, f=f, ps=ps, dg=dg:
                 q.activation(f.kact[:, dg, :], ps[:, :], AF.Relu), c.s_act, 1)
        act_relu = c.s_act.count
        P.wait("scalar", c.s_act, act_relu)
        P.do("scalar", lambda q, f=f:
             q.activation(f.kact[:, :, :], f.kact[:, :, :], AF.Square), c.s_act, 1)
        act_sq = c.s_act.count
        P.wait("tensor", c.s_act, act_sq)
        for dg in range(CCH):
            ps = f.p_ps if dg % 2 == 0 else f.p2_ps
            P.wait("tensor", c.s_dv, c.s_dv.count)
            for ccc in range(CCH):
                P.do("tensor", lambda q, f=f, ps=ps, dg=dg, ccc=ccc:
                     q.matmul(ps[:, :], f.w_v[:, ccc, dg * 128:(dg + 1) * 128],
                              f.kact[:, ccc, :], start=(ccc == 0), stop=(ccc == CCH - 1)),
                     c.s_pe, 0 if ccc < CCH - 1 else 1)
            pe = c.s_pe.count
            P.wait("vector", c.s_pe, pe)
            P.do("vector", lambda q, f=f, ps=ps, dg=dg:
                 q.tensor_tensor(f.vtmp[:, :], ps[:, :], f.r[:, dg, :], ALU.mult))
            P.do("vector", lambda q, c=c, f=f, dg=dg, t0=t0:
                 q.tensor_tensor(c.xcur[:, dg, t0:t0 + SEG],
                                 c.xcur[:, dg, t0:t0 + SEG], f.vtmp[:, :], ALU.add),
                 c.s_dv, 1)
    return c.s_dv.count



NKC = T // 128
QS = 1024
NQS = T // QS
RT8 = 0.35355339059327373


def alloc_attn0(nc, A=None):
    a = Ctx()
    A = A or nc.alloc_sbuf_tensor
    a.w_v0 = A("a_wv", [128, CCH, C], BF16)
    a.wqk = A("a_wqk", [128, 2, CCH, 128], BF16)
    a.bq = A("a_bq", [64, NH], F32)
    a.bk = A("a_bk", [64, NH], F32)
    a.bvrow = A("a_bvrow", [1, C], F32)
    a.bvb = A("a_bvb", [128, C], F32)
    a.qp = A("a_qp", [65, T], BF16)
    a.kp = A("a_kp", [65, T], BF16)
    a.vp = A("a_vp", [128, NKC, 65], BF16)
    a.qsq = A("a_qsq", [64, T], BF16)
    a.ksq = A("a_ksq", [64, T], BF16)
    a.kmax = A("a_kmax", [64, 1], BF16)
    a.negmk2h = A("a_negmk2h", [1, 1], F32)
    a.pt = A("a_pt", [128, 2, QS], BF16)
    a.rl = A("a_rl", [1, QS], F32)
    a.rb = A("a_rb", [64, QS], F32)
    a.otmp = A("a_otmp", [128, QS], F32)
    return a


def emit_attn0_weights(cx, a, wv_ap, bq_ap, bk_ap, bv_ap):
    P = cx.P
    wv_tok = cx.gdma.start("gpsimd", lambda q, a=a, w=wv_ap:
                           q.dma_start(a.w_v0[:, :, :],
                                       w.rearrange("(cc p) d -> p cc d", p=128)))
    a.wv_tok = wv_tok
    t1 = cx.dma.start("sync", lambda q, a=a, b=bq_ap: q.dma_start(a.bq[:, :], b))
    t2 = cx.dma.start("sync", lambda q, a=a, b=bk_ap: q.dma_start(a.bk[:, :], b))
    t3 = cx.dma.start("sync", lambda q, a=a, b=bv_ap: q.dma_start(a.bvrow[:, :], b))
    for t in (t1, t2, t3):
        cx.dma.wait("vector", t)
    P.do("vector", lambda q, a=a: q.tensor_scalar_mul(a.bq[:, :], a.bq[:, :], RT8))
    P.do("vector", lambda q, a=a: q.tensor_scalar_mul(a.bk[:, :], a.bk[:, :], RT8),
         cx.s_dv, 1)
    bb = cx.pool[:, 2, :]
    cx.dma.wait("tensor", t3)
    P.wait("tensor", cx.s_dv, cx.dv_consts)
    for s in range(2):
        P.do("tensor", lambda q, c=cx, a=a, s=s, bb=bb:
             q.matmul(bb[:, s * 512:(s + 1) * 512], c.ones1_f[:, :],
                      a.bvrow[0:1, s * 512:(s + 1) * 512], start=True, stop=True),
             cx.s_pe, 1)
    pe = cx.s_pe.count
    P.wait("scalar", cx.s_pe, pe)
    P.do("scalar", lambda q, a=a, bb=bb: q.copy(a.bvb[:, :], bb[:, :]), cx.s_act, 1)
    return cx.s_act.count


def emit_wqk_load(cx, a, wqk_heads_ap, h):
    return cx.gdma.start("gpsimd", lambda q, a=a, w=wqk_heads_ap, h=h:
                         q.dma_start(a.wqk[:, h % 2, :, :],
                                     w[h].rearrange("(cc p) d -> p cc d", p=128)))


def emit_attn0_batch(cx, a, wqk_heads_ap, wg_first):
    P = cx.P
    c = cx
    pool = c.pool
    st_pair = [pool[:, 0, :], pool[:, 1, :]]
    o_ps = pool[0:65, 2, :]
    rb_ps = pool[0:64, 3, :]
    q_ps = pool[0:64, 2, :]
    k_ps = pool[0:64, 3, :]
    v_ps = pool[:, 2, 0:64]
    qs2_ps = pool[0:1, 0:2, :].rearrange("p a s -> p (a s)")
    mk2_ps = pool[0:1, 3, 512:513]

    P.do("vector", lambda q, a=a: q.memset(a.vp[:, :, 64:65], 1.0))
    P.do("vector", lambda q, a=a: q.memset(a.kp[64:65, :], 1.0), c.s_dv, 1)

    wg_prev = wg_first
    for h in range(NH):
        par = h % 2
        if h + 1 < NH:
            P.wait("gpsimd", c.s_pe, c.s_pe.count)
            wg_next = emit_wqk_load(cx, a, wqk_heads_ap, h + 1)
        c.gdma.wait("tensor", wg_prev)
        for half in range(2):
            P.wait("tensor", c.s_dv, c.s_dv.count)
            for s2 in range(2):
                t0 = half * 1024 + s2 * 512
                for ccc in range(CCH):
                    P.do("tensor", lambda q, a=a, par=par, ccc=ccc, s2=s2, t0=t0, q_ps=q_ps:
                         q.matmul(q_ps[:, s2 * 512:(s2 + 1) * 512],
                                  a.wqk[:, par, ccc, 0:64], c.lnx[:, ccc, t0:t0 + 512],
                                  start=(ccc == 0), stop=(ccc == CCH - 1)),
                         c.s_pe, 0 if ccc < CCH - 1 else 1)
            pe = c.s_pe.count
            P.wait("vector", c.s_pe, pe)
            P.do("vector", lambda q, a=a, h=h, half=half, q_ps=q_ps:
                 q.tensor_scalar(a.qp[0:64, half * 1024:(half + 1) * 1024], q_ps[:, :],
                                 RT8, a.bq[:, h:h + 1], ALU.mult, ALU.add), c.s_dv, 1)
        for half in range(2):
            P.wait("tensor", c.s_act, c.s_act.count)
            P.wait("tensor", c.s_dv, c.s_dv.count)
            for s2 in range(2):
                t0 = half * 1024 + s2 * 512
                for ccc in range(CCH):
                    P.do("tensor", lambda q, a=a, par=par, ccc=ccc, s2=s2, t0=t0, k_ps=k_ps:
                         q.matmul(k_ps[:, s2 * 512:(s2 + 1) * 512],
                                  a.wqk[:, par, ccc, 64:128], c.lnx[:, ccc, t0:t0 + 512],
                                  start=(ccc == 0), stop=(ccc == CCH - 1)),
                         c.s_pe, 0 if ccc < CCH - 1 else 1)
            pe = c.s_pe.count
            P.wait("vector", c.s_pe, pe)
            P.do("vector", lambda q, a=a, h=h, half=half, k_ps=k_ps:
                 q.tensor_scalar(a.kp[0:64, half * 1024:(half + 1) * 1024], k_ps[:, :],
                                 RT8, a.bk[:, h:h + 1], ALU.mult, ALU.add), c.s_dv, 1)
        dv_qk = c.s_dv.count
        P.wait("scalar", c.s_dv, dv_qk)
        P.do("scalar", lambda q, a=a:
             q.activation(a.qsq[:, :], a.qp[0:64, :], AF.Square))
        P.do("scalar", lambda q, a=a:
             q.activation(a.ksq[:, :], a.kp[0:64, :], AF.Square), c.s_act, 1)
        act_sq = c.s_act.count
        P.wait("tensor", c.s_act, act_sq)
        for s in range(4):
            P.do("tensor", lambda q, c=c, a=a, s=s, qs2_ps=qs2_ps:
                 q.matmul(qs2_ps[:, s * 512:(s + 1) * 512], c.ones128_b[0:64, :],
                          a.qsq[:, s * 512:(s + 1) * 512], start=True, stop=True),
                 c.s_pe, 0 if s < 3 else 1)
        P.wait("vector", c.s_act, act_sq)
        P.do("vector", lambda q, a=a:
             q.reduce_max(a.kmax[:, :], a.ksq[:, :], mybir.AxisListType.X), c.s_dv, 1)
        dv_kmax = c.s_dv.count
        P.wait("tensor", c.s_dv, dv_kmax)
        P.do("tensor", lambda q, c=c, a=a, mk2_ps=mk2_ps:
             q.matmul(mk2_ps, c.ones128_b[0:64, :], a.kmax[:, :],
                      start=True, stop=True), c.s_pe, 1)
        pe_bound = c.s_pe.count
        P.wait("scalar", c.s_pe, pe_bound)
        P.do("scalar", lambda q, a=a, mk2_ps=mk2_ps:
             q.mul(a.negmk2h[:, :], mk2_ps, -0.5), c.s_act, 1)
        act_mk2 = c.s_act.count
        P.wait("vector", c.s_act, act_mk2)
        P.do("vector", lambda q, a=a, qs2_ps=qs2_ps:
             q.tensor_scalar(a.qp[64:65, :], qs2_ps, -0.5, a.negmk2h[:, :],
                             ALU.mult, ALU.add), c.s_dv, 1)
        dv_bound = c.s_dv.count
        for tg in range(NKC):
            P.wait("tensor", c.s_dv, c.s_dv.count)
            for ccc in range(CCH):
                P.do("tensor", lambda q, c=c, a=a, tg=tg, ccc=ccc, h=h, v_ps=v_ps:
                     q.matmul(v_ps, c.lnx[:, ccc, tg * 128:(tg + 1) * 128],
                              a.w_v0[:, ccc, h * 64:(h + 1) * 64],
                              start=(ccc == 0), stop=(ccc == CCH - 1)),
                     c.s_pe, 0 if ccc < CCH - 1 else 1)
            pe = c.s_pe.count
            P.wait("vector", c.s_pe, pe)
            P.do("vector", lambda q, a=a, tg=tg, h=h, v_ps=v_ps:
                 q.tensor_tensor(a.vp[:, tg, 0:64], v_ps,
                                 a.bvb[:, h * 64:(h + 1) * 64], ALU.add), c.s_dv, 1)
        for qs in range(NQS):
            nkc = (qs + 1) * (QS // 128)
            for kc in range(nkc):
                sp = kc % 2
                stv = st_pair[sp]
                if kc < 2 and qs == 0:
                    P.wait("tensor", c.s_dv, dv_bound)
                P.wait("tensor", c.s_act, c.s_act.count)
                for s2 in range(2):
                    P.do("tensor", lambda q, a=a, kc=kc, qs=qs, s2=s2, stv=stv:
                         q.matmul(stv[:, s2 * 512:(s2 + 1) * 512],
                                  a.kp[:, kc * 128:(kc + 1) * 128],
                                  a.qp[:, qs * QS + s2 * 512:qs * QS + (s2 + 1) * 512],
                                  start=True, stop=True),
                         c.s_pe, 0 if s2 == 0 else 1)
                pe_st = c.s_pe.count
                P.wait("scalar", c.s_pe, pe_st)
                P.do("scalar", lambda q, a=a, sp=sp, stv=stv:
                     q.activation(a.pt[:, sp, :], stv[:, :], AF.Exp), c.s_act, 1)
                act_exp = c.s_act.count
                diag = (kc >= qs * (QS // 128))
                if diag:
                    P.wait("gpsimd", c.s_act, act_exp)
                    P.do("gpsimd", lambda q, a=a, sp=sp, qs=qs, kc=kc:
                         q.affine_select(a.pt[:, sp, :], a.pt[:, sp, :],
                                         [[1, QS]], ALU.is_ge, 0.0,
                                         base=qs * QS - kc * 128,
                                         channel_multiplier=-1), c.s_pool, 1)
                    P.wait("tensor", c.s_pool, c.s_pool.count)
                else:
                    P.wait("tensor", c.s_act, act_exp)
                if kc == 0:
                    P.wait("tensor", c.s_dv, c.s_dv.count)
                for s2 in range(2):
                    P.do("tensor", lambda q, a=a, kc=kc, sp=sp, s2=s2, nkc=nkc, o_ps=o_ps:
                         q.matmul(o_ps[:, s2 * 512:(s2 + 1) * 512],
                                  a.vp[:, kc, :], a.pt[:, sp, s2 * 512:(s2 + 1) * 512],
                                  start=(kc == 0), stop=(kc == nkc - 1)),
                         c.s_pe, 0 if (s2 == 0 or kc < nkc - 1) else 1)
            pe_pv = c.s_pe.count
            P.wait("vector", c.s_pe, pe_pv)
            P.do("vector", lambda q, a=a, o_ps=o_ps:
                 q.reciprocal(a.rl[:, :], o_ps[64:65, :]), c.s_dv, 1)
            dv_rl = c.s_dv.count
            P.wait("tensor", c.s_dv, dv_rl)
            for s2 in range(2):
                P.do("tensor", lambda q, c=c, a=a, s2=s2, rb_ps=rb_ps:
                     q.matmul(rb_ps[:, s2 * 512:(s2 + 1) * 512], c.ones1_f[:, 0:64],
                              a.rl[:, s2 * 512:(s2 + 1) * 512], start=True, stop=True),
                     c.s_pe, 0 if s2 == 0 else 1)
            pe_rb = c.s_pe.count
            P.wait("scalar", c.s_pe, pe_rb)
            P.do("scalar", lambda q, a=a, rb_ps=rb_ps:
                 q.copy(a.rb[:, :], rb_ps), c.s_act, 1)
            act_rb = c.s_act.count
            P.wait("vector", c.s_act, act_rb)
            base = (h % 2) * 64
            cc_t = h // 2
            P.do("vector", lambda q, a=a, base=base, o_ps=o_ps:
                 q.tensor_tensor(a.otmp[base:base + 64, :], o_ps[0:64, :],
                                 a.rb[:, :], ALU.mult))
            P.do("vector", lambda q, c=c, a=a, base=base, cc_t=cc_t, qs=qs:
                 q.tensor_tensor(c.xcur[base:base + 64, cc_t, qs * QS:(qs + 1) * QS],
                                 c.xcur[base:base + 64, cc_t, qs * QS:(qs + 1) * QS],
                                 a.otmp[base:base + 64, :], ALU.add), c.s_dv, 1)
        if h + 1 < NH:
            wg_prev = wg_next
    return c.s_dv.count


def plan_p2(nc, P, cx, A, wqk_h, wv, bq, bk, bv, x_in, x_out):
    if A is not None:
        cx.lnx = A("lnx", [128, CCH, T + 1], BF16)
    a = alloc_attn0(nc, A)
    emit_attn0_weights(cx, a, wv, bq, bk, bv)
    wg0 = emit_wqk_load(cx, a, wqk_h, 0)

    wg_for_batch = wg0
    for b in range(2):
        ld_tok = emit_load_xcur(cx, x_in, b)
        cx.dma.wait("vector", ld_tok)
        for s in range(NSEG):
            emit_ln_seg(cx, s * SEG, lambda cc, c=cx, s=s:
                        c.lnx[:, cc, s * SEG:(s + 1) * SEG])
        P.wait("tensor", cx.s_dv, cx.s_dv.count)
        cx.gdma.wait("tensor", a.wv_tok)
        emit_attn0_batch(cx, a, wqk_h, wg_for_batch)
        if b == 0:
            P.wait("gpsimd", cx.s_pe, cx.s_pe.count)
            wg_for_batch = emit_wqk_load(cx, a, wqk_h, 0)
        dv_done = cx.s_dv.count
        P.wait("sync", cx.s_dv, dv_done)
        st_tok = emit_store_xcur(cx, x_out, b)
        note_store(cx, b, st_tok)
        if b == 0:
            for e in ("sync", "vector", "tensor", "scalar"):
                cx.stst.wait(e, st_tok)


def build_p2():
    nc = bass.Bass("TRN2", target_bir_lowering=False, debug=False,
                   detect_race_conditions=False)
    x_in = nc.dram_tensor("x_in", [2, C, T], F32, kind="ExternalInput").ap()
    wqk_h = nc.dram_tensor("wqk_h", [NH, C, 128], F32, kind="ExternalInput").ap()
    wv = nc.dram_tensor("wv", [C, C], F32, kind="ExternalInput").ap()
    bq = nc.dram_tensor("bq", [64, NH], F32, kind="ExternalInput").ap()
    bk = nc.dram_tensor("bk", [64, NH], F32, kind="ExternalInput").ap()
    bv = nc.dram_tensor("bv", [1, C], F32, kind="ExternalInput").ap()
    x_out = nc.dram_tensor("x_out", [2, C, T], F32, kind="ExternalOutput").ap()
    P = Prog(nc)
    cx = alloc_common(nc, P)
    cx.lnx = nc.alloc_sbuf_tensor("lnx", [128, CCH, T + 1], BF16)
    emit_consts(cx)
    plan_p2(nc, P, cx, None, wqk_h, wv, bq, bk, bv, x_in, x_out)
    cx.stst.wait_all("sync")
    P.build()
    return nc



MT = 4096
P4_DEBUG = False
TTS = 512
NTT = MT // TTS


def alloc_moe(nc, A=None):
    m = Ctx()
    A = A or nc.alloc_sbuf_tensor
    m.idx = A("m_idx", [128, 64], I32)
    m.idxf = A("m_idxf", [128, 64], F32)
    m.mcmp = A("m_mcmp", [128, 64], F32)
    m.ew = A("m_ew", [128, 64], F32)
    m.maskp = A("m_maskp", [128, 32], F32)
    m.eid = A("m_eid", [128, 1], F32)
    m.gate = A("m_gate", [128, 2], F32)
    m.bias = A("m_bias", [128, CCH], F32)
    m.mrow = A("m_mrow", [1, 2, TTS], F32)
    m.lns = A("m_lns", [128, CCH, SEG], BF16)
    m.wbb = A("m_wbb", [128, FF], BF16)
    m.hbf = m.wbb[:, :].rearrange("p (a t) -> p a t", a=CCH)
    m.w1f = A("m_w1f", [128, 2, CCH, 1024], BF16)
    m.w2f = A("m_w2f", [128, CCH, 1024], BF16)
    m.xm = A("m_xm", [128, CCH, TTS], BF16)
    m.ysb = A("m_ysb", [128, CCH, TTS], F32)
    m.rsb = m.ysb
    return m


def emit_moe_mask(cx, m):
    P = cx.P
    P.do("vector", lambda q, m=m: q.tensor_copy(m.idxf[:, :], m.idx[:, :]))
    P.do("vector", lambda q, m=m:
         q.tensor_scalar(m.mcmp[:, :], m.idxf[:, :], m.eid[:, :], None, ALU.is_equal))
    P.do("vector", lambda q, m=m:
         q.tensor_tensor(m.mcmp[:, :], m.mcmp[:, :], m.ew[:, :], ALU.mult))
    P.do("vector", lambda q, m=m:
         q.tensor_copy(m.maskp[:, :], m.mcmp[:, 32:64]))
    P.do("vector", lambda q, m=m:
         q.tensor_tensor(m.maskp[:, :], m.maskp[:, :], m.mcmp[:, 0:32],
                         ALU.add))
    P.do("vector", lambda q, m=m:
         q.tensor_scalar_mul(m.maskp[:, :], m.maskp[:, :], 0.5), cx.s_dv, 1)
    P.wait("sync", cx.s_dv, cx.s_dv.count)
    t = cx.stst.start("sync", lambda q, m=m, mask_d=m.mask_d:
                      q.dma_start(mask_d.rearrange("o (p f) -> (o p) f", p=128),
                                  m.maskp[:, :]))
    return t


def emit_weight_cache(cx, m, w_ap, wbf_ap, rows):
    toks = []
    for r0 in range(0, rows, 128):
        t1 = cx.gdma.start("gpsimd", lambda q, m=m, w=w_ap, r0=r0:
                           q.dma_start(m.wbb[:, 0:w.shape[1]], w[r0:r0 + 128, :]))
        cx.gdma.wait("sync", t1)
        t2 = cx.stst.start("sync", lambda q, m=m, w=wbf_ap, r0=r0:
                           q.dma_start(w[r0:r0 + 128, :], m.wbb[:, 0:w.shape[1]]))
        cx.stst.wait("gpsimd", t2)
        toks.append(t2)
    return toks


def plan_p4(nc, P, cx, A, w1c, w2c, idx, ew, eid, gate, bias, dd,
            x_in, x_out, convert=True):
    w1bf = dd["w1bf"]; w2bf = dd["w2bf"]; cc_in = dd["cc_in"]
    cc_out = dd["cc_out"]; arin = dd["arin"]; arout = dd["arout"]
    mask_d = dd["mask_d"]
    m = alloc_moe(nc, A)
    m.mask_d = mask_d
    h_ps = [cx.pool[:, 2, 0:512], cx.pool[:, 2, 512:1024]]
    y_ps = [cx.pool[:, 3, 0:512], cx.pool[:, 3, 512:1024]]
    mb_ps = cx.pool[:, 0, 0:512]

    emit_consts(cx)
    for (sb, ap) in ((m.idx, idx), (m.ew, ew), (m.eid, eid), (m.gate, gate),
                     (m.bias, bias)):
        t = cx.dma.start("sync", lambda q, sb=sb, ap=ap: q.dma_start(sb[:, :], ap))
        cx.dma.wait("vector", t)
    mask_tok = emit_moe_mask(cx, m)
    P.do("vector", lambda q, m=m:
         q.tensor_scalar_mul(m.bias[:, :], m.bias[:, :], 0.5), cx.s_dv, 1)
    dv_pre = cx.s_dv.count

    if convert:
        w1toks = emit_weight_cache(cx, m, w1c, w1bf, C)
        w2toks = emit_weight_cache(cx, m, w2c, w2bf, FF)

    ldA = emit_load_xcur(cx, x_in, 0)
    cx.dma.wait("vector", ldA)
    ctoks = []
    for s in range(NSEG):
        emit_ln_seg(cx, s * SEG, lambda cc, m=m: m.lns[:, cc, :])
        dv_ln = cx.s_dv.count
        P.wait("sync", cx.s_dv, dv_ln)
        ct = cx.stst.start("sync", lambda q, m=m, cc_in=cc_in, s=s:
                           q.dma_start(cc_in[:, s * SEG:(s + 1) * SEG]
                                       .rearrange("(cc p) t -> p cc t", p=128),
                                       m.lns[:, :, :]))
        ctoks.append(ct)
        cx.stst.wait("vector", ct)
    cc_sem = P.sem()
    for ct in ctoks:
        cx.stst.wait("gpsimd", ct)
    P.do("gpsimd", lambda q, cc_in=cc_in, cc_out=cc_out:
         q.collective_compute("AllGather", ALU.bypass,
                              replica_groups=[list(range(NCORES))],
                              ins=[cc_in.opt()], outs=[cc_out.opt()]),
         cc_sem, 1)
    P.wait("sync", cc_sem, 1)

    if convert:
        for t in w1toks + w2toks:
            cx.stst.wait("gpsimd", t)
        cx.stst.wait("scalar", w1toks[-1])
        cx.stst.wait("scalar", w2toks[-1])
    ytoks = []
    for tt in range(NTT):
        rank = tt // (NTT // 2)
        col0 = (tt % (NTT // 2)) * TTS
        P.wait("sync", cx.s_pe, cx.s_pe.count)
        xt = cx.dma.start("sync", lambda q, m=m, cc_out=cc_out, rank=rank, col0=col0:
                          q.dma_start(m.xm[:, :, :],
                                      cc_out[rank * C:(rank + 1) * C,
                                             col0:col0 + TTS]
                                      .rearrange("(cc p) t -> p cc t", p=128)))
        cx.dma.wait("tensor", xt)
        mpar = tt % 2
        if tt == 0:
            cx.stst.wait("sync", mask_tok)
        P.wait("sync", cx.s_pe, cx.s_pe.count)
        mrt = cx.dma.start("sync", lambda q, m=m, tt=tt, mpar=mpar:
                           q.dma_start(m.mrow[:, mpar, :],
                                       m.mask_d[:, tt * TTS:(tt + 1) * TTS]))
        if tt == 0:
            P.wait("tensor", cx.s_dv, dv_pre)
        cx.dma.wait("tensor", mrt)
        P.wait("tensor", cx.s_dv, cx.s_dv.count)
        P.do("tensor", lambda q, c=cx, m=m, tt=tt, mpar=mpar:
             q.matmul(mb_ps, c.ones1_f[:, :], m.mrow[:, mpar, :],
                      start=True, stop=True), cx.s_pe, 1)
        for fc in range(4):
            par = fc % 2
            P.wait("gpsimd", cx.s_pe, cx.s_pe.count)
            w1t = cx.gdma.start("gpsimd", lambda q, m=m, w1bf=w1bf, fc=fc, par=par:
                                q.dma_start(m.w1f[:, par, :, :],
                                            w1bf[:, fc * 1024:(fc + 1) * 1024]
                                            .rearrange("(cc p) f -> p cc f", p=128)))
            cx.gdma.wait("tensor", w1t)
            if P4_DEBUG and tt == 0 and fc == 0:
                dbg_xm = nc.dram_tensor("dbg_xm", [128, CCH, TTS], F32,
                                        kind="ExternalOutput").ap()
                cx.dma.wait("gpsimd", xt)
                dxm = cx.stst.start("gpsimd", lambda q, m=m, o=dbg_xm:
                                    q.dma_start(o, m.xm[:, :, :]))
                cx.stst.wait("gpsimd", dxm)
            for mg in range(CCH):
                hp = h_ps[mg % 2]
                P.wait("tensor", cx.s_act, cx.s_act.count)
                for ccc in range(CCH):
                    P.do("tensor", lambda q, m=m, par=par, mg=mg, ccc=ccc, hp=hp:
                         q.matmul(hp, m.w1f[:, par, ccc, mg * 128:(mg + 1) * 128],
                                  m.xm[:, ccc, :],
                                  start=(ccc == 0), stop=(ccc == CCH - 1)),
                         cx.s_pe, 0 if ccc < CCH - 1 else 1)
                pe = cx.s_pe.count
                P.wait("scalar", cx.s_pe, pe)
                P.do("scalar", lambda q, m=m, mg=mg, hp=hp:
                     q.activation(m.hbf[:, mg, :], hp, AF.Gelu), cx.s_act, 1)
            act_h = cx.s_act.count
            if P4_DEBUG and tt == 0 and fc == 0:
                dbg_h = nc.dram_tensor("dbg_h", [128, CCH, TTS], F32,
                                       kind="ExternalOutput").ap()
                P.wait("gpsimd", cx.s_act, act_h)
                dh = cx.stst.start("gpsimd", lambda q, m=m, o=dbg_h:
                                   q.dma_start(o, m.hbf[:, :, :]))
                cx.stst.wait("gpsimd", dh)
                cx.stst.wait("scalar", dh)
                cx.stst.wait("tensor", dh)
            P.wait("gpsimd", cx.s_pe, cx.s_pe.count)
            w2t = cx.gdma.start("gpsimd", lambda q, m=m, w2bf=w2bf, fc=fc:
                                q.dma_start(m.w2f[:, :, :],
                                            w2bf[fc * 1024:(fc + 1) * 1024, :]
                                            .rearrange("(fm p) d -> p fm d", p=128)))
            cx.gdma.wait("tensor", w2t)
            P.wait("tensor", cx.s_act, act_h)
            for dg in range(CCH):
                yp = y_ps[dg % 2]
                P.wait("tensor", cx.s_dv, cx.s_dv.count)
                for fm in range(CCH):
                    P.do("tensor", lambda q, m=m, fm=fm, dg=dg, yp=yp:
                         q.matmul(yp,
                                  m.w2f[:, fm, dg * 128:(dg + 1) * 128],
                                  m.hbf[:, fm, :],
                                  start=(fm == 0), stop=(fm == CCH - 1)),
                         cx.s_pe, 0 if fm < CCH - 1 else 1)
                pe = cx.s_pe.count
                P.wait("vector", cx.s_pe, pe)
                if fc == 0:
                    P.do("vector", lambda q, m=m, dg=dg, yp=yp:
                         q.tensor_copy(m.ysb[:, dg, :], yp))
                else:
                    P.do("vector", lambda q, m=m, dg=dg, yp=yp:
                         q.tensor_tensor(m.ysb[:, dg, :], m.ysb[:, dg, :],
                                         yp, ALU.add))
        if P4_DEBUG and tt == 0:
            dbg_y0 = nc.dram_tensor("dbg_y0", [128, CCH, TTS], F32,
                                    kind="ExternalOutput").ap()
            P.wait("sync", cx.s_dv, cx.s_dv.count)
            dy0 = cx.stst.start("sync", lambda q, m=m, o=dbg_y0:
                                q.dma_start(o, m.ysb[:, :, :]))
            cx.stst.wait("sync", dy0)
            cx.stst.wait("vector", dy0)
        pe_mb = cx.s_pe.count
        P.wait("vector", cx.s_pe, pe_mb)
        P.do("vector", lambda q, m=m, mb_ps=mb_ps:
             q.tensor_tensor(m.ysb[:, :, :], m.ysb[:, :, :],
                             mb_ps[:, :].unsqueeze(1).broadcast_to([128, CCH, TTS]),
                             ALU.mult), cx.s_dv, 1)
        dv_y = cx.s_dv.count
        P.wait("sync", cx.s_dv, dv_y)
        yt = cx.stst.start("sync", lambda q, m=m, arin=arin, tt=tt:
                           q.dma_start(arin[:, tt * TTS:(tt + 1) * TTS]
                                       .rearrange("(cc p) t -> p cc t", p=128),
                                       m.ysb[:, :, :]))
        ytoks.append(yt)
        cx.stst.wait("vector", yt)

    if P4_DEBUG:
        dbg_maskp = nc.dram_tensor("dbg_maskp", [128, 32], F32, kind="ExternalOutput").ap()
        dbg_mcmp = nc.dram_tensor("dbg_mcmp", [128, 64], F32, kind="ExternalOutput").ap()
        dbg_idxf = nc.dram_tensor("dbg_idxf", [128, 64], F32, kind="ExternalOutput").ap()
        dbg_ew = nc.dram_tensor("dbg_ew", [128, 64], F32, kind="ExternalOutput").ap()
        dbg_mask = nc.dram_tensor("dbg_mask", [1, MT], F32, kind="ExternalOutput").ap()
        dbg_gather = nc.dram_tensor("dbg_gather", [C, T], F32, kind="ExternalOutput").ap()
        dbg_arin = nc.dram_tensor("dbg_arin", [C, MT], F32, kind="ExternalOutput").ap()
        cx.stst.wait("sync", mask_tok)
        dt0 = cx.stst.start("sync", lambda q: q.dma_start(dbg_maskp, m.maskp[:, :]))
        dta = cx.stst.start("sync", lambda q: q.dma_start(dbg_mcmp, m.mcmp[:, :]))
        dtb = cx.stst.start("sync", lambda q: q.dma_start(dbg_idxf, m.idxf[:, :]))
        dtc = cx.stst.start("sync", lambda q: q.dma_start(dbg_ew, m.ew[:, :]))
        for _t in (dta, dtb, dtc):
            cx.stst.wait("sync", _t)
        dt1 = cx.stst.start("sync", lambda q: q.dma_start(dbg_mask, m.mask_d))
        cx.stst.wait("sync", dt0)
        P.wait("gpsimd", cc_sem, 1)
        dt2 = cx.stst.start("gpsimd", lambda q: q.dma_start(dbg_gather, cc_out[0:C, :]))
        for t in ytoks:
            cx.stst.wait("sync", t)
        dt3 = cx.stst.start("sync", lambda q: q.dma_start(dbg_arin, arin))
        cx.stst.wait("sync", dt1)
        cx.stst.wait("sync", dt3)
        cx.stst.wait("gpsimd", dt2)
    ar_sem = P.sem()
    for t in ytoks:
        cx.stst.wait("gpsimd", t)
    P.do("gpsimd", lambda q, arin=arin, arout=arout:
         q.collective_compute("AllReduce", ALU.add,
                              replica_groups=[list(range(NCORES))],
                              ins=[arin.opt()], outs=[arout.opt()]),
         ar_sem, 1)
    P.wait("sync", ar_sem, 1)

    if P4_DEBUG:
        dbg_arout = nc.dram_tensor("dbg_arout", [C, MT], F32, kind="ExternalOutput").ap()
        P.wait("sync", ar_sem, 1)
        dao = cx.stst.start("sync", lambda q: q.dma_start(dbg_arout, arout))
        cx.stst.wait("sync", dao)
    for half in range(2):
        for s in range(NSEG):
            P.wait("sync", cx.s_dv, cx.s_dv.count)
            rt = cx.dma.start("sync", lambda q, m=m, arout=arout, half=half, s=s:
                              q.dma_start(m.rsb[:, :, :],
                                          arout[:, half * T + s * SEG:
                                                half * T + (s + 1) * SEG]
                                          .rearrange("(cc p) t -> p cc t", p=128)))
            cx.dma.wait("vector", rt)
            for cc in range(CCH):
                P.do("vector", lambda q, c=cx, m=m, half=half, s=s, cc=cc:
                     q.scalar_tensor_tensor(
                         c.xcur[:, cc, s * SEG:(s + 1) * SEG],
                         m.rsb[:, cc, :], m.gate[:, half:half + 1],
                         c.xcur[:, cc, s * SEG:(s + 1) * SEG],
                         ALU.mult, ALU.add),
                     cx.s_dv if cc == CCH - 1 else None, 1)
    for cc in range(CCH):
        P.do("vector", lambda q, c=cx, m=m, cc=cc:
             q.tensor_scalar(c.xcur[:, cc, :], c.xcur[:, cc, :],
                             m.bias[:, cc:cc + 1], None, ALU.add),
             cx.s_dv if cc == CCH - 1 else None, 1)
    dvA = cx.s_dv.count
    P.wait("sync", cx.s_dv, dvA)
    stA = emit_store_xcur(cx, x_out, 0)
    note_store(cx, 0, stA)
    for e in ("sync", "vector"):
        cx.stst.wait(e, stA)
    ldB = emit_load_xcur(cx, x_in, 1)
    cx.dma.wait("vector", ldB)
    for cc in range(CCH):
        P.do("vector", lambda q, c=cx, m=m, cc=cc:
             q.tensor_scalar(c.xcur[:, cc, :], c.xcur[:, cc, :],
                             m.bias[:, cc:cc + 1], None, ALU.add),
             cx.s_dv if cc == CCH - 1 else None, 1)
    dvB = cx.s_dv.count
    P.wait("sync", cx.s_dv, dvB)
    note_store(cx, 1, emit_store_xcur(cx, x_out, 1))


def moe_dram(nc):
    return {
        "w1bf": nc.dram_tensor("w1bf", [C, FF], BF16, kind="Internal").ap(),
        "w2bf": nc.dram_tensor("w2bf", [FF, C], BF16, kind="Internal").ap(),
        "cc_in": nc.dram_tensor("cc_in", [C, T], BF16, kind="Internal").ap(),
        "cc_out": nc.dram_tensor("cc_out", [NCORES * C, T], BF16,
                                 kind="Internal", addr_space="Shared").ap(),
        "arin": nc.dram_tensor("arin", [C, MT], F32, kind="Internal").ap(),
        "arout": nc.dram_tensor("arout", [C, MT], F32, kind="Internal",
                                addr_space="Shared").ap(),
        "mask_d": nc.dram_tensor("mask_d", [1, MT], F32, kind="Internal").ap(),
    }


def build_p4():
    nc = bass.Bass("TRN2", target_bir_lowering=False, debug=False,
                   detect_race_conditions=False)
    x_in = nc.dram_tensor("x_in", [2, C, T], F32, kind="ExternalInput").ap()
    w1c = nc.dram_tensor("w1c", [C, FF], F32, kind="ExternalInput").ap()
    w2c = nc.dram_tensor("w2c", [FF, C], F32, kind="ExternalInput").ap()
    idx = nc.dram_tensor("idx", [128, 64], I32, kind="ExternalInput").ap()
    ew = nc.dram_tensor("ew", [128, 64], F32, kind="ExternalInput").ap()
    eid = nc.dram_tensor("eid", [128, 1], F32, kind="ExternalInput").ap()
    gate = nc.dram_tensor("gate", [128, 2], F32, kind="ExternalInput").ap()
    bias = nc.dram_tensor("bias", [128, CCH], F32, kind="ExternalInput").ap()
    x_out = nc.dram_tensor("x_out", [2, C, T], F32, kind="ExternalOutput").ap()
    dd = moe_dram(nc)
    P = Prog(nc)
    cx = alloc_common(nc, P)
    emit_consts(cx)
    plan_p4(nc, P, cx, None, w1c, w2c, idx, ew, eid, gate, bias, dd,
            x_in, x_out, convert=True)
    cx.stst.wait_all("sync")
    P.build()
    return nc



PAIRS = [(0, 1), (0, 2), (0, 3), (1, 2), (1, 3), (2, 3)]
JSIGN = [1.0, -1.0, 1.0, 1.0, -1.0, 1.0]
LN6H = -0.5 * math.log(6.0)


def alloc_attn1(nc, A=None):
    a = Ctx()
    A = A or nc.alloc_sbuf_tensor
    a.wpq = A("b_wpq", [128, CCH, 128], BF16)
    a.wpk = A("b_wpk", [128, CCH, 128], BF16)
    a.w_vh = A("b_wvh", [128, 2, CCH, 64], BF16)
    a.w_out = A("b_wout", [128, CCH, C], BF16)
    a.bvb = A("b_bvb", [128, C], F32)
    a.bout = A("b_bout", [128, CCH], F32)
    a.AO = A("b_AO", [128, CCH, T], BF16)
    a.lqe = A("b_lqe", [128, 6, 256], BF16)
    a.jlke = A("b_jlke", [128, 6, 256], BF16)

    a.vp = A("b_vp", [128, NKC, 65], BF16)
    a.pt = A("b_pt", [128, 1, QS], BF16)
    a.rl = A("b_rl", [1, QS], F32)
    a.rb = A("b_rb", [64, QS], F32)
    a.ytmp = A("b_ytmp", [128, SEG], F32)
    a.ln6h = A("b_ln6h", [128, 1], F32)
    a.bvrow = a.rb[0:1, :]
    return a


def bind_attn1_aliases(cx, a):
    xb = cx.xbf[0:6, :, :].rearrange("p a t -> p (a t)")
    a.lqh = xb[:, 0:T]
    a.jlkh = xb[:, T:2 * T]
    a.pairq = a.AO[:, 0, :]
    a.pairk = a.AO[:, 1, :]
    a.q1e = a.AO[:, 2, 0:1024].rearrange("p (c t) -> p c t", c=4)
    a.k1e = a.AO[:, 2, 1024:2048].rearrange("p (c t) -> p c t", c=4)
    a.q2e = a.AO[:, 3, 0:1024].rearrange("p (c t) -> p c t", c=4)
    a.k2e = a.AO[:, 3, 1024:2048].rearrange("p (c t) -> p c t", c=4)
    arena = a.AO[:, 4:8, :].rearrange("p a t -> p (a t)").bitcast(F32)
    a.lx = arena[:, 0:1536].rearrange("p (c t) -> p c t", c=6)
    a.sq = arena[:, 1536:3072].rearrange("p (c t) -> p c t", c=6)
    a.acc = arena[:, 3072:3328]
    a.rn = arena[:, 3328:3584]
    a.rnn = arena[:, 3584:3840]
    a.tmp1 = arena[:, 3840:4096]
    return a


def emit_wv1_load(cx, a, wvh_ap, h):
    return cx.gdma.start("gpsimd", lambda q, a=a, w=wvh_ap, h=h:
                         q.dma_start(a.w_vh[:, h % 2, :, :],
                                     w[h].rearrange("(cc p) d -> p cc d", p=128)))


def emit_attn1_weights(cx, a, wpq_ap, wpk_ap, wout_ap, bv_ap, bout_ap):
    P = cx.P
    toks = []
    for (sb, ap) in ((a.wpq, wpq_ap), (a.wpk, wpk_ap)):
        toks.append(cx.gdma.start("gpsimd", lambda q, sb=sb, ap=ap:
                    q.dma_start(sb[:, :, :],
                                ap.rearrange("(cc p) d -> p cc d", p=128))))
    a.wout_tok = cx.gdma.start("gpsimd", lambda q, a=a, w=wout_ap:
                               q.dma_start(a.w_out[:, :, :],
                                           w.rearrange("(cc p) d -> p cc d", p=128)))
    t3 = cx.dma.start("sync", lambda q, a=a, b=bv_ap: q.dma_start(a.bvrow, b))
    t4 = cx.dma.start("sync", lambda q, a=a, b=bout_ap: q.dma_start(a.bout[:, :], b))
    a.bout_tok = t4
    bb = cx.pool[:, 2, :]
    cx.dma.wait("tensor", t3)
    P.wait("tensor", cx.s_dv, cx.dv_consts)
    for s in range(2):
        P.do("tensor", lambda q, c=cx, a=a, s=s, bb=bb:
             q.matmul(bb[:, s * 512:(s + 1) * 512], c.ones1_f[:, :],
                      a.bvrow[0:1, s * 512:(s + 1) * 512], start=True, stop=True),
             cx.s_pe, 1)
    pe = cx.s_pe.count
    P.wait("scalar", cx.s_pe, pe)
    P.do("scalar", lambda q, a=a, bb=bb: q.copy(a.bvb[:, :], bb[:, :]))
    P.do("vector", lambda q, a=a: q.memset(a.ln6h[:, :], LN6H))
    a.wpq_toks = toks
    return toks


def emit_attn1_batch(cx, a):
    P = cx.P
    c = cx
    pool = c.pool
    st_pair = [pool[:, 0, :], pool[:, 1, :]]
    o_ps = pool[0:65, 2, :]
    rb_ps = pool[0:64, 3, :]
    pk_ps = pool[:, 2, 0:512]
    v_ps = pool[:, 3, 0:64]
    y_ps = [pool[:, 2, 0:512], pool[:, 2, 512:1024]]

    for t in a.wpq_toks:
        cx.gdma.wait("tensor", t)
    for (wp, dst) in ((a.wpq, a.pairq), (a.wpk, a.pairk)):
        for s in range(NSEG):
            P.wait("tensor", c.s_dv, c.s_dv.count)
            for ccc in range(CCH):
                P.do("tensor", lambda q, c=c, wp=wp, ccc=ccc, s=s:
                     q.matmul(pk_ps, wp[:, ccc, :], c.lnx[:, ccc, s * SEG:(s + 1) * SEG],
                              start=(ccc == 0), stop=(ccc == CCH - 1)),
                     c.s_pe, 0 if ccc < CCH - 1 else 1)
            pe = c.s_pe.count
            P.wait("vector", c.s_pe, pe)
            P.do("vector", lambda q, dst=dst, s=s, pk_ps=pk_ps:
                 q.tensor_copy(dst[:, s * SEG:(s + 1) * SEG], pk_ps))
    dv_proj = c.s_dv.count

    rp = []
    P.wait("sync", c.s_dv, dv_proj)
    for (src, dst) in ((a.pairq, a.q1e), (a.pairk, a.k1e)):
        for d4 in range(4):
            rp.append(cx.dma.start("sync", lambda q, src=src, dst=dst, d4=d4:
                      q.dma_start(
                          dst[:, d4, :], src[d4:64:4, :])))
    for (src, dst) in ((a.pairq, a.q2e), (a.pairk, a.k2e)):
        for d4 in range(4):
            rp.append(cx.dma.start("sync", lambda q, src=src, dst=dst, d4=d4:
                      q.dma_start(
                          dst[:, d4, :], src[64 + d4:128:4, :])))
    for t in rp:
        cx.dma.wait("vector", t)

    for side in range(2):
        e1 = a.q1e if side == 0 else a.k1e
        e2 = a.q2e if side == 0 else a.k2e
        for ci, (i, j) in enumerate(PAIRS):
            P.do("vector", lambda q, a=a, e1=e1, e2=e2, i=i, j=j:
                 q.tensor_tensor(a.tmp1[:, :], e1[:, i, :], e2[:, j, :], ALU.mult))
            P.do("vector", lambda q, a=a, e1=e1, e2=e2, i=i, j=j, ci=ci:
                 q.tensor_tensor(a.sq[:, ci, :], e1[:, j, :], e2[:, i, :], ALU.mult))
            P.do("vector", lambda q, a=a, ci=ci:
                 q.tensor_tensor(a.lx[:, ci, :], a.tmp1[:, :], a.sq[:, ci, :],
                                 ALU.subtract))
        dv_l = c.s_dv.count
        P.wait("scalar", c.s_dv, dv_l)
        P.do("scalar", lambda q, a=a:
             q.activation(a.sq[:, :, :], a.lx[:, :, :], AF.Square), c.s_act, 1)
        act_sq = c.s_act.count
        P.wait("vector", c.s_act, act_sq)
        P.do("vector", lambda q, a=a: q.tensor_copy(a.acc[:, :], a.sq[:, 0, :]))
        for ci in range(1, 6):
            P.do("vector", lambda q, a=a, ci=ci:
                 q.tensor_tensor(a.acc[:, :], a.acc[:, :], a.sq[:, ci, :], ALU.add))
        P.do("vector", lambda q, a=a:
             q.tensor_scalar_max(a.acc[:, :], a.acc[:, :], 1e-24), c.s_dv, 1)
        dv_acc = c.s_dv.count
        P.wait("scalar", c.s_dv, dv_acc)
        P.do("scalar", lambda q, a=a: q.activation(a.tmp1[:, :], a.acc[:, :], AF.Ln))
        if side == 0:
            P.do("scalar", lambda q, a=a:
                 q.activation(a.rn[:, :], a.tmp1[:, :], AF.Exp, scale=-0.5,
                              bias=a.ln6h[:, :]), c.s_act, 1)
        else:
            P.do("scalar", lambda q, a=a:
                 q.activation(a.rn[:, :], a.tmp1[:, :], AF.Exp, scale=-0.5),
                 c.s_act, 1)
        act_rn = c.s_act.count
        P.wait("vector", c.s_act, act_rn)
        if side == 0:
            for ci in range(6):
                P.do("vector", lambda q, a=a, ci=ci:
                     q.tensor_tensor(a.lqe[:, ci, :], a.lx[:, ci, :], a.rn[:, :],
                                     ALU.mult))
        else:
            P.do("vector", lambda q, a=a:
                 q.tensor_scalar_mul(a.rnn[:, :], a.rn[:, :], -1.0))
            for aa in range(6):
                src_c = 5 - aa
                rn_t = "rn" if JSIGN[aa] > 0 else "rnn"
                P.do("vector", lambda q, a=a, aa=aa, src_c=src_c, rn_t=rn_t:
                     q.tensor_tensor(a.jlke[:, aa, :], a.lx[:, src_c, :],
                                     getattr(a, rn_t)[:, :], ALU.mult))
    dv_ext = c.s_dv.count

    P.do("vector", lambda q, a=a: q.memset(a.vp[:, :, 64:65], 1.0))

    wvt_prev = a.wv0_tok
    for h in range(NH):
        par = h % 2
        if h + 1 < NH:
            P.wait("gpsimd", c.s_pe, c.s_pe.count)
            wvt_next = emit_wv1_load(cx, a, a.wvh_ap, h + 1)
        P.wait("sync", c.s_pe, c.s_pe.count)
        if h == 0:
            P.wait("sync", c.s_dv, dv_ext)
        rp_toks = []
        for ci in range(6):
            qn = "sync" if ci % 2 == 0 else "scalar"
            rp_toks.append(cx.dma.start(qn, lambda q, a=a, h=h, ci=ci:
                           q.dma_start(a.lqh[ci:ci + 1, :],
                                       a.lqe[h * 8:(h + 1) * 8, ci, :])))
            rp_toks.append(cx.dma.start(qn, lambda q, a=a, h=h, ci=ci:
                           q.dma_start(a.jlkh[ci:ci + 1, :],
                                       a.jlke[h * 8:(h + 1) * 8, ci, :])))
        for t in rp_toks:
            cx.dma.wait("tensor", t)
        cx.gdma.wait("tensor", wvt_prev)
        for tg in range(NKC):
            P.wait("tensor", c.s_dv, c.s_dv.count)
            for ccc in range(CCH):
                P.do("tensor", lambda q, c=c, a=a, tg=tg, ccc=ccc, par=par, v_ps=v_ps:
                     q.matmul(v_ps, c.lnx[:, ccc, tg * 128:(tg + 1) * 128],
                              a.w_vh[:, par, ccc, :],
                              start=(ccc == 0), stop=(ccc == CCH - 1)),
                     c.s_pe, 0 if ccc < CCH - 1 else 1)
            pe = c.s_pe.count
            P.wait("vector", c.s_pe, pe)
            P.do("vector", lambda q, a=a, tg=tg, h=h, v_ps=v_ps:
                 q.tensor_tensor(a.vp[:, tg, 0:64], v_ps,
                                 a.bvb[:, h * 64:(h + 1) * 64], ALU.add))
        for qs in range(NQS):
            nkc = (qs + 1) * (QS // 128)
            for kc in range(nkc):
                sp = 0
                stv = st_pair[kc % 2]
                P.wait("tensor", c.s_act, c.s_act.count)
                P.wait("scalar", c.s_pe, c.s_pe.count)
                for s2 in range(2):
                    P.do("tensor", lambda q, a=a, kc=kc, qs=qs, s2=s2, stv=stv:
                         q.matmul(stv[:, s2 * 512:(s2 + 1) * 512],
                                  a.jlkh[:, kc * 128:(kc + 1) * 128],
                                  a.lqh[:, qs * QS + s2 * 512:qs * QS + (s2 + 1) * 512],
                                  start=True, stop=True),
                         c.s_pe, 0 if s2 == 0 else 1)
                pe_st = c.s_pe.count
                P.wait("scalar", c.s_pe, pe_st)
                P.do("scalar", lambda q, a=a, sp=sp, stv=stv:
                     q.activation(a.pt[:, sp, :], stv[:, :], AF.Exp), c.s_act, 1)
                act_exp = c.s_act.count
                diag = (kc >= qs * (QS // 128))
                if diag:
                    P.wait("gpsimd", c.s_act, act_exp)
                    P.do("gpsimd", lambda q, a=a, sp=sp, qs=qs, kc=kc:
                         q.affine_select(a.pt[:, sp, :], a.pt[:, sp, :],
                                         [[1, QS]], ALU.is_ge, 0.0,
                                         base=qs * QS - kc * 128,
                                         channel_multiplier=-1))
                    P.wait("tensor", c.s_pool, c.s_pool.count)
                else:
                    P.wait("tensor", c.s_act, act_exp)
                if kc == 0:
                    P.wait("tensor", c.s_dv, c.s_dv.count)
                for s2 in range(2):
                    P.do("tensor", lambda q, a=a, kc=kc, sp=sp, s2=s2, nkc=nkc, o_ps=o_ps:
                         q.matmul(o_ps[:, s2 * 512:(s2 + 1) * 512],
                                  a.vp[:, kc, :], a.pt[:, sp, s2 * 512:(s2 + 1) * 512],
                                  start=(kc == 0), stop=(kc == nkc - 1)),
                         c.s_pe, 0 if (s2 == 0 or kc < nkc - 1) else 1)
            pe_pv = c.s_pe.count
            P.wait("vector", c.s_pe, pe_pv)
            P.do("vector", lambda q, a=a, o_ps=o_ps:
                 q.reciprocal(a.rl[:, :], o_ps[64:65, :]), c.s_dv, 1)
            dv_rl = c.s_dv.count
            P.wait("tensor", c.s_dv, dv_rl)
            for s2 in range(2):
                P.do("tensor", lambda q, c=c, a=a, s2=s2, rb_ps=rb_ps:
                     q.matmul(rb_ps[:, s2 * 512:(s2 + 1) * 512], c.ones1_f[:, 0:64],
                              a.rl[:, s2 * 512:(s2 + 1) * 512], start=True, stop=True),
                     c.s_pe, 0 if s2 == 0 else 1)
            pe_rb = c.s_pe.count
            P.wait("scalar", c.s_pe, pe_rb)
            P.do("scalar", lambda q, a=a, rb_ps=rb_ps:
                 q.copy(a.rb[:, :], rb_ps), c.s_act, 1)
            act_rb = c.s_act.count
            P.wait("vector", c.s_act, act_rb)
            base = (h % 2) * 64
            cc_t = h // 2
            P.do("vector", lambda q, c=c, a=a, base=base, cc_t=cc_t, qs=qs, o_ps=o_ps:
                 q.tensor_tensor(a.AO[base:base + 64, cc_t, qs * QS:(qs + 1) * QS],
                                 o_ps[0:64, :], a.rb[:, :], ALU.mult), c.s_dv, 1)
        if h + 1 < NH:
            wvt_prev = wvt_next
    cx.gdma.wait("tensor", a.wout_tok)
    cx.dma.wait("vector", a.bout_tok)
    dv_ao = c.s_dv.count
    P.wait("tensor", c.s_dv, dv_ao)
    for s in range(NSEG):
        for dg in range(CCH):
            yp = y_ps[dg % 2]
            P.wait("tensor", c.s_dv, c.s_dv.count)
            for ccc in range(CCH):
                P.do("tensor", lambda q, c=c, a=a, s=s, dg=dg, ccc=ccc, yp=yp:
                     q.matmul(yp, a.w_out[:, ccc, dg * 128:(dg + 1) * 128],
                              a.AO[:, ccc, s * SEG:(s + 1) * SEG],
                              start=(ccc == 0), stop=(ccc == CCH - 1)),
                     c.s_pe, 0 if ccc < CCH - 1 else 1)
            pe = c.s_pe.count
            P.wait("vector", c.s_pe, pe)
            P.do("vector", lambda q, a=a, dg=dg, yp=yp:
                 q.tensor_scalar(a.ytmp[:, :], yp, a.bout[:, dg:dg + 1], None,
                                 ALU.add))
            P.do("vector", lambda q, c=c, a=a, dg=dg, s=s:
                 q.tensor_tensor(c.xcur[:, dg, s * SEG:(s + 1) * SEG],
                                 c.xcur[:, dg, s * SEG:(s + 1) * SEG],
                                 a.ytmp[:, :], ALU.add), c.s_dv, 1)
    return c.s_dv.count


def plan_p5(nc, P, cx, A, wpq, wpk, wvh, wout, bv, bout, x_in, x_out):
    if A is not None:
        cx.lnx = A("lnx", [128, CCH, T + 1], BF16)
    a = alloc_attn1(nc, A)
    bind_attn1_aliases(cx, a)
    a.wvh_ap = wvh
    emit_attn1_weights(cx, a, wpq, wpk, wout, bv, bout)
    a.wv0_tok = emit_wv1_load(cx, a, wvh, 0)

    for b in range(2):
        ld_tok = emit_load_xcur(cx, x_in, b)
        cx.dma.wait("vector", ld_tok)
        for s in range(NSEG):
            emit_ln_seg(cx, s * SEG, lambda cc, c=cx, s=s:
                        c.lnx[:, cc, s * SEG:(s + 1) * SEG])
        P.wait("tensor", cx.s_dv, cx.s_dv.count)
        emit_attn1_batch(cx, a)
        if b == 0:
            P.wait("gpsimd", cx.s_pe, cx.s_pe.count)
            a.wv0_tok = emit_wv1_load(cx, a, wvh, 0)
        dv_done = cx.s_dv.count
        P.wait("sync", cx.s_dv, dv_done)
        st_tok = emit_store_xcur(cx, x_out, b)
        note_store(cx, b, st_tok)
        if b == 0:
            for e in ("sync", "vector", "tensor", "scalar"):
                cx.stst.wait(e, st_tok)


def build_p5():
    nc = bass.Bass("TRN2", target_bir_lowering=False, debug=False,
                   detect_race_conditions=False)
    x_in = nc.dram_tensor("x_in", [2, C, T], F32, kind="ExternalInput").ap()
    wpq = nc.dram_tensor("wpq", [C, 128], F32, kind="ExternalInput").ap()
    wpk = nc.dram_tensor("wpk", [C, 128], F32, kind="ExternalInput").ap()
    wvh = nc.dram_tensor("wvh", [NH, C, 64], F32, kind="ExternalInput").ap()
    wout = nc.dram_tensor("wout", [C, C], F32, kind="ExternalInput").ap()
    bv = nc.dram_tensor("bv", [1, C], F32, kind="ExternalInput").ap()
    bout = nc.dram_tensor("bout", [128, CCH], F32, kind="ExternalInput").ap()
    x_out = nc.dram_tensor("x_out", [2, C, T], F32, kind="ExternalOutput").ap()
    P = Prog(nc)
    cx = alloc_common(nc, P)
    cx.lnx = nc.alloc_sbuf_tensor("lnx", [128, CCH, T + 1], BF16)
    emit_consts(cx)
    plan_p5(nc, P, cx, None, wpq, wpk, wvh, wout, bv, bout, x_in, x_out)
    cx.stst.wait_all("sync")
    P.build()
    return nc


def build_fused():
    from contextlib import ExitStack
    nc = bass.Bass("TRN2", target_bir_lowering=False, debug=False,
                   detect_race_conditions=False)
    x_fm = nc.dram_tensor("x_fm", [2, C, T], F32, kind="ExternalInput").ap()
    wr = nc.dram_tensor("wr", [C, C], F32, kind="ExternalInput").ap()
    wk = nc.dram_tensor("wk", [C, C], F32, kind="ExternalInput").ap()
    wv_f0 = nc.dram_tensor("wv_f0", [C, C], F32, kind="ExternalInput").ap()
    mu = nc.dram_tensor("mu", [128, 2, CCH], F32, kind="ExternalInput").ap()
    wqk_h = nc.dram_tensor("wqk_h", [NH, C, 128], F32, kind="ExternalInput").ap()
    wv0 = nc.dram_tensor("wv0", [C, C], F32, kind="ExternalInput").ap()
    bq = nc.dram_tensor("bq", [64, NH], F32, kind="ExternalInput").ap()
    bk = nc.dram_tensor("bk", [64, NH], F32, kind="ExternalInput").ap()
    bv0 = nc.dram_tensor("bv0", [1, C], F32, kind="ExternalInput").ap()
    w1c = nc.dram_tensor("w1c", [C, FF], F32, kind="ExternalInput").ap()
    w2c = nc.dram_tensor("w2c", [FF, C], F32, kind="ExternalInput").ap()
    idx = nc.dram_tensor("idx", [128, 64], I32, kind="ExternalInput").ap()
    ew = nc.dram_tensor("ew", [128, 64], F32, kind="ExternalInput").ap()
    eid = nc.dram_tensor("eid", [128, 1], F32, kind="ExternalInput").ap()
    gate = nc.dram_tensor("gate", [128, 2], F32, kind="ExternalInput").ap()
    bias = nc.dram_tensor("bias", [128, CCH], F32, kind="ExternalInput").ap()
    wpq = nc.dram_tensor("wpq", [C, 128], F32, kind="ExternalInput").ap()
    wpk = nc.dram_tensor("wpk", [C, 128], F32, kind="ExternalInput").ap()
    wvh = nc.dram_tensor("wvh", [NH, C, 64], F32, kind="ExternalInput").ap()
    wout = nc.dram_tensor("wout", [C, C], F32, kind="ExternalInput").ap()
    bv1 = nc.dram_tensor("bv1", [1, C], F32, kind="ExternalInput").ap()
    bout = nc.dram_tensor("bout", [128, CCH], F32, kind="ExternalInput").ap()
    y_fm = nc.dram_tensor("y_fm", [2, C, T], F32, kind="ExternalOutput").ap()

    xstore = nc.dram_tensor("xstore", [2, C, T], F32, kind="Internal").ap()
    dd = moe_dram(nc)

    P = Prog(nc)
    cx = alloc_common(nc, P)
    emit_consts(cx)

    def scoped(tag, stack):
        def A(name, shape, dtype):
            return stack.enter_context(nc.sbuf_tensor(name + tag, shape, dtype))
        return A

    with ExitStack() as es:
        plan_p1(nc, P, cx, scoped("_a", es), wr, wk, wv_f0, mu, x_fm, xstore)
    phase_barrier(P)
    with ExitStack() as es:
        plan_p2(nc, P, cx, scoped("_b", es), wqk_h, wv0, bq, bk, bv0,
                xstore, xstore)
    phase_barrier(P)
    with ExitStack() as es:
        plan_p1(nc, P, cx, scoped("_c", es), wr, wk, wv_f0, mu, xstore, xstore)
    phase_barrier(P)
    with ExitStack() as es:
        plan_p4(nc, P, cx, scoped("_d", es), w1c, w2c, idx, ew, eid, gate, bias,
                dd, xstore, xstore, convert=True)
    phase_barrier(P)
    with ExitStack() as es:
        plan_p5(nc, P, cx, scoped("_e", es), wpq, wpk, wvh, wout, bv1, bout,
                xstore, xstore)
    phase_barrier(P)
    with ExitStack() as es:
        plan_p4(nc, P, cx, scoped("_f", es), w1c, w2c, idx, ew, eid, gate, bias,
                dd, xstore, y_fm, convert=False)
    cx.stst.wait_all("sync")
    P.build()
    return nc



def plan_p1(nc, P, cx, A, wr, wk, wv, mu, x_in, x_out, tag=""):
    if A is not None:
        cx.lnx = A("lnx", [128, CCH, T + 1], BF16)
    f = alloc_ffn0(nc, A)
    f.p_ps = cx.pool[:, 2, 0:SEG]
    f.p2_ps = cx.pool[:, 2, SEG:2 * SEG]
    mu_tok = cx.dma.start("sync", lambda q, f=f, mu=mu: q.dma_start(f.mu[:, :, :], mu))
    cx.dma.wait("vector", mu_tok)
    wdv, wtoks = emit_ffn0_weights(cx, f, wr, wk, wv)
    P.wait("tensor", cx.s_dv, wdv)

    for b in range(2):
        ld_for_batch = emit_load_xcur(cx, x_in, b)
        emit_ffn0_batch(cx, f, wdv, wtoks, ld_for_batch)
        dv_done = cx.s_dv.count
        P.wait("sync", cx.s_dv, dv_done)
        st_tok = emit_store_xcur(cx, x_out, b)
        note_store(cx, b, st_tok)
        if b == 0:
            for e in ("sync", "vector", "tensor", "scalar"):
                cx.stst.wait(e, st_tok)


def build_p1():
    nc = bass.Bass("TRN2", target_bir_lowering=False, debug=False,
                   detect_race_conditions=False)
    x_in = nc.dram_tensor("x_in", [2, C, T], F32, kind="ExternalInput").ap()
    wr = nc.dram_tensor("wr", [C, C], F32, kind="ExternalInput").ap()
    wk = nc.dram_tensor("wk", [C, C], F32, kind="ExternalInput").ap()
    wv = nc.dram_tensor("wv", [C, C], F32, kind="ExternalInput").ap()
    mu = nc.dram_tensor("mu", [128, 2, CCH], F32, kind="ExternalInput").ap()
    x_out = nc.dram_tensor("x_out", [2, C, T], F32, kind="ExternalOutput").ap()
    P = Prog(nc)
    cx = alloc_common(nc, P)
    cx.lnx = nc.alloc_sbuf_tensor("lnx", [128, CCH, T + 1], BF16)
    emit_consts(cx)
    plan_p1(nc, P, cx, None, wr, wk, wv, mu, x_in, x_out)
    cx.stst.wait_all("sync")
    P.build()
    return nc



def _prep_mu(mu_r, mu_k):
    m = np.stack([mu_r.reshape(CCH, 128).T, mu_k.reshape(CCH, 128).T], axis=1)
    return np.ascontiguousarray(m.astype(np.float32))


def kernel(**inputs):
    f32 = lambda k: np.asarray(inputs[k]).astype(np.float32)
    xs = f32("L0_attn_hidden_states")
    wr_t = np.ascontiguousarray(f32("L0_ffn_W_r").T)
    wk_t = np.ascontiguousarray(f32("L0_ffn_W_k").T)
    wv_t = np.ascontiguousarray(f32("L0_ffn_W_v").T)
    mu = _prep_mu(f32("L0_ffn_mu_r"), f32("L0_ffn_mu_k"))
    Wqk = f32("L0_attn_Wqk")
    wqk_h = np.ascontiguousarray(np.stack(
        [np.concatenate([Wqk[:, h * 64:(h + 1) * 64],
                         Wqk[:, C + h * 64:C + (h + 1) * 64]], axis=1)
         for h in range(NH)]))
    bqk = f32("L0_attn_bqk")
    bq = np.ascontiguousarray(bqk[:C].reshape(NH, 64).T)
    bk = np.ascontiguousarray(bqk[C:].reshape(NH, 64).T)
    wv0 = np.ascontiguousarray(f32("L0_attn_Wv"))
    bv0 = np.ascontiguousarray(f32("L0_attn_bv").reshape(1, C))
    w1 = f32("L1_ffn_w1")
    w2 = f32("L1_ffn_w2")
    idx_h = np.ascontiguousarray(
        np.asarray(inputs["L1_ffn_expert_indices"]).astype(np.int32)
        .reshape(128, 32, 2).transpose(0, 2, 1).reshape(128, 64))
    ew_h = np.ascontiguousarray(f32("L1_ffn_expert_weights")
                                .reshape(128, 32, 2).transpose(0, 2, 1).reshape(128, 64))
    l1bias = np.ascontiguousarray(f32("L1_ffn_bias").reshape(CCH, 128).T)
    wpq = np.ascontiguousarray(np.concatenate(
        [f32("L1_attn_W1_q").T, f32("L1_attn_W2_q").T], axis=1))
    wpk = np.ascontiguousarray(np.concatenate(
        [f32("L1_attn_W1_k").T, f32("L1_attn_W2_k").T], axis=1))
    wv1T = f32("L1_attn_W_v").T
    wv1_h = np.ascontiguousarray(np.stack(
        [wv1T[:, h * 64:(h + 1) * 64] for h in range(NH)]))
    wout_t = np.ascontiguousarray(f32("L1_attn_W_out").T)
    bv1 = np.ascontiguousarray(f32("L1_attn_b_v").reshape(1, C))
    bout = np.ascontiguousarray(f32("L1_attn_b_out").reshape(CCH, 128).T)

    ncores = list(range(NCORES))
    xst = [np.ascontiguousarray(np.stack([xs[c].T, xs[c + 8].T]))
           for c in range(NCORES)]

    nc = build_fused()
    in_maps = []
    for c in range(NCORES):
        gate = np.zeros((128, 2), np.float32)
        if c < 2:
            gate[:, c] = 1.0
        in_maps.append({
            "x_fm": xst[c],
            "wr": wr_t, "wk": wk_t, "wv_f0": wv_t, "mu": mu,
            "wqk_h": wqk_h, "wv0": wv0, "bq": bq, "bk": bk, "bv0": bv0,
            "w1c": np.ascontiguousarray(w1[c]),
            "w2c": np.ascontiguousarray(w2[c]),
            "idx": idx_h, "ew": ew_h,
            "eid": np.full((128, 1), float(c), np.float32),
            "gate": gate, "bias": l1bias,
            "wpq": wpq, "wpk": wpk, "wvh": wv1_h, "wout": wout_t,
            "bv1": bv1, "bout": bout,
        })
    res = run_bass_kernel_spmd(nc, in_maps, core_ids=ncores)
    out = np.empty((16, T, C), dtype=np.float32)
    for c in range(NCORES):
        y = res.results[c]["y_fm"]
        out[c] = y[0].T
        out[c + 8] = y[1].T
    return out


# version 03
# speedup vs baseline: 1.2194x; 1.2002x over previous
import math
import numpy as np
import concourse.bass as bass
import concourse.mybir as mybir
from concourse.bass_utils import run_bass_kernel_spmd

F32 = mybir.dt.float32
BF16 = mybir.dt.bfloat16
I32 = mybir.dt.int32
AF = mybir.ActivationFunctionType
ALU = mybir.AluOpType

NCORES = 8
T = 2048
C = 1024
CCH = 8
NH = 16
HD = 64
FF = 4096
SEG = 512
NSEG = T // SEG

ENGINES = ("sync", "tensor", "vector", "scalar", "gpsimd")


class SemC:

    def __init__(self, h):
        self.h = h
        self.count = 0


class Prog:

    SERIAL = ("vector", "scalar", "gpsimd")

    def __init__(self, nc):
        self.nc = nc
        self.ops = {e: [] for e in ENGINES}
        self._semid = 0
        self.esem = {}
        for e in self.SERIAL:
            self.esem[e] = self.sem(f"e_{e}")

    def sem(self, name=None):
        self._semid += 1
        h = self.nc.alloc_semaphore(name or f"s{self._semid}")
        return SemC(h)

    def emit(self, eng, fn):
        self.ops[eng].append(fn)

    def do(self, eng, fn, sem=None, n=1):
        if eng in self.SERIAL and n == 1 and (sem is None or sem is self.esem[eng]):
            s = self.esem[eng]
            if s.count > 0:
                self.ops[eng].append(lambda q, s=s, v=s.count: q.wait_ge(s.h, v))

            def wrapped(q, fn=fn, s=s):
                fn(q).then_inc(s.h, 1)
            self.ops[eng].append(wrapped)
            s.count += 1
            return s.count
        if sem is None:
            self.ops[eng].append(fn)
            return None
        def wrapped(q, fn=fn, s=sem, n=n):
            fn(q).then_inc(s.h, n)
        self.ops[eng].append(wrapped)
        sem.count += n
        return sem.count

    def wait(self, eng, sem, val=None):
        v = sem.count if val is None else val
        self.ops[eng].append(lambda q, s=sem, v=v: q.wait_ge(s.h, v))

    def barrier(self, name=None):
        s = self.sem(name)
        for e in ENGINES:
            self.do(e, lambda q: q.engine_nop() if hasattr(q, "engine_nop") else q.nop(), s, 1)
        for e in ENGINES:
            self.wait(e, s, len(ENGINES))
        return s

    def build(self):
        with self.nc.Block() as block:
            @block.sync
            def _(q):
                for fn in self.ops["sync"]:
                    fn(q)

            @block.tensor
            def _(q):
                for fn in self.ops["tensor"]:
                    fn(q)

            @block.vector
            def _(q):
                for fn in self.ops["vector"]:
                    fn(q)

            @block.scalar
            def _(q):
                for fn in self.ops["scalar"]:
                    fn(q)

            @block.gpsimd
            def _(q):
                for fn in self.ops["gpsimd"]:
                    fn(q)


class DmaStream:

    def __init__(self, P, name, rot=4):
        self.P = P
        self.sems = [P.sem(f"{name}{i}") for i in range(rot)]
        self.rot = rot
        self.idx = 0

    def start(self, eng, fn):
        s = self.sems[self.idx % self.rot]
        self.idx += 1
        if s.count > 0:
            self.P.wait(eng, s, s.count)
        self.P.do(eng, fn, s, 16)
        return (s, s.count)

    def wait(self, eng, tok):
        self.P.wait(eng, tok[0], tok[1])

    def wait_all(self, eng):
        for s in self.sems:
            if s.count > 0:
                self.P.wait(eng, s, s.count)


def _nop_inc(P, eng, sem):
    P.do(eng, lambda q: q.nop(), sem, 1)


def phase_barrier(P, name=None):
    s = P.sem(name)
    for e in ENGINES:
        _nop_inc(P, e, s)
    for e in ENGINES:
        P.wait(e, s, len(ENGINES))
    return s



class Ctx:
    pass


def alloc_common(nc, P):
    cx = Ctx()
    cx.nc = nc
    cx.P = P
    A = nc.alloc_sbuf_tensor
    cx.xcur = A("xcur", [128, CCH, T], F32)
    cx.ones1_f = A("ones1_f", [1, 128], F32)
    cx.ones128_b = A("ones128_b", [128, 1], BF16)
    cx.ident_f = A("ident_f", [128, 128], F32)
    cx.xbf = A("ln_xbf", [128, CCH, SEG], BF16)
    cx.sqbf = A("ln_sqbf", [128, CCH, SEG], BF16)
    cx.rstd = A("ln_rstd", [1, SEG], F32)
    cx.bneg = A("ln_bneg", [1, SEG], F32)
    cx.tstat = A("ln_tstat", [1, SEG], F32)
    cx.a_sb = A("ln_a_sb", [128, SEG], F32)
    cx.eps = A("ln_eps", [1, 1], F32)
    cx.ssum = A("ln_ssum", [1, SEG], F32)
    cx.b_sb = A("ln_b_sb", [128, SEG], F32)
    cx.lntmp = cx.sqbf
    cx.pool = nc.alloc_psum_tensor("pool_ps", [128, 4, 1024], F32)
    cx.s_ps = cx.pool[0:1, 0, 0:SEG]
    cx.ss_ps = cx.pool[0:1, 0, SEG:2 * SEG]
    cx.ab_ps = cx.pool[:, 1, :].rearrange("p (a s) -> p a s", a=2)

    cx.dma = DmaStream(P, "c_dma", rot=4)
    cx.gdma = DmaStream(P, "c_gdma", rot=4)
    cx.stst = DmaStream(P, "c_st", rot=2)
    cx.s_dv = P.esem["vector"]
    cx.s_act = P.esem["scalar"]
    cx.s_pool = P.esem["gpsimd"]
    cx.s_pe = P.sem("c_pe")
    return cx


def emit_consts(cx):
    P = cx.P
    P.do("vector", lambda q, c=cx: q.memset(c.ones1_f[:, :], 1.0))
    P.do("vector", lambda q, c=cx: q.memset(c.ones128_b[:, :], 1.0))
    P.do("vector", lambda q, c=cx: q.memset(c.eps[:, :], 1e-5))
    P.do("vector", lambda q, c=cx: q.memset(c.ident_f[:, :], 1.0), cx.s_dv, 1)
    P.wait("gpsimd", cx.s_dv)
    P.do("gpsimd",
         lambda q, c=cx: q.affine_select(c.ident_f[:, :], c.ident_f[:, :],
                                         [[-1, 128]], ALU.is_equal, 0.0,
                                         base=0, channel_multiplier=1),
         cx.s_pool, 1)
    cx.dv_consts = cx.s_dv.count



def xstore_view(xstore_ap, b):
    return xstore_ap[b].rearrange("(cc p) t -> p cc t", p=128)


def emit_load_xcur(cx, xstore_ap, b):
    prev = getattr(cx, "last_store", {}).get(b)
    if prev is not None:
        cx.stst.wait("sync", prev)
    return cx.dma.start("sync", lambda q, c=cx, x=xstore_ap, b=b:
                        q.dma_start(c.xcur[:, :, :], xstore_view(x, b)))


def note_store(cx, b, tok):
    if not hasattr(cx, "last_store"):
        cx.last_store = {}
    cx.last_store[b] = tok


def emit_store_xcur(cx, xstore_ap, b, eng="sync"):
    return cx.stst.start(eng, lambda q, c=cx, x=xstore_ap, b=b:
                         q.dma_start(xstore_view(x, b), c.xcur[:, :, :]))



def emit_input_transpose(cx, x_ap, b, tp_ps, stage):
    P = cx.P
    pe_before = cx.s_pe.count
    for cc in range(CCH):
        for tg4 in range(T // 512):
            P.wait("sync", cx.s_pe, cx.s_pe.count)
            tok = cx.dma.start("sync", lambda q, c=cx, x=x_ap, b=b, cc=cc, tg4=tg4, st=stage:
                               q.dma_start(
                                   st[:, :].rearrange("p (i c) -> p i c", i=4),
                                   x[b, tg4 * 512:(tg4 + 1) * 512, cc * 128:(cc + 1) * 128]
                                   .rearrange("(i p) c -> p i c", p=128)))
            cx.dma.wait("tensor", tok)
            if cc == 0 and tg4 == 0:
                P.wait("tensor", cx.s_pool, 1)
            for i in range(4):
                P.wait("tensor", cx.s_dv, cx.s_dv.count)
                P.do("tensor", lambda q, c=cx, ps=tp_ps, st=stage, i=i:
                     q.transpose(ps[:, i * 128:(i + 1) * 128],
                                 st[:, i * 128:(i + 1) * 128], c.ident_f[:, :]),
                     cx.s_pe, 1)
            pe = cx.s_pe.count
            P.wait("vector", cx.s_pe, pe)
            P.do("vector", lambda q, c=cx, ps=tp_ps, cc=cc, tg4=tg4:
                 q.tensor_copy(c.xcur[:, cc, tg4 * 512:(tg4 + 1) * 512], ps[:, :]),
                 cx.s_dv, 1)
    return cx.s_dv.count


def emit_output_transpose(cx, y_ap, b, tp_ps, stage):
    P = cx.P
    for tg in range(T // 128):
        for half in range(2):
            for j in range(4):
                cc = half * 4 + j
                P.wait("tensor", cx.s_dv, cx.s_dv.count)
                P.do("tensor", lambda q, c=cx, ps=tp_ps, cc=cc, tg=tg, j=j:
                     q.transpose(ps[:, j * 128:(j + 1) * 128],
                                 c.xcur[:, cc, tg * 128:(tg + 1) * 128],
                                 c.ident_f[:, :]),
                     cx.s_pe, 1)
            pe = cx.s_pe.count
            P.wait("vector", cx.s_pe, pe)
            P.do("vector", lambda q, st=stage, half=half, ps=tp_ps:
                 q.tensor_copy(st[:, half * 512:(half + 1) * 512], ps[:, :]),
                 cx.s_dv, 1)
        dv = cx.s_dv.count
        P.wait("sync", cx.s_dv, dv)
        tok = cx.stst.start("sync", lambda q, y=y_ap, b=b, tg=tg, st=stage:
                            q.dma_start(y[b, tg * 128:(tg + 1) * 128, :], st[:, :]))
        cx.stst.wait("vector", tok)
    return None



def emit_ln_seg(cx, t0, dst_fn, sem_out=None):
    P = cx.P
    c = cx
    P.do("vector", lambda q, c=c, t0=t0:
         q.tensor_copy(c.xbf[:, :, :], c.xcur[:, :, t0:t0 + SEG]), c.s_dv, 1)
    dv_cast = c.s_dv.count
    P.wait("scalar", c.s_dv, dv_cast)
    P.do("scalar", lambda q, c=c:
         q.activation(c.sqbf[:, :, :], c.xbf[:, :, :], AF.Square), c.s_act, 1)
    act_sq = c.s_act.count
    P.wait("tensor", c.s_dv, dv_cast)
    for cc in range(CCH):
        P.do("tensor", lambda q, c=c, cc=cc:
             q.matmul(c.s_ps[:, :], c.ones128_b[:, :], c.xbf[:, cc, :],
                      start=(cc == 0), stop=(cc == CCH - 1)),
             c.s_pe, 0 if cc < CCH - 1 else 1)
    P.wait("tensor", c.s_act, act_sq)
    for cc in range(CCH):
        P.do("tensor", lambda q, c=c, cc=cc:
             q.matmul(c.ss_ps[:, :], c.ones128_b[:, :], c.sqbf[:, cc, :],
                      start=(cc == 0), stop=(cc == CCH - 1)),
             c.s_pe, 0 if cc < CCH - 1 else 1)
    pe_stats = c.s_pe.count
    P.wait("vector", c.s_pe, pe_stats)
    P.do("vector", lambda q, c=c: q.tensor_copy(c.ssum[:, :], c.s_ps[:, :]))
    P.do("vector", lambda q, c=c:
         q.scalar_tensor_tensor(c.tstat[:, :], c.ssum[:, :], 1.0 / (1024.0 * 1024.0),
                                c.ssum[:, :], ALU.mult, ALU.mult))
    P.do("vector", lambda q, c=c:
         q.scalar_tensor_tensor(c.rstd[:, :], c.ss_ps[:, :], 1.0 / 1024.0,
                                c.tstat[:, :], ALU.mult, ALU.subtract), c.s_dv, 1)
    dv_var = c.s_dv.count
    P.wait("scalar", c.s_dv, dv_var)
    P.do("scalar", lambda q, c=c:
         q.activation(c.tstat[:, :], c.rstd[:, :], AF.Sqrt, bias=c.eps[:, :]), c.s_act, 1)
    act_std = c.s_act.count
    P.wait("vector", c.s_act, act_std)
    P.do("vector", lambda q, c=c: q.reciprocal(c.rstd[:, :], c.tstat[:, :]))
    P.do("vector", lambda q, c=c:
         q.scalar_tensor_tensor(c.bneg[:, :], c.ssum[:, :], -1.0 / 1024.0,
                                c.rstd[:, :], ALU.mult, ALU.mult), c.s_dv, 1)
    dv_stats = c.s_dv.count
    P.wait("tensor", c.s_dv, dv_stats)
    P.do("tensor", lambda q, c=c:
         q.matmul(c.ab_ps[:, 0, :], c.ones1_f[:, :], c.rstd[:, :],
                  start=True, stop=True))
    P.do("tensor", lambda q, c=c:
         q.matmul(c.ab_ps[:, 1, :], c.ones1_f[:, :], c.bneg[:, :],
                  start=True, stop=True), c.s_pe, 1)
    pe_bc = c.s_pe.count
    P.wait("scalar", c.s_pe, pe_bc)
    P.do("scalar", lambda q, c=c: q.copy(c.a_sb[:, :], c.ab_ps[:, 0, :]))
    P.do("scalar", lambda q, c=c: q.copy(c.b_sb[:, :], c.ab_ps[:, 1, :]), c.s_act, 1)
    act_ab = c.s_act.count
    P.wait("vector", c.s_act, act_ab)
    P.do("vector", lambda q, c=c:
         q.tensor_tensor(c.lntmp[:, :, :], c.xbf[:, :, :],
                         c.a_sb[:, :].unsqueeze(1).broadcast_to([128, CCH, SEG]),
                         ALU.mult))
    for cc in range(CCH):
        P.do("vector", lambda q, c=c, cc=cc, dst_fn=dst_fn:
             q.tensor_tensor(dst_fn(cc), c.lntmp[:, cc, :], c.b_sb[:, :], ALU.add),
             c.s_dv if (cc == CCH - 1) else None, 1)
    if sem_out is not None:
        _nop_inc(P, "vector", sem_out)
    return c.s_dv.count



def alloc_ffn0(nc, A=None):
    f = Ctx()
    A = A or nc.alloc_sbuf_tensor
    f.w_r = A("f_w_r", [128, CCH, C], BF16)
    f.w_k = A("f_w_k", [128, CCH, C], BF16)
    f.w_v = A("f_w_v", [128, CCH, C], BF16)
    f.mu = A("f_mu", [128, 2, CCH], F32)
    f.diff = A("f_diff", [128, CCH, SEG], BF16)
    f.xr = A("f_xr", [128, CCH, SEG], BF16)
    f.xk = f.xr
    f.r = A("f_r", [128, CCH, SEG], F32)
    f.kact = f.diff
    f.vtmp = A("f_vtmp", [128, SEG], F32)
    return f


def emit_ffn0_weights(cx, f, wr_ap, wk_ap, wv_ap):
    P = cx.P
    toks = []
    for (w_sb, w_ap) in ((f.w_r, wr_ap), (f.w_k, wk_ap), (f.w_v, wv_ap)):
        toks.append(cx.gdma.start("gpsimd", lambda q, w_sb=w_sb, w_ap=w_ap:
                    q.dma_start(w_sb[:, :, :],
                                w_ap.rearrange("(cc p) d -> p cc d", p=128))))
    for t in toks:
        cx.gdma.wait("vector", t)
    P.do("vector", lambda q, f=f:
         q.tensor_scalar_mul(f.w_v[:, :, :], f.w_v[:, :, :], 0.5), cx.s_dv, 1)
    return cx.s_dv.count, toks


def emit_ffn0_batch(cx, f, w_dv, w_toks, ld_tok):
    P = cx.P
    c = cx
    if ld_tok is not None:
        cx.dma.wait("vector", ld_tok)
    P.do("vector", lambda q, c=c: q.memset(c.lnx[:, :, 0:1], 0.0))
    for t in w_toks:
        cx.gdma.wait("tensor", t)
    for s in range(NSEG):
        t0 = s * SEG
        emit_ln_seg(cx, t0, lambda cc, c=c: c.lnx[:, cc, 1:SEG + 1])
        dv_ln = c.s_dv.count
        P.do("vector", lambda q, c=c, f=f:
             q.tensor_tensor(f.diff[:, :, :], c.lnx[:, :, 1:SEG + 1],
                             c.lnx[:, :, 0:SEG], ALU.subtract))
        for cc in range(CCH):
            P.do("vector", lambda q, c=c, f=f, cc=cc:
                 q.scalar_tensor_tensor(f.xr[:, cc, :], f.diff[:, cc, :],
                                        f.mu[:, 0, cc:cc + 1], c.lnx[:, cc, 0:SEG],
                                        ALU.mult, ALU.add))
        P.do("vector", lambda q, c=c, f=f:
             q.engine_nop(), c.s_dv, 1)
        dv_lerp = c.s_dv.count

        P.wait("tensor", c.s_dv, dv_lerp)
        for dg in range(CCH):
            ps = f.p_ps if dg % 2 == 0 else f.p2_ps
            P.wait("tensor", c.s_act, c.s_act.count)
            for ccc in range(CCH):
                P.do("tensor", lambda q, f=f, ps=ps, dg=dg, ccc=ccc:
                     q.matmul(ps[:, :], f.w_r[:, ccc, dg * 128:(dg + 1) * 128],
                              f.xr[:, ccc, :], start=(ccc == 0), stop=(ccc == CCH - 1)),
                     c.s_pe, 0 if ccc < CCH - 1 else 1)
            pe = c.s_pe.count
            P.wait("scalar", c.s_pe, pe)
            P.do("scalar", lambda q, f=f, ps=ps, dg=dg:
                 q.activation(f.r[:, dg, :], ps[:, :], AF.Sigmoid), c.s_act, 1)
        pe_r_done = c.s_pe.count
        P.wait("vector", c.s_pe, pe_r_done)
        for cc in range(CCH):
            P.do("vector", lambda q, c=c, f=f, cc=cc:
                 q.scalar_tensor_tensor(f.xk[:, cc, :], f.diff[:, cc, :],
                                        f.mu[:, 1, cc:cc + 1], c.lnx[:, cc, 0:SEG],
                                        ALU.mult, ALU.add))
        P.do("vector", lambda q, c=c:
             q.tensor_copy(c.lnx[:, :, 0:1], c.lnx[:, :, SEG:SEG + 1]), c.s_dv, 1)
        dv_xk = c.s_dv.count
        P.wait("tensor", c.s_dv, dv_xk)
        for dg in range(CCH):
            ps = f.p_ps if dg % 2 == 0 else f.p2_ps
            P.wait("tensor", c.s_act, c.s_act.count)
            for ccc in range(CCH):
                P.do("tensor", lambda q, f=f, ps=ps, dg=dg, ccc=ccc:
                     q.matmul(ps[:, :], f.w_k[:, ccc, dg * 128:(dg + 1) * 128],
                              f.xk[:, ccc, :], start=(ccc == 0), stop=(ccc == CCH - 1)),
                     c.s_pe, 0 if ccc < CCH - 1 else 1)
            pe = c.s_pe.count
            P.wait("scalar", c.s_pe, pe)
            P.do("scalar", lambda q, f=f, ps=ps, dg=dg:
                 q.activation(f.kact[:, dg, :], ps[:, :], AF.Relu), c.s_act, 1)
        act_relu = c.s_act.count
        P.wait("scalar", c.s_act, act_relu)
        P.do("scalar", lambda q, f=f:
             q.activation(f.kact[:, :, :], f.kact[:, :, :], AF.Square), c.s_act, 1)
        act_sq = c.s_act.count
        P.wait("tensor", c.s_act, act_sq)
        for dg in range(CCH):
            ps = f.p_ps if dg % 2 == 0 else f.p2_ps
            P.wait("tensor", c.s_dv, c.s_dv.count)
            for ccc in range(CCH):
                P.do("tensor", lambda q, f=f, ps=ps, dg=dg, ccc=ccc:
                     q.matmul(ps[:, :], f.w_v[:, ccc, dg * 128:(dg + 1) * 128],
                              f.kact[:, ccc, :], start=(ccc == 0), stop=(ccc == CCH - 1)),
                     c.s_pe, 0 if ccc < CCH - 1 else 1)
            pe = c.s_pe.count
            P.wait("vector", c.s_pe, pe)
            P.do("vector", lambda q, f=f, ps=ps, dg=dg:
                 q.tensor_tensor(f.vtmp[:, :], ps[:, :], f.r[:, dg, :], ALU.mult))
            P.do("vector", lambda q, c=c, f=f, dg=dg, t0=t0:
                 q.tensor_tensor(c.xcur[:, dg, t0:t0 + SEG],
                                 c.xcur[:, dg, t0:t0 + SEG], f.vtmp[:, :], ALU.add),
                 c.s_dv, 1)
    return c.s_dv.count



NKC = T // 128
QS = 1024
NQS = T // QS
RT8 = 0.35355339059327373


def alloc_attn0(nc, A=None):
    a = Ctx()
    A = A or nc.alloc_sbuf_tensor
    a.w_v0 = A("a_wv", [128, CCH, C], BF16)
    a.wqk = A("a_wqk", [128, 2, CCH, 128], BF16)
    a.bq = A("a_bq", [64, NH], F32)
    a.bk = A("a_bk", [64, NH], F32)
    a.bvrow = A("a_bvrow", [1, C], F32)
    a.bvb = A("a_bvb", [128, C], F32)
    a.qp = A("a_qp", [65, T], BF16)
    a.kp = A("a_kp", [65, T], BF16)
    a.vp = A("a_vp", [128, NKC, 65], BF16)
    a.qsq = A("a_qsq", [64, T], BF16)
    a.ksq = A("a_ksq", [64, T], BF16)
    a.kmax = A("a_kmax", [64, 1], BF16)
    a.negmk2h = A("a_negmk2h", [1, 1], F32)
    a.pt = A("a_pt", [128, 2, QS], BF16)
    a.rl = A("a_rl", [1, QS], F32)
    a.rb = A("a_rb", [64, QS], F32)
    a.otmp = A("a_otmp", [128, QS], F32)
    return a


def emit_attn0_weights(cx, a, wv_ap, bq_ap, bk_ap, bv_ap):
    P = cx.P
    wv_tok = cx.gdma.start("gpsimd", lambda q, a=a, w=wv_ap:
                           q.dma_start(a.w_v0[:, :, :],
                                       w.rearrange("(cc p) d -> p cc d", p=128)))
    a.wv_tok = wv_tok
    t1 = cx.dma.start("sync", lambda q, a=a, b=bq_ap: q.dma_start(a.bq[:, :], b))
    t2 = cx.dma.start("sync", lambda q, a=a, b=bk_ap: q.dma_start(a.bk[:, :], b))
    t3 = cx.dma.start("sync", lambda q, a=a, b=bv_ap: q.dma_start(a.bvrow[:, :], b))
    for t in (t1, t2, t3):
        cx.dma.wait("vector", t)
    P.do("vector", lambda q, a=a: q.tensor_scalar_mul(a.bq[:, :], a.bq[:, :], RT8))
    P.do("vector", lambda q, a=a: q.tensor_scalar_mul(a.bk[:, :], a.bk[:, :], RT8),
         cx.s_dv, 1)
    bb = cx.pool[:, 2, :]
    cx.dma.wait("tensor", t3)
    P.wait("tensor", cx.s_dv, cx.dv_consts)
    for s in range(2):
        P.do("tensor", lambda q, c=cx, a=a, s=s, bb=bb:
             q.matmul(bb[:, s * 512:(s + 1) * 512], c.ones1_f[:, :],
                      a.bvrow[0:1, s * 512:(s + 1) * 512], start=True, stop=True),
             cx.s_pe, 1)
    pe = cx.s_pe.count
    P.wait("scalar", cx.s_pe, pe)
    P.do("scalar", lambda q, a=a, bb=bb: q.copy(a.bvb[:, :], bb[:, :]), cx.s_act, 1)
    return cx.s_act.count


def emit_wqk_load(cx, a, wqk_heads_ap, h):
    return cx.gdma.start("gpsimd", lambda q, a=a, w=wqk_heads_ap, h=h:
                         q.dma_start(a.wqk[:, h % 2, :, :],
                                     w[h].rearrange("(cc p) d -> p cc d", p=128)))


def emit_attn0_batch(cx, a, wqk_heads_ap, wg_first):
    P = cx.P
    c = cx
    pool = c.pool
    st_pair = [pool[:, 0, :], pool[:, 1, :]]
    o_ps = pool[0:65, 2, :]
    rb_ps = pool[0:64, 3, :]
    q_ps = pool[0:64, 2, :]
    k_ps = pool[0:64, 3, :]
    v_ps = pool[:, 2, 0:64]
    qs2_ps = pool[0:1, 0:2, :].rearrange("p a s -> p (a s)")
    mk2_ps = pool[0:1, 3, 512:513]

    P.do("vector", lambda q, a=a: q.memset(a.vp[:, :, 64:65], 1.0))
    P.do("vector", lambda q, a=a: q.memset(a.kp[64:65, :], 1.0), c.s_dv, 1)

    wg_prev = wg_first
    for h in range(NH):
        par = h % 2
        if h + 1 < NH:
            P.wait("gpsimd", c.s_pe, c.s_pe.count)
            wg_next = emit_wqk_load(cx, a, wqk_heads_ap, h + 1)
        c.gdma.wait("tensor", wg_prev)
        for half in range(2):
            P.wait("tensor", c.s_dv, c.s_dv.count)
            for s2 in range(2):
                t0 = half * 1024 + s2 * 512
                for ccc in range(CCH):
                    P.do("tensor", lambda q, a=a, par=par, ccc=ccc, s2=s2, t0=t0, q_ps=q_ps:
                         q.matmul(q_ps[:, s2 * 512:(s2 + 1) * 512],
                                  a.wqk[:, par, ccc, 0:64], c.lnx[:, ccc, t0:t0 + 512],
                                  start=(ccc == 0), stop=(ccc == CCH - 1)),
                         c.s_pe, 0 if ccc < CCH - 1 else 1)
            pe = c.s_pe.count
            P.wait("vector", c.s_pe, pe)
            P.do("vector", lambda q, a=a, h=h, half=half, q_ps=q_ps:
                 q.tensor_scalar(a.qp[0:64, half * 1024:(half + 1) * 1024], q_ps[:, :],
                                 RT8, a.bq[:, h:h + 1], ALU.mult, ALU.add), c.s_dv, 1)
        for half in range(2):
            P.wait("tensor", c.s_act, c.s_act.count)
            P.wait("tensor", c.s_dv, c.s_dv.count)
            for s2 in range(2):
                t0 = half * 1024 + s2 * 512
                for ccc in range(CCH):
                    P.do("tensor", lambda q, a=a, par=par, ccc=ccc, s2=s2, t0=t0, k_ps=k_ps:
                         q.matmul(k_ps[:, s2 * 512:(s2 + 1) * 512],
                                  a.wqk[:, par, ccc, 64:128], c.lnx[:, ccc, t0:t0 + 512],
                                  start=(ccc == 0), stop=(ccc == CCH - 1)),
                         c.s_pe, 0 if ccc < CCH - 1 else 1)
            pe = c.s_pe.count
            P.wait("vector", c.s_pe, pe)
            P.do("vector", lambda q, a=a, h=h, half=half, k_ps=k_ps:
                 q.tensor_scalar(a.kp[0:64, half * 1024:(half + 1) * 1024], k_ps[:, :],
                                 RT8, a.bk[:, h:h + 1], ALU.mult, ALU.add), c.s_dv, 1)
        dv_qk = c.s_dv.count
        P.wait("scalar", c.s_dv, dv_qk)
        P.do("scalar", lambda q, a=a:
             q.activation(a.qsq[:, :], a.qp[0:64, :], AF.Square))
        P.do("scalar", lambda q, a=a:
             q.activation(a.ksq[:, :], a.kp[0:64, :], AF.Square), c.s_act, 1)
        act_sq = c.s_act.count
        P.wait("tensor", c.s_act, act_sq)
        for s in range(4):
            P.do("tensor", lambda q, c=c, a=a, s=s, qs2_ps=qs2_ps:
                 q.matmul(qs2_ps[:, s * 512:(s + 1) * 512], c.ones128_b[0:64, :],
                          a.qsq[:, s * 512:(s + 1) * 512], start=True, stop=True),
                 c.s_pe, 0 if s < 3 else 1)
        P.wait("vector", c.s_act, act_sq)
        P.do("vector", lambda q, a=a:
             q.reduce_max(a.kmax[:, :], a.ksq[:, :], mybir.AxisListType.X), c.s_dv, 1)
        dv_kmax = c.s_dv.count
        P.wait("tensor", c.s_dv, dv_kmax)
        P.do("tensor", lambda q, c=c, a=a, mk2_ps=mk2_ps:
             q.matmul(mk2_ps, c.ones128_b[0:64, :], a.kmax[:, :],
                      start=True, stop=True), c.s_pe, 1)
        pe_bound = c.s_pe.count
        P.wait("scalar", c.s_pe, pe_bound)
        P.do("scalar", lambda q, a=a, mk2_ps=mk2_ps:
             q.mul(a.negmk2h[:, :], mk2_ps, -0.5), c.s_act, 1)
        act_mk2 = c.s_act.count
        P.wait("vector", c.s_act, act_mk2)
        P.do("vector", lambda q, a=a, qs2_ps=qs2_ps:
             q.tensor_scalar(a.qp[64:65, :], qs2_ps, -0.5, a.negmk2h[:, :],
                             ALU.mult, ALU.add), c.s_dv, 1)
        dv_bound = c.s_dv.count
        vslots = [pool[:, 2, 0:64], pool[:, 2, 512:576],
                  pool[:, 3, 0:64], pool[:, 3, 512:576]]
        evac_dv = [None] * NKC
        for tg in range(NKC):
            vs = vslots[tg % 4]
            if tg >= 4:
                P.wait("tensor", c.s_dv, evac_dv[tg - 4])
            elif tg == 0:
                P.wait("tensor", c.s_dv, c.s_dv.count)
            for ccc in range(CCH):
                P.do("tensor", lambda q, c=c, a=a, tg=tg, ccc=ccc, h=h, vs=vs:
                     q.matmul(vs, c.lnx[:, ccc, tg * 128:(tg + 1) * 128],
                              a.w_v0[:, ccc, h * 64:(h + 1) * 64],
                              start=(ccc == 0), stop=(ccc == CCH - 1)),
                     c.s_pe, 0 if ccc < CCH - 1 else 1)
            pe = c.s_pe.count
            P.wait("vector", c.s_pe, pe)
            P.do("vector", lambda q, a=a, tg=tg, h=h, vs=vs:
                 q.tensor_tensor(a.vp[:, tg, 0:64], vs,
                                 a.bvb[:, h * 64:(h + 1) * 64], ALU.add), c.s_dv, 1)
            evac_dv[tg] = c.s_dv.count
        for qs in range(NQS):
            nkc = (qs + 1) * (QS // 128)
            exp_done = [None] * nkc
            pool_done = [None] * nkc
            for step in range(nkc + 1):
                if step < nkc:
                    kc = step
                    sp = kc % 2
                    stv = st_pair[sp]
                    if kc < 2 and qs == 0:
                        P.wait("tensor", c.s_dv, dv_bound)
                    if kc >= 2:
                        P.wait("tensor", c.s_act, exp_done[kc - 2])
                    for s2 in range(2):
                        P.do("tensor", lambda q, a=a, kc=kc, qs=qs, s2=s2, stv=stv:
                             q.matmul(stv[:, s2 * 512:(s2 + 1) * 512],
                                      a.kp[:, kc * 128:(kc + 1) * 128],
                                      a.qp[:, qs * QS + s2 * 512:qs * QS + (s2 + 1) * 512],
                                      start=True, stop=True),
                             c.s_pe, 0 if s2 == 0 else 1)
                    pe_st = c.s_pe.count
                    P.wait("scalar", c.s_pe, pe_st)
                    P.do("scalar", lambda q, a=a, sp=sp, stv=stv:
                         q.activation(a.pt[:, sp, :], stv[:, :], AF.Exp), c.s_act, 1)
                    exp_done[kc] = c.s_act.count
                    if kc >= qs * (QS // 128):
                        P.wait("gpsimd", c.s_act, exp_done[kc])
                        P.do("gpsimd", lambda q, a=a, sp=sp, qs=qs, kc=kc:
                             q.affine_select(a.pt[:, sp, :], a.pt[:, sp, :],
                                             [[1, QS]], ALU.is_ge, 0.0,
                                             base=qs * QS - kc * 128,
                                             channel_multiplier=-1), c.s_pool, 1)
                        pool_done[kc] = c.s_pool.count
                if step >= 1:
                    kc = step - 1
                    sp = kc % 2
                    if pool_done[kc] is not None:
                        P.wait("tensor", c.s_pool, pool_done[kc])
                    else:
                        P.wait("tensor", c.s_act, exp_done[kc])
                    if kc == 0:
                        P.wait("tensor", c.s_dv, c.s_dv.count)
                    for s2 in range(2):
                        P.do("tensor", lambda q, a=a, kc=kc, sp=sp, s2=s2, nkc=nkc, o_ps=o_ps:
                             q.matmul(o_ps[:, s2 * 512:(s2 + 1) * 512],
                                      a.vp[:, kc, :], a.pt[:, sp, s2 * 512:(s2 + 1) * 512],
                                      start=(kc == 0), stop=(kc == nkc - 1)),
                             c.s_pe, 0 if (s2 == 0 or kc < nkc - 1) else 1)
            pe_pv = c.s_pe.count
            P.wait("vector", c.s_pe, pe_pv)
            P.do("vector", lambda q, a=a, o_ps=o_ps:
                 q.reciprocal(a.rl[:, :], o_ps[64:65, :]), c.s_dv, 1)
            dv_rl = c.s_dv.count
            P.wait("tensor", c.s_dv, dv_rl)
            for s2 in range(2):
                P.do("tensor", lambda q, c=c, a=a, s2=s2, rb_ps=rb_ps:
                     q.matmul(rb_ps[:, s2 * 512:(s2 + 1) * 512], c.ones1_f[:, 0:64],
                              a.rl[:, s2 * 512:(s2 + 1) * 512], start=True, stop=True),
                     c.s_pe, 0 if s2 == 0 else 1)
            pe_rb = c.s_pe.count
            P.wait("scalar", c.s_pe, pe_rb)
            P.do("scalar", lambda q, a=a, rb_ps=rb_ps:
                 q.copy(a.rb[:, :], rb_ps), c.s_act, 1)
            act_rb = c.s_act.count
            P.wait("vector", c.s_act, act_rb)
            base = (h % 2) * 64
            cc_t = h // 2
            P.do("vector", lambda q, a=a, base=base, o_ps=o_ps:
                 q.tensor_tensor(a.otmp[base:base + 64, :], o_ps[0:64, :],
                                 a.rb[:, :], ALU.mult))
            P.do("vector", lambda q, c=c, a=a, base=base, cc_t=cc_t, qs=qs:
                 q.tensor_tensor(c.xcur[base:base + 64, cc_t, qs * QS:(qs + 1) * QS],
                                 c.xcur[base:base + 64, cc_t, qs * QS:(qs + 1) * QS],
                                 a.otmp[base:base + 64, :], ALU.add), c.s_dv, 1)
        if h + 1 < NH:
            wg_prev = wg_next
    return c.s_dv.count


def plan_p2(nc, P, cx, A, wqk_h, wv, bq, bk, bv, x_in, x_out):
    if A is not None:
        cx.lnx = A("lnx", [128, CCH, T + 1], BF16)
    a = alloc_attn0(nc, A)
    emit_attn0_weights(cx, a, wv, bq, bk, bv)
    wg0 = emit_wqk_load(cx, a, wqk_h, 0)

    wg_for_batch = wg0
    for b in range(2):
        ld_tok = emit_load_xcur(cx, x_in, b)
        cx.dma.wait("vector", ld_tok)
        for s in range(NSEG):
            emit_ln_seg(cx, s * SEG, lambda cc, c=cx, s=s:
                        c.lnx[:, cc, s * SEG:(s + 1) * SEG])
        P.wait("tensor", cx.s_dv, cx.s_dv.count)
        cx.gdma.wait("tensor", a.wv_tok)
        emit_attn0_batch(cx, a, wqk_h, wg_for_batch)
        if b == 0:
            P.wait("gpsimd", cx.s_pe, cx.s_pe.count)
            wg_for_batch = emit_wqk_load(cx, a, wqk_h, 0)
        dv_done = cx.s_dv.count
        P.wait("sync", cx.s_dv, dv_done)
        st_tok = emit_store_xcur(cx, x_out, b)
        note_store(cx, b, st_tok)
        if b == 0:
            for e in ("sync", "vector", "tensor", "scalar"):
                cx.stst.wait(e, st_tok)


def build_p2():
    nc = bass.Bass("TRN2", target_bir_lowering=False, debug=False,
                   detect_race_conditions=False)
    x_in = nc.dram_tensor("x_in", [2, C, T], F32, kind="ExternalInput").ap()
    wqk_h = nc.dram_tensor("wqk_h", [NH, C, 128], F32, kind="ExternalInput").ap()
    wv = nc.dram_tensor("wv", [C, C], F32, kind="ExternalInput").ap()
    bq = nc.dram_tensor("bq", [64, NH], F32, kind="ExternalInput").ap()
    bk = nc.dram_tensor("bk", [64, NH], F32, kind="ExternalInput").ap()
    bv = nc.dram_tensor("bv", [1, C], F32, kind="ExternalInput").ap()
    x_out = nc.dram_tensor("x_out", [2, C, T], F32, kind="ExternalOutput").ap()
    P = Prog(nc)
    cx = alloc_common(nc, P)
    cx.lnx = nc.alloc_sbuf_tensor("lnx", [128, CCH, T + 1], BF16)
    emit_consts(cx)
    plan_p2(nc, P, cx, None, wqk_h, wv, bq, bk, bv, x_in, x_out)
    cx.stst.wait_all("sync")
    P.build()
    return nc



MT = 4096
P4_DEBUG = False
TTS = 512
NTT = MT // TTS


def alloc_moe(nc, A=None):
    m = Ctx()
    A = A or nc.alloc_sbuf_tensor
    m.idx = A("m_idx", [128, 64], I32)
    m.idxf = A("m_idxf", [128, 64], F32)
    m.mcmp = A("m_mcmp", [128, 64], F32)
    m.ew = A("m_ew", [128, 64], F32)
    m.maskp = A("m_maskp", [128, 32], F32)
    m.eid = A("m_eid", [128, 1], F32)
    m.gate = A("m_gate", [128, 2], F32)
    m.bias = A("m_bias", [128, CCH], F32)
    m.mrow = A("m_mrow", [1, 2, TTS], F32)
    m.lns = A("m_lns", [128, CCH, SEG], BF16)
    m.wbb = A("m_wbb", [128, FF], BF16)
    m.hbf = m.wbb[:, :].rearrange("p (a t) -> p a t", a=CCH)
    m.w1f = A("m_w1f", [128, 2, CCH, 1024], BF16)
    m.w2f = A("m_w2f", [128, CCH, 1024], BF16)
    m.xm = A("m_xm", [128, CCH, TTS], BF16)
    m.ysb = A("m_ysb", [128, CCH, TTS], F32)
    m.rsb = m.ysb
    return m


def emit_moe_mask(cx, m):
    P = cx.P
    P.do("vector", lambda q, m=m: q.tensor_copy(m.idxf[:, :], m.idx[:, :]))
    P.do("vector", lambda q, m=m:
         q.tensor_scalar(m.mcmp[:, :], m.idxf[:, :], m.eid[:, :], None, ALU.is_equal))
    P.do("vector", lambda q, m=m:
         q.tensor_tensor(m.mcmp[:, :], m.mcmp[:, :], m.ew[:, :], ALU.mult))
    P.do("vector", lambda q, m=m:
         q.tensor_copy(m.maskp[:, :], m.mcmp[:, 32:64]))
    P.do("vector", lambda q, m=m:
         q.tensor_tensor(m.maskp[:, :], m.maskp[:, :], m.mcmp[:, 0:32],
                         ALU.add))
    P.do("vector", lambda q, m=m:
         q.tensor_scalar_mul(m.maskp[:, :], m.maskp[:, :], 0.5), cx.s_dv, 1)
    P.wait("sync", cx.s_dv, cx.s_dv.count)
    t = cx.stst.start("sync", lambda q, m=m, mask_d=m.mask_d:
                      q.dma_start(mask_d.rearrange("o (p f) -> (o p) f", p=128),
                                  m.maskp[:, :]))
    return t


def emit_weight_cache(cx, m, w_ap, wbf_ap, rows):
    toks = []
    for r0 in range(0, rows, 128):
        t1 = cx.gdma.start("gpsimd", lambda q, m=m, w=w_ap, r0=r0:
                           q.dma_start(m.wbb[:, 0:w.shape[1]], w[r0:r0 + 128, :]))
        cx.gdma.wait("sync", t1)
        t2 = cx.stst.start("sync", lambda q, m=m, w=wbf_ap, r0=r0:
                           q.dma_start(w[r0:r0 + 128, :], m.wbb[:, 0:w.shape[1]]))
        cx.stst.wait("gpsimd", t2)
        toks.append(t2)
    return toks


def plan_p4(nc, P, cx, A, w1c, w2c, idx, ew, eid, gate, bias, dd,
            x_in, x_out, convert=True):
    w1bf = dd["w1bf"]; w2bf = dd["w2bf"]; cc_in = dd["cc_in"]
    cc_out = dd["cc_out"]; arin = dd["arin"]; arout = dd["arout"]
    mask_d = dd["mask_d"]
    m = alloc_moe(nc, A)
    m.mask_d = mask_d
    h_ps = [cx.pool[:, 2, 0:512], cx.pool[:, 2, 512:1024]]
    y_ps = [cx.pool[:, 3, 0:512], cx.pool[:, 3, 512:1024]]
    mb_ps = cx.pool[:, 0, 0:512]

    emit_consts(cx)
    for (sb, ap) in ((m.idx, idx), (m.ew, ew), (m.eid, eid), (m.gate, gate),
                     (m.bias, bias)):
        t = cx.dma.start("sync", lambda q, sb=sb, ap=ap: q.dma_start(sb[:, :], ap))
        cx.dma.wait("vector", t)
    mask_tok = emit_moe_mask(cx, m)
    P.do("vector", lambda q, m=m:
         q.tensor_scalar_mul(m.bias[:, :], m.bias[:, :], 0.5), cx.s_dv, 1)
    dv_pre = cx.s_dv.count

    if convert:
        w1toks = emit_weight_cache(cx, m, w1c, w1bf, C)
        w2toks = emit_weight_cache(cx, m, w2c, w2bf, FF)

    ldA = emit_load_xcur(cx, x_in, 0)
    cx.dma.wait("vector", ldA)
    ctoks = []
    for s in range(NSEG):
        emit_ln_seg(cx, s * SEG, lambda cc, m=m: m.lns[:, cc, :])
        dv_ln = cx.s_dv.count
        P.wait("sync", cx.s_dv, dv_ln)
        ct = cx.stst.start("sync", lambda q, m=m, cc_in=cc_in, s=s:
                           q.dma_start(cc_in[:, s * SEG:(s + 1) * SEG]
                                       .rearrange("(cc p) t -> p cc t", p=128),
                                       m.lns[:, :, :]))
        ctoks.append(ct)
        cx.stst.wait("vector", ct)
    cc_sem = P.sem()
    for ct in ctoks:
        cx.stst.wait("gpsimd", ct)
    P.do("gpsimd", lambda q, cc_in=cc_in, cc_out=cc_out:
         q.collective_compute("AllGather", ALU.bypass,
                              replica_groups=[list(range(NCORES))],
                              ins=[cc_in.opt()], outs=[cc_out.opt()]),
         cc_sem, 1)
    P.wait("sync", cc_sem, 1)

    if convert:
        for t in w1toks + w2toks:
            cx.stst.wait("gpsimd", t)
        cx.stst.wait("scalar", w1toks[-1])
        cx.stst.wait("scalar", w2toks[-1])
    ytoks = []
    for tt in range(NTT):
        rank = tt // (NTT // 2)
        col0 = (tt % (NTT // 2)) * TTS
        P.wait("sync", cx.s_pe, cx.s_pe.count)
        xt = cx.dma.start("sync", lambda q, m=m, cc_out=cc_out, rank=rank, col0=col0:
                          q.dma_start(m.xm[:, :, :],
                                      cc_out[rank * C:(rank + 1) * C,
                                             col0:col0 + TTS]
                                      .rearrange("(cc p) t -> p cc t", p=128)))
        cx.dma.wait("tensor", xt)
        mpar = tt % 2
        if tt == 0:
            cx.stst.wait("sync", mask_tok)
        P.wait("sync", cx.s_pe, cx.s_pe.count)
        mrt = cx.dma.start("sync", lambda q, m=m, tt=tt, mpar=mpar:
                           q.dma_start(m.mrow[:, mpar, :],
                                       m.mask_d[:, tt * TTS:(tt + 1) * TTS]))
        if tt == 0:
            P.wait("tensor", cx.s_dv, dv_pre)
        cx.dma.wait("tensor", mrt)
        P.wait("tensor", cx.s_dv, cx.s_dv.count)
        P.do("tensor", lambda q, c=cx, m=m, tt=tt, mpar=mpar:
             q.matmul(mb_ps, c.ones1_f[:, :], m.mrow[:, mpar, :],
                      start=True, stop=True), cx.s_pe, 1)
        for fc in range(4):
            par = fc % 2
            P.wait("gpsimd", cx.s_pe, cx.s_pe.count)
            w1t = cx.gdma.start("gpsimd", lambda q, m=m, w1bf=w1bf, fc=fc, par=par:
                                q.dma_start(m.w1f[:, par, :, :],
                                            w1bf[:, fc * 1024:(fc + 1) * 1024]
                                            .rearrange("(cc p) f -> p cc f", p=128)))
            cx.gdma.wait("tensor", w1t)
            if P4_DEBUG and tt == 0 and fc == 0:
                dbg_xm = nc.dram_tensor("dbg_xm", [128, CCH, TTS], F32,
                                        kind="ExternalOutput").ap()
                cx.dma.wait("gpsimd", xt)
                dxm = cx.stst.start("gpsimd", lambda q, m=m, o=dbg_xm:
                                    q.dma_start(o, m.xm[:, :, :]))
                cx.stst.wait("gpsimd", dxm)
            for mg in range(CCH):
                hp = h_ps[mg % 2]
                P.wait("tensor", cx.s_act, cx.s_act.count)
                for ccc in range(CCH):
                    P.do("tensor", lambda q, m=m, par=par, mg=mg, ccc=ccc, hp=hp:
                         q.matmul(hp, m.w1f[:, par, ccc, mg * 128:(mg + 1) * 128],
                                  m.xm[:, ccc, :],
                                  start=(ccc == 0), stop=(ccc == CCH - 1)),
                         cx.s_pe, 0 if ccc < CCH - 1 else 1)
                pe = cx.s_pe.count
                P.wait("scalar", cx.s_pe, pe)
                P.do("scalar", lambda q, m=m, mg=mg, hp=hp:
                     q.activation(m.hbf[:, mg, :], hp, AF.Gelu), cx.s_act, 1)
            act_h = cx.s_act.count
            if P4_DEBUG and tt == 0 and fc == 0:
                dbg_h = nc.dram_tensor("dbg_h", [128, CCH, TTS], F32,
                                       kind="ExternalOutput").ap()
                P.wait("gpsimd", cx.s_act, act_h)
                dh = cx.stst.start("gpsimd", lambda q, m=m, o=dbg_h:
                                   q.dma_start(o, m.hbf[:, :, :]))
                cx.stst.wait("gpsimd", dh)
                cx.stst.wait("scalar", dh)
                cx.stst.wait("tensor", dh)
            P.wait("gpsimd", cx.s_pe, cx.s_pe.count)
            w2t = cx.gdma.start("gpsimd", lambda q, m=m, w2bf=w2bf, fc=fc:
                                q.dma_start(m.w2f[:, :, :],
                                            w2bf[fc * 1024:(fc + 1) * 1024, :]
                                            .rearrange("(fm p) d -> p fm d", p=128)))
            cx.gdma.wait("tensor", w2t)
            P.wait("tensor", cx.s_act, act_h)
            for dg in range(CCH):
                yp = y_ps[dg % 2]
                P.wait("tensor", cx.s_dv, cx.s_dv.count)
                for fm in range(CCH):
                    P.do("tensor", lambda q, m=m, fm=fm, dg=dg, yp=yp:
                         q.matmul(yp,
                                  m.w2f[:, fm, dg * 128:(dg + 1) * 128],
                                  m.hbf[:, fm, :],
                                  start=(fm == 0), stop=(fm == CCH - 1)),
                         cx.s_pe, 0 if fm < CCH - 1 else 1)
                pe = cx.s_pe.count
                P.wait("vector", cx.s_pe, pe)
                if fc == 0:
                    P.do("vector", lambda q, m=m, dg=dg, yp=yp:
                         q.tensor_copy(m.ysb[:, dg, :], yp))
                else:
                    P.do("vector", lambda q, m=m, dg=dg, yp=yp:
                         q.tensor_tensor(m.ysb[:, dg, :], m.ysb[:, dg, :],
                                         yp, ALU.add))
        if P4_DEBUG and tt == 0:
            dbg_y0 = nc.dram_tensor("dbg_y0", [128, CCH, TTS], F32,
                                    kind="ExternalOutput").ap()
            P.wait("sync", cx.s_dv, cx.s_dv.count)
            dy0 = cx.stst.start("sync", lambda q, m=m, o=dbg_y0:
                                q.dma_start(o, m.ysb[:, :, :]))
            cx.stst.wait("sync", dy0)
            cx.stst.wait("vector", dy0)
        pe_mb = cx.s_pe.count
        P.wait("vector", cx.s_pe, pe_mb)
        P.do("vector", lambda q, m=m, mb_ps=mb_ps:
             q.tensor_tensor(m.ysb[:, :, :], m.ysb[:, :, :],
                             mb_ps[:, :].unsqueeze(1).broadcast_to([128, CCH, TTS]),
                             ALU.mult), cx.s_dv, 1)
        dv_y = cx.s_dv.count
        P.wait("sync", cx.s_dv, dv_y)
        yt = cx.stst.start("sync", lambda q, m=m, arin=arin, tt=tt:
                           q.dma_start(arin[:, tt * TTS:(tt + 1) * TTS]
                                       .rearrange("(cc p) t -> p cc t", p=128),
                                       m.ysb[:, :, :]))
        ytoks.append(yt)
        cx.stst.wait("vector", yt)

    if P4_DEBUG:
        dbg_maskp = nc.dram_tensor("dbg_maskp", [128, 32], F32, kind="ExternalOutput").ap()
        dbg_mcmp = nc.dram_tensor("dbg_mcmp", [128, 64], F32, kind="ExternalOutput").ap()
        dbg_idxf = nc.dram_tensor("dbg_idxf", [128, 64], F32, kind="ExternalOutput").ap()
        dbg_ew = nc.dram_tensor("dbg_ew", [128, 64], F32, kind="ExternalOutput").ap()
        dbg_mask = nc.dram_tensor("dbg_mask", [1, MT], F32, kind="ExternalOutput").ap()
        dbg_gather = nc.dram_tensor("dbg_gather", [C, T], F32, kind="ExternalOutput").ap()
        dbg_arin = nc.dram_tensor("dbg_arin", [C, MT], F32, kind="ExternalOutput").ap()
        cx.stst.wait("sync", mask_tok)
        dt0 = cx.stst.start("sync", lambda q: q.dma_start(dbg_maskp, m.maskp[:, :]))
        dta = cx.stst.start("sync", lambda q: q.dma_start(dbg_mcmp, m.mcmp[:, :]))
        dtb = cx.stst.start("sync", lambda q: q.dma_start(dbg_idxf, m.idxf[:, :]))
        dtc = cx.stst.start("sync", lambda q: q.dma_start(dbg_ew, m.ew[:, :]))
        for _t in (dta, dtb, dtc):
            cx.stst.wait("sync", _t)
        dt1 = cx.stst.start("sync", lambda q: q.dma_start(dbg_mask, m.mask_d))
        cx.stst.wait("sync", dt0)
        P.wait("gpsimd", cc_sem, 1)
        dt2 = cx.stst.start("gpsimd", lambda q: q.dma_start(dbg_gather, cc_out[0:C, :]))
        for t in ytoks:
            cx.stst.wait("sync", t)
        dt3 = cx.stst.start("sync", lambda q: q.dma_start(dbg_arin, arin))
        cx.stst.wait("sync", dt1)
        cx.stst.wait("sync", dt3)
        cx.stst.wait("gpsimd", dt2)
    ar_sem = P.sem()
    for t in ytoks:
        cx.stst.wait("gpsimd", t)
    P.do("gpsimd", lambda q, arin=arin, arout=arout:
         q.collective_compute("AllReduce", ALU.add,
                              replica_groups=[list(range(NCORES))],
                              ins=[arin.opt()], outs=[arout.opt()]),
         ar_sem, 1)
    P.wait("sync", ar_sem, 1)

    if P4_DEBUG:
        dbg_arout = nc.dram_tensor("dbg_arout", [C, MT], F32, kind="ExternalOutput").ap()
        P.wait("sync", ar_sem, 1)
        dao = cx.stst.start("sync", lambda q: q.dma_start(dbg_arout, arout))
        cx.stst.wait("sync", dao)
    for half in range(2):
        for s in range(NSEG):
            P.wait("sync", cx.s_dv, cx.s_dv.count)
            rt = cx.dma.start("sync", lambda q, m=m, arout=arout, half=half, s=s:
                              q.dma_start(m.rsb[:, :, :],
                                          arout[:, half * T + s * SEG:
                                                half * T + (s + 1) * SEG]
                                          .rearrange("(cc p) t -> p cc t", p=128)))
            cx.dma.wait("vector", rt)
            for cc in range(CCH):
                P.do("vector", lambda q, c=cx, m=m, half=half, s=s, cc=cc:
                     q.scalar_tensor_tensor(
                         c.xcur[:, cc, s * SEG:(s + 1) * SEG],
                         m.rsb[:, cc, :], m.gate[:, half:half + 1],
                         c.xcur[:, cc, s * SEG:(s + 1) * SEG],
                         ALU.mult, ALU.add),
                     cx.s_dv if cc == CCH - 1 else None, 1)
    for cc in range(CCH):
        P.do("vector", lambda q, c=cx, m=m, cc=cc:
             q.tensor_scalar(c.xcur[:, cc, :], c.xcur[:, cc, :],
                             m.bias[:, cc:cc + 1], None, ALU.add),
             cx.s_dv if cc == CCH - 1 else None, 1)
    dvA = cx.s_dv.count
    P.wait("sync", cx.s_dv, dvA)
    stA = emit_store_xcur(cx, x_out, 0)
    note_store(cx, 0, stA)
    for e in ("sync", "vector"):
        cx.stst.wait(e, stA)
    ldB = emit_load_xcur(cx, x_in, 1)
    cx.dma.wait("vector", ldB)
    for cc in range(CCH):
        P.do("vector", lambda q, c=cx, m=m, cc=cc:
             q.tensor_scalar(c.xcur[:, cc, :], c.xcur[:, cc, :],
                             m.bias[:, cc:cc + 1], None, ALU.add),
             cx.s_dv if cc == CCH - 1 else None, 1)
    dvB = cx.s_dv.count
    P.wait("sync", cx.s_dv, dvB)
    note_store(cx, 1, emit_store_xcur(cx, x_out, 1))


def moe_dram(nc):
    return {
        "w1bf": nc.dram_tensor("w1bf", [C, FF], BF16, kind="Internal").ap(),
        "w2bf": nc.dram_tensor("w2bf", [FF, C], BF16, kind="Internal").ap(),
        "cc_in": nc.dram_tensor("cc_in", [C, T], BF16, kind="Internal").ap(),
        "cc_out": nc.dram_tensor("cc_out", [NCORES * C, T], BF16,
                                 kind="Internal", addr_space="Shared").ap(),
        "arin": nc.dram_tensor("arin", [C, MT], F32, kind="Internal").ap(),
        "arout": nc.dram_tensor("arout", [C, MT], F32, kind="Internal",
                                addr_space="Shared").ap(),
        "mask_d": nc.dram_tensor("mask_d", [1, MT], F32, kind="Internal").ap(),
    }


def build_p4():
    nc = bass.Bass("TRN2", target_bir_lowering=False, debug=False,
                   detect_race_conditions=False)
    x_in = nc.dram_tensor("x_in", [2, C, T], F32, kind="ExternalInput").ap()
    w1c = nc.dram_tensor("w1c", [C, FF], F32, kind="ExternalInput").ap()
    w2c = nc.dram_tensor("w2c", [FF, C], F32, kind="ExternalInput").ap()
    idx = nc.dram_tensor("idx", [128, 64], I32, kind="ExternalInput").ap()
    ew = nc.dram_tensor("ew", [128, 64], F32, kind="ExternalInput").ap()
    eid = nc.dram_tensor("eid", [128, 1], F32, kind="ExternalInput").ap()
    gate = nc.dram_tensor("gate", [128, 2], F32, kind="ExternalInput").ap()
    bias = nc.dram_tensor("bias", [128, CCH], F32, kind="ExternalInput").ap()
    x_out = nc.dram_tensor("x_out", [2, C, T], F32, kind="ExternalOutput").ap()
    dd = moe_dram(nc)
    P = Prog(nc)
    cx = alloc_common(nc, P)
    emit_consts(cx)
    plan_p4(nc, P, cx, None, w1c, w2c, idx, ew, eid, gate, bias, dd,
            x_in, x_out, convert=True)
    cx.stst.wait_all("sync")
    P.build()
    return nc



PAIRS = [(0, 1), (0, 2), (0, 3), (1, 2), (1, 3), (2, 3)]
JSIGN = [1.0, -1.0, 1.0, 1.0, -1.0, 1.0]
LN6H = -0.5 * math.log(6.0)


def alloc_attn1(nc, A=None):
    a = Ctx()
    A = A or nc.alloc_sbuf_tensor
    a.wpq = A("b_wpq", [128, CCH, 128], BF16)
    a.wpk = A("b_wpk", [128, CCH, 128], BF16)
    a.w_vh = A("b_wvh", [128, 2, CCH, 64], BF16)
    a.w_out = A("b_wout", [128, CCH, C], BF16)
    a.bvb = A("b_bvb", [128, C], F32)
    a.bout = A("b_bout", [128, CCH], F32)
    a.AO = A("b_AO", [128, CCH, T], BF16)
    a.lqe = A("b_lqe", [128, 6, 256], BF16)
    a.jlke = A("b_jlke", [128, 6, 256], BF16)

    a.vp = A("b_vp", [128, NKC, 65], BF16)
    a.pt = A("b_pt", [128, 2, QS], BF16)
    a.rl = A("b_rl", [1, QS], F32)
    a.rb = A("b_rb", [64, QS], F32)
    a.ytmp = A("b_ytmp", [128, SEG], F32)
    a.ln6h = A("b_ln6h", [128, 1], F32)
    a.bvrow = a.rb[0:1, :]
    return a


def bind_attn1_aliases(cx, a):
    xb = cx.xbf[0:6, :, :].rearrange("p a t -> p (a t)")
    a.lqh = xb[:, 0:T]
    a.jlkh = xb[:, T:2 * T]
    a.pairq = a.AO[:, 0, :]
    a.pairk = a.AO[:, 1, :]
    a.q1e = a.AO[:, 2, 0:1024].rearrange("p (c t) -> p c t", c=4)
    a.k1e = a.AO[:, 2, 1024:2048].rearrange("p (c t) -> p c t", c=4)
    a.q2e = a.AO[:, 3, 0:1024].rearrange("p (c t) -> p c t", c=4)
    a.k2e = a.AO[:, 3, 1024:2048].rearrange("p (c t) -> p c t", c=4)
    arena = a.AO[:, 4:8, :].rearrange("p a t -> p (a t)").bitcast(F32)
    a.lx = arena[:, 0:1536].rearrange("p (c t) -> p c t", c=6)
    a.sq = arena[:, 1536:3072].rearrange("p (c t) -> p c t", c=6)
    a.acc = arena[:, 3072:3328]
    a.rn = arena[:, 3328:3584]
    a.rnn = arena[:, 3584:3840]
    a.tmp1 = arena[:, 3840:4096]
    return a


def emit_wv1_load(cx, a, wvh_ap, h):
    return cx.gdma.start("gpsimd", lambda q, a=a, w=wvh_ap, h=h:
                         q.dma_start(a.w_vh[:, h % 2, :, :],
                                     w[h].rearrange("(cc p) d -> p cc d", p=128)))


def emit_attn1_weights(cx, a, wpq_ap, wpk_ap, wout_ap, bv_ap, bout_ap):
    P = cx.P
    toks = []
    for (sb, ap) in ((a.wpq, wpq_ap), (a.wpk, wpk_ap)):
        toks.append(cx.gdma.start("gpsimd", lambda q, sb=sb, ap=ap:
                    q.dma_start(sb[:, :, :],
                                ap.rearrange("(cc p) d -> p cc d", p=128))))
    a.wout_tok = cx.gdma.start("gpsimd", lambda q, a=a, w=wout_ap:
                               q.dma_start(a.w_out[:, :, :],
                                           w.rearrange("(cc p) d -> p cc d", p=128)))
    t3 = cx.dma.start("sync", lambda q, a=a, b=bv_ap: q.dma_start(a.bvrow, b))
    t4 = cx.dma.start("sync", lambda q, a=a, b=bout_ap: q.dma_start(a.bout[:, :], b))
    a.bout_tok = t4
    bb = cx.pool[:, 2, :]
    cx.dma.wait("tensor", t3)
    P.wait("tensor", cx.s_dv, cx.dv_consts)
    for s in range(2):
        P.do("tensor", lambda q, c=cx, a=a, s=s, bb=bb:
             q.matmul(bb[:, s * 512:(s + 1) * 512], c.ones1_f[:, :],
                      a.bvrow[0:1, s * 512:(s + 1) * 512], start=True, stop=True),
             cx.s_pe, 1)
    pe = cx.s_pe.count
    P.wait("scalar", cx.s_pe, pe)
    P.do("scalar", lambda q, a=a, bb=bb: q.copy(a.bvb[:, :], bb[:, :]))
    P.do("vector", lambda q, a=a: q.memset(a.ln6h[:, :], LN6H))
    a.wpq_toks = toks
    return toks


def emit_attn1_batch(cx, a):
    P = cx.P
    c = cx
    pool = c.pool
    st_pair = [pool[:, 0, :], pool[:, 1, :]]
    o_ps = pool[0:65, 2, :]
    rb_ps = pool[0:64, 3, :]
    pk_ps = pool[:, 2, 0:512]
    v_ps = pool[:, 3, 0:64]
    y_ps = [pool[:, 2, 0:512], pool[:, 2, 512:1024]]

    for t in a.wpq_toks:
        cx.gdma.wait("tensor", t)
    for (wp, dst) in ((a.wpq, a.pairq), (a.wpk, a.pairk)):
        for s in range(NSEG):
            P.wait("tensor", c.s_dv, c.s_dv.count)
            for ccc in range(CCH):
                P.do("tensor", lambda q, c=c, wp=wp, ccc=ccc, s=s:
                     q.matmul(pk_ps, wp[:, ccc, :], c.lnx[:, ccc, s * SEG:(s + 1) * SEG],
                              start=(ccc == 0), stop=(ccc == CCH - 1)),
                     c.s_pe, 0 if ccc < CCH - 1 else 1)
            pe = c.s_pe.count
            P.wait("vector", c.s_pe, pe)
            P.do("vector", lambda q, dst=dst, s=s, pk_ps=pk_ps:
                 q.tensor_copy(dst[:, s * SEG:(s + 1) * SEG], pk_ps))
    dv_proj = c.s_dv.count

    rp = []
    P.wait("sync", c.s_dv, dv_proj)
    for (src, dst) in ((a.pairq, a.q1e), (a.pairk, a.k1e)):
        for d4 in range(4):
            rp.append(cx.dma.start("sync", lambda q, src=src, dst=dst, d4=d4:
                      q.dma_start(
                          dst[:, d4, :], src[d4:64:4, :])))
    for (src, dst) in ((a.pairq, a.q2e), (a.pairk, a.k2e)):
        for d4 in range(4):
            rp.append(cx.dma.start("sync", lambda q, src=src, dst=dst, d4=d4:
                      q.dma_start(
                          dst[:, d4, :], src[64 + d4:128:4, :])))
    for t in rp:
        cx.dma.wait("vector", t)

    for side in range(2):
        e1 = a.q1e if side == 0 else a.k1e
        e2 = a.q2e if side == 0 else a.k2e
        for ci, (i, j) in enumerate(PAIRS):
            P.do("vector", lambda q, a=a, e1=e1, e2=e2, i=i, j=j:
                 q.tensor_tensor(a.tmp1[:, :], e1[:, i, :], e2[:, j, :], ALU.mult))
            P.do("vector", lambda q, a=a, e1=e1, e2=e2, i=i, j=j, ci=ci:
                 q.tensor_tensor(a.sq[:, ci, :], e1[:, j, :], e2[:, i, :], ALU.mult))
            P.do("vector", lambda q, a=a, ci=ci:
                 q.tensor_tensor(a.lx[:, ci, :], a.tmp1[:, :], a.sq[:, ci, :],
                                 ALU.subtract))
        dv_l = c.s_dv.count
        P.wait("scalar", c.s_dv, dv_l)
        P.do("scalar", lambda q, a=a:
             q.activation(a.sq[:, :, :], a.lx[:, :, :], AF.Square), c.s_act, 1)
        act_sq = c.s_act.count
        P.wait("vector", c.s_act, act_sq)
        P.do("vector", lambda q, a=a: q.tensor_copy(a.acc[:, :], a.sq[:, 0, :]))
        for ci in range(1, 6):
            P.do("vector", lambda q, a=a, ci=ci:
                 q.tensor_tensor(a.acc[:, :], a.acc[:, :], a.sq[:, ci, :], ALU.add))
        P.do("vector", lambda q, a=a:
             q.tensor_scalar_max(a.acc[:, :], a.acc[:, :], 1e-24), c.s_dv, 1)
        dv_acc = c.s_dv.count
        P.wait("scalar", c.s_dv, dv_acc)
        P.do("scalar", lambda q, a=a: q.activation(a.tmp1[:, :], a.acc[:, :], AF.Ln))
        if side == 0:
            P.do("scalar", lambda q, a=a:
                 q.activation(a.rn[:, :], a.tmp1[:, :], AF.Exp, scale=-0.5,
                              bias=a.ln6h[:, :]), c.s_act, 1)
        else:
            P.do("scalar", lambda q, a=a:
                 q.activation(a.rn[:, :], a.tmp1[:, :], AF.Exp, scale=-0.5),
                 c.s_act, 1)
        act_rn = c.s_act.count
        P.wait("vector", c.s_act, act_rn)
        if side == 0:
            for ci in range(6):
                P.do("vector", lambda q, a=a, ci=ci:
                     q.tensor_tensor(a.lqe[:, ci, :], a.lx[:, ci, :], a.rn[:, :],
                                     ALU.mult))
        else:
            P.do("vector", lambda q, a=a:
                 q.tensor_scalar_mul(a.rnn[:, :], a.rn[:, :], -1.0))
            for aa in range(6):
                src_c = 5 - aa
                rn_t = "rn" if JSIGN[aa] > 0 else "rnn"
                P.do("vector", lambda q, a=a, aa=aa, src_c=src_c, rn_t=rn_t:
                     q.tensor_tensor(a.jlke[:, aa, :], a.lx[:, src_c, :],
                                     getattr(a, rn_t)[:, :], ALU.mult))
    dv_ext = c.s_dv.count

    P.do("vector", lambda q, a=a: q.memset(a.vp[:, :, 64:65], 1.0))

    wvt_prev = a.wv0_tok
    for h in range(NH):
        par = h % 2
        if h + 1 < NH:
            P.wait("gpsimd", c.s_pe, c.s_pe.count)
            wvt_next = emit_wv1_load(cx, a, a.wvh_ap, h + 1)
        P.wait("sync", c.s_pe, c.s_pe.count)
        if h == 0:
            P.wait("sync", c.s_dv, dv_ext)
        rp_toks = []
        for ci in range(6):
            qn = "sync" if ci % 2 == 0 else "scalar"
            rp_toks.append(cx.dma.start(qn, lambda q, a=a, h=h, ci=ci:
                           q.dma_start(a.lqh[ci:ci + 1, :],
                                       a.lqe[h * 8:(h + 1) * 8, ci, :])))
            rp_toks.append(cx.dma.start(qn, lambda q, a=a, h=h, ci=ci:
                           q.dma_start(a.jlkh[ci:ci + 1, :],
                                       a.jlke[h * 8:(h + 1) * 8, ci, :])))
        for t in rp_toks:
            cx.dma.wait("tensor", t)
        cx.gdma.wait("tensor", wvt_prev)
        for tg in range(NKC):
            P.wait("tensor", c.s_dv, c.s_dv.count)
            for ccc in range(CCH):
                P.do("tensor", lambda q, c=c, a=a, tg=tg, ccc=ccc, par=par, v_ps=v_ps:
                     q.matmul(v_ps, c.lnx[:, ccc, tg * 128:(tg + 1) * 128],
                              a.w_vh[:, par, ccc, :],
                              start=(ccc == 0), stop=(ccc == CCH - 1)),
                     c.s_pe, 0 if ccc < CCH - 1 else 1)
            pe = c.s_pe.count
            P.wait("vector", c.s_pe, pe)
            P.do("vector", lambda q, a=a, tg=tg, h=h, v_ps=v_ps:
                 q.tensor_tensor(a.vp[:, tg, 0:64], v_ps,
                                 a.bvb[:, h * 64:(h + 1) * 64], ALU.add))
        for qs in range(NQS):
            nkc = (qs + 1) * (QS // 128)
            exp_done = [None] * nkc
            pool_done = [None] * nkc
            for step in range(nkc + 1):
                if step < nkc:
                    kc = step
                    sp = kc % 2
                    stv = st_pair[sp]
                    if kc >= 2:
                        P.wait("tensor", c.s_act, exp_done[kc - 2])
                    for s2 in range(2):
                        P.do("tensor", lambda q, a=a, kc=kc, qs=qs, s2=s2, stv=stv:
                             q.matmul(stv[:, s2 * 512:(s2 + 1) * 512],
                                      a.jlkh[:, kc * 128:(kc + 1) * 128],
                                      a.lqh[:, qs * QS + s2 * 512:qs * QS + (s2 + 1) * 512],
                                      start=True, stop=True),
                             c.s_pe, 0 if s2 == 0 else 1)
                    pe_st = c.s_pe.count
                    P.wait("scalar", c.s_pe, pe_st)
                    P.do("scalar", lambda q, a=a, sp=sp, stv=stv:
                         q.activation(a.pt[:, sp, :], stv[:, :], AF.Exp), c.s_act, 1)
                    exp_done[kc] = c.s_act.count
                    if kc >= qs * (QS // 128):
                        P.wait("gpsimd", c.s_act, exp_done[kc])
                        P.do("gpsimd", lambda q, a=a, sp=sp, qs=qs, kc=kc:
                             q.affine_select(a.pt[:, sp, :], a.pt[:, sp, :],
                                             [[1, QS]], ALU.is_ge, 0.0,
                                             base=qs * QS - kc * 128,
                                             channel_multiplier=-1), c.s_pool, 1)
                        pool_done[kc] = c.s_pool.count
                if step >= 1:
                    kc = step - 1
                    sp = kc % 2
                    if pool_done[kc] is not None:
                        P.wait("tensor", c.s_pool, pool_done[kc])
                    else:
                        P.wait("tensor", c.s_act, exp_done[kc])
                    if kc == 0:
                        P.wait("tensor", c.s_dv, c.s_dv.count)
                    for s2 in range(2):
                        P.do("tensor", lambda q, a=a, kc=kc, sp=sp, s2=s2, nkc=nkc, o_ps=o_ps:
                             q.matmul(o_ps[:, s2 * 512:(s2 + 1) * 512],
                                      a.vp[:, kc, :], a.pt[:, sp, s2 * 512:(s2 + 1) * 512],
                                      start=(kc == 0), stop=(kc == nkc - 1)),
                             c.s_pe, 0 if (s2 == 0 or kc < nkc - 1) else 1)
            pe_pv = c.s_pe.count
            P.wait("vector", c.s_pe, pe_pv)
            P.do("vector", lambda q, a=a, o_ps=o_ps:
                 q.reciprocal(a.rl[:, :], o_ps[64:65, :]), c.s_dv, 1)
            dv_rl = c.s_dv.count
            P.wait("tensor", c.s_dv, dv_rl)
            for s2 in range(2):
                P.do("tensor", lambda q, c=c, a=a, s2=s2, rb_ps=rb_ps:
                     q.matmul(rb_ps[:, s2 * 512:(s2 + 1) * 512], c.ones1_f[:, 0:64],
                              a.rl[:, s2 * 512:(s2 + 1) * 512], start=True, stop=True),
                     c.s_pe, 0 if s2 == 0 else 1)
            pe_rb = c.s_pe.count
            P.wait("scalar", c.s_pe, pe_rb)
            P.do("scalar", lambda q, a=a, rb_ps=rb_ps:
                 q.copy(a.rb[:, :], rb_ps), c.s_act, 1)
            act_rb = c.s_act.count
            P.wait("vector", c.s_act, act_rb)
            base = (h % 2) * 64
            cc_t = h // 2
            P.do("vector", lambda q, c=c, a=a, base=base, cc_t=cc_t, qs=qs, o_ps=o_ps:
                 q.tensor_tensor(a.AO[base:base + 64, cc_t, qs * QS:(qs + 1) * QS],
                                 o_ps[0:64, :], a.rb[:, :], ALU.mult), c.s_dv, 1)
        if h + 1 < NH:
            wvt_prev = wvt_next
    cx.gdma.wait("tensor", a.wout_tok)
    cx.dma.wait("vector", a.bout_tok)
    dv_ao = c.s_dv.count
    P.wait("tensor", c.s_dv, dv_ao)
    for s in range(NSEG):
        for dg in range(CCH):
            yp = y_ps[dg % 2]
            P.wait("tensor", c.s_dv, c.s_dv.count)
            for ccc in range(CCH):
                P.do("tensor", lambda q, c=c, a=a, s=s, dg=dg, ccc=ccc, yp=yp:
                     q.matmul(yp, a.w_out[:, ccc, dg * 128:(dg + 1) * 128],
                              a.AO[:, ccc, s * SEG:(s + 1) * SEG],
                              start=(ccc == 0), stop=(ccc == CCH - 1)),
                     c.s_pe, 0 if ccc < CCH - 1 else 1)
            pe = c.s_pe.count
            P.wait("vector", c.s_pe, pe)
            P.do("vector", lambda q, a=a, dg=dg, yp=yp:
                 q.tensor_scalar(a.ytmp[:, :], yp, a.bout[:, dg:dg + 1], None,
                                 ALU.add))
            P.do("vector", lambda q, c=c, a=a, dg=dg, s=s:
                 q.tensor_tensor(c.xcur[:, dg, s * SEG:(s + 1) * SEG],
                                 c.xcur[:, dg, s * SEG:(s + 1) * SEG],
                                 a.ytmp[:, :], ALU.add), c.s_dv, 1)
    return c.s_dv.count


def plan_p5(nc, P, cx, A, wpq, wpk, wvh, wout, bv, bout, x_in, x_out):
    if A is not None:
        cx.lnx = A("lnx", [128, CCH, T + 1], BF16)
    a = alloc_attn1(nc, A)
    bind_attn1_aliases(cx, a)
    a.wvh_ap = wvh
    emit_attn1_weights(cx, a, wpq, wpk, wout, bv, bout)
    a.wv0_tok = emit_wv1_load(cx, a, wvh, 0)

    for b in range(2):
        ld_tok = emit_load_xcur(cx, x_in, b)
        cx.dma.wait("vector", ld_tok)
        for s in range(NSEG):
            emit_ln_seg(cx, s * SEG, lambda cc, c=cx, s=s:
                        c.lnx[:, cc, s * SEG:(s + 1) * SEG])
        P.wait("tensor", cx.s_dv, cx.s_dv.count)
        emit_attn1_batch(cx, a)
        if b == 0:
            P.wait("gpsimd", cx.s_pe, cx.s_pe.count)
            a.wv0_tok = emit_wv1_load(cx, a, wvh, 0)
        dv_done = cx.s_dv.count
        P.wait("sync", cx.s_dv, dv_done)
        st_tok = emit_store_xcur(cx, x_out, b)
        note_store(cx, b, st_tok)
        if b == 0:
            for e in ("sync", "vector", "tensor", "scalar"):
                cx.stst.wait(e, st_tok)


def build_p5():
    nc = bass.Bass("TRN2", target_bir_lowering=False, debug=False,
                   detect_race_conditions=False)
    x_in = nc.dram_tensor("x_in", [2, C, T], F32, kind="ExternalInput").ap()
    wpq = nc.dram_tensor("wpq", [C, 128], F32, kind="ExternalInput").ap()
    wpk = nc.dram_tensor("wpk", [C, 128], F32, kind="ExternalInput").ap()
    wvh = nc.dram_tensor("wvh", [NH, C, 64], F32, kind="ExternalInput").ap()
    wout = nc.dram_tensor("wout", [C, C], F32, kind="ExternalInput").ap()
    bv = nc.dram_tensor("bv", [1, C], F32, kind="ExternalInput").ap()
    bout = nc.dram_tensor("bout", [128, CCH], F32, kind="ExternalInput").ap()
    x_out = nc.dram_tensor("x_out", [2, C, T], F32, kind="ExternalOutput").ap()
    P = Prog(nc)
    cx = alloc_common(nc, P)
    cx.lnx = nc.alloc_sbuf_tensor("lnx", [128, CCH, T + 1], BF16)
    emit_consts(cx)
    plan_p5(nc, P, cx, None, wpq, wpk, wvh, wout, bv, bout, x_in, x_out)
    cx.stst.wait_all("sync")
    P.build()
    return nc


def build_fused():
    from contextlib import ExitStack
    nc = bass.Bass("TRN2", target_bir_lowering=False, debug=False,
                   detect_race_conditions=False)
    x_fm = nc.dram_tensor("x_fm", [2, C, T], F32, kind="ExternalInput").ap()
    wr = nc.dram_tensor("wr", [C, C], F32, kind="ExternalInput").ap()
    wk = nc.dram_tensor("wk", [C, C], F32, kind="ExternalInput").ap()
    wv_f0 = nc.dram_tensor("wv_f0", [C, C], F32, kind="ExternalInput").ap()
    mu = nc.dram_tensor("mu", [128, 2, CCH], F32, kind="ExternalInput").ap()
    wqk_h = nc.dram_tensor("wqk_h", [NH, C, 128], F32, kind="ExternalInput").ap()
    wv0 = nc.dram_tensor("wv0", [C, C], F32, kind="ExternalInput").ap()
    bq = nc.dram_tensor("bq", [64, NH], F32, kind="ExternalInput").ap()
    bk = nc.dram_tensor("bk", [64, NH], F32, kind="ExternalInput").ap()
    bv0 = nc.dram_tensor("bv0", [1, C], F32, kind="ExternalInput").ap()
    w1c = nc.dram_tensor("w1c", [C, FF], F32, kind="ExternalInput").ap()
    w2c = nc.dram_tensor("w2c", [FF, C], F32, kind="ExternalInput").ap()
    idx = nc.dram_tensor("idx", [128, 64], I32, kind="ExternalInput").ap()
    ew = nc.dram_tensor("ew", [128, 64], F32, kind="ExternalInput").ap()
    eid = nc.dram_tensor("eid", [128, 1], F32, kind="ExternalInput").ap()
    gate = nc.dram_tensor("gate", [128, 2], F32, kind="ExternalInput").ap()
    bias = nc.dram_tensor("bias", [128, CCH], F32, kind="ExternalInput").ap()
    wpq = nc.dram_tensor("wpq", [C, 128], F32, kind="ExternalInput").ap()
    wpk = nc.dram_tensor("wpk", [C, 128], F32, kind="ExternalInput").ap()
    wvh = nc.dram_tensor("wvh", [NH, C, 64], F32, kind="ExternalInput").ap()
    wout = nc.dram_tensor("wout", [C, C], F32, kind="ExternalInput").ap()
    bv1 = nc.dram_tensor("bv1", [1, C], F32, kind="ExternalInput").ap()
    bout = nc.dram_tensor("bout", [128, CCH], F32, kind="ExternalInput").ap()
    y_fm = nc.dram_tensor("y_fm", [2, C, T], F32, kind="ExternalOutput").ap()

    xstore = nc.dram_tensor("xstore", [2, C, T], F32, kind="Internal").ap()
    dd = moe_dram(nc)

    P = Prog(nc)
    cx = alloc_common(nc, P)
    emit_consts(cx)

    def scoped(tag, stack):
        def A(name, shape, dtype):
            return stack.enter_context(nc.sbuf_tensor(name + tag, shape, dtype))
        return A

    with ExitStack() as es:
        plan_p1(nc, P, cx, scoped("_a", es), wr, wk, wv_f0, mu, x_fm, xstore)
    phase_barrier(P)
    with ExitStack() as es:
        plan_p2(nc, P, cx, scoped("_b", es), wqk_h, wv0, bq, bk, bv0,
                xstore, xstore)
    phase_barrier(P)
    with ExitStack() as es:
        plan_p1(nc, P, cx, scoped("_c", es), wr, wk, wv_f0, mu, xstore, xstore)
    phase_barrier(P)
    with ExitStack() as es:
        plan_p4(nc, P, cx, scoped("_d", es), w1c, w2c, idx, ew, eid, gate, bias,
                dd, xstore, xstore, convert=True)
    phase_barrier(P)
    with ExitStack() as es:
        plan_p5(nc, P, cx, scoped("_e", es), wpq, wpk, wvh, wout, bv1, bout,
                xstore, xstore)
    phase_barrier(P)
    with ExitStack() as es:
        plan_p4(nc, P, cx, scoped("_f", es), w1c, w2c, idx, ew, eid, gate, bias,
                dd, xstore, y_fm, convert=False)
    cx.stst.wait_all("sync")
    P.build()
    return nc



def plan_p1(nc, P, cx, A, wr, wk, wv, mu, x_in, x_out, tag=""):
    if A is not None:
        cx.lnx = A("lnx", [128, CCH, T + 1], BF16)
    f = alloc_ffn0(nc, A)
    f.p_ps = cx.pool[:, 2, 0:SEG]
    f.p2_ps = cx.pool[:, 2, SEG:2 * SEG]
    mu_tok = cx.dma.start("sync", lambda q, f=f, mu=mu: q.dma_start(f.mu[:, :, :], mu))
    cx.dma.wait("vector", mu_tok)
    wdv, wtoks = emit_ffn0_weights(cx, f, wr, wk, wv)
    P.wait("tensor", cx.s_dv, wdv)

    for b in range(2):
        ld_for_batch = emit_load_xcur(cx, x_in, b)
        emit_ffn0_batch(cx, f, wdv, wtoks, ld_for_batch)
        dv_done = cx.s_dv.count
        P.wait("sync", cx.s_dv, dv_done)
        st_tok = emit_store_xcur(cx, x_out, b)
        note_store(cx, b, st_tok)
        if b == 0:
            for e in ("sync", "vector", "tensor", "scalar"):
                cx.stst.wait(e, st_tok)


def build_p1():
    nc = bass.Bass("TRN2", target_bir_lowering=False, debug=False,
                   detect_race_conditions=False)
    x_in = nc.dram_tensor("x_in", [2, C, T], F32, kind="ExternalInput").ap()
    wr = nc.dram_tensor("wr", [C, C], F32, kind="ExternalInput").ap()
    wk = nc.dram_tensor("wk", [C, C], F32, kind="ExternalInput").ap()
    wv = nc.dram_tensor("wv", [C, C], F32, kind="ExternalInput").ap()
    mu = nc.dram_tensor("mu", [128, 2, CCH], F32, kind="ExternalInput").ap()
    x_out = nc.dram_tensor("x_out", [2, C, T], F32, kind="ExternalOutput").ap()
    P = Prog(nc)
    cx = alloc_common(nc, P)
    cx.lnx = nc.alloc_sbuf_tensor("lnx", [128, CCH, T + 1], BF16)
    emit_consts(cx)
    plan_p1(nc, P, cx, None, wr, wk, wv, mu, x_in, x_out)
    cx.stst.wait_all("sync")
    P.build()
    return nc



def _prep_mu(mu_r, mu_k):
    m = np.stack([mu_r.reshape(CCH, 128).T, mu_k.reshape(CCH, 128).T], axis=1)
    return np.ascontiguousarray(m.astype(np.float32))


def kernel(**inputs):
    f32 = lambda k: np.asarray(inputs[k]).astype(np.float32)
    xs = f32("L0_attn_hidden_states")
    wr_t = np.ascontiguousarray(f32("L0_ffn_W_r").T)
    wk_t = np.ascontiguousarray(f32("L0_ffn_W_k").T)
    wv_t = np.ascontiguousarray(f32("L0_ffn_W_v").T)
    mu = _prep_mu(f32("L0_ffn_mu_r"), f32("L0_ffn_mu_k"))
    Wqk = f32("L0_attn_Wqk")
    wqk_h = np.ascontiguousarray(np.stack(
        [np.concatenate([Wqk[:, h * 64:(h + 1) * 64],
                         Wqk[:, C + h * 64:C + (h + 1) * 64]], axis=1)
         for h in range(NH)]))
    bqk = f32("L0_attn_bqk")
    bq = np.ascontiguousarray(bqk[:C].reshape(NH, 64).T)
    bk = np.ascontiguousarray(bqk[C:].reshape(NH, 64).T)
    wv0 = np.ascontiguousarray(f32("L0_attn_Wv"))
    bv0 = np.ascontiguousarray(f32("L0_attn_bv").reshape(1, C))
    w1 = f32("L1_ffn_w1")
    w2 = f32("L1_ffn_w2")
    idx_h = np.ascontiguousarray(
        np.asarray(inputs["L1_ffn_expert_indices"]).astype(np.int32)
        .reshape(128, 32, 2).transpose(0, 2, 1).reshape(128, 64))
    ew_h = np.ascontiguousarray(f32("L1_ffn_expert_weights")
                                .reshape(128, 32, 2).transpose(0, 2, 1).reshape(128, 64))
    l1bias = np.ascontiguousarray(f32("L1_ffn_bias").reshape(CCH, 128).T)
    wpq = np.ascontiguousarray(np.concatenate(
        [f32("L1_attn_W1_q").T, f32("L1_attn_W2_q").T], axis=1))
    wpk = np.ascontiguousarray(np.concatenate(
        [f32("L1_attn_W1_k").T, f32("L1_attn_W2_k").T], axis=1))
    wv1T = f32("L1_attn_W_v").T
    wv1_h = np.ascontiguousarray(np.stack(
        [wv1T[:, h * 64:(h + 1) * 64] for h in range(NH)]))
    wout_t = np.ascontiguousarray(f32("L1_attn_W_out").T)
    bv1 = np.ascontiguousarray(f32("L1_attn_b_v").reshape(1, C))
    bout = np.ascontiguousarray(f32("L1_attn_b_out").reshape(CCH, 128).T)

    ncores = list(range(NCORES))
    xst = [np.ascontiguousarray(np.stack([xs[c].T, xs[c + 8].T]))
           for c in range(NCORES)]

    nc = build_fused()
    in_maps = []
    for c in range(NCORES):
        gate = np.zeros((128, 2), np.float32)
        if c < 2:
            gate[:, c] = 1.0
        in_maps.append({
            "x_fm": xst[c],
            "wr": wr_t, "wk": wk_t, "wv_f0": wv_t, "mu": mu,
            "wqk_h": wqk_h, "wv0": wv0, "bq": bq, "bk": bk, "bv0": bv0,
            "w1c": np.ascontiguousarray(w1[c]),
            "w2c": np.ascontiguousarray(w2[c]),
            "idx": idx_h, "ew": ew_h,
            "eid": np.full((128, 1), float(c), np.float32),
            "gate": gate, "bias": l1bias,
            "wpq": wpq, "wpk": wpk, "wvh": wv1_h, "wout": wout_t,
            "bv1": bv1, "bout": bout,
        })
    res = run_bass_kernel_spmd(nc, in_maps, core_ids=ncores)
    out = np.empty((16, T, C), dtype=np.float32)
    for c in range(NCORES):
        y = res.results[c]["y_fm"]
        out[c] = y[0].T
        out[c + 8] = y[1].T
    return out


# version 04
# speedup vs baseline: 1.2815x; 1.0509x over previous
import math
import numpy as np
import concourse.bass as bass
import concourse.mybir as mybir
from concourse.bass_utils import run_bass_kernel_spmd

F32 = mybir.dt.float32
BF16 = mybir.dt.bfloat16
I32 = mybir.dt.int32
AF = mybir.ActivationFunctionType
ALU = mybir.AluOpType

NCORES = 8
T = 2048
C = 1024
CCH = 8
NH = 16
HD = 64
FF = 4096
SEG = 512
NSEG = T // SEG

ENGINES = ("sync", "tensor", "vector", "scalar", "gpsimd")


class SemC:

    def __init__(self, h):
        self.h = h
        self.count = 0


class Prog:

    SERIAL = ("vector", "scalar", "gpsimd")

    def __init__(self, nc):
        self.nc = nc
        self.ops = {e: [] for e in ENGINES}
        self._semid = 0
        self.esem = {}
        for e in self.SERIAL:
            self.esem[e] = self.sem(f"e_{e}")

    def sem(self, name=None):
        self._semid += 1
        h = self.nc.alloc_semaphore(name or f"s{self._semid}")
        return SemC(h)

    def emit(self, eng, fn):
        self.ops[eng].append(fn)

    def do(self, eng, fn, sem=None, n=1):
        if eng in self.SERIAL and n == 1 and (sem is None or sem is self.esem[eng]):
            s = self.esem[eng]
            if s.count > 0:
                self.ops[eng].append(lambda q, s=s, v=s.count: q.wait_ge(s.h, v))

            def wrapped(q, fn=fn, s=s):
                fn(q).then_inc(s.h, 1)
            self.ops[eng].append(wrapped)
            s.count += 1
            return s.count
        if sem is None:
            self.ops[eng].append(fn)
            return None
        def wrapped(q, fn=fn, s=sem, n=n):
            fn(q).then_inc(s.h, n)
        self.ops[eng].append(wrapped)
        sem.count += n
        return sem.count

    def wait(self, eng, sem, val=None):
        v = sem.count if val is None else val
        self.ops[eng].append(lambda q, s=sem, v=v: q.wait_ge(s.h, v))

    def barrier(self, name=None):
        s = self.sem(name)
        for e in ENGINES:
            self.do(e, lambda q: q.engine_nop() if hasattr(q, "engine_nop") else q.nop(), s, 1)
        for e in ENGINES:
            self.wait(e, s, len(ENGINES))
        return s

    def build(self):
        with self.nc.Block() as block:
            @block.sync
            def _(q):
                for fn in self.ops["sync"]:
                    fn(q)

            @block.tensor
            def _(q):
                for fn in self.ops["tensor"]:
                    fn(q)

            @block.vector
            def _(q):
                for fn in self.ops["vector"]:
                    fn(q)

            @block.scalar
            def _(q):
                for fn in self.ops["scalar"]:
                    fn(q)

            @block.gpsimd
            def _(q):
                for fn in self.ops["gpsimd"]:
                    fn(q)


class DmaStream:

    def __init__(self, P, name, rot=4):
        self.P = P
        self.sems = [P.sem(f"{name}{i}") for i in range(rot)]
        self.rot = rot
        self.idx = 0

    def start(self, eng, fn):
        s = self.sems[self.idx % self.rot]
        self.idx += 1
        if s.count > 0:
            self.P.wait(eng, s, s.count)
        self.P.do(eng, fn, s, 16)
        return (s, s.count)

    def wait(self, eng, tok):
        self.P.wait(eng, tok[0], tok[1])

    def wait_all(self, eng):
        for s in self.sems:
            if s.count > 0:
                self.P.wait(eng, s, s.count)


def _nop_inc(P, eng, sem):
    P.do(eng, lambda q: q.nop(), sem, 1)


def phase_barrier(P, name=None):
    s = P.sem(name)
    for e in ENGINES:
        _nop_inc(P, e, s)
    for e in ENGINES:
        P.wait(e, s, len(ENGINES))
    return s



class Ctx:
    pass


def alloc_common(nc, P):
    cx = Ctx()
    cx.nc = nc
    cx.P = P
    A = nc.alloc_sbuf_tensor
    cx.xcur = A("xcur", [128, CCH, T], F32)
    cx.ones1_f = A("ones1_f", [1, 128], F32)
    cx.ones128_b = A("ones128_b", [128, 1], BF16)
    cx.ident_f = A("ident_f", [128, 128], F32)
    cx.xbf = A("ln_xbf", [128, CCH, SEG], BF16)
    cx.sqbf = A("ln_sqbf", [128, CCH, SEG], BF16)
    cx.rstd = A("ln_rstd", [1, SEG], F32)
    cx.bneg = A("ln_bneg", [1, SEG], F32)
    cx.tstat = A("ln_tstat", [1, SEG], F32)
    cx.a_sb = A("ln_a_sb", [128, SEG], F32)
    cx.eps = A("ln_eps", [1, 1], F32)
    cx.ssum = A("ln_ssum", [1, SEG], F32)
    cx.b_sb = A("ln_b_sb", [128, SEG], F32)
    cx.lntmp = cx.sqbf
    cx.pool = nc.alloc_psum_tensor("pool_ps", [128, 4, 1024], F32)
    cx.s_ps = cx.pool[0:1, 0, 0:SEG]
    cx.ss_ps = cx.pool[0:1, 0, SEG:2 * SEG]
    cx.ab_ps = cx.pool[:, 1, :].rearrange("p (a s) -> p a s", a=2)

    cx.dma = DmaStream(P, "c_dma", rot=4)
    cx.gdma = DmaStream(P, "c_gdma", rot=4)
    cx.stst = DmaStream(P, "c_st", rot=2)
    cx.s_dv = P.esem["vector"]
    cx.s_act = P.esem["scalar"]
    cx.s_pool = P.esem["gpsimd"]
    cx.s_pe = P.sem("c_pe")
    return cx


def emit_consts(cx):
    P = cx.P
    P.do("vector", lambda q, c=cx: q.memset(c.ones1_f[:, :], 1.0))
    P.do("vector", lambda q, c=cx: q.memset(c.ones128_b[:, :], 1.0))
    P.do("vector", lambda q, c=cx: q.memset(c.eps[:, :], 1e-5))
    P.do("vector", lambda q, c=cx: q.memset(c.ident_f[:, :], 1.0), cx.s_dv, 1)
    P.wait("gpsimd", cx.s_dv)
    P.do("gpsimd",
         lambda q, c=cx: q.affine_select(c.ident_f[:, :], c.ident_f[:, :],
                                         [[-1, 128]], ALU.is_equal, 0.0,
                                         base=0, channel_multiplier=1),
         cx.s_pool, 1)
    cx.dv_consts = cx.s_dv.count



def xstore_view(xstore_ap, b):
    return xstore_ap[b].rearrange("(cc p) t -> p cc t", p=128)


def emit_load_xcur(cx, xstore_ap, b):
    prev = getattr(cx, "last_store", {}).get(b)
    if prev is not None:
        cx.stst.wait("sync", prev)
    return cx.dma.start("sync", lambda q, c=cx, x=xstore_ap, b=b:
                        q.dma_start(c.xcur[:, :, :], xstore_view(x, b)))


def note_store(cx, b, tok):
    if not hasattr(cx, "last_store"):
        cx.last_store = {}
    cx.last_store[b] = tok


def emit_store_xcur(cx, xstore_ap, b, eng="sync"):
    return cx.stst.start(eng, lambda q, c=cx, x=xstore_ap, b=b:
                         q.dma_start(xstore_view(x, b), c.xcur[:, :, :]))



def emit_input_transpose(cx, x_ap, b, tp_ps, stage):
    P = cx.P
    pe_before = cx.s_pe.count
    for cc in range(CCH):
        for tg4 in range(T // 512):
            P.wait("sync", cx.s_pe, cx.s_pe.count)
            tok = cx.dma.start("sync", lambda q, c=cx, x=x_ap, b=b, cc=cc, tg4=tg4, st=stage:
                               q.dma_start(
                                   st[:, :].rearrange("p (i c) -> p i c", i=4),
                                   x[b, tg4 * 512:(tg4 + 1) * 512, cc * 128:(cc + 1) * 128]
                                   .rearrange("(i p) c -> p i c", p=128)))
            cx.dma.wait("tensor", tok)
            if cc == 0 and tg4 == 0:
                P.wait("tensor", cx.s_pool, 1)
            for i in range(4):
                P.wait("tensor", cx.s_dv, cx.s_dv.count)
                P.do("tensor", lambda q, c=cx, ps=tp_ps, st=stage, i=i:
                     q.transpose(ps[:, i * 128:(i + 1) * 128],
                                 st[:, i * 128:(i + 1) * 128], c.ident_f[:, :]),
                     cx.s_pe, 1)
            pe = cx.s_pe.count
            P.wait("vector", cx.s_pe, pe)
            P.do("vector", lambda q, c=cx, ps=tp_ps, cc=cc, tg4=tg4:
                 q.tensor_copy(c.xcur[:, cc, tg4 * 512:(tg4 + 1) * 512], ps[:, :]),
                 cx.s_dv, 1)
    return cx.s_dv.count


def emit_output_transpose(cx, y_ap, b, tp_ps, stage):
    P = cx.P
    for tg in range(T // 128):
        for half in range(2):
            for j in range(4):
                cc = half * 4 + j
                P.wait("tensor", cx.s_dv, cx.s_dv.count)
                P.do("tensor", lambda q, c=cx, ps=tp_ps, cc=cc, tg=tg, j=j:
                     q.transpose(ps[:, j * 128:(j + 1) * 128],
                                 c.xcur[:, cc, tg * 128:(tg + 1) * 128],
                                 c.ident_f[:, :]),
                     cx.s_pe, 1)
            pe = cx.s_pe.count
            P.wait("vector", cx.s_pe, pe)
            P.do("vector", lambda q, st=stage, half=half, ps=tp_ps:
                 q.tensor_copy(st[:, half * 512:(half + 1) * 512], ps[:, :]),
                 cx.s_dv, 1)
        dv = cx.s_dv.count
        P.wait("sync", cx.s_dv, dv)
        tok = cx.stst.start("sync", lambda q, y=y_ap, b=b, tg=tg, st=stage:
                            q.dma_start(y[b, tg * 128:(tg + 1) * 128, :], st[:, :]))
        cx.stst.wait("vector", tok)
    return None



def emit_ln_seg(cx, t0, dst_fn, sem_out=None):
    P = cx.P
    c = cx
    P.do("vector", lambda q, c=c, t0=t0:
         q.tensor_copy(c.xbf[:, :, :], c.xcur[:, :, t0:t0 + SEG]), c.s_dv, 1)
    dv_cast = c.s_dv.count
    P.wait("scalar", c.s_dv, dv_cast)
    P.do("scalar", lambda q, c=c:
         q.activation(c.sqbf[:, :, :], c.xbf[:, :, :], AF.Square), c.s_act, 1)
    act_sq = c.s_act.count
    P.wait("tensor", c.s_dv, dv_cast)
    for cc in range(CCH):
        P.do("tensor", lambda q, c=c, cc=cc:
             q.matmul(c.s_ps[:, :], c.ones128_b[:, :], c.xbf[:, cc, :],
                      start=(cc == 0), stop=(cc == CCH - 1)),
             c.s_pe, 0 if cc < CCH - 1 else 1)
    P.wait("tensor", c.s_act, act_sq)
    for cc in range(CCH):
        P.do("tensor", lambda q, c=c, cc=cc:
             q.matmul(c.ss_ps[:, :], c.ones128_b[:, :], c.sqbf[:, cc, :],
                      start=(cc == 0), stop=(cc == CCH - 1)),
             c.s_pe, 0 if cc < CCH - 1 else 1)
    pe_stats = c.s_pe.count
    P.wait("vector", c.s_pe, pe_stats)
    P.do("vector", lambda q, c=c: q.tensor_copy(c.ssum[:, :], c.s_ps[:, :]))
    P.do("vector", lambda q, c=c:
         q.scalar_tensor_tensor(c.tstat[:, :], c.ssum[:, :], 1.0 / (1024.0 * 1024.0),
                                c.ssum[:, :], ALU.mult, ALU.mult))
    P.do("vector", lambda q, c=c:
         q.scalar_tensor_tensor(c.rstd[:, :], c.ss_ps[:, :], 1.0 / 1024.0,
                                c.tstat[:, :], ALU.mult, ALU.subtract), c.s_dv, 1)
    dv_var = c.s_dv.count
    P.wait("scalar", c.s_dv, dv_var)
    P.do("scalar", lambda q, c=c:
         q.activation(c.tstat[:, :], c.rstd[:, :], AF.Sqrt, bias=c.eps[:, :]), c.s_act, 1)
    act_std = c.s_act.count
    P.wait("vector", c.s_act, act_std)
    P.do("vector", lambda q, c=c: q.reciprocal(c.rstd[:, :], c.tstat[:, :]))
    P.do("vector", lambda q, c=c:
         q.scalar_tensor_tensor(c.bneg[:, :], c.ssum[:, :], -1.0 / 1024.0,
                                c.rstd[:, :], ALU.mult, ALU.mult), c.s_dv, 1)
    dv_stats = c.s_dv.count
    P.wait("tensor", c.s_dv, dv_stats)
    P.do("tensor", lambda q, c=c:
         q.matmul(c.ab_ps[:, 0, :], c.ones1_f[:, :], c.rstd[:, :],
                  start=True, stop=True))
    P.do("tensor", lambda q, c=c:
         q.matmul(c.ab_ps[:, 1, :], c.ones1_f[:, :], c.bneg[:, :],
                  start=True, stop=True), c.s_pe, 1)
    pe_bc = c.s_pe.count
    P.wait("scalar", c.s_pe, pe_bc)
    P.do("scalar", lambda q, c=c: q.copy(c.a_sb[:, :], c.ab_ps[:, 0, :]))
    P.do("scalar", lambda q, c=c: q.copy(c.b_sb[:, :], c.ab_ps[:, 1, :]), c.s_act, 1)
    act_ab = c.s_act.count
    P.wait("vector", c.s_act, act_ab)
    P.do("vector", lambda q, c=c:
         q.tensor_tensor(c.lntmp[:, :, :], c.xbf[:, :, :],
                         c.a_sb[:, :].unsqueeze(1).broadcast_to([128, CCH, SEG]),
                         ALU.mult))
    for cc in range(CCH):
        P.do("vector", lambda q, c=c, cc=cc, dst_fn=dst_fn:
             q.tensor_tensor(dst_fn(cc), c.lntmp[:, cc, :], c.b_sb[:, :], ALU.add),
             c.s_dv if (cc == CCH - 1) else None, 1)
    if sem_out is not None:
        _nop_inc(P, "vector", sem_out)
    return c.s_dv.count



def alloc_ffn0(nc, A=None):
    f = Ctx()
    A = A or nc.alloc_sbuf_tensor
    f.w_r = A("f_w_r", [128, CCH, C], BF16)
    f.w_k = A("f_w_k", [128, CCH, C], BF16)
    f.w_v = A("f_w_v", [128, CCH, C], BF16)
    f.mu = A("f_mu", [128, 2, CCH], F32)
    f.diff = A("f_diff", [128, CCH, SEG], BF16)
    f.xr = A("f_xr", [128, CCH, SEG], BF16)
    f.xk = f.xr
    f.r = A("f_r", [128, CCH, SEG], F32)
    f.kact = f.diff
    f.vtmp = A("f_vtmp", [128, SEG], F32)
    return f


def emit_ffn0_weights(cx, f, wr_ap, wk_ap, wv_ap):
    P = cx.P
    toks = []
    for (w_sb, w_ap) in ((f.w_r, wr_ap), (f.w_k, wk_ap), (f.w_v, wv_ap)):
        toks.append(cx.gdma.start("gpsimd", lambda q, w_sb=w_sb, w_ap=w_ap:
                    q.dma_start(w_sb[:, :, :],
                                w_ap.rearrange("(cc p) d -> p cc d", p=128))))
    for t in toks:
        cx.gdma.wait("vector", t)
    P.do("vector", lambda q, f=f:
         q.tensor_scalar_mul(f.w_v[:, :, :], f.w_v[:, :, :], 0.5), cx.s_dv, 1)
    return cx.s_dv.count, toks


def emit_ffn0_batch(cx, f, w_dv, w_toks, ld_tok):
    P = cx.P
    c = cx
    if ld_tok is not None:
        cx.dma.wait("vector", ld_tok)
    P.do("vector", lambda q, c=c: q.memset(c.lnx[:, :, 0:1], 0.0))
    for t in w_toks:
        cx.gdma.wait("tensor", t)
    for s in range(NSEG):
        t0 = s * SEG
        emit_ln_seg(cx, t0, lambda cc, c=c: c.lnx[:, cc, 1:SEG + 1])
        dv_ln = c.s_dv.count
        P.do("vector", lambda q, c=c, f=f:
             q.tensor_tensor(f.diff[:, :, :], c.lnx[:, :, 1:SEG + 1],
                             c.lnx[:, :, 0:SEG], ALU.subtract))
        for cc in range(CCH):
            P.do("vector", lambda q, c=c, f=f, cc=cc:
                 q.scalar_tensor_tensor(f.xr[:, cc, :], f.diff[:, cc, :],
                                        f.mu[:, 0, cc:cc + 1], c.lnx[:, cc, 0:SEG],
                                        ALU.mult, ALU.add))
        P.do("vector", lambda q, c=c, f=f:
             q.engine_nop(), c.s_dv, 1)
        dv_lerp = c.s_dv.count

        P.wait("tensor", c.s_dv, dv_lerp)
        for dg in range(CCH):
            ps = f.pslots[dg % 4]
            P.wait("tensor", c.s_act, max(0, c.s_act.count - 3))
            for ccc in range(CCH):
                P.do("tensor", lambda q, f=f, ps=ps, dg=dg, ccc=ccc:
                     q.matmul(ps[:, :], f.w_r[:, ccc, dg * 128:(dg + 1) * 128],
                              f.xr[:, ccc, :], start=(ccc == 0), stop=(ccc == CCH - 1)),
                     c.s_pe, 0 if ccc < CCH - 1 else 1)
            pe = c.s_pe.count
            P.wait("scalar", c.s_pe, pe)
            P.do("scalar", lambda q, f=f, ps=ps, dg=dg:
                 q.activation(f.r[:, dg, :], ps[:, :], AF.Sigmoid), c.s_act, 1)
        pe_r_done = c.s_pe.count
        P.wait("vector", c.s_pe, pe_r_done)
        for cc in range(CCH):
            P.do("vector", lambda q, c=c, f=f, cc=cc:
                 q.scalar_tensor_tensor(f.xk[:, cc, :], f.diff[:, cc, :],
                                        f.mu[:, 1, cc:cc + 1], c.lnx[:, cc, 0:SEG],
                                        ALU.mult, ALU.add))
        P.do("vector", lambda q, c=c:
             q.tensor_copy(c.lnx[:, :, 0:1], c.lnx[:, :, SEG:SEG + 1]), c.s_dv, 1)
        dv_xk = c.s_dv.count
        P.wait("tensor", c.s_dv, dv_xk)
        for dg in range(CCH):
            ps = f.pslots[dg % 4]
            P.wait("tensor", c.s_act, max(0, c.s_act.count - 3))
            for ccc in range(CCH):
                P.do("tensor", lambda q, f=f, ps=ps, dg=dg, ccc=ccc:
                     q.matmul(ps[:, :], f.w_k[:, ccc, dg * 128:(dg + 1) * 128],
                              f.xk[:, ccc, :], start=(ccc == 0), stop=(ccc == CCH - 1)),
                     c.s_pe, 0 if ccc < CCH - 1 else 1)
            pe = c.s_pe.count
            P.wait("scalar", c.s_pe, pe)
            P.do("scalar", lambda q, f=f, ps=ps, dg=dg:
                 q.activation(f.kact[:, dg, :], ps[:, :], AF.Relu), c.s_act, 1)
        act_relu = c.s_act.count
        P.wait("scalar", c.s_act, act_relu)
        P.do("scalar", lambda q, f=f:
             q.activation(f.kact[:, :, :], f.kact[:, :, :], AF.Square), c.s_act, 1)
        act_sq = c.s_act.count
        P.wait("tensor", c.s_act, act_sq)
        for dg in range(CCH):
            ps = f.pslots[dg % 4]
            P.wait("tensor", c.s_dv, max(0, c.s_dv.count - 7))
            for ccc in range(CCH):
                P.do("tensor", lambda q, f=f, ps=ps, dg=dg, ccc=ccc:
                     q.matmul(ps[:, :], f.w_v[:, ccc, dg * 128:(dg + 1) * 128],
                              f.kact[:, ccc, :], start=(ccc == 0), stop=(ccc == CCH - 1)),
                     c.s_pe, 0 if ccc < CCH - 1 else 1)
            pe = c.s_pe.count
            P.wait("vector", c.s_pe, pe)
            P.do("vector", lambda q, f=f, ps=ps, dg=dg:
                 q.tensor_tensor(f.vtmp[:, :], ps[:, :], f.r[:, dg, :], ALU.mult))
            P.do("vector", lambda q, c=c, f=f, dg=dg, t0=t0:
                 q.tensor_tensor(c.xcur[:, dg, t0:t0 + SEG],
                                 c.xcur[:, dg, t0:t0 + SEG], f.vtmp[:, :], ALU.add),
                 c.s_dv, 1)
    return c.s_dv.count



NKC = T // 128
QS = 1024
NQS = T // QS
RT8 = 0.35355339059327373


def alloc_attn0(nc, A=None):
    a = Ctx()
    A = A or nc.alloc_sbuf_tensor
    a.w_v0 = A("a_wv", [128, CCH, C], BF16)
    a.wqk = A("a_wqk", [128, 2, CCH, 128], BF16)
    a.bq = A("a_bq", [64, NH], F32)
    a.bk = A("a_bk", [64, NH], F32)
    a.bvrow = A("a_bvrow", [1, C], F32)
    a.bvb = A("a_bvb", [128, C], F32)
    a.qp = A("a_qp", [65, T], BF16)
    a.kp = A("a_kp", [65, T], BF16)
    a.vp = A("a_vp", [128, NKC, 65], BF16)
    a.qsq = A("a_qsq", [64, T], BF16)
    a.ksq = A("a_ksq", [64, T], BF16)
    a.kmax = A("a_kmax", [64, 1], BF16)
    a.negmk2h = A("a_negmk2h", [1, 1], F32)
    a.pt = A("a_pt", [128, 2, QS], BF16)
    a.rl = A("a_rl", [1, QS], F32)
    a.rb = A("a_rb", [64, QS], F32)
    a.otmp = A("a_otmp", [128, QS], F32)
    return a


def emit_attn0_weights(cx, a, wv_ap, bq_ap, bk_ap, bv_ap):
    P = cx.P
    wv_tok = cx.gdma.start("gpsimd", lambda q, a=a, w=wv_ap:
                           q.dma_start(a.w_v0[:, :, :],
                                       w.rearrange("(cc p) d -> p cc d", p=128)))
    a.wv_tok = wv_tok
    t1 = cx.dma.start("sync", lambda q, a=a, b=bq_ap: q.dma_start(a.bq[:, :], b))
    t2 = cx.dma.start("sync", lambda q, a=a, b=bk_ap: q.dma_start(a.bk[:, :], b))
    t3 = cx.dma.start("sync", lambda q, a=a, b=bv_ap: q.dma_start(a.bvrow[:, :], b))
    for t in (t1, t2, t3):
        cx.dma.wait("vector", t)
    P.do("vector", lambda q, a=a: q.tensor_scalar_mul(a.bq[:, :], a.bq[:, :], RT8))
    P.do("vector", lambda q, a=a: q.tensor_scalar_mul(a.bk[:, :], a.bk[:, :], RT8),
         cx.s_dv, 1)
    bb = cx.pool[:, 2, :]
    cx.dma.wait("tensor", t3)
    P.wait("tensor", cx.s_dv, cx.dv_consts)
    for s in range(2):
        P.do("tensor", lambda q, c=cx, a=a, s=s, bb=bb:
             q.matmul(bb[:, s * 512:(s + 1) * 512], c.ones1_f[:, :],
                      a.bvrow[0:1, s * 512:(s + 1) * 512], start=True, stop=True),
             cx.s_pe, 1)
    pe = cx.s_pe.count
    P.wait("scalar", cx.s_pe, pe)
    P.do("scalar", lambda q, a=a, bb=bb: q.copy(a.bvb[:, :], bb[:, :]), cx.s_act, 1)
    return cx.s_act.count


def emit_wqk_load(cx, a, wqk_heads_ap, h):
    return cx.gdma.start("gpsimd", lambda q, a=a, w=wqk_heads_ap, h=h:
                         q.dma_start(a.wqk[:, h % 2, :, :],
                                     w[h].rearrange("(cc p) d -> p cc d", p=128)))


def emit_attn0_batch(cx, a, wqk_heads_ap, wg_first):
    P = cx.P
    c = cx
    pool = c.pool
    st_pair = [pool[:, 0, :], pool[:, 1, :]]
    o_ps = pool[0:65, 2, :]
    rb_ps = pool[0:64, 3, :]
    q_ps = pool[0:64, 2, :]
    k_ps = pool[0:64, 3, :]
    v_ps = pool[:, 2, 0:64]
    qs2_ps = pool[0:1, 0:2, :].rearrange("p a s -> p (a s)")
    mk2_ps = pool[0:1, 3, 640:641]

    P.do("vector", lambda q, a=a: q.memset(a.vp[:, :, 64:65], 1.0))
    P.do("vector", lambda q, a=a: q.memset(a.kp[64:65, :], 1.0), c.s_dv, 1)

    wg_prev = wg_first
    for h in range(NH):
        par = h % 2
        if h + 1 < NH:
            P.wait("gpsimd", c.s_pe, c.s_pe.count)
            wg_next = emit_wqk_load(cx, a, wqk_heads_ap, h + 1)
        c.gdma.wait("tensor", wg_prev)
        for half in range(2):
            P.wait("tensor", c.s_dv, c.s_dv.count)
            for s2 in range(2):
                t0 = half * 1024 + s2 * 512
                for ccc in range(CCH):
                    P.do("tensor", lambda q, a=a, par=par, ccc=ccc, s2=s2, t0=t0, q_ps=q_ps:
                         q.matmul(q_ps[:, s2 * 512:(s2 + 1) * 512],
                                  a.wqk[:, par, ccc, 0:64], c.lnx[:, ccc, t0:t0 + 512],
                                  start=(ccc == 0), stop=(ccc == CCH - 1)),
                         c.s_pe, 0 if ccc < CCH - 1 else 1)
            pe = c.s_pe.count
            P.wait("vector", c.s_pe, pe)
            P.do("vector", lambda q, a=a, h=h, half=half, q_ps=q_ps:
                 q.tensor_scalar(a.qp[0:64, half * 1024:(half + 1) * 1024], q_ps[:, :],
                                 RT8, a.bq[:, h:h + 1], ALU.mult, ALU.add), c.s_dv, 1)
        for half in range(2):
            P.wait("tensor", c.s_act, c.s_act.count)
            P.wait("tensor", c.s_dv, c.s_dv.count)
            for s2 in range(2):
                t0 = half * 1024 + s2 * 512
                for ccc in range(CCH):
                    P.do("tensor", lambda q, a=a, par=par, ccc=ccc, s2=s2, t0=t0, k_ps=k_ps:
                         q.matmul(k_ps[:, s2 * 512:(s2 + 1) * 512],
                                  a.wqk[:, par, ccc, 64:128], c.lnx[:, ccc, t0:t0 + 512],
                                  start=(ccc == 0), stop=(ccc == CCH - 1)),
                         c.s_pe, 0 if ccc < CCH - 1 else 1)
            pe = c.s_pe.count
            P.wait("vector", c.s_pe, pe)
            P.do("vector", lambda q, a=a, h=h, half=half, k_ps=k_ps:
                 q.tensor_scalar(a.kp[0:64, half * 1024:(half + 1) * 1024], k_ps[:, :],
                                 RT8, a.bk[:, h:h + 1], ALU.mult, ALU.add), c.s_dv, 1)
        dv_qk = c.s_dv.count
        vslots = [pool[:, 0, 0:64], pool[:, 0, 512:576],
                  pool[:, 1, 0:64], pool[:, 1, 512:576],
                  pool[:, 2, 0:64], pool[:, 2, 512:576],
                  pool[:, 3, 0:64], pool[:, 3, 512:576]]
        evac_dv = [None] * NKC
        for tg in range(NKC):
            vs = vslots[tg % 8]
            if tg >= 8:
                P.wait("tensor", c.s_dv, evac_dv[tg - 8])
            elif tg == 0:
                P.wait("tensor", c.s_dv, c.s_dv.count)
            for ccc in range(CCH):
                P.do("tensor", lambda q, c=c, a=a, tg=tg, ccc=ccc, h=h, vs=vs:
                     q.matmul(vs, c.lnx[:, ccc, tg * 128:(tg + 1) * 128],
                              a.w_v0[:, ccc, h * 64:(h + 1) * 64],
                              start=(ccc == 0), stop=(ccc == CCH - 1)),
                     c.s_pe, 0 if ccc < CCH - 1 else 1)
            pe = c.s_pe.count
            P.wait("vector", c.s_pe, pe)
            P.do("vector", lambda q, a=a, tg=tg, h=h, vs=vs:
                 q.tensor_tensor(a.vp[:, tg, 0:64], vs,
                                 a.bvb[:, h * 64:(h + 1) * 64], ALU.add), c.s_dv, 1)
            evac_dv[tg] = c.s_dv.count
        P.wait("scalar", c.s_dv, dv_qk)
        P.do("scalar", lambda q, a=a:
             q.activation(a.qsq[:, :], a.qp[0:64, :], AF.Square))
        P.do("scalar", lambda q, a=a:
             q.activation(a.ksq[:, :], a.kp[0:64, :], AF.Square), c.s_act, 1)
        act_sq = c.s_act.count
        P.wait("tensor", c.s_act, act_sq)
        P.wait("tensor", c.s_dv, c.s_dv.count)
        for s in range(4):
            P.do("tensor", lambda q, c=c, a=a, s=s, qs2_ps=qs2_ps:
                 q.matmul(qs2_ps[:, s * 512:(s + 1) * 512], c.ones128_b[0:64, :],
                          a.qsq[:, s * 512:(s + 1) * 512], start=True, stop=True),
                 c.s_pe, 0 if s < 3 else 1)
        P.wait("vector", c.s_act, act_sq)
        P.do("vector", lambda q, a=a:
             q.reduce_max(a.kmax[:, :], a.ksq[:, :], mybir.AxisListType.X), c.s_dv, 1)
        dv_kmax = c.s_dv.count
        P.wait("tensor", c.s_dv, dv_kmax)
        P.do("tensor", lambda q, c=c, a=a, mk2_ps=mk2_ps:
             q.matmul(mk2_ps, c.ones128_b[0:64, :], a.kmax[:, :],
                      start=True, stop=True), c.s_pe, 1)
        pe_bound = c.s_pe.count
        P.wait("scalar", c.s_pe, pe_bound)
        P.do("scalar", lambda q, a=a, mk2_ps=mk2_ps:
             q.mul(a.negmk2h[:, :], mk2_ps, -0.5), c.s_act, 1)
        act_mk2 = c.s_act.count
        P.wait("vector", c.s_act, act_mk2)
        P.do("vector", lambda q, a=a, qs2_ps=qs2_ps:
             q.tensor_scalar(a.qp[64:65, :], qs2_ps, -0.5, a.negmk2h[:, :],
                             ALU.mult, ALU.add), c.s_dv, 1)
        dv_bound = c.s_dv.count
        for qs in range(NQS):
            nkc = (qs + 1) * (QS // 128)
            exp_done = [None] * nkc
            pool_done = [None] * nkc
            for step in range(nkc + 1):
                if step < nkc:
                    kc = step
                    sp = kc % 2
                    stv = st_pair[sp]
                    if kc < 2 and qs == 0:
                        P.wait("tensor", c.s_dv, dv_bound)
                    if kc >= 2:
                        P.wait("tensor", c.s_act, exp_done[kc - 2])
                    for s2 in range(2):
                        P.do("tensor", lambda q, a=a, kc=kc, qs=qs, s2=s2, stv=stv:
                             q.matmul(stv[:, s2 * 512:(s2 + 1) * 512],
                                      a.kp[:, kc * 128:(kc + 1) * 128],
                                      a.qp[:, qs * QS + s2 * 512:qs * QS + (s2 + 1) * 512],
                                      start=True, stop=True),
                             c.s_pe, 0 if s2 == 0 else 1)
                    pe_st = c.s_pe.count
                    P.wait("scalar", c.s_pe, pe_st)
                    P.do("scalar", lambda q, a=a, sp=sp, stv=stv:
                         q.activation(a.pt[:, sp, :], stv[:, :], AF.Exp), c.s_act, 1)
                    exp_done[kc] = c.s_act.count
                    if kc >= qs * (QS // 128):
                        P.wait("gpsimd", c.s_act, exp_done[kc])
                        P.do("gpsimd", lambda q, a=a, sp=sp, qs=qs, kc=kc:
                             q.affine_select(a.pt[:, sp, :], a.pt[:, sp, :],
                                             [[1, QS]], ALU.is_ge, 0.0,
                                             base=qs * QS - kc * 128,
                                             channel_multiplier=-1), c.s_pool, 1)
                        pool_done[kc] = c.s_pool.count
                if step >= 1:
                    kc = step - 1
                    sp = kc % 2
                    if pool_done[kc] is not None:
                        P.wait("tensor", c.s_pool, pool_done[kc])
                    else:
                        P.wait("tensor", c.s_act, exp_done[kc])
                    if kc == 0:
                        P.wait("tensor", c.s_dv, c.s_dv.count)
                    for s2 in range(2):
                        P.do("tensor", lambda q, a=a, kc=kc, sp=sp, s2=s2, nkc=nkc, o_ps=o_ps:
                             q.matmul(o_ps[:, s2 * 512:(s2 + 1) * 512],
                                      a.vp[:, kc, :], a.pt[:, sp, s2 * 512:(s2 + 1) * 512],
                                      start=(kc == 0), stop=(kc == nkc - 1)),
                             c.s_pe, 0 if (s2 == 0 or kc < nkc - 1) else 1)
            pe_pv = c.s_pe.count
            P.wait("vector", c.s_pe, pe_pv)
            P.do("vector", lambda q, a=a, o_ps=o_ps:
                 q.reciprocal(a.rl[:, :], o_ps[64:65, :]), c.s_dv, 1)
            dv_rl = c.s_dv.count
            P.wait("tensor", c.s_dv, dv_rl)
            for s2 in range(2):
                P.do("tensor", lambda q, c=c, a=a, s2=s2, rb_ps=rb_ps:
                     q.matmul(rb_ps[:, s2 * 512:(s2 + 1) * 512], c.ones1_f[:, 0:64],
                              a.rl[:, s2 * 512:(s2 + 1) * 512], start=True, stop=True),
                     c.s_pe, 0 if s2 == 0 else 1)
            pe_rb = c.s_pe.count
            P.wait("scalar", c.s_pe, pe_rb)
            P.do("scalar", lambda q, a=a, rb_ps=rb_ps:
                 q.copy(a.rb[:, :], rb_ps), c.s_act, 1)
            act_rb = c.s_act.count
            P.wait("vector", c.s_act, act_rb)
            base = (h % 2) * 64
            cc_t = h // 2
            P.do("vector", lambda q, a=a, base=base, o_ps=o_ps:
                 q.tensor_tensor(a.otmp[base:base + 64, :], o_ps[0:64, :],
                                 a.rb[:, :], ALU.mult))
            P.do("vector", lambda q, c=c, a=a, base=base, cc_t=cc_t, qs=qs:
                 q.tensor_tensor(c.xcur[base:base + 64, cc_t, qs * QS:(qs + 1) * QS],
                                 c.xcur[base:base + 64, cc_t, qs * QS:(qs + 1) * QS],
                                 a.otmp[base:base + 64, :], ALU.add), c.s_dv, 1)
        if h + 1 < NH:
            wg_prev = wg_next
    return c.s_dv.count


def plan_p2(nc, P, cx, A, wqk_h, wv, bq, bk, bv, x_in, x_out):
    if A is not None:
        cx.lnx = A("lnx", [128, CCH, T + 1], BF16)
    a = alloc_attn0(nc, A)
    emit_attn0_weights(cx, a, wv, bq, bk, bv)
    wg0 = emit_wqk_load(cx, a, wqk_h, 0)

    wg_for_batch = wg0
    for b in range(2):
        ld_tok = emit_load_xcur(cx, x_in, b)
        cx.dma.wait("vector", ld_tok)
        for s in range(NSEG):
            emit_ln_seg(cx, s * SEG, lambda cc, c=cx, s=s:
                        c.lnx[:, cc, s * SEG:(s + 1) * SEG])
        P.wait("tensor", cx.s_dv, cx.s_dv.count)
        cx.gdma.wait("tensor", a.wv_tok)
        emit_attn0_batch(cx, a, wqk_h, wg_for_batch)
        if b == 0:
            P.wait("gpsimd", cx.s_pe, cx.s_pe.count)
            wg_for_batch = emit_wqk_load(cx, a, wqk_h, 0)
        dv_done = cx.s_dv.count
        P.wait("sync", cx.s_dv, dv_done)
        st_tok = emit_store_xcur(cx, x_out, b)
        note_store(cx, b, st_tok)
        if b == 0:
            for e in ("sync", "vector", "tensor", "scalar"):
                cx.stst.wait(e, st_tok)


def build_p2():
    nc = bass.Bass("TRN2", target_bir_lowering=False, debug=False,
                   detect_race_conditions=False)
    x_in = nc.dram_tensor("x_in", [2, C, T], F32, kind="ExternalInput").ap()
    wqk_h = nc.dram_tensor("wqk_h", [NH, C, 128], F32, kind="ExternalInput").ap()
    wv = nc.dram_tensor("wv", [C, C], F32, kind="ExternalInput").ap()
    bq = nc.dram_tensor("bq", [64, NH], F32, kind="ExternalInput").ap()
    bk = nc.dram_tensor("bk", [64, NH], F32, kind="ExternalInput").ap()
    bv = nc.dram_tensor("bv", [1, C], F32, kind="ExternalInput").ap()
    x_out = nc.dram_tensor("x_out", [2, C, T], F32, kind="ExternalOutput").ap()
    P = Prog(nc)
    cx = alloc_common(nc, P)
    cx.lnx = nc.alloc_sbuf_tensor("lnx", [128, CCH, T + 1], BF16)
    emit_consts(cx)
    plan_p2(nc, P, cx, None, wqk_h, wv, bq, bk, bv, x_in, x_out)
    cx.stst.wait_all("sync")
    P.build()
    return nc



MT = 4096
P4_DEBUG = False
TTS = 512
NTT = MT // TTS


def alloc_moe(nc, A=None):
    m = Ctx()
    A = A or nc.alloc_sbuf_tensor
    m.idx = A("m_idx", [128, 64], I32)
    m.idxf = A("m_idxf", [128, 64], F32)
    m.mcmp = A("m_mcmp", [128, 64], F32)
    m.ew = A("m_ew", [128, 64], F32)
    m.maskp = A("m_maskp", [128, 32], F32)
    m.eid = A("m_eid", [128, 1], F32)
    m.gate = A("m_gate", [128, 2], F32)
    m.bias = A("m_bias", [128, CCH], F32)
    m.mrow = A("m_mrow", [1, 2, TTS], F32)
    m.lns = A("m_lns", [128, CCH, SEG], BF16)
    m.wbb = A("m_wbb", [128, FF], BF16)
    m.hbf = m.wbb[:, :].rearrange("p (a t) -> p a t", a=CCH)
    m.w1f = A("m_w1f", [128, 2, CCH, 1024], BF16)
    m.w2f = A("m_w2f", [128, CCH, 1024], BF16)
    m.xm = A("m_xm", [128, CCH, TTS], BF16)
    m.ysb = A("m_ysb", [128, CCH, TTS], F32)
    m.rsb = m.ysb
    return m


def emit_moe_mask(cx, m):
    P = cx.P
    P.do("vector", lambda q, m=m: q.tensor_copy(m.idxf[:, :], m.idx[:, :]))
    P.do("vector", lambda q, m=m:
         q.tensor_scalar(m.mcmp[:, :], m.idxf[:, :], m.eid[:, :], None, ALU.is_equal))
    P.do("vector", lambda q, m=m:
         q.tensor_tensor(m.mcmp[:, :], m.mcmp[:, :], m.ew[:, :], ALU.mult))
    P.do("vector", lambda q, m=m:
         q.tensor_copy(m.maskp[:, :], m.mcmp[:, 32:64]))
    P.do("vector", lambda q, m=m:
         q.tensor_tensor(m.maskp[:, :], m.maskp[:, :], m.mcmp[:, 0:32],
                         ALU.add))
    P.do("vector", lambda q, m=m:
         q.tensor_scalar_mul(m.maskp[:, :], m.maskp[:, :], 0.5), cx.s_dv, 1)
    P.wait("sync", cx.s_dv, cx.s_dv.count)
    t = cx.stst.start("sync", lambda q, m=m, mask_d=m.mask_d:
                      q.dma_start(mask_d.rearrange("o (p f) -> (o p) f", p=128),
                                  m.maskp[:, :]))
    return t


def emit_weight_cache(cx, m, w_ap, wbf_ap, rows):
    toks = []
    for r0 in range(0, rows, 128):
        t1 = cx.gdma.start("gpsimd", lambda q, m=m, w=w_ap, r0=r0:
                           q.dma_start(m.wbb[:, 0:w.shape[1]], w[r0:r0 + 128, :]))
        cx.gdma.wait("sync", t1)
        t2 = cx.stst.start("sync", lambda q, m=m, w=wbf_ap, r0=r0:
                           q.dma_start(w[r0:r0 + 128, :], m.wbb[:, 0:w.shape[1]]))
        cx.stst.wait("gpsimd", t2)
        toks.append(t2)
    return toks


def plan_p4(nc, P, cx, A, w1c, w2c, idx, ew, eid, gate, bias, dd,
            x_in, x_out, convert=True):
    w1bf = dd["w1bf"]; w2bf = dd["w2bf"]; cc_in = dd["cc_in"]
    cc_out = dd["cc_out"]; arin = dd["arin"]; arout = dd["arout"]
    mask_d = dd["mask_d"]
    m = alloc_moe(nc, A)
    m.mask_d = mask_d
    h_ps = [cx.pool[:, 2, 0:512], cx.pool[:, 2, 512:1024]]
    y_ps = [cx.pool[:, 3, 0:512], cx.pool[:, 3, 512:1024]]
    mb_ps = cx.pool[:, 0, 0:512]

    emit_consts(cx)
    for (sb, ap) in ((m.idx, idx), (m.ew, ew), (m.eid, eid), (m.gate, gate),
                     (m.bias, bias)):
        t = cx.dma.start("sync", lambda q, sb=sb, ap=ap: q.dma_start(sb[:, :], ap))
        cx.dma.wait("vector", t)
    mask_tok = emit_moe_mask(cx, m)
    P.do("vector", lambda q, m=m:
         q.tensor_scalar_mul(m.bias[:, :], m.bias[:, :], 0.5), cx.s_dv, 1)
    dv_pre = cx.s_dv.count

    if convert:
        w1toks = emit_weight_cache(cx, m, w1c, w1bf, C)
        w2toks = emit_weight_cache(cx, m, w2c, w2bf, FF)

    ldA = emit_load_xcur(cx, x_in, 0)
    cx.dma.wait("vector", ldA)
    ctoks = []
    for s in range(NSEG):
        emit_ln_seg(cx, s * SEG, lambda cc, m=m: m.lns[:, cc, :])
        dv_ln = cx.s_dv.count
        P.wait("sync", cx.s_dv, dv_ln)
        ct = cx.stst.start("sync", lambda q, m=m, cc_in=cc_in, s=s:
                           q.dma_start(cc_in[:, s * SEG:(s + 1) * SEG]
                                       .rearrange("(cc p) t -> p cc t", p=128),
                                       m.lns[:, :, :]))
        ctoks.append(ct)
        cx.stst.wait("vector", ct)
    cc_sem = P.sem()
    for ct in ctoks:
        cx.stst.wait("gpsimd", ct)
    P.do("gpsimd", lambda q, cc_in=cc_in, cc_out=cc_out:
         q.collective_compute("AllGather", ALU.bypass,
                              replica_groups=[list(range(NCORES))],
                              ins=[cc_in.opt()], outs=[cc_out.opt()]),
         cc_sem, 1)
    P.wait("sync", cc_sem, 1)

    if convert:
        for t in w1toks + w2toks:
            cx.stst.wait("gpsimd", t)
        cx.stst.wait("scalar", w1toks[-1])
        cx.stst.wait("scalar", w2toks[-1])
    ytoks = []
    for tt in range(NTT):
        rank = tt // (NTT // 2)
        col0 = (tt % (NTT // 2)) * TTS
        P.wait("sync", cx.s_pe, cx.s_pe.count)
        xt = cx.dma.start("sync", lambda q, m=m, cc_out=cc_out, rank=rank, col0=col0:
                          q.dma_start(m.xm[:, :, :],
                                      cc_out[rank * C:(rank + 1) * C,
                                             col0:col0 + TTS]
                                      .rearrange("(cc p) t -> p cc t", p=128)))
        cx.dma.wait("tensor", xt)
        mpar = tt % 2
        if tt == 0:
            cx.stst.wait("sync", mask_tok)
        P.wait("sync", cx.s_pe, cx.s_pe.count)
        mrt = cx.dma.start("sync", lambda q, m=m, tt=tt, mpar=mpar:
                           q.dma_start(m.mrow[:, mpar, :],
                                       m.mask_d[:, tt * TTS:(tt + 1) * TTS]))
        if tt == 0:
            P.wait("tensor", cx.s_dv, dv_pre)
        cx.dma.wait("tensor", mrt)
        P.wait("tensor", cx.s_dv, cx.s_dv.count)
        P.do("tensor", lambda q, c=cx, m=m, tt=tt, mpar=mpar:
             q.matmul(mb_ps, c.ones1_f[:, :], m.mrow[:, mpar, :],
                      start=True, stop=True), cx.s_pe, 1)
        for fc in range(4):
            par = fc % 2
            P.wait("gpsimd", cx.s_pe, cx.s_pe.count)
            w2t = cx.gdma.start("gpsimd", lambda q, m=m, w2bf=w2bf, fc=fc:
                                q.dma_start(m.w2f[:, :, :],
                                            w2bf[fc * 1024:(fc + 1) * 1024, :]
                                            .rearrange("(fm p) d -> p fm d", p=128)))
            w1t = cx.gdma.start("gpsimd", lambda q, m=m, w1bf=w1bf, fc=fc, par=par:
                                q.dma_start(m.w1f[:, par, :, :],
                                            w1bf[:, fc * 1024:(fc + 1) * 1024]
                                            .rearrange("(cc p) f -> p cc f", p=128)))
            cx.gdma.wait("tensor", w1t)
            if P4_DEBUG and tt == 0 and fc == 0:
                dbg_xm = nc.dram_tensor("dbg_xm", [128, CCH, TTS], F32,
                                        kind="ExternalOutput").ap()
                cx.dma.wait("gpsimd", xt)
                dxm = cx.stst.start("gpsimd", lambda q, m=m, o=dbg_xm:
                                    q.dma_start(o, m.xm[:, :, :]))
                cx.stst.wait("gpsimd", dxm)
            for mg in range(CCH):
                hp = h_ps[mg % 2]
                P.wait("tensor", cx.s_act, cx.s_act.count)
                for ccc in range(CCH):
                    P.do("tensor", lambda q, m=m, par=par, mg=mg, ccc=ccc, hp=hp:
                         q.matmul(hp, m.w1f[:, par, ccc, mg * 128:(mg + 1) * 128],
                                  m.xm[:, ccc, :],
                                  start=(ccc == 0), stop=(ccc == CCH - 1)),
                         cx.s_pe, 0 if ccc < CCH - 1 else 1)
                pe = cx.s_pe.count
                P.wait("scalar", cx.s_pe, pe)
                P.do("scalar", lambda q, m=m, mg=mg, hp=hp:
                     q.activation(m.hbf[:, mg, :], hp, AF.Gelu), cx.s_act, 1)
            act_h = cx.s_act.count
            if P4_DEBUG and tt == 0 and fc == 0:
                dbg_h = nc.dram_tensor("dbg_h", [128, CCH, TTS], F32,
                                       kind="ExternalOutput").ap()
                P.wait("gpsimd", cx.s_act, act_h)
                dh = cx.stst.start("gpsimd", lambda q, m=m, o=dbg_h:
                                   q.dma_start(o, m.hbf[:, :, :]))
                cx.stst.wait("gpsimd", dh)
                cx.stst.wait("scalar", dh)
                cx.stst.wait("tensor", dh)
            cx.gdma.wait("tensor", w2t)
            P.wait("tensor", cx.s_act, act_h)
            for dg in range(CCH):
                yp = y_ps[dg % 2]
                P.wait("tensor", cx.s_dv, cx.s_dv.count)
                for fm in range(CCH):
                    P.do("tensor", lambda q, m=m, fm=fm, dg=dg, yp=yp:
                         q.matmul(yp,
                                  m.w2f[:, fm, dg * 128:(dg + 1) * 128],
                                  m.hbf[:, fm, :],
                                  start=(fm == 0), stop=(fm == CCH - 1)),
                         cx.s_pe, 0 if fm < CCH - 1 else 1)
                pe = cx.s_pe.count
                P.wait("vector", cx.s_pe, pe)
                if fc == 0:
                    P.do("vector", lambda q, m=m, dg=dg, yp=yp:
                         q.tensor_copy(m.ysb[:, dg, :], yp))
                else:
                    P.do("vector", lambda q, m=m, dg=dg, yp=yp:
                         q.tensor_tensor(m.ysb[:, dg, :], m.ysb[:, dg, :],
                                         yp, ALU.add))
        if P4_DEBUG and tt == 0:
            dbg_y0 = nc.dram_tensor("dbg_y0", [128, CCH, TTS], F32,
                                    kind="ExternalOutput").ap()
            P.wait("sync", cx.s_dv, cx.s_dv.count)
            dy0 = cx.stst.start("sync", lambda q, m=m, o=dbg_y0:
                                q.dma_start(o, m.ysb[:, :, :]))
            cx.stst.wait("sync", dy0)
            cx.stst.wait("vector", dy0)
        pe_mb = cx.s_pe.count
        P.wait("vector", cx.s_pe, pe_mb)
        P.do("vector", lambda q, m=m, mb_ps=mb_ps:
             q.tensor_tensor(m.ysb[:, :, :], m.ysb[:, :, :],
                             mb_ps[:, :].unsqueeze(1).broadcast_to([128, CCH, TTS]),
                             ALU.mult), cx.s_dv, 1)
        dv_y = cx.s_dv.count
        P.wait("sync", cx.s_dv, dv_y)
        yt = cx.stst.start("sync", lambda q, m=m, arin=arin, tt=tt:
                           q.dma_start(arin[:, tt * TTS:(tt + 1) * TTS]
                                       .rearrange("(cc p) t -> p cc t", p=128),
                                       m.ysb[:, :, :]))
        ytoks.append(yt)
        cx.stst.wait("vector", yt)

    if P4_DEBUG:
        dbg_maskp = nc.dram_tensor("dbg_maskp", [128, 32], F32, kind="ExternalOutput").ap()
        dbg_mcmp = nc.dram_tensor("dbg_mcmp", [128, 64], F32, kind="ExternalOutput").ap()
        dbg_idxf = nc.dram_tensor("dbg_idxf", [128, 64], F32, kind="ExternalOutput").ap()
        dbg_ew = nc.dram_tensor("dbg_ew", [128, 64], F32, kind="ExternalOutput").ap()
        dbg_mask = nc.dram_tensor("dbg_mask", [1, MT], F32, kind="ExternalOutput").ap()
        dbg_gather = nc.dram_tensor("dbg_gather", [C, T], F32, kind="ExternalOutput").ap()
        dbg_arin = nc.dram_tensor("dbg_arin", [C, MT], F32, kind="ExternalOutput").ap()
        cx.stst.wait("sync", mask_tok)
        dt0 = cx.stst.start("sync", lambda q: q.dma_start(dbg_maskp, m.maskp[:, :]))
        dta = cx.stst.start("sync", lambda q: q.dma_start(dbg_mcmp, m.mcmp[:, :]))
        dtb = cx.stst.start("sync", lambda q: q.dma_start(dbg_idxf, m.idxf[:, :]))
        dtc = cx.stst.start("sync", lambda q: q.dma_start(dbg_ew, m.ew[:, :]))
        for _t in (dta, dtb, dtc):
            cx.stst.wait("sync", _t)
        dt1 = cx.stst.start("sync", lambda q: q.dma_start(dbg_mask, m.mask_d))
        cx.stst.wait("sync", dt0)
        P.wait("gpsimd", cc_sem, 1)
        dt2 = cx.stst.start("gpsimd", lambda q: q.dma_start(dbg_gather, cc_out[0:C, :]))
        for t in ytoks:
            cx.stst.wait("sync", t)
        dt3 = cx.stst.start("sync", lambda q: q.dma_start(dbg_arin, arin))
        cx.stst.wait("sync", dt1)
        cx.stst.wait("sync", dt3)
        cx.stst.wait("gpsimd", dt2)
    ar_sem = P.sem()
    for t in ytoks:
        cx.stst.wait("gpsimd", t)
    P.do("gpsimd", lambda q, arin=arin, arout=arout:
         q.collective_compute("AllReduce", ALU.add,
                              replica_groups=[list(range(NCORES))],
                              ins=[arin.opt()], outs=[arout.opt()]),
         ar_sem, 1)
    P.wait("sync", ar_sem, 1)

    if P4_DEBUG:
        dbg_arout = nc.dram_tensor("dbg_arout", [C, MT], F32, kind="ExternalOutput").ap()
        P.wait("sync", ar_sem, 1)
        dao = cx.stst.start("sync", lambda q: q.dma_start(dbg_arout, arout))
        cx.stst.wait("sync", dao)
    for half in range(2):
        for s in range(NSEG):
            P.wait("sync", cx.s_dv, cx.s_dv.count)
            rt = cx.dma.start("sync", lambda q, m=m, arout=arout, half=half, s=s:
                              q.dma_start(m.rsb[:, :, :],
                                          arout[:, half * T + s * SEG:
                                                half * T + (s + 1) * SEG]
                                          .rearrange("(cc p) t -> p cc t", p=128)))
            cx.dma.wait("vector", rt)
            for cc in range(CCH):
                P.do("vector", lambda q, c=cx, m=m, half=half, s=s, cc=cc:
                     q.scalar_tensor_tensor(
                         c.xcur[:, cc, s * SEG:(s + 1) * SEG],
                         m.rsb[:, cc, :], m.gate[:, half:half + 1],
                         c.xcur[:, cc, s * SEG:(s + 1) * SEG],
                         ALU.mult, ALU.add),
                     cx.s_dv if cc == CCH - 1 else None, 1)
    for cc in range(CCH):
        P.do("vector", lambda q, c=cx, m=m, cc=cc:
             q.tensor_scalar(c.xcur[:, cc, :], c.xcur[:, cc, :],
                             m.bias[:, cc:cc + 1], None, ALU.add),
             cx.s_dv if cc == CCH - 1 else None, 1)
    dvA = cx.s_dv.count
    P.wait("sync", cx.s_dv, dvA)
    stA = emit_store_xcur(cx, x_out, 0)
    note_store(cx, 0, stA)
    for e in ("sync", "vector"):
        cx.stst.wait(e, stA)
    ldB = emit_load_xcur(cx, x_in, 1)
    cx.dma.wait("vector", ldB)
    for cc in range(CCH):
        P.do("vector", lambda q, c=cx, m=m, cc=cc:
             q.tensor_scalar(c.xcur[:, cc, :], c.xcur[:, cc, :],
                             m.bias[:, cc:cc + 1], None, ALU.add),
             cx.s_dv if cc == CCH - 1 else None, 1)
    dvB = cx.s_dv.count
    P.wait("sync", cx.s_dv, dvB)
    note_store(cx, 1, emit_store_xcur(cx, x_out, 1))


def moe_dram(nc):
    return {
        "w1bf": nc.dram_tensor("w1bf", [C, FF], BF16, kind="Internal").ap(),
        "w2bf": nc.dram_tensor("w2bf", [FF, C], BF16, kind="Internal").ap(),
        "cc_in": nc.dram_tensor("cc_in", [C, T], BF16, kind="Internal").ap(),
        "cc_out": nc.dram_tensor("cc_out", [NCORES * C, T], BF16,
                                 kind="Internal", addr_space="Shared").ap(),
        "arin": nc.dram_tensor("arin", [C, MT], F32, kind="Internal").ap(),
        "arout": nc.dram_tensor("arout", [C, MT], F32, kind="Internal",
                                addr_space="Shared").ap(),
        "mask_d": nc.dram_tensor("mask_d", [1, MT], F32, kind="Internal").ap(),
    }


def build_p4():
    nc = bass.Bass("TRN2", target_bir_lowering=False, debug=False,
                   detect_race_conditions=False)
    x_in = nc.dram_tensor("x_in", [2, C, T], F32, kind="ExternalInput").ap()
    w1c = nc.dram_tensor("w1c", [C, FF], F32, kind="ExternalInput").ap()
    w2c = nc.dram_tensor("w2c", [FF, C], F32, kind="ExternalInput").ap()
    idx = nc.dram_tensor("idx", [128, 64], I32, kind="ExternalInput").ap()
    ew = nc.dram_tensor("ew", [128, 64], F32, kind="ExternalInput").ap()
    eid = nc.dram_tensor("eid", [128, 1], F32, kind="ExternalInput").ap()
    gate = nc.dram_tensor("gate", [128, 2], F32, kind="ExternalInput").ap()
    bias = nc.dram_tensor("bias", [128, CCH], F32, kind="ExternalInput").ap()
    x_out = nc.dram_tensor("x_out", [2, C, T], F32, kind="ExternalOutput").ap()
    dd = moe_dram(nc)
    P = Prog(nc)
    cx = alloc_common(nc, P)
    emit_consts(cx)
    plan_p4(nc, P, cx, None, w1c, w2c, idx, ew, eid, gate, bias, dd,
            x_in, x_out, convert=True)
    cx.stst.wait_all("sync")
    P.build()
    return nc



PAIRS = [(0, 1), (0, 2), (0, 3), (1, 2), (1, 3), (2, 3)]
JSIGN = [1.0, -1.0, 1.0, 1.0, -1.0, 1.0]
LN6H = -0.5 * math.log(6.0)


def alloc_attn1(nc, A=None):
    a = Ctx()
    A = A or nc.alloc_sbuf_tensor
    a.wpq = A("b_wpq", [128, CCH, 128], BF16)
    a.wpk = A("b_wpk", [128, CCH, 128], BF16)
    a.w_vh = A("b_wvh", [128, 2, CCH, 64], BF16)
    a.w_out = A("b_wout", [128, CCH, C], BF16)
    a.bvb = A("b_bvb", [128, C], F32)
    a.bout = A("b_bout", [128, CCH], F32)
    a.AO = A("b_AO", [128, CCH, T], BF16)
    a.lqe = A("b_lqe", [128, 6, 256], BF16)
    a.jlke = A("b_jlke", [128, 6, 256], BF16)

    a.vp = A("b_vp", [128, NKC, 65], BF16)
    a.pt = A("b_pt", [128, 2, QS], BF16)
    a.rl = A("b_rl", [1, QS], F32)
    a.rb = A("b_rb", [64, QS], F32)
    a.ytmp = A("b_ytmp", [128, SEG], F32)
    a.ln6h = A("b_ln6h", [128, 1], F32)
    a.bvrow = a.rb[0:1, :]
    return a


def bind_attn1_aliases(cx, a):
    xb = cx.xbf[0:6, :, :].rearrange("p a t -> p (a t)")
    a.lqh = xb[:, 0:T]
    a.jlkh = xb[:, T:2 * T]
    a.pairq = a.AO[:, 0, :]
    a.pairk = a.AO[:, 1, :]
    a.q1e = a.AO[:, 2, 0:1024].rearrange("p (c t) -> p c t", c=4)
    a.k1e = a.AO[:, 2, 1024:2048].rearrange("p (c t) -> p c t", c=4)
    a.q2e = a.AO[:, 3, 0:1024].rearrange("p (c t) -> p c t", c=4)
    a.k2e = a.AO[:, 3, 1024:2048].rearrange("p (c t) -> p c t", c=4)
    arena = a.AO[:, 4:8, :].rearrange("p a t -> p (a t)").bitcast(F32)
    a.lx = arena[:, 0:1536].rearrange("p (c t) -> p c t", c=6)
    a.sq = arena[:, 1536:3072].rearrange("p (c t) -> p c t", c=6)
    a.acc = arena[:, 3072:3328]
    a.rn = arena[:, 3328:3584]
    a.rnn = arena[:, 3584:3840]
    a.tmp1 = arena[:, 3840:4096]
    return a


def emit_wv1_load(cx, a, wvh_ap, h):
    return cx.gdma.start("gpsimd", lambda q, a=a, w=wvh_ap, h=h:
                         q.dma_start(a.w_vh[:, h % 2, :, :],
                                     w[h].rearrange("(cc p) d -> p cc d", p=128)))


def emit_attn1_weights(cx, a, wpq_ap, wpk_ap, wout_ap, bv_ap, bout_ap):
    P = cx.P
    toks = []
    for (sb, ap) in ((a.wpq, wpq_ap), (a.wpk, wpk_ap)):
        toks.append(cx.gdma.start("gpsimd", lambda q, sb=sb, ap=ap:
                    q.dma_start(sb[:, :, :],
                                ap.rearrange("(cc p) d -> p cc d", p=128))))
    a.wout_tok = cx.gdma.start("gpsimd", lambda q, a=a, w=wout_ap:
                               q.dma_start(a.w_out[:, :, :],
                                           w.rearrange("(cc p) d -> p cc d", p=128)))
    t3 = cx.dma.start("sync", lambda q, a=a, b=bv_ap: q.dma_start(a.bvrow, b))
    t4 = cx.dma.start("sync", lambda q, a=a, b=bout_ap: q.dma_start(a.bout[:, :], b))
    a.bout_tok = t4
    bb = cx.pool[:, 2, :]
    cx.dma.wait("tensor", t3)
    P.wait("tensor", cx.s_dv, cx.dv_consts)
    for s in range(2):
        P.do("tensor", lambda q, c=cx, a=a, s=s, bb=bb:
             q.matmul(bb[:, s * 512:(s + 1) * 512], c.ones1_f[:, :],
                      a.bvrow[0:1, s * 512:(s + 1) * 512], start=True, stop=True),
             cx.s_pe, 1)
    pe = cx.s_pe.count
    P.wait("scalar", cx.s_pe, pe)
    P.do("scalar", lambda q, a=a, bb=bb: q.copy(a.bvb[:, :], bb[:, :]))
    P.do("vector", lambda q, a=a: q.memset(a.ln6h[:, :], LN6H))
    a.wpq_toks = toks
    return toks


def emit_attn1_batch(cx, a):
    P = cx.P
    c = cx
    pool = c.pool
    st_pair = [pool[:, 0, :], pool[:, 1, :]]
    o_ps = pool[0:65, 2, :]
    rb_ps = pool[0:64, 3, :]
    pk_ps = pool[:, 2, 0:512]
    v_ps = pool[:, 3, 0:64]
    y_ps = [pool[:, 2, 0:512], pool[:, 2, 512:1024]]

    for t in a.wpq_toks:
        cx.gdma.wait("tensor", t)
    for (wp, dst) in ((a.wpq, a.pairq), (a.wpk, a.pairk)):
        for s in range(NSEG):
            P.wait("tensor", c.s_dv, c.s_dv.count)
            for ccc in range(CCH):
                P.do("tensor", lambda q, c=c, wp=wp, ccc=ccc, s=s:
                     q.matmul(pk_ps, wp[:, ccc, :], c.lnx[:, ccc, s * SEG:(s + 1) * SEG],
                              start=(ccc == 0), stop=(ccc == CCH - 1)),
                     c.s_pe, 0 if ccc < CCH - 1 else 1)
            pe = c.s_pe.count
            P.wait("vector", c.s_pe, pe)
            P.do("vector", lambda q, dst=dst, s=s, pk_ps=pk_ps:
                 q.tensor_copy(dst[:, s * SEG:(s + 1) * SEG], pk_ps))
    dv_proj = c.s_dv.count

    rp = []
    P.wait("sync", c.s_dv, dv_proj)
    for (src, dst) in ((a.pairq, a.q1e), (a.pairk, a.k1e)):
        for d4 in range(4):
            rp.append(cx.dma.start("sync", lambda q, src=src, dst=dst, d4=d4:
                      q.dma_start(
                          dst[:, d4, :], src[d4:64:4, :])))
    for (src, dst) in ((a.pairq, a.q2e), (a.pairk, a.k2e)):
        for d4 in range(4):
            rp.append(cx.dma.start("sync", lambda q, src=src, dst=dst, d4=d4:
                      q.dma_start(
                          dst[:, d4, :], src[64 + d4:128:4, :])))
    for t in rp:
        cx.dma.wait("vector", t)

    for side in range(2):
        e1 = a.q1e if side == 0 else a.k1e
        e2 = a.q2e if side == 0 else a.k2e
        for ci, (i, j) in enumerate(PAIRS):
            P.do("vector", lambda q, a=a, e1=e1, e2=e2, i=i, j=j:
                 q.tensor_tensor(a.tmp1[:, :], e1[:, i, :], e2[:, j, :], ALU.mult))
            P.do("vector", lambda q, a=a, e1=e1, e2=e2, i=i, j=j, ci=ci:
                 q.tensor_tensor(a.sq[:, ci, :], e1[:, j, :], e2[:, i, :], ALU.mult))
            P.do("vector", lambda q, a=a, ci=ci:
                 q.tensor_tensor(a.lx[:, ci, :], a.tmp1[:, :], a.sq[:, ci, :],
                                 ALU.subtract))
        dv_l = c.s_dv.count
        P.wait("scalar", c.s_dv, dv_l)
        P.do("scalar", lambda q, a=a:
             q.activation(a.sq[:, :, :], a.lx[:, :, :], AF.Square), c.s_act, 1)
        act_sq = c.s_act.count
        P.wait("vector", c.s_act, act_sq)
        P.do("vector", lambda q, a=a: q.tensor_copy(a.acc[:, :], a.sq[:, 0, :]))
        for ci in range(1, 6):
            P.do("vector", lambda q, a=a, ci=ci:
                 q.tensor_tensor(a.acc[:, :], a.acc[:, :], a.sq[:, ci, :], ALU.add))
        P.do("vector", lambda q, a=a:
             q.tensor_scalar_max(a.acc[:, :], a.acc[:, :], 1e-24), c.s_dv, 1)
        dv_acc = c.s_dv.count
        P.wait("scalar", c.s_dv, dv_acc)
        P.do("scalar", lambda q, a=a: q.activation(a.tmp1[:, :], a.acc[:, :], AF.Ln))
        if side == 0:
            P.do("scalar", lambda q, a=a:
                 q.activation(a.rn[:, :], a.tmp1[:, :], AF.Exp, scale=-0.5,
                              bias=a.ln6h[:, :]), c.s_act, 1)
        else:
            P.do("scalar", lambda q, a=a:
                 q.activation(a.rn[:, :], a.tmp1[:, :], AF.Exp, scale=-0.5),
                 c.s_act, 1)
        act_rn = c.s_act.count
        P.wait("vector", c.s_act, act_rn)
        if side == 0:
            for ci in range(6):
                P.do("vector", lambda q, a=a, ci=ci:
                     q.tensor_tensor(a.lqe[:, ci, :], a.lx[:, ci, :], a.rn[:, :],
                                     ALU.mult))
        else:
            P.do("vector", lambda q, a=a:
                 q.tensor_scalar_mul(a.rnn[:, :], a.rn[:, :], -1.0))
            for aa in range(6):
                src_c = 5 - aa
                rn_t = "rn" if JSIGN[aa] > 0 else "rnn"
                P.do("vector", lambda q, a=a, aa=aa, src_c=src_c, rn_t=rn_t:
                     q.tensor_tensor(a.jlke[:, aa, :], a.lx[:, src_c, :],
                                     getattr(a, rn_t)[:, :], ALU.mult))
    dv_ext = c.s_dv.count

    P.do("vector", lambda q, a=a: q.memset(a.vp[:, :, 64:65], 1.0))

    wvt_prev = a.wv0_tok
    for h in range(NH):
        par = h % 2
        if h + 1 < NH:
            P.wait("gpsimd", c.s_pe, c.s_pe.count)
            wvt_next = emit_wv1_load(cx, a, a.wvh_ap, h + 1)
        P.wait("sync", c.s_pe, c.s_pe.count)
        if h == 0:
            P.wait("sync", c.s_dv, dv_ext)
        rp_toks = []
        for ci in range(6):
            qn = "sync" if ci % 2 == 0 else "scalar"
            rp_toks.append(cx.dma.start(qn, lambda q, a=a, h=h, ci=ci:
                           q.dma_start(a.lqh[ci:ci + 1, :],
                                       a.lqe[h * 8:(h + 1) * 8, ci, :])))
            rp_toks.append(cx.dma.start(qn, lambda q, a=a, h=h, ci=ci:
                           q.dma_start(a.jlkh[ci:ci + 1, :],
                                       a.jlke[h * 8:(h + 1) * 8, ci, :])))
        for t in rp_toks:
            cx.dma.wait("tensor", t)
        cx.gdma.wait("tensor", wvt_prev)
        for tg in range(NKC):
            P.wait("tensor", c.s_dv, c.s_dv.count)
            for ccc in range(CCH):
                P.do("tensor", lambda q, c=c, a=a, tg=tg, ccc=ccc, par=par, v_ps=v_ps:
                     q.matmul(v_ps, c.lnx[:, ccc, tg * 128:(tg + 1) * 128],
                              a.w_vh[:, par, ccc, :],
                              start=(ccc == 0), stop=(ccc == CCH - 1)),
                     c.s_pe, 0 if ccc < CCH - 1 else 1)
            pe = c.s_pe.count
            P.wait("vector", c.s_pe, pe)
            P.do("vector", lambda q, a=a, tg=tg, h=h, v_ps=v_ps:
                 q.tensor_tensor(a.vp[:, tg, 0:64], v_ps,
                                 a.bvb[:, h * 64:(h + 1) * 64], ALU.add))
        for qs in range(NQS):
            nkc = (qs + 1) * (QS // 128)
            exp_done = [None] * nkc
            pool_done = [None] * nkc
            for step in range(nkc + 1):
                if step < nkc:
                    kc = step
                    sp = kc % 2
                    stv = st_pair[sp]
                    if kc >= 2:
                        P.wait("tensor", c.s_act, exp_done[kc - 2])
                    for s2 in range(2):
                        P.do("tensor", lambda q, a=a, kc=kc, qs=qs, s2=s2, stv=stv:
                             q.matmul(stv[:, s2 * 512:(s2 + 1) * 512],
                                      a.jlkh[:, kc * 128:(kc + 1) * 128],
                                      a.lqh[:, qs * QS + s2 * 512:qs * QS + (s2 + 1) * 512],
                                      start=True, stop=True),
                             c.s_pe, 0 if s2 == 0 else 1)
                    pe_st = c.s_pe.count
                    P.wait("scalar", c.s_pe, pe_st)
                    P.do("scalar", lambda q, a=a, sp=sp, stv=stv:
                         q.activation(a.pt[:, sp, :], stv[:, :], AF.Exp), c.s_act, 1)
                    exp_done[kc] = c.s_act.count
                    if kc >= qs * (QS // 128):
                        P.wait("gpsimd", c.s_act, exp_done[kc])
                        P.do("gpsimd", lambda q, a=a, sp=sp, qs=qs, kc=kc:
                             q.affine_select(a.pt[:, sp, :], a.pt[:, sp, :],
                                             [[1, QS]], ALU.is_ge, 0.0,
                                             base=qs * QS - kc * 128,
                                             channel_multiplier=-1), c.s_pool, 1)
                        pool_done[kc] = c.s_pool.count
                if step >= 1:
                    kc = step - 1
                    sp = kc % 2
                    if pool_done[kc] is not None:
                        P.wait("tensor", c.s_pool, pool_done[kc])
                    else:
                        P.wait("tensor", c.s_act, exp_done[kc])
                    if kc == 0:
                        P.wait("tensor", c.s_dv, c.s_dv.count)
                    for s2 in range(2):
                        P.do("tensor", lambda q, a=a, kc=kc, sp=sp, s2=s2, nkc=nkc, o_ps=o_ps:
                             q.matmul(o_ps[:, s2 * 512:(s2 + 1) * 512],
                                      a.vp[:, kc, :], a.pt[:, sp, s2 * 512:(s2 + 1) * 512],
                                      start=(kc == 0), stop=(kc == nkc - 1)),
                             c.s_pe, 0 if (s2 == 0 or kc < nkc - 1) else 1)
            pe_pv = c.s_pe.count
            P.wait("vector", c.s_pe, pe_pv)
            P.do("vector", lambda q, a=a, o_ps=o_ps:
                 q.reciprocal(a.rl[:, :], o_ps[64:65, :]), c.s_dv, 1)
            dv_rl = c.s_dv.count
            P.wait("tensor", c.s_dv, dv_rl)
            for s2 in range(2):
                P.do("tensor", lambda q, c=c, a=a, s2=s2, rb_ps=rb_ps:
                     q.matmul(rb_ps[:, s2 * 512:(s2 + 1) * 512], c.ones1_f[:, 0:64],
                              a.rl[:, s2 * 512:(s2 + 1) * 512], start=True, stop=True),
                     c.s_pe, 0 if s2 == 0 else 1)
            pe_rb = c.s_pe.count
            P.wait("scalar", c.s_pe, pe_rb)
            P.do("scalar", lambda q, a=a, rb_ps=rb_ps:
                 q.copy(a.rb[:, :], rb_ps), c.s_act, 1)
            act_rb = c.s_act.count
            P.wait("vector", c.s_act, act_rb)
            base = (h % 2) * 64
            cc_t = h // 2
            P.do("vector", lambda q, c=c, a=a, base=base, cc_t=cc_t, qs=qs, o_ps=o_ps:
                 q.tensor_tensor(a.AO[base:base + 64, cc_t, qs * QS:(qs + 1) * QS],
                                 o_ps[0:64, :], a.rb[:, :], ALU.mult), c.s_dv, 1)
        if h + 1 < NH:
            wvt_prev = wvt_next
    cx.gdma.wait("tensor", a.wout_tok)
    cx.dma.wait("vector", a.bout_tok)
    dv_ao = c.s_dv.count
    P.wait("tensor", c.s_dv, dv_ao)
    for s in range(NSEG):
        for dg in range(CCH):
            yp = y_ps[dg % 2]
            P.wait("tensor", c.s_dv, c.s_dv.count)
            for ccc in range(CCH):
                P.do("tensor", lambda q, c=c, a=a, s=s, dg=dg, ccc=ccc, yp=yp:
                     q.matmul(yp, a.w_out[:, ccc, dg * 128:(dg + 1) * 128],
                              a.AO[:, ccc, s * SEG:(s + 1) * SEG],
                              start=(ccc == 0), stop=(ccc == CCH - 1)),
                     c.s_pe, 0 if ccc < CCH - 1 else 1)
            pe = c.s_pe.count
            P.wait("vector", c.s_pe, pe)
            P.do("vector", lambda q, a=a, dg=dg, yp=yp:
                 q.tensor_scalar(a.ytmp[:, :], yp, a.bout[:, dg:dg + 1], None,
                                 ALU.add))
            P.do("vector", lambda q, c=c, a=a, dg=dg, s=s:
                 q.tensor_tensor(c.xcur[:, dg, s * SEG:(s + 1) * SEG],
                                 c.xcur[:, dg, s * SEG:(s + 1) * SEG],
                                 a.ytmp[:, :], ALU.add), c.s_dv, 1)
    return c.s_dv.count


def plan_p5(nc, P, cx, A, wpq, wpk, wvh, wout, bv, bout, x_in, x_out):
    if A is not None:
        cx.lnx = A("lnx", [128, CCH, T + 1], BF16)
    a = alloc_attn1(nc, A)
    bind_attn1_aliases(cx, a)
    a.wvh_ap = wvh
    emit_attn1_weights(cx, a, wpq, wpk, wout, bv, bout)
    a.wv0_tok = emit_wv1_load(cx, a, wvh, 0)

    for b in range(2):
        ld_tok = emit_load_xcur(cx, x_in, b)
        cx.dma.wait("vector", ld_tok)
        for s in range(NSEG):
            emit_ln_seg(cx, s * SEG, lambda cc, c=cx, s=s:
                        c.lnx[:, cc, s * SEG:(s + 1) * SEG])
        P.wait("tensor", cx.s_dv, cx.s_dv.count)
        emit_attn1_batch(cx, a)
        if b == 0:
            P.wait("gpsimd", cx.s_pe, cx.s_pe.count)
            a.wv0_tok = emit_wv1_load(cx, a, wvh, 0)
        dv_done = cx.s_dv.count
        P.wait("sync", cx.s_dv, dv_done)
        st_tok = emit_store_xcur(cx, x_out, b)
        note_store(cx, b, st_tok)
        if b == 0:
            for e in ("sync", "vector", "tensor", "scalar"):
                cx.stst.wait(e, st_tok)


def build_p5():
    nc = bass.Bass("TRN2", target_bir_lowering=False, debug=False,
                   detect_race_conditions=False)
    x_in = nc.dram_tensor("x_in", [2, C, T], F32, kind="ExternalInput").ap()
    wpq = nc.dram_tensor("wpq", [C, 128], F32, kind="ExternalInput").ap()
    wpk = nc.dram_tensor("wpk", [C, 128], F32, kind="ExternalInput").ap()
    wvh = nc.dram_tensor("wvh", [NH, C, 64], F32, kind="ExternalInput").ap()
    wout = nc.dram_tensor("wout", [C, C], F32, kind="ExternalInput").ap()
    bv = nc.dram_tensor("bv", [1, C], F32, kind="ExternalInput").ap()
    bout = nc.dram_tensor("bout", [128, CCH], F32, kind="ExternalInput").ap()
    x_out = nc.dram_tensor("x_out", [2, C, T], F32, kind="ExternalOutput").ap()
    P = Prog(nc)
    cx = alloc_common(nc, P)
    cx.lnx = nc.alloc_sbuf_tensor("lnx", [128, CCH, T + 1], BF16)
    emit_consts(cx)
    plan_p5(nc, P, cx, None, wpq, wpk, wvh, wout, bv, bout, x_in, x_out)
    cx.stst.wait_all("sync")
    P.build()
    return nc


def build_fused():
    from contextlib import ExitStack
    nc = bass.Bass("TRN2", target_bir_lowering=False, debug=False,
                   detect_race_conditions=False)
    x_fm = nc.dram_tensor("x_fm", [2, C, T], F32, kind="ExternalInput").ap()
    wr = nc.dram_tensor("wr", [C, C], F32, kind="ExternalInput").ap()
    wk = nc.dram_tensor("wk", [C, C], F32, kind="ExternalInput").ap()
    wv_f0 = nc.dram_tensor("wv_f0", [C, C], F32, kind="ExternalInput").ap()
    mu = nc.dram_tensor("mu", [128, 2, CCH], F32, kind="ExternalInput").ap()
    wqk_h = nc.dram_tensor("wqk_h", [NH, C, 128], F32, kind="ExternalInput").ap()
    wv0 = nc.dram_tensor("wv0", [C, C], F32, kind="ExternalInput").ap()
    bq = nc.dram_tensor("bq", [64, NH], F32, kind="ExternalInput").ap()
    bk = nc.dram_tensor("bk", [64, NH], F32, kind="ExternalInput").ap()
    bv0 = nc.dram_tensor("bv0", [1, C], F32, kind="ExternalInput").ap()
    w1c = nc.dram_tensor("w1c", [C, FF], F32, kind="ExternalInput").ap()
    w2c = nc.dram_tensor("w2c", [FF, C], F32, kind="ExternalInput").ap()
    idx = nc.dram_tensor("idx", [128, 64], I32, kind="ExternalInput").ap()
    ew = nc.dram_tensor("ew", [128, 64], F32, kind="ExternalInput").ap()
    eid = nc.dram_tensor("eid", [128, 1], F32, kind="ExternalInput").ap()
    gate = nc.dram_tensor("gate", [128, 2], F32, kind="ExternalInput").ap()
    bias = nc.dram_tensor("bias", [128, CCH], F32, kind="ExternalInput").ap()
    wpq = nc.dram_tensor("wpq", [C, 128], F32, kind="ExternalInput").ap()
    wpk = nc.dram_tensor("wpk", [C, 128], F32, kind="ExternalInput").ap()
    wvh = nc.dram_tensor("wvh", [NH, C, 64], F32, kind="ExternalInput").ap()
    wout = nc.dram_tensor("wout", [C, C], F32, kind="ExternalInput").ap()
    bv1 = nc.dram_tensor("bv1", [1, C], F32, kind="ExternalInput").ap()
    bout = nc.dram_tensor("bout", [128, CCH], F32, kind="ExternalInput").ap()
    y_fm = nc.dram_tensor("y_fm", [2, C, T], F32, kind="ExternalOutput").ap()

    xstore = nc.dram_tensor("xstore", [2, C, T], F32, kind="Internal").ap()
    dd = moe_dram(nc)

    P = Prog(nc)
    cx = alloc_common(nc, P)
    emit_consts(cx)

    def scoped(tag, stack):
        def A(name, shape, dtype):
            return stack.enter_context(nc.sbuf_tensor(name + tag, shape, dtype))
        return A

    with ExitStack() as es:
        plan_p1(nc, P, cx, scoped("_a", es), wr, wk, wv_f0, mu, x_fm, xstore)
    phase_barrier(P)
    with ExitStack() as es:
        plan_p2(nc, P, cx, scoped("_b", es), wqk_h, wv0, bq, bk, bv0,
                xstore, xstore)
    phase_barrier(P)
    with ExitStack() as es:
        plan_p1(nc, P, cx, scoped("_c", es), wr, wk, wv_f0, mu, xstore, xstore)
    phase_barrier(P)
    with ExitStack() as es:
        plan_p4(nc, P, cx, scoped("_d", es), w1c, w2c, idx, ew, eid, gate, bias,
                dd, xstore, xstore, convert=True)
    phase_barrier(P)
    with ExitStack() as es:
        plan_p5(nc, P, cx, scoped("_e", es), wpq, wpk, wvh, wout, bv1, bout,
                xstore, xstore)
    phase_barrier(P)
    with ExitStack() as es:
        plan_p4(nc, P, cx, scoped("_f", es), w1c, w2c, idx, ew, eid, gate, bias,
                dd, xstore, y_fm, convert=False)
    cx.stst.wait_all("sync")
    P.build()
    return nc



def plan_p1(nc, P, cx, A, wr, wk, wv, mu, x_in, x_out, tag=""):
    if A is not None:
        cx.lnx = A("lnx", [128, CCH, T + 1], BF16)
    f = alloc_ffn0(nc, A)
    f.pslots = [cx.pool[:, 2, 0:SEG], cx.pool[:, 2, SEG:2 * SEG],
                cx.pool[:, 3, 0:SEG], cx.pool[:, 3, SEG:2 * SEG]]
    mu_tok = cx.dma.start("sync", lambda q, f=f, mu=mu: q.dma_start(f.mu[:, :, :], mu))
    cx.dma.wait("vector", mu_tok)
    wdv, wtoks = emit_ffn0_weights(cx, f, wr, wk, wv)
    P.wait("tensor", cx.s_dv, wdv)

    for b in range(2):
        ld_for_batch = emit_load_xcur(cx, x_in, b)
        emit_ffn0_batch(cx, f, wdv, wtoks, ld_for_batch)
        dv_done = cx.s_dv.count
        P.wait("sync", cx.s_dv, dv_done)
        st_tok = emit_store_xcur(cx, x_out, b)
        note_store(cx, b, st_tok)
        if b == 0:
            for e in ("sync", "vector", "tensor", "scalar"):
                cx.stst.wait(e, st_tok)


def build_p1():
    nc = bass.Bass("TRN2", target_bir_lowering=False, debug=False,
                   detect_race_conditions=False)
    x_in = nc.dram_tensor("x_in", [2, C, T], F32, kind="ExternalInput").ap()
    wr = nc.dram_tensor("wr", [C, C], F32, kind="ExternalInput").ap()
    wk = nc.dram_tensor("wk", [C, C], F32, kind="ExternalInput").ap()
    wv = nc.dram_tensor("wv", [C, C], F32, kind="ExternalInput").ap()
    mu = nc.dram_tensor("mu", [128, 2, CCH], F32, kind="ExternalInput").ap()
    x_out = nc.dram_tensor("x_out", [2, C, T], F32, kind="ExternalOutput").ap()
    P = Prog(nc)
    cx = alloc_common(nc, P)
    cx.lnx = nc.alloc_sbuf_tensor("lnx", [128, CCH, T + 1], BF16)
    emit_consts(cx)
    plan_p1(nc, P, cx, None, wr, wk, wv, mu, x_in, x_out)
    cx.stst.wait_all("sync")
    P.build()
    return nc



def _prep_mu(mu_r, mu_k):
    m = np.stack([mu_r.reshape(CCH, 128).T, mu_k.reshape(CCH, 128).T], axis=1)
    return np.ascontiguousarray(m.astype(np.float32))


def kernel(**inputs):
    f32 = lambda k: np.asarray(inputs[k]).astype(np.float32)
    xs = f32("L0_attn_hidden_states")
    wr_t = np.ascontiguousarray(f32("L0_ffn_W_r").T)
    wk_t = np.ascontiguousarray(f32("L0_ffn_W_k").T)
    wv_t = np.ascontiguousarray(f32("L0_ffn_W_v").T)
    mu = _prep_mu(f32("L0_ffn_mu_r"), f32("L0_ffn_mu_k"))
    Wqk = f32("L0_attn_Wqk")
    wqk_h = np.ascontiguousarray(np.stack(
        [np.concatenate([Wqk[:, h * 64:(h + 1) * 64],
                         Wqk[:, C + h * 64:C + (h + 1) * 64]], axis=1)
         for h in range(NH)]))
    bqk = f32("L0_attn_bqk")
    bq = np.ascontiguousarray(bqk[:C].reshape(NH, 64).T)
    bk = np.ascontiguousarray(bqk[C:].reshape(NH, 64).T)
    wv0 = np.ascontiguousarray(f32("L0_attn_Wv"))
    bv0 = np.ascontiguousarray(f32("L0_attn_bv").reshape(1, C))
    w1 = f32("L1_ffn_w1")
    w2 = f32("L1_ffn_w2")
    idx_h = np.ascontiguousarray(
        np.asarray(inputs["L1_ffn_expert_indices"]).astype(np.int32)
        .reshape(128, 32, 2).transpose(0, 2, 1).reshape(128, 64))
    ew_h = np.ascontiguousarray(f32("L1_ffn_expert_weights")
                                .reshape(128, 32, 2).transpose(0, 2, 1).reshape(128, 64))
    l1bias = np.ascontiguousarray(f32("L1_ffn_bias").reshape(CCH, 128).T)
    wpq = np.ascontiguousarray(np.concatenate(
        [f32("L1_attn_W1_q").T, f32("L1_attn_W2_q").T], axis=1))
    wpk = np.ascontiguousarray(np.concatenate(
        [f32("L1_attn_W1_k").T, f32("L1_attn_W2_k").T], axis=1))
    wv1T = f32("L1_attn_W_v").T
    wv1_h = np.ascontiguousarray(np.stack(
        [wv1T[:, h * 64:(h + 1) * 64] for h in range(NH)]))
    wout_t = np.ascontiguousarray(f32("L1_attn_W_out").T)
    bv1 = np.ascontiguousarray(f32("L1_attn_b_v").reshape(1, C))
    bout = np.ascontiguousarray(f32("L1_attn_b_out").reshape(CCH, 128).T)

    ncores = list(range(NCORES))
    xst = [np.ascontiguousarray(np.stack([xs[c].T, xs[c + 8].T]))
           for c in range(NCORES)]

    nc = build_fused()
    in_maps = []
    for c in range(NCORES):
        gate = np.zeros((128, 2), np.float32)
        if c < 2:
            gate[:, c] = 1.0
        in_maps.append({
            "x_fm": xst[c],
            "wr": wr_t, "wk": wk_t, "wv_f0": wv_t, "mu": mu,
            "wqk_h": wqk_h, "wv0": wv0, "bq": bq, "bk": bk, "bv0": bv0,
            "w1c": np.ascontiguousarray(w1[c]),
            "w2c": np.ascontiguousarray(w2[c]),
            "idx": idx_h, "ew": ew_h,
            "eid": np.full((128, 1), float(c), np.float32),
            "gate": gate, "bias": l1bias,
            "wpq": wpq, "wpk": wpk, "wvh": wv1_h, "wout": wout_t,
            "bv1": bv1, "bout": bout,
        })
    res = run_bass_kernel_spmd(nc, in_maps, core_ids=ncores)
    out = np.empty((16, T, C), dtype=np.float32)
    for c in range(NCORES):
        y = res.results[c]["y_fm"]
        out[c] = y[0].T
        out[c + 8] = y[1].T
    return out


# version 05 (final)
# speedup vs baseline: 1.3078x; 1.0205x over previous
import math
import numpy as np
import concourse.bass as bass
import concourse.mybir as mybir
from concourse.bass_utils import run_bass_kernel_spmd

F32 = mybir.dt.float32
BF16 = mybir.dt.bfloat16
I32 = mybir.dt.int32
AF = mybir.ActivationFunctionType
ALU = mybir.AluOpType

NCORES = 8
T = 2048
C = 1024
CCH = 8
NH = 16
HD = 64
FF = 4096
SEG = 512
NSEG = T // SEG

ENGINES = ("sync", "tensor", "vector", "scalar", "gpsimd")


class SemC:

    def __init__(self, h):
        self.h = h
        self.count = 0


class Prog:

    SERIAL = ("vector", "scalar", "gpsimd")

    def __init__(self, nc):
        self.nc = nc
        self.ops = {e: [] for e in ENGINES}
        self._semid = 0
        self.esem = {}
        for e in self.SERIAL:
            self.esem[e] = self.sem(f"e_{e}")

    def sem(self, name=None):
        self._semid += 1
        h = self.nc.alloc_semaphore(name or f"s{self._semid}")
        return SemC(h)

    def emit(self, eng, fn):
        self.ops[eng].append(fn)

    def do(self, eng, fn, sem=None, n=1):
        if eng in self.SERIAL and n == 1 and (sem is None or sem is self.esem[eng]):
            s = self.esem[eng]
            if s.count > 0:
                self.ops[eng].append(lambda q, s=s, v=s.count: q.wait_ge(s.h, v))

            def wrapped(q, fn=fn, s=s):
                fn(q).then_inc(s.h, 1)
            self.ops[eng].append(wrapped)
            s.count += 1
            return s.count
        if sem is None:
            self.ops[eng].append(fn)
            return None
        def wrapped(q, fn=fn, s=sem, n=n):
            fn(q).then_inc(s.h, n)
        self.ops[eng].append(wrapped)
        sem.count += n
        return sem.count

    def wait(self, eng, sem, val=None):
        v = sem.count if val is None else val
        self.ops[eng].append(lambda q, s=sem, v=v: q.wait_ge(s.h, v))

    def barrier(self, name=None):
        s = self.sem(name)
        for e in ENGINES:
            self.do(e, lambda q: q.engine_nop() if hasattr(q, "engine_nop") else q.nop(), s, 1)
        for e in ENGINES:
            self.wait(e, s, len(ENGINES))
        return s

    def build(self):
        with self.nc.Block() as block:
            @block.sync
            def _(q):
                for fn in self.ops["sync"]:
                    fn(q)

            @block.tensor
            def _(q):
                for fn in self.ops["tensor"]:
                    fn(q)

            @block.vector
            def _(q):
                for fn in self.ops["vector"]:
                    fn(q)

            @block.scalar
            def _(q):
                for fn in self.ops["scalar"]:
                    fn(q)

            @block.gpsimd
            def _(q):
                for fn in self.ops["gpsimd"]:
                    fn(q)


class DmaStream:

    def __init__(self, P, name, rot=4):
        self.P = P
        self.sems = [P.sem(f"{name}{i}") for i in range(rot)]
        self.rot = rot
        self.idx = 0

    def start(self, eng, fn):
        s = self.sems[self.idx % self.rot]
        self.idx += 1
        if s.count > 0:
            self.P.wait(eng, s, s.count)
        self.P.do(eng, fn, s, 16)
        return (s, s.count)

    def wait(self, eng, tok):
        self.P.wait(eng, tok[0], tok[1])

    def wait_all(self, eng):
        for s in self.sems:
            if s.count > 0:
                self.P.wait(eng, s, s.count)


def _nop_inc(P, eng, sem):
    P.do(eng, lambda q: q.nop(), sem, 1)


def phase_barrier(P, name=None):
    s = P.sem(name)
    for e in ENGINES:
        _nop_inc(P, e, s)
    for e in ENGINES:
        P.wait(e, s, len(ENGINES))
    return s



class Ctx:
    pass


def alloc_common(nc, P):
    cx = Ctx()
    cx.nc = nc
    cx.P = P
    A = nc.alloc_sbuf_tensor
    cx.xcur = A("xcur", [128, CCH, T], F32)
    cx.ones1_f = A("ones1_f", [1, 128], F32)
    cx.ones128_b = A("ones128_b", [128, 1], BF16)
    cx.ident_f = A("ident_f", [128, 128], F32)
    cx.xbf = A("ln_xbf", [128, CCH, SEG], BF16)
    cx.sqbf = A("ln_sqbf", [128, CCH, SEG], BF16)
    cx.rstd = A("ln_rstd", [1, SEG], F32)
    cx.bneg = A("ln_bneg", [1, SEG], F32)
    cx.tstat = A("ln_tstat", [1, SEG], F32)
    cx.a_sb = A("ln_a_sb", [128, SEG], F32)
    cx.eps = A("ln_eps", [1, 1], F32)
    cx.ssum = A("ln_ssum", [1, SEG], F32)
    cx.b_sb = A("ln_b_sb", [128, SEG], F32)
    cx.lntmp = cx.sqbf
    cx.pool = nc.alloc_psum_tensor("pool_ps", [128, 4, 1024], F32)
    cx.s_ps = cx.pool[0:1, 0, 0:SEG]
    cx.ss_ps = cx.pool[0:1, 0, SEG:2 * SEG]
    cx.ab_ps = cx.pool[:, 1, :].rearrange("p (a s) -> p a s", a=2)

    cx.dma = DmaStream(P, "c_dma", rot=4)
    cx.gdma = DmaStream(P, "c_gdma", rot=4)
    cx.stst = DmaStream(P, "c_st", rot=2)
    cx.s_dv = P.esem["vector"]
    cx.s_act = P.esem["scalar"]
    cx.s_pool = P.esem["gpsimd"]
    cx.s_pe = P.sem("c_pe")
    return cx


def emit_consts(cx):
    P = cx.P
    P.do("vector", lambda q, c=cx: q.memset(c.ones1_f[:, :], 1.0))
    P.do("vector", lambda q, c=cx: q.memset(c.ones128_b[:, :], 1.0))
    P.do("vector", lambda q, c=cx: q.memset(c.eps[:, :], 1e-5))
    P.do("vector", lambda q, c=cx: q.memset(c.ident_f[:, :], 1.0), cx.s_dv, 1)
    P.wait("gpsimd", cx.s_dv)
    P.do("gpsimd",
         lambda q, c=cx: q.affine_select(c.ident_f[:, :], c.ident_f[:, :],
                                         [[-1, 128]], ALU.is_equal, 0.0,
                                         base=0, channel_multiplier=1),
         cx.s_pool, 1)
    cx.dv_consts = cx.s_dv.count



def xstore_view(xstore_ap, b):
    return xstore_ap[b].rearrange("(cc p) t -> p cc t", p=128)


def emit_load_xcur(cx, xstore_ap, b):
    prev = getattr(cx, "last_store", {}).get(b)
    if prev is not None:
        cx.stst.wait("sync", prev)
    return cx.dma.start("sync", lambda q, c=cx, x=xstore_ap, b=b:
                        q.dma_start(c.xcur[:, :, :], xstore_view(x, b)))


def note_store(cx, b, tok):
    if not hasattr(cx, "last_store"):
        cx.last_store = {}
    cx.last_store[b] = tok


def emit_store_xcur(cx, xstore_ap, b, eng="sync"):
    return cx.stst.start(eng, lambda q, c=cx, x=xstore_ap, b=b:
                         q.dma_start(xstore_view(x, b), c.xcur[:, :, :]))



def emit_input_transpose(cx, x_ap, b, tp_ps, stage):
    P = cx.P
    pe_before = cx.s_pe.count
    for cc in range(CCH):
        for tg4 in range(T // 512):
            P.wait("sync", cx.s_pe, cx.s_pe.count)
            tok = cx.dma.start("sync", lambda q, c=cx, x=x_ap, b=b, cc=cc, tg4=tg4, st=stage:
                               q.dma_start(
                                   st[:, :].rearrange("p (i c) -> p i c", i=4),
                                   x[b, tg4 * 512:(tg4 + 1) * 512, cc * 128:(cc + 1) * 128]
                                   .rearrange("(i p) c -> p i c", p=128)))
            cx.dma.wait("tensor", tok)
            if cc == 0 and tg4 == 0:
                P.wait("tensor", cx.s_pool, 1)
            for i in range(4):
                P.wait("tensor", cx.s_dv, cx.s_dv.count)
                P.do("tensor", lambda q, c=cx, ps=tp_ps, st=stage, i=i:
                     q.transpose(ps[:, i * 128:(i + 1) * 128],
                                 st[:, i * 128:(i + 1) * 128], c.ident_f[:, :]),
                     cx.s_pe, 1)
            pe = cx.s_pe.count
            P.wait("vector", cx.s_pe, pe)
            P.do("vector", lambda q, c=cx, ps=tp_ps, cc=cc, tg4=tg4:
                 q.tensor_copy(c.xcur[:, cc, tg4 * 512:(tg4 + 1) * 512], ps[:, :]),
                 cx.s_dv, 1)
    return cx.s_dv.count


def emit_output_transpose(cx, y_ap, b, tp_ps, stage):
    P = cx.P
    for tg in range(T // 128):
        for half in range(2):
            for j in range(4):
                cc = half * 4 + j
                P.wait("tensor", cx.s_dv, cx.s_dv.count)
                P.do("tensor", lambda q, c=cx, ps=tp_ps, cc=cc, tg=tg, j=j:
                     q.transpose(ps[:, j * 128:(j + 1) * 128],
                                 c.xcur[:, cc, tg * 128:(tg + 1) * 128],
                                 c.ident_f[:, :]),
                     cx.s_pe, 1)
            pe = cx.s_pe.count
            P.wait("vector", cx.s_pe, pe)
            P.do("vector", lambda q, st=stage, half=half, ps=tp_ps:
                 q.tensor_copy(st[:, half * 512:(half + 1) * 512], ps[:, :]),
                 cx.s_dv, 1)
        dv = cx.s_dv.count
        P.wait("sync", cx.s_dv, dv)
        tok = cx.stst.start("sync", lambda q, y=y_ap, b=b, tg=tg, st=stage:
                            q.dma_start(y[b, tg * 128:(tg + 1) * 128, :], st[:, :]))
        cx.stst.wait("vector", tok)
    return None



def emit_ln_seg(cx, t0, dst_fn, sem_out=None):
    P = cx.P
    c = cx
    P.do("vector", lambda q, c=c, t0=t0:
         q.tensor_copy(c.xbf[:, :, :], c.xcur[:, :, t0:t0 + SEG]), c.s_dv, 1)
    dv_cast = c.s_dv.count
    P.wait("scalar", c.s_dv, dv_cast)
    P.do("scalar", lambda q, c=c:
         q.activation(c.sqbf[:, :, :], c.xbf[:, :, :], AF.Square), c.s_act, 1)
    act_sq = c.s_act.count
    P.wait("tensor", c.s_dv, dv_cast)
    for cc in range(CCH):
        P.do("tensor", lambda q, c=c, cc=cc:
             q.matmul(c.s_ps[:, :], c.ones128_b[:, :], c.xbf[:, cc, :],
                      start=(cc == 0), stop=(cc == CCH - 1)),
             c.s_pe, 0 if cc < CCH - 1 else 1)
    P.wait("tensor", c.s_act, act_sq)
    for cc in range(CCH):
        P.do("tensor", lambda q, c=c, cc=cc:
             q.matmul(c.ss_ps[:, :], c.ones128_b[:, :], c.sqbf[:, cc, :],
                      start=(cc == 0), stop=(cc == CCH - 1)),
             c.s_pe, 0 if cc < CCH - 1 else 1)
    pe_stats = c.s_pe.count
    P.wait("vector", c.s_pe, pe_stats)
    P.do("vector", lambda q, c=c: q.tensor_copy(c.ssum[:, :], c.s_ps[:, :]))
    P.do("vector", lambda q, c=c:
         q.scalar_tensor_tensor(c.tstat[:, :], c.ssum[:, :], 1.0 / (1024.0 * 1024.0),
                                c.ssum[:, :], ALU.mult, ALU.mult))
    P.do("vector", lambda q, c=c:
         q.scalar_tensor_tensor(c.rstd[:, :], c.ss_ps[:, :], 1.0 / 1024.0,
                                c.tstat[:, :], ALU.mult, ALU.subtract), c.s_dv, 1)
    dv_var = c.s_dv.count
    P.wait("scalar", c.s_dv, dv_var)
    P.do("scalar", lambda q, c=c:
         q.activation(c.tstat[:, :], c.rstd[:, :], AF.Sqrt, bias=c.eps[:, :]), c.s_act, 1)
    act_std = c.s_act.count
    P.wait("vector", c.s_act, act_std)
    P.do("vector", lambda q, c=c: q.reciprocal(c.rstd[:, :], c.tstat[:, :]))
    P.do("vector", lambda q, c=c:
         q.scalar_tensor_tensor(c.bneg[:, :], c.ssum[:, :], -1.0 / 1024.0,
                                c.rstd[:, :], ALU.mult, ALU.mult), c.s_dv, 1)
    dv_stats = c.s_dv.count
    P.wait("tensor", c.s_dv, dv_stats)
    P.do("tensor", lambda q, c=c:
         q.matmul(c.ab_ps[:, 0, :], c.ones1_f[:, :], c.rstd[:, :],
                  start=True, stop=True))
    P.do("tensor", lambda q, c=c:
         q.matmul(c.ab_ps[:, 1, :], c.ones1_f[:, :], c.bneg[:, :],
                  start=True, stop=True), c.s_pe, 1)
    pe_bc = c.s_pe.count
    P.wait("scalar", c.s_pe, pe_bc)
    P.do("scalar", lambda q, c=c: q.copy(c.a_sb[:, :], c.ab_ps[:, 0, :]))
    P.do("scalar", lambda q, c=c: q.copy(c.b_sb[:, :], c.ab_ps[:, 1, :]), c.s_act, 1)
    act_ab = c.s_act.count
    P.wait("vector", c.s_act, act_ab)
    P.do("vector", lambda q, c=c:
         q.tensor_tensor(c.lntmp[:, :, :], c.xbf[:, :, :],
                         c.a_sb[:, :].unsqueeze(1).broadcast_to([128, CCH, SEG]),
                         ALU.mult))
    for cc in range(CCH):
        P.do("vector", lambda q, c=c, cc=cc, dst_fn=dst_fn:
             q.tensor_tensor(dst_fn(cc), c.lntmp[:, cc, :], c.b_sb[:, :], ALU.add),
             c.s_dv if (cc == CCH - 1) else None, 1)
    if sem_out is not None:
        _nop_inc(P, "vector", sem_out)
    return c.s_dv.count



def alloc_ffn0(nc, A=None):
    f = Ctx()
    A = A or nc.alloc_sbuf_tensor
    f.w_r = A("f_w_r", [128, CCH, C], BF16)
    f.w_k = A("f_w_k", [128, CCH, C], BF16)
    f.w_v = A("f_w_v", [128, CCH, C], BF16)
    f.mu = A("f_mu", [128, 2, CCH], F32)
    f.diff = A("f_diff", [128, CCH, SEG], BF16)
    f.xr = A("f_xr", [128, CCH, SEG], BF16)
    f.xk = f.xr
    f.r = A("f_r", [128, CCH, SEG], F32)
    f.kact = f.diff
    f.vtmp = A("f_vtmp", [128, SEG], F32)
    return f


def emit_ffn0_weights(cx, f, wr_ap, wk_ap, wv_ap):
    P = cx.P
    toks = []
    for (w_sb, w_ap) in ((f.w_r, wr_ap), (f.w_k, wk_ap), (f.w_v, wv_ap)):
        toks.append(cx.gdma.start("gpsimd", lambda q, w_sb=w_sb, w_ap=w_ap:
                    q.dma_start(w_sb[:, :, :],
                                w_ap.rearrange("(cc p) d -> p cc d", p=128))))
    for t in toks:
        cx.gdma.wait("vector", t)
    P.do("vector", lambda q, f=f:
         q.tensor_scalar_mul(f.w_v[:, :, :], f.w_v[:, :, :], 0.5), cx.s_dv, 1)
    return cx.s_dv.count, toks


def emit_ffn0_batch(cx, f, w_dv, w_toks, ld_tok):
    P = cx.P
    c = cx
    if ld_tok is not None:
        cx.dma.wait("vector", ld_tok)
    P.do("vector", lambda q, c=c: q.memset(c.lnx[:, :, 0:1], 0.0))
    for t in w_toks:
        cx.gdma.wait("tensor", t)
    for s in range(NSEG):
        t0 = s * SEG
        emit_ln_seg(cx, t0, lambda cc, c=c: c.lnx[:, cc, 1:SEG + 1])
        dv_ln = c.s_dv.count
        P.do("vector", lambda q, c=c, f=f:
             q.tensor_tensor(f.diff[:, :, :], c.lnx[:, :, 1:SEG + 1],
                             c.lnx[:, :, 0:SEG], ALU.subtract))
        for cc in range(CCH):
            P.do("vector", lambda q, c=c, f=f, cc=cc:
                 q.scalar_tensor_tensor(f.xr[:, cc, :], f.diff[:, cc, :],
                                        f.mu[:, 0, cc:cc + 1], c.lnx[:, cc, 0:SEG],
                                        ALU.mult, ALU.add))
        P.do("vector", lambda q, c=c, f=f:
             q.engine_nop(), c.s_dv, 1)
        dv_lerp = c.s_dv.count

        P.wait("tensor", c.s_dv, dv_lerp)
        for dg in range(CCH):
            ps = f.pslots[dg % 4]
            P.wait("tensor", c.s_act, max(0, c.s_act.count - 3))
            for ccc in range(CCH):
                P.do("tensor", lambda q, f=f, ps=ps, dg=dg, ccc=ccc:
                     q.matmul(ps[:, :], f.w_r[:, ccc, dg * 128:(dg + 1) * 128],
                              f.xr[:, ccc, :], start=(ccc == 0), stop=(ccc == CCH - 1)),
                     c.s_pe, 0 if ccc < CCH - 1 else 1)
            pe = c.s_pe.count
            P.wait("scalar", c.s_pe, pe)
            P.do("scalar", lambda q, f=f, ps=ps, dg=dg:
                 q.activation(f.r[:, dg, :], ps[:, :], AF.Sigmoid), c.s_act, 1)
        pe_r_done = c.s_pe.count
        P.wait("vector", c.s_pe, pe_r_done)
        for cc in range(CCH):
            P.do("vector", lambda q, c=c, f=f, cc=cc:
                 q.scalar_tensor_tensor(f.xk[:, cc, :], f.diff[:, cc, :],
                                        f.mu[:, 1, cc:cc + 1], c.lnx[:, cc, 0:SEG],
                                        ALU.mult, ALU.add))
        P.do("vector", lambda q, c=c:
             q.tensor_copy(c.lnx[:, :, 0:1], c.lnx[:, :, SEG:SEG + 1]), c.s_dv, 1)
        dv_xk = c.s_dv.count
        P.wait("tensor", c.s_dv, dv_xk)
        for dg in range(CCH):
            ps = f.pslots[dg % 4]
            P.wait("tensor", c.s_act, max(0, c.s_act.count - 3))
            for ccc in range(CCH):
                P.do("tensor", lambda q, f=f, ps=ps, dg=dg, ccc=ccc:
                     q.matmul(ps[:, :], f.w_k[:, ccc, dg * 128:(dg + 1) * 128],
                              f.xk[:, ccc, :], start=(ccc == 0), stop=(ccc == CCH - 1)),
                     c.s_pe, 0 if ccc < CCH - 1 else 1)
            pe = c.s_pe.count
            P.wait("scalar", c.s_pe, pe)
            P.do("scalar", lambda q, f=f, ps=ps, dg=dg:
                 q.activation(f.kact[:, dg, :], ps[:, :], AF.Relu), c.s_act, 1)
        act_relu = c.s_act.count
        P.wait("scalar", c.s_act, act_relu)
        P.do("scalar", lambda q, f=f:
             q.activation(f.kact[:, :, :], f.kact[:, :, :], AF.Square), c.s_act, 1)
        act_sq = c.s_act.count
        P.wait("tensor", c.s_act, act_sq)
        for dg in range(CCH):
            ps = f.pslots[dg % 4]
            P.wait("tensor", c.s_dv, max(0, c.s_dv.count - 7))
            for ccc in range(CCH):
                P.do("tensor", lambda q, f=f, ps=ps, dg=dg, ccc=ccc:
                     q.matmul(ps[:, :], f.w_v[:, ccc, dg * 128:(dg + 1) * 128],
                              f.kact[:, ccc, :], start=(ccc == 0), stop=(ccc == CCH - 1)),
                     c.s_pe, 0 if ccc < CCH - 1 else 1)
            pe = c.s_pe.count
            P.wait("vector", c.s_pe, pe)
            P.do("vector", lambda q, f=f, ps=ps, dg=dg:
                 q.tensor_tensor(f.vtmp[:, :], ps[:, :], f.r[:, dg, :], ALU.mult))
            P.do("vector", lambda q, c=c, f=f, dg=dg, t0=t0:
                 q.tensor_tensor(c.xcur[:, dg, t0:t0 + SEG],
                                 c.xcur[:, dg, t0:t0 + SEG], f.vtmp[:, :], ALU.add),
                 c.s_dv, 1)
    return c.s_dv.count



NKC = T // 128
QS = 1024
NQS = T // QS
RT8 = 0.35355339059327373


def alloc_attn0(nc, A=None):
    a = Ctx()
    A = A or nc.alloc_sbuf_tensor
    a.w_v0 = A("a_wv", [128, CCH, C], BF16)
    a.wqk = A("a_wqk", [128, 2, CCH, 128], BF16)
    a.bq = A("a_bq", [64, NH], F32)
    a.bk = A("a_bk", [64, NH], F32)
    a.bvrow = A("a_bvrow", [1, C], F32)
    a.bvb = A("a_bvb", [128, C], F32)
    a.qp = A("a_qp", [65, T], BF16)
    a.kp = A("a_kp", [65, T], BF16)
    a.vp = A("a_vp", [128, NKC, 65], BF16)
    a.qsq = A("a_qsq", [64, T], BF16)
    a.ksq = A("a_ksq", [64, T], BF16)
    a.kmax = A("a_kmax", [64, 1], BF16)
    a.negmk2h = A("a_negmk2h", [1, 1], F32)
    a.pt = A("a_pt", [128, 2, QS], BF16)
    a.rl = A("a_rl", [1, QS], F32)
    a.rb = A("a_rb", [64, QS], F32)
    a.otmp = A("a_otmp", [128, QS], F32)
    return a


def emit_attn0_weights(cx, a, wv_ap, bq_ap, bk_ap, bv_ap):
    P = cx.P
    wv_tok = cx.gdma.start("gpsimd", lambda q, a=a, w=wv_ap:
                           q.dma_start(a.w_v0[:, :, :],
                                       w.rearrange("(cc p) d -> p cc d", p=128)))
    a.wv_tok = wv_tok
    t1 = cx.dma.start("sync", lambda q, a=a, b=bq_ap: q.dma_start(a.bq[:, :], b))
    t2 = cx.dma.start("sync", lambda q, a=a, b=bk_ap: q.dma_start(a.bk[:, :], b))
    t3 = cx.dma.start("sync", lambda q, a=a, b=bv_ap: q.dma_start(a.bvrow[:, :], b))
    for t in (t1, t2, t3):
        cx.dma.wait("vector", t)
    P.do("vector", lambda q, a=a: q.tensor_scalar_mul(a.bq[:, :], a.bq[:, :], RT8))
    P.do("vector", lambda q, a=a: q.tensor_scalar_mul(a.bk[:, :], a.bk[:, :], RT8),
         cx.s_dv, 1)
    bb = cx.pool[:, 2, :]
    cx.dma.wait("tensor", t3)
    P.wait("tensor", cx.s_dv, cx.dv_consts)
    for s in range(2):
        P.do("tensor", lambda q, c=cx, a=a, s=s, bb=bb:
             q.matmul(bb[:, s * 512:(s + 1) * 512], c.ones1_f[:, :],
                      a.bvrow[0:1, s * 512:(s + 1) * 512], start=True, stop=True),
             cx.s_pe, 1)
    pe = cx.s_pe.count
    P.wait("scalar", cx.s_pe, pe)
    P.do("scalar", lambda q, a=a, bb=bb: q.copy(a.bvb[:, :], bb[:, :]), cx.s_act, 1)
    return cx.s_act.count


def emit_wqk_load(cx, a, wqk_heads_ap, h):
    return cx.gdma.start("gpsimd", lambda q, a=a, w=wqk_heads_ap, h=h:
                         q.dma_start(a.wqk[:, h % 2, :, :],
                                     w[h].rearrange("(cc p) d -> p cc d", p=128)))


def emit_attn0_batch(cx, a, wqk_heads_ap, wg_first):
    P = cx.P
    c = cx
    pool = c.pool
    st_pair = [pool[:, 0, :], pool[:, 1, :]]
    o_ps = pool[0:65, 2, :]
    rb_ps = pool[0:64, 3, :]
    q_ps = pool[0:64, 2, :]
    k_ps = pool[0:64, 3, :]
    v_ps = pool[:, 2, 0:64]
    qs2_ps = pool[0:1, 0:2, :].rearrange("p a s -> p (a s)")
    mk2_ps = pool[0:1, 3, 640:641]

    P.do("vector", lambda q, a=a: q.memset(a.vp[:, :, 64:65], 1.0))
    P.do("vector", lambda q, a=a: q.memset(a.kp[64:65, :], 1.0), c.s_dv, 1)

    wg_prev = wg_first
    for h in range(NH):
        par = h % 2
        if h + 1 < NH:
            P.wait("gpsimd", c.s_pe, c.s_pe.count)
            wg_next = emit_wqk_load(cx, a, wqk_heads_ap, h + 1)
        c.gdma.wait("tensor", wg_prev)
        for half in range(2):
            P.wait("tensor", c.s_dv, c.s_dv.count)
            for s2 in range(2):
                t0 = half * 1024 + s2 * 512
                for ccc in range(CCH):
                    P.do("tensor", lambda q, a=a, par=par, ccc=ccc, s2=s2, t0=t0, q_ps=q_ps:
                         q.matmul(q_ps[:, s2 * 512:(s2 + 1) * 512],
                                  a.wqk[:, par, ccc, 0:64], c.lnx[:, ccc, t0:t0 + 512],
                                  start=(ccc == 0), stop=(ccc == CCH - 1)),
                         c.s_pe, 0 if ccc < CCH - 1 else 1)
            pe = c.s_pe.count
            P.wait("vector", c.s_pe, pe)
            P.do("vector", lambda q, a=a, h=h, half=half, q_ps=q_ps:
                 q.tensor_scalar(a.qp[0:64, half * 1024:(half + 1) * 1024], q_ps[:, :],
                                 RT8, a.bq[:, h:h + 1], ALU.mult, ALU.add), c.s_dv, 1)
        for half in range(2):
            P.wait("tensor", c.s_act, c.s_act.count)
            P.wait("tensor", c.s_dv, c.s_dv.count)
            for s2 in range(2):
                t0 = half * 1024 + s2 * 512
                for ccc in range(CCH):
                    P.do("tensor", lambda q, a=a, par=par, ccc=ccc, s2=s2, t0=t0, k_ps=k_ps:
                         q.matmul(k_ps[:, s2 * 512:(s2 + 1) * 512],
                                  a.wqk[:, par, ccc, 64:128], c.lnx[:, ccc, t0:t0 + 512],
                                  start=(ccc == 0), stop=(ccc == CCH - 1)),
                         c.s_pe, 0 if ccc < CCH - 1 else 1)
            pe = c.s_pe.count
            P.wait("vector", c.s_pe, pe)
            P.do("vector", lambda q, a=a, h=h, half=half, k_ps=k_ps:
                 q.tensor_scalar(a.kp[0:64, half * 1024:(half + 1) * 1024], k_ps[:, :],
                                 RT8, a.bk[:, h:h + 1], ALU.mult, ALU.add), c.s_dv, 1)
        dv_qk = c.s_dv.count
        vslots = [pool[:, 0, 0:64], pool[:, 0, 512:576],
                  pool[:, 1, 0:64], pool[:, 1, 512:576],
                  pool[:, 2, 0:64], pool[:, 2, 512:576],
                  pool[:, 3, 0:64], pool[:, 3, 512:576]]
        evac_dv = [None] * NKC
        for tg in range(NKC):
            vs = vslots[tg % 8]
            if tg >= 8:
                P.wait("tensor", c.s_dv, evac_dv[tg - 8])
            elif tg == 0:
                P.wait("tensor", c.s_dv, c.s_dv.count)
            for ccc in range(CCH):
                P.do("tensor", lambda q, c=c, a=a, tg=tg, ccc=ccc, h=h, vs=vs:
                     q.matmul(vs, c.lnx[:, ccc, tg * 128:(tg + 1) * 128],
                              a.w_v0[:, ccc, h * 64:(h + 1) * 64],
                              start=(ccc == 0), stop=(ccc == CCH - 1)),
                     c.s_pe, 0 if ccc < CCH - 1 else 1)
            pe = c.s_pe.count
            P.wait("vector", c.s_pe, pe)
            P.do("vector", lambda q, a=a, tg=tg, h=h, vs=vs:
                 q.tensor_tensor(a.vp[:, tg, 0:64], vs,
                                 a.bvb[:, h * 64:(h + 1) * 64], ALU.add), c.s_dv, 1)
            evac_dv[tg] = c.s_dv.count
        P.wait("scalar", c.s_dv, dv_qk)
        P.do("scalar", lambda q, a=a:
             q.activation(a.qsq[:, :], a.qp[0:64, :], AF.Square))
        P.do("scalar", lambda q, a=a:
             q.activation(a.ksq[:, :], a.kp[0:64, :], AF.Square), c.s_act, 1)
        act_sq = c.s_act.count
        P.wait("tensor", c.s_act, act_sq)
        P.wait("tensor", c.s_dv, c.s_dv.count)
        for s in range(4):
            P.do("tensor", lambda q, c=c, a=a, s=s, qs2_ps=qs2_ps:
                 q.matmul(qs2_ps[:, s * 512:(s + 1) * 512], c.ones128_b[0:64, :],
                          a.qsq[:, s * 512:(s + 1) * 512], start=True, stop=True),
                 c.s_pe, 0 if s < 3 else 1)
        P.wait("vector", c.s_act, act_sq)
        P.do("vector", lambda q, a=a:
             q.reduce_max(a.kmax[:, :], a.ksq[:, :], mybir.AxisListType.X), c.s_dv, 1)
        dv_kmax = c.s_dv.count
        P.wait("tensor", c.s_dv, dv_kmax)
        P.do("tensor", lambda q, c=c, a=a, mk2_ps=mk2_ps:
             q.matmul(mk2_ps, c.ones128_b[0:64, :], a.kmax[:, :],
                      start=True, stop=True), c.s_pe, 1)
        pe_bound = c.s_pe.count
        P.wait("scalar", c.s_pe, pe_bound)
        P.do("scalar", lambda q, a=a, mk2_ps=mk2_ps:
             q.mul(a.negmk2h[:, :], mk2_ps, -0.5), c.s_act, 1)
        act_mk2 = c.s_act.count
        P.wait("vector", c.s_act, act_mk2)
        P.do("vector", lambda q, a=a, qs2_ps=qs2_ps:
             q.tensor_scalar(a.qp[64:65, :], qs2_ps, -0.5, a.negmk2h[:, :],
                             ALU.mult, ALU.add), c.s_dv, 1)
        dv_bound = c.s_dv.count
        for qs in range(NQS):
            nkc = (qs + 1) * (QS // 128)
            exp_done = [None] * nkc
            pool_done = [None] * nkc
            for step in range(nkc + 1):
                if step < nkc:
                    kc = step
                    sp = kc % 2
                    stv = st_pair[sp]
                    if kc < 2 and qs == 0:
                        P.wait("tensor", c.s_dv, dv_bound)
                    if kc >= 2:
                        P.wait("tensor", c.s_act, exp_done[kc - 2])
                    for s2 in range(2):
                        P.do("tensor", lambda q, a=a, kc=kc, qs=qs, s2=s2, stv=stv:
                             q.matmul(stv[:, s2 * 512:(s2 + 1) * 512],
                                      a.kp[:, kc * 128:(kc + 1) * 128],
                                      a.qp[:, qs * QS + s2 * 512:qs * QS + (s2 + 1) * 512],
                                      start=True, stop=True),
                             c.s_pe, 0 if s2 == 0 else 1)
                    pe_st = c.s_pe.count
                    P.wait("scalar", c.s_pe, pe_st)
                    P.do("scalar", lambda q, a=a, sp=sp, stv=stv:
                         q.activation(a.pt[:, sp, :], stv[:, :], AF.Exp), c.s_act, 1)
                    exp_done[kc] = c.s_act.count
                    if kc >= qs * (QS // 128):
                        P.wait("gpsimd", c.s_act, exp_done[kc])
                        P.do("gpsimd", lambda q, a=a, sp=sp, qs=qs, kc=kc:
                             q.affine_select(a.pt[:, sp, :], a.pt[:, sp, :],
                                             [[1, QS]], ALU.is_ge, 0.0,
                                             base=qs * QS - kc * 128,
                                             channel_multiplier=-1), c.s_pool, 1)
                        pool_done[kc] = c.s_pool.count
                if step >= 1:
                    kc = step - 1
                    sp = kc % 2
                    if pool_done[kc] is not None:
                        P.wait("tensor", c.s_pool, pool_done[kc])
                    else:
                        P.wait("tensor", c.s_act, exp_done[kc])
                    if kc == 0:
                        P.wait("tensor", c.s_dv, c.s_dv.count)
                    for s2 in range(2):
                        P.do("tensor", lambda q, a=a, kc=kc, sp=sp, s2=s2, nkc=nkc, o_ps=o_ps:
                             q.matmul(o_ps[:, s2 * 512:(s2 + 1) * 512],
                                      a.vp[:, kc, :], a.pt[:, sp, s2 * 512:(s2 + 1) * 512],
                                      start=(kc == 0), stop=(kc == nkc - 1)),
                             c.s_pe, 0 if (s2 == 0 or kc < nkc - 1) else 1)
            pe_pv = c.s_pe.count
            P.wait("vector", c.s_pe, pe_pv)
            P.do("vector", lambda q, a=a, o_ps=o_ps:
                 q.reciprocal(a.rl[:, :], o_ps[64:65, :]), c.s_dv, 1)
            dv_rl = c.s_dv.count
            P.wait("tensor", c.s_dv, dv_rl)
            for s2 in range(2):
                P.do("tensor", lambda q, c=c, a=a, s2=s2, rb_ps=rb_ps:
                     q.matmul(rb_ps[:, s2 * 512:(s2 + 1) * 512], c.ones1_f[:, 0:64],
                              a.rl[:, s2 * 512:(s2 + 1) * 512], start=True, stop=True),
                     c.s_pe, 0 if s2 == 0 else 1)
            pe_rb = c.s_pe.count
            P.wait("scalar", c.s_pe, pe_rb)
            P.do("scalar", lambda q, a=a, rb_ps=rb_ps:
                 q.copy(a.rb[:, :], rb_ps), c.s_act, 1)
            act_rb = c.s_act.count
            P.wait("vector", c.s_act, act_rb)
            base = (h % 2) * 64
            cc_t = h // 2
            P.do("vector", lambda q, a=a, base=base, o_ps=o_ps:
                 q.tensor_tensor(a.otmp[base:base + 64, :], o_ps[0:64, :],
                                 a.rb[:, :], ALU.mult))
            P.do("vector", lambda q, c=c, a=a, base=base, cc_t=cc_t, qs=qs:
                 q.tensor_tensor(c.xcur[base:base + 64, cc_t, qs * QS:(qs + 1) * QS],
                                 c.xcur[base:base + 64, cc_t, qs * QS:(qs + 1) * QS],
                                 a.otmp[base:base + 64, :], ALU.add), c.s_dv, 1)
        if h + 1 < NH:
            wg_prev = wg_next
    return c.s_dv.count


def plan_p2(nc, P, cx, A, wqk_h, wv, bq, bk, bv, x_in, x_out):
    if A is not None:
        cx.lnx = A("lnx", [128, CCH, T + 1], BF16)
    a = alloc_attn0(nc, A)
    emit_attn0_weights(cx, a, wv, bq, bk, bv)
    wg0 = emit_wqk_load(cx, a, wqk_h, 0)

    wg_for_batch = wg0
    for b in range(2):
        ld_tok = emit_load_xcur(cx, x_in, b)
        cx.dma.wait("vector", ld_tok)
        for s in range(NSEG):
            emit_ln_seg(cx, s * SEG, lambda cc, c=cx, s=s:
                        c.lnx[:, cc, s * SEG:(s + 1) * SEG])
        P.wait("tensor", cx.s_dv, cx.s_dv.count)
        cx.gdma.wait("tensor", a.wv_tok)
        emit_attn0_batch(cx, a, wqk_h, wg_for_batch)
        if b == 0:
            P.wait("gpsimd", cx.s_pe, cx.s_pe.count)
            wg_for_batch = emit_wqk_load(cx, a, wqk_h, 0)
        dv_done = cx.s_dv.count
        P.wait("sync", cx.s_dv, dv_done)
        st_tok = emit_store_xcur(cx, x_out, b)
        note_store(cx, b, st_tok)
        if b == 0:
            for e in ("sync", "vector", "tensor", "scalar"):
                cx.stst.wait(e, st_tok)


def build_p2():
    nc = bass.Bass("TRN2", target_bir_lowering=False, debug=False,
                   detect_race_conditions=False)
    x_in = nc.dram_tensor("x_in", [2, C, T], F32, kind="ExternalInput").ap()
    wqk_h = nc.dram_tensor("wqk_h", [NH, C, 128], F32, kind="ExternalInput").ap()
    wv = nc.dram_tensor("wv", [C, C], F32, kind="ExternalInput").ap()
    bq = nc.dram_tensor("bq", [64, NH], F32, kind="ExternalInput").ap()
    bk = nc.dram_tensor("bk", [64, NH], F32, kind="ExternalInput").ap()
    bv = nc.dram_tensor("bv", [1, C], F32, kind="ExternalInput").ap()
    x_out = nc.dram_tensor("x_out", [2, C, T], F32, kind="ExternalOutput").ap()
    P = Prog(nc)
    cx = alloc_common(nc, P)
    cx.lnx = nc.alloc_sbuf_tensor("lnx", [128, CCH, T + 1], BF16)
    emit_consts(cx)
    plan_p2(nc, P, cx, None, wqk_h, wv, bq, bk, bv, x_in, x_out)
    cx.stst.wait_all("sync")
    P.build()
    return nc



MT = 4096
P4_DEBUG = False
TTS = 512
NTT = MT // TTS


def alloc_moe(nc, A=None):
    m = Ctx()
    A = A or nc.alloc_sbuf_tensor
    m.idx = A("m_idx", [128, 64], I32)
    m.idxf = A("m_idxf", [128, 64], F32)
    m.mcmp = A("m_mcmp", [128, 64], F32)
    m.ew = A("m_ew", [128, 64], F32)
    m.maskp = A("m_maskp", [128, 32], F32)
    m.eid = A("m_eid", [128, 1], F32)
    m.gate = A("m_gate", [128, 2], F32)
    m.bias = A("m_bias", [128, CCH], F32)
    m.mrow = A("m_mrow", [1, 2, TTS], F32)
    m.lns = A("m_lns", [128, CCH, SEG], BF16)
    m.wbb = A("m_wbb", [128, FF], BF16)
    m.hbf = m.wbb[:, :].rearrange("p (a t) -> p a t", a=CCH)
    m.w1f = A("m_w1f", [128, 2, CCH, 1024], BF16)
    m.w2f = A("m_w2f", [128, CCH, 1024], BF16)
    m.xm = A("m_xm", [128, CCH, TTS], BF16)
    m.ysb = A("m_ysb", [128, CCH, TTS], F32)
    m.rsb = m.ysb
    return m


def emit_moe_mask(cx, m):
    P = cx.P
    P.do("vector", lambda q, m=m: q.tensor_copy(m.idxf[:, :], m.idx[:, :]))
    P.do("vector", lambda q, m=m:
         q.tensor_scalar(m.mcmp[:, :], m.idxf[:, :], m.eid[:, :], None, ALU.is_equal))
    P.do("vector", lambda q, m=m:
         q.tensor_tensor(m.mcmp[:, :], m.mcmp[:, :], m.ew[:, :], ALU.mult))
    P.do("vector", lambda q, m=m:
         q.tensor_copy(m.maskp[:, :], m.mcmp[:, 32:64]))
    P.do("vector", lambda q, m=m:
         q.tensor_tensor(m.maskp[:, :], m.maskp[:, :], m.mcmp[:, 0:32],
                         ALU.add))
    P.do("vector", lambda q, m=m:
         q.tensor_scalar_mul(m.maskp[:, :], m.maskp[:, :], 0.5), cx.s_dv, 1)
    P.wait("sync", cx.s_dv, cx.s_dv.count)
    t = cx.stst.start("sync", lambda q, m=m, mask_d=m.mask_d:
                      q.dma_start(mask_d.rearrange("o (p f) -> (o p) f", p=128),
                                  m.maskp[:, :]))
    return t


def emit_weight_cache(cx, m, w_ap, wbf_ap, rows):
    toks = []
    cols = w_ap.shape[1]
    chunks = []
    for r0 in range(0, rows, 128):
        for c0 in range(0, cols, 2048):
            chunks.append((r0, c0, min(2048, cols - c0)))
    wbb2 = m.wbb[:, :].rearrange("p (s c) -> p s c", s=2)
    if not hasattr(cx, "_wc_prev"):
        cx._wc_prev = [None, None]
    prev_store = cx._wc_prev
    for k, (r0, c0, w_) in enumerate(chunks):
        sl = k % 2
        if prev_store[sl] is not None:
            cx.stst.wait("gpsimd", prev_store[sl])
        t1 = cx.gdma.start("gpsimd", lambda q, m=m, w=w_ap, r0=r0, c0=c0, w_=w_, sl=sl, wbb2=wbb2:
                           q.dma_start(wbb2[:, sl, 0:w_], w[r0:r0 + 128, c0:c0 + w_]))
        qn = "sync" if k % 2 == 0 else "scalar"
        cx.gdma.wait(qn, t1)
        t2 = cx.stst.start(qn, lambda q, w=wbf_ap, r0=r0, c0=c0, w_=w_, sl=sl, wbb2=wbb2:
                           q.dma_start(w[r0:r0 + 128, c0:c0 + w_], wbb2[:, sl, 0:w_]))
        prev_store[sl] = t2
        toks.append(t2)
    return toks


def plan_p4(nc, P, cx, A, w1c, w2c, idx, ew, eid, gate, bias, dd,
            x_in, x_out, convert=True):
    w1bf = dd["w1bf"]; w2bf = dd["w2bf"]; cc_in = dd["cc_in"]
    cc_out = dd["cc_out"]; arin = dd["arin"]; arout = dd["arout"]
    mask_d = dd["mask_d"]
    m = alloc_moe(nc, A)
    m.mask_d = mask_d
    h_ps = [cx.pool[:, 2, 0:512], cx.pool[:, 2, 512:1024]]
    y_ps = [cx.pool[:, 3, 0:512], cx.pool[:, 3, 512:1024]]
    mb_ps = cx.pool[:, 0, 0:512]

    emit_consts(cx)
    for (sb, ap) in ((m.idx, idx), (m.ew, ew), (m.eid, eid), (m.gate, gate),
                     (m.bias, bias)):
        t = cx.dma.start("sync", lambda q, sb=sb, ap=ap: q.dma_start(sb[:, :], ap))
        cx.dma.wait("vector", t)
    mask_tok = emit_moe_mask(cx, m)
    P.do("vector", lambda q, m=m:
         q.tensor_scalar_mul(m.bias[:, :], m.bias[:, :], 0.5), cx.s_dv, 1)
    dv_pre = cx.s_dv.count

    ldA = emit_load_xcur(cx, x_in, 0)
    cx.dma.wait("vector", ldA)
    ctoks = []
    for s in range(NSEG):
        emit_ln_seg(cx, s * SEG, lambda cc, m=m: m.lns[:, cc, :])
        dv_ln = cx.s_dv.count
        P.wait("sync", cx.s_dv, dv_ln)
        ct = cx.stst.start("sync", lambda q, m=m, cc_in=cc_in, s=s:
                           q.dma_start(cc_in[:, s * SEG:(s + 1) * SEG]
                                       .rearrange("(cc p) t -> p cc t", p=128),
                                       m.lns[:, :, :]))
        ctoks.append(ct)
        cx.stst.wait("vector", ct)
    if convert:
        w1toks = emit_weight_cache(cx, m, w1c, w1bf, C)
        w2toks = emit_weight_cache(cx, m, w2c, w2bf, FF)

    cc_sem = P.sem()
    for ct in ctoks:
        cx.stst.wait("gpsimd", ct)
    P.do("gpsimd", lambda q, cc_in=cc_in, cc_out=cc_out:
         q.collective_compute("AllGather", ALU.bypass,
                              replica_groups=[list(range(NCORES))],
                              ins=[cc_in.opt()], outs=[cc_out.opt()]),
         cc_sem, 1)
    P.wait("sync", cc_sem, 1)

    if convert:
        for t in w1toks + w2toks:
            cx.stst.wait("gpsimd", t)
        cx.stst.wait("scalar", w1toks[-1])
        cx.stst.wait("scalar", w2toks[-1])
    ytoks = []
    for tt in range(NTT):
        rank = tt // (NTT // 2)
        col0 = (tt % (NTT // 2)) * TTS
        P.wait("sync", cx.s_pe, cx.s_pe.count)
        xt = cx.dma.start("sync", lambda q, m=m, cc_out=cc_out, rank=rank, col0=col0:
                          q.dma_start(m.xm[:, :, :],
                                      cc_out[rank * C:(rank + 1) * C,
                                             col0:col0 + TTS]
                                      .rearrange("(cc p) t -> p cc t", p=128)))
        cx.dma.wait("tensor", xt)
        mpar = tt % 2
        if tt == 0:
            cx.stst.wait("sync", mask_tok)
        P.wait("sync", cx.s_pe, cx.s_pe.count)
        mrt = cx.dma.start("sync", lambda q, m=m, tt=tt, mpar=mpar:
                           q.dma_start(m.mrow[:, mpar, :],
                                       m.mask_d[:, tt * TTS:(tt + 1) * TTS]))
        if tt == 0:
            P.wait("tensor", cx.s_dv, dv_pre)
        cx.dma.wait("tensor", mrt)
        P.wait("tensor", cx.s_dv, cx.s_dv.count)
        P.do("tensor", lambda q, c=cx, m=m, tt=tt, mpar=mpar:
             q.matmul(mb_ps, c.ones1_f[:, :], m.mrow[:, mpar, :],
                      start=True, stop=True), cx.s_pe, 1)
        for fc in range(4):
            par = fc % 2
            P.wait("gpsimd", cx.s_pe, cx.s_pe.count)
            w2t = cx.gdma.start("gpsimd", lambda q, m=m, w2bf=w2bf, fc=fc:
                                q.dma_start(m.w2f[:, :, :],
                                            w2bf[fc * 1024:(fc + 1) * 1024, :]
                                            .rearrange("(fm p) d -> p fm d", p=128)))
            w1t = cx.gdma.start("gpsimd", lambda q, m=m, w1bf=w1bf, fc=fc, par=par:
                                q.dma_start(m.w1f[:, par, :, :],
                                            w1bf[:, fc * 1024:(fc + 1) * 1024]
                                            .rearrange("(cc p) f -> p cc f", p=128)))
            cx.gdma.wait("tensor", w1t)
            if P4_DEBUG and tt == 0 and fc == 0:
                dbg_xm = nc.dram_tensor("dbg_xm", [128, CCH, TTS], F32,
                                        kind="ExternalOutput").ap()
                cx.dma.wait("gpsimd", xt)
                dxm = cx.stst.start("gpsimd", lambda q, m=m, o=dbg_xm:
                                    q.dma_start(o, m.xm[:, :, :]))
                cx.stst.wait("gpsimd", dxm)
            for mg in range(CCH):
                hp = h_ps[mg % 2]
                P.wait("tensor", cx.s_act, cx.s_act.count)
                for ccc in range(CCH):
                    P.do("tensor", lambda q, m=m, par=par, mg=mg, ccc=ccc, hp=hp:
                         q.matmul(hp, m.w1f[:, par, ccc, mg * 128:(mg + 1) * 128],
                                  m.xm[:, ccc, :],
                                  start=(ccc == 0), stop=(ccc == CCH - 1)),
                         cx.s_pe, 0 if ccc < CCH - 1 else 1)
                pe = cx.s_pe.count
                P.wait("scalar", cx.s_pe, pe)
                P.do("scalar", lambda q, m=m, mg=mg, hp=hp:
                     q.activation(m.hbf[:, mg, :], hp, AF.Gelu), cx.s_act, 1)
            act_h = cx.s_act.count
            if P4_DEBUG and tt == 0 and fc == 0:
                dbg_h = nc.dram_tensor("dbg_h", [128, CCH, TTS], F32,
                                       kind="ExternalOutput").ap()
                P.wait("gpsimd", cx.s_act, act_h)
                dh = cx.stst.start("gpsimd", lambda q, m=m, o=dbg_h:
                                   q.dma_start(o, m.hbf[:, :, :]))
                cx.stst.wait("gpsimd", dh)
                cx.stst.wait("scalar", dh)
                cx.stst.wait("tensor", dh)
            cx.gdma.wait("tensor", w2t)
            P.wait("tensor", cx.s_act, act_h)
            for dg in range(CCH):
                yp = y_ps[dg % 2]
                P.wait("tensor", cx.s_dv, cx.s_dv.count)
                for fm in range(CCH):
                    P.do("tensor", lambda q, m=m, fm=fm, dg=dg, yp=yp:
                         q.matmul(yp,
                                  m.w2f[:, fm, dg * 128:(dg + 1) * 128],
                                  m.hbf[:, fm, :],
                                  start=(fm == 0), stop=(fm == CCH - 1)),
                         cx.s_pe, 0 if fm < CCH - 1 else 1)
                pe = cx.s_pe.count
                P.wait("vector", cx.s_pe, pe)
                if fc == 0:
                    P.do("vector", lambda q, m=m, dg=dg, yp=yp:
                         q.tensor_copy(m.ysb[:, dg, :], yp))
                else:
                    P.do("vector", lambda q, m=m, dg=dg, yp=yp:
                         q.tensor_tensor(m.ysb[:, dg, :], m.ysb[:, dg, :],
                                         yp, ALU.add))
        if P4_DEBUG and tt == 0:
            dbg_y0 = nc.dram_tensor("dbg_y0", [128, CCH, TTS], F32,
                                    kind="ExternalOutput").ap()
            P.wait("sync", cx.s_dv, cx.s_dv.count)
            dy0 = cx.stst.start("sync", lambda q, m=m, o=dbg_y0:
                                q.dma_start(o, m.ysb[:, :, :]))
            cx.stst.wait("sync", dy0)
            cx.stst.wait("vector", dy0)
        pe_mb = cx.s_pe.count
        P.wait("vector", cx.s_pe, pe_mb)
        P.do("vector", lambda q, m=m, mb_ps=mb_ps:
             q.tensor_tensor(m.ysb[:, :, :], m.ysb[:, :, :],
                             mb_ps[:, :].unsqueeze(1).broadcast_to([128, CCH, TTS]),
                             ALU.mult), cx.s_dv, 1)
        dv_y = cx.s_dv.count
        P.wait("sync", cx.s_dv, dv_y)
        yt = cx.stst.start("sync", lambda q, m=m, arin=arin, rank=rank, col0=col0:
                           q.dma_start(arin[rank][:, col0:col0 + TTS]
                                       .rearrange("(cc p) t -> p cc t", p=128),
                                       m.ysb[:, :, :]))
        ytoks.append(yt)
        cx.stst.wait("vector", yt)

    if P4_DEBUG:
        dbg_maskp = nc.dram_tensor("dbg_maskp", [128, 32], F32, kind="ExternalOutput").ap()
        dbg_mcmp = nc.dram_tensor("dbg_mcmp", [128, 64], F32, kind="ExternalOutput").ap()
        dbg_idxf = nc.dram_tensor("dbg_idxf", [128, 64], F32, kind="ExternalOutput").ap()
        dbg_ew = nc.dram_tensor("dbg_ew", [128, 64], F32, kind="ExternalOutput").ap()
        dbg_mask = nc.dram_tensor("dbg_mask", [1, MT], F32, kind="ExternalOutput").ap()
        dbg_gather = nc.dram_tensor("dbg_gather", [C, T], F32, kind="ExternalOutput").ap()
        dbg_arin = nc.dram_tensor("dbg_arin", [C, MT], F32, kind="ExternalOutput").ap()
        cx.stst.wait("sync", mask_tok)
        dt0 = cx.stst.start("sync", lambda q: q.dma_start(dbg_maskp, m.maskp[:, :]))
        dta = cx.stst.start("sync", lambda q: q.dma_start(dbg_mcmp, m.mcmp[:, :]))
        dtb = cx.stst.start("sync", lambda q: q.dma_start(dbg_idxf, m.idxf[:, :]))
        dtc = cx.stst.start("sync", lambda q: q.dma_start(dbg_ew, m.ew[:, :]))
        for _t in (dta, dtb, dtc):
            cx.stst.wait("sync", _t)
        dt1 = cx.stst.start("sync", lambda q: q.dma_start(dbg_mask, m.mask_d))
        cx.stst.wait("sync", dt0)
        P.wait("gpsimd", cc_sem, 1)
        dt2 = cx.stst.start("gpsimd", lambda q: q.dma_start(dbg_gather, cc_out[0:C, :]))
        for t in ytoks:
            cx.stst.wait("sync", t)
        dt3 = cx.stst.start("sync", lambda q: q.dma_start(dbg_arin, arin))
        cx.stst.wait("sync", dt1)
        cx.stst.wait("sync", dt3)
        cx.stst.wait("gpsimd", dt2)
    ar_sem = P.sem()
    for t in ytoks[:NTT // 2]:
        cx.stst.wait("gpsimd", t)
    P.do("gpsimd", lambda q, arin=arin, arout=arout:
         q.collective_compute("AllReduce", ALU.add,
                              replica_groups=[list(range(NCORES))],
                              ins=[arin[0].opt()], outs=[arout[0].opt()]),
         ar_sem, 1)
    for t in ytoks[NTT // 2:]:
        cx.stst.wait("gpsimd", t)
    P.do("gpsimd", lambda q, arin=arin, arout=arout:
         q.collective_compute("AllReduce", ALU.add,
                              replica_groups=[list(range(NCORES))],
                              ins=[arin[1].opt()], outs=[arout[1].opt()]),
         ar_sem, 1)

    ldB = emit_load_xcur(cx, x_in, 1)
    cx.dma.wait("vector", ldB)
    for cc in range(CCH):
        P.do("vector", lambda q, c=cx, m=m, cc=cc:
             q.tensor_scalar(c.xcur[:, cc, :], c.xcur[:, cc, :],
                             m.bias[:, cc:cc + 1], None, ALU.add),
             cx.s_dv if cc == CCH - 1 else None, 1)
    dvB = cx.s_dv.count
    P.wait("sync", cx.s_dv, dvB)
    stB = emit_store_xcur(cx, x_out, 1)
    note_store(cx, 1, stB)

    cx.stst.wait("sync", stB)
    ldA2 = emit_load_xcur(cx, x_in, 0)
    cx.dma.wait("vector", ldA2)
    P.wait("sync", ar_sem, 2)
    for half in range(2):
        for s in range(NSEG):
            P.wait("sync", cx.s_dv, cx.s_dv.count)
            rt = cx.dma.start("sync", lambda q, m=m, arout=arout, half=half, s=s:
                              q.dma_start(m.rsb[:, :, :],
                                          arout[half][:, s * SEG:(s + 1) * SEG]
                                          .rearrange("(cc p) t -> p cc t", p=128)))
            cx.dma.wait("vector", rt)
            for cc in range(CCH):
                P.do("vector", lambda q, c=cx, m=m, half=half, s=s, cc=cc:
                     q.scalar_tensor_tensor(
                         c.xcur[:, cc, s * SEG:(s + 1) * SEG],
                         m.rsb[:, cc, :], m.gate[:, half:half + 1],
                         c.xcur[:, cc, s * SEG:(s + 1) * SEG],
                         ALU.mult, ALU.add),
                     cx.s_dv if cc == CCH - 1 else None, 1)
    for cc in range(CCH):
        P.do("vector", lambda q, c=cx, m=m, cc=cc:
             q.tensor_scalar(c.xcur[:, cc, :], c.xcur[:, cc, :],
                             m.bias[:, cc:cc + 1], None, ALU.add),
             cx.s_dv if cc == CCH - 1 else None, 1)
    dvA = cx.s_dv.count
    P.wait("sync", cx.s_dv, dvA)
    stA = emit_store_xcur(cx, x_out, 0)
    note_store(cx, 0, stA)
    for e in ("sync", "vector"):
        cx.stst.wait(e, stA)


def moe_dram(nc):
    return {
        "w1bf": nc.dram_tensor("w1bf", [C, FF], BF16, kind="Internal").ap(),
        "w2bf": nc.dram_tensor("w2bf", [FF, C], BF16, kind="Internal").ap(),
        "cc_in": nc.dram_tensor("cc_in", [C, T], BF16, kind="Internal").ap(),
        "cc_out": nc.dram_tensor("cc_out", [NCORES * C, T], BF16,
                                 kind="Internal", addr_space="Shared").ap(),
        "arin": [nc.dram_tensor("arin0", [C, T], F32, kind="Internal").ap(),
                 nc.dram_tensor("arin1", [C, T], F32, kind="Internal").ap()],
        "arout": [nc.dram_tensor("arout0", [C, T], F32, kind="Internal",
                                 addr_space="Shared").ap(),
                  nc.dram_tensor("arout1", [C, T], F32, kind="Internal",
                                 addr_space="Shared").ap()],
        "mask_d": nc.dram_tensor("mask_d", [1, MT], F32, kind="Internal").ap(),
    }


def build_p4():
    nc = bass.Bass("TRN2", target_bir_lowering=False, debug=False,
                   detect_race_conditions=False)
    x_in = nc.dram_tensor("x_in", [2, C, T], F32, kind="ExternalInput").ap()
    w1c = nc.dram_tensor("w1c", [C, FF], F32, kind="ExternalInput").ap()
    w2c = nc.dram_tensor("w2c", [FF, C], F32, kind="ExternalInput").ap()
    idx = nc.dram_tensor("idx", [128, 64], I32, kind="ExternalInput").ap()
    ew = nc.dram_tensor("ew", [128, 64], F32, kind="ExternalInput").ap()
    eid = nc.dram_tensor("eid", [128, 1], F32, kind="ExternalInput").ap()
    gate = nc.dram_tensor("gate", [128, 2], F32, kind="ExternalInput").ap()
    bias = nc.dram_tensor("bias", [128, CCH], F32, kind="ExternalInput").ap()
    x_out = nc.dram_tensor("x_out", [2, C, T], F32, kind="ExternalOutput").ap()
    dd = moe_dram(nc)
    P = Prog(nc)
    cx = alloc_common(nc, P)
    emit_consts(cx)
    plan_p4(nc, P, cx, None, w1c, w2c, idx, ew, eid, gate, bias, dd,
            x_in, x_out, convert=True)
    cx.stst.wait_all("sync")
    P.build()
    return nc



PAIRS = [(0, 1), (0, 2), (0, 3), (1, 2), (1, 3), (2, 3)]
JSIGN = [1.0, -1.0, 1.0, 1.0, -1.0, 1.0]
LN6H = -0.5 * math.log(6.0)


def alloc_attn1(nc, A=None):
    a = Ctx()
    A = A or nc.alloc_sbuf_tensor
    a.wpq = A("b_wpq", [128, CCH, 128], BF16)
    a.wpk = A("b_wpk", [128, CCH, 128], BF16)
    a.w_vh = A("b_wvh", [128, 2, CCH, 64], BF16)
    a.w_out = A("b_wout", [128, CCH, C], BF16)
    a.bvb = A("b_bvb", [128, C], F32)
    a.bout = A("b_bout", [128, CCH], F32)
    a.AO = A("b_AO", [128, CCH, T], BF16)
    a.lqe = A("b_lqe", [128, 6, 256], BF16)
    a.jlke = A("b_jlke", [128, 6, 256], BF16)

    a.vp = A("b_vp", [128, NKC, 65], BF16)
    a.pt = A("b_pt", [128, 2, QS], BF16)
    a.rl = A("b_rl", [1, QS], F32)
    a.rb = A("b_rb", [64, QS], F32)
    a.ytmp = A("b_ytmp", [128, SEG], F32)
    a.ln6h = A("b_ln6h", [128, 1], F32)
    a.bvrow = a.rb[0:1, :]
    return a


def bind_attn1_aliases(cx, a):
    xb = cx.xbf[0:6, :, :].rearrange("p a t -> p (a t)")
    a.lqh = xb[:, 0:T]
    a.jlkh = xb[:, T:2 * T]
    a.pairq = a.AO[:, 0, :]
    a.pairk = a.AO[:, 1, :]
    a.q1e = a.AO[:, 2, 0:1024].rearrange("p (c t) -> p c t", c=4)
    a.k1e = a.AO[:, 2, 1024:2048].rearrange("p (c t) -> p c t", c=4)
    a.q2e = a.AO[:, 3, 0:1024].rearrange("p (c t) -> p c t", c=4)
    a.k2e = a.AO[:, 3, 1024:2048].rearrange("p (c t) -> p c t", c=4)
    arena = a.AO[:, 4:8, :].rearrange("p a t -> p (a t)").bitcast(F32)
    a.lx = arena[:, 0:1536].rearrange("p (c t) -> p c t", c=6)
    a.sq = arena[:, 1536:3072].rearrange("p (c t) -> p c t", c=6)
    a.acc = arena[:, 3072:3328]
    a.rn = arena[:, 3328:3584]
    a.rnn = arena[:, 3584:3840]
    a.tmp1 = arena[:, 3840:4096]
    return a


def emit_wv1_load(cx, a, wvh_ap, h):
    return cx.gdma.start("gpsimd", lambda q, a=a, w=wvh_ap, h=h:
                         q.dma_start(a.w_vh[:, h % 2, :, :],
                                     w[h].rearrange("(cc p) d -> p cc d", p=128)))


def emit_attn1_weights(cx, a, wpq_ap, wpk_ap, wout_ap, bv_ap, bout_ap):
    P = cx.P
    toks = []
    for (sb, ap) in ((a.wpq, wpq_ap), (a.wpk, wpk_ap)):
        toks.append(cx.gdma.start("gpsimd", lambda q, sb=sb, ap=ap:
                    q.dma_start(sb[:, :, :],
                                ap.rearrange("(cc p) d -> p cc d", p=128))))
    a.wout_tok = cx.gdma.start("gpsimd", lambda q, a=a, w=wout_ap:
                               q.dma_start(a.w_out[:, :, :],
                                           w.rearrange("(cc p) d -> p cc d", p=128)))
    t3 = cx.dma.start("sync", lambda q, a=a, b=bv_ap: q.dma_start(a.bvrow, b))
    t4 = cx.dma.start("sync", lambda q, a=a, b=bout_ap: q.dma_start(a.bout[:, :], b))
    a.bout_tok = t4
    bb = cx.pool[:, 2, :]
    cx.dma.wait("tensor", t3)
    P.wait("tensor", cx.s_dv, cx.dv_consts)
    for s in range(2):
        P.do("tensor", lambda q, c=cx, a=a, s=s, bb=bb:
             q.matmul(bb[:, s * 512:(s + 1) * 512], c.ones1_f[:, :],
                      a.bvrow[0:1, s * 512:(s + 1) * 512], start=True, stop=True),
             cx.s_pe, 1)
    pe = cx.s_pe.count
    P.wait("scalar", cx.s_pe, pe)
    P.do("scalar", lambda q, a=a, bb=bb: q.copy(a.bvb[:, :], bb[:, :]))
    P.do("vector", lambda q, a=a: q.memset(a.ln6h[:, :], LN6H))
    a.wpq_toks = toks
    return toks


def emit_attn1_batch(cx, a):
    P = cx.P
    c = cx
    pool = c.pool
    st_pair = [pool[:, 0, :], pool[:, 1, :]]
    o_ps = pool[0:65, 2, :]
    rb_ps = pool[0:64, 3, :]
    pk_ps = pool[:, 2, 0:512]
    v_ps = pool[:, 3, 0:64]
    y_ps = [pool[:, 2, 0:512], pool[:, 2, 512:1024]]

    for t in a.wpq_toks:
        cx.gdma.wait("tensor", t)
    for (wp, dst) in ((a.wpq, a.pairq), (a.wpk, a.pairk)):
        for s in range(NSEG):
            P.wait("tensor", c.s_dv, c.s_dv.count)
            for ccc in range(CCH):
                P.do("tensor", lambda q, c=c, wp=wp, ccc=ccc, s=s:
                     q.matmul(pk_ps, wp[:, ccc, :], c.lnx[:, ccc, s * SEG:(s + 1) * SEG],
                              start=(ccc == 0), stop=(ccc == CCH - 1)),
                     c.s_pe, 0 if ccc < CCH - 1 else 1)
            pe = c.s_pe.count
            P.wait("vector", c.s_pe, pe)
            P.do("vector", lambda q, dst=dst, s=s, pk_ps=pk_ps:
                 q.tensor_copy(dst[:, s * SEG:(s + 1) * SEG], pk_ps))
    dv_proj = c.s_dv.count

    rp = []
    P.wait("sync", c.s_dv, dv_proj)
    for (src, dst) in ((a.pairq, a.q1e), (a.pairk, a.k1e)):
        for d4 in range(4):
            rp.append(cx.dma.start("sync", lambda q, src=src, dst=dst, d4=d4:
                      q.dma_start(
                          dst[:, d4, :], src[d4:64:4, :])))
    for (src, dst) in ((a.pairq, a.q2e), (a.pairk, a.k2e)):
        for d4 in range(4):
            rp.append(cx.dma.start("sync", lambda q, src=src, dst=dst, d4=d4:
                      q.dma_start(
                          dst[:, d4, :], src[64 + d4:128:4, :])))
    for t in rp:
        cx.dma.wait("vector", t)

    for side in range(2):
        e1 = a.q1e if side == 0 else a.k1e
        e2 = a.q2e if side == 0 else a.k2e
        for ci, (i, j) in enumerate(PAIRS):
            P.do("vector", lambda q, a=a, e1=e1, e2=e2, i=i, j=j:
                 q.tensor_tensor(a.tmp1[:, :], e1[:, i, :], e2[:, j, :], ALU.mult))
            P.do("vector", lambda q, a=a, e1=e1, e2=e2, i=i, j=j, ci=ci:
                 q.tensor_tensor(a.sq[:, ci, :], e1[:, j, :], e2[:, i, :], ALU.mult))
            P.do("vector", lambda q, a=a, ci=ci:
                 q.tensor_tensor(a.lx[:, ci, :], a.tmp1[:, :], a.sq[:, ci, :],
                                 ALU.subtract))
        dv_l = c.s_dv.count
        P.wait("scalar", c.s_dv, dv_l)
        P.do("scalar", lambda q, a=a:
             q.activation(a.sq[:, :, :], a.lx[:, :, :], AF.Square), c.s_act, 1)
        act_sq = c.s_act.count
        P.wait("vector", c.s_act, act_sq)
        P.do("vector", lambda q, a=a: q.tensor_copy(a.acc[:, :], a.sq[:, 0, :]))
        for ci in range(1, 6):
            P.do("vector", lambda q, a=a, ci=ci:
                 q.tensor_tensor(a.acc[:, :], a.acc[:, :], a.sq[:, ci, :], ALU.add))
        P.do("vector", lambda q, a=a:
             q.tensor_scalar_max(a.acc[:, :], a.acc[:, :], 1e-24), c.s_dv, 1)
        dv_acc = c.s_dv.count
        P.wait("scalar", c.s_dv, dv_acc)
        P.do("scalar", lambda q, a=a: q.activation(a.tmp1[:, :], a.acc[:, :], AF.Ln))
        if side == 0:
            P.do("scalar", lambda q, a=a:
                 q.activation(a.rn[:, :], a.tmp1[:, :], AF.Exp, scale=-0.5,
                              bias=a.ln6h[:, :]), c.s_act, 1)
        else:
            P.do("scalar", lambda q, a=a:
                 q.activation(a.rn[:, :], a.tmp1[:, :], AF.Exp, scale=-0.5),
                 c.s_act, 1)
        act_rn = c.s_act.count
        P.wait("vector", c.s_act, act_rn)
        if side == 0:
            for ci in range(6):
                P.do("vector", lambda q, a=a, ci=ci:
                     q.tensor_tensor(a.lqe[:, ci, :], a.lx[:, ci, :], a.rn[:, :],
                                     ALU.mult))
        else:
            P.do("vector", lambda q, a=a:
                 q.tensor_scalar_mul(a.rnn[:, :], a.rn[:, :], -1.0))
            for aa in range(6):
                src_c = 5 - aa
                rn_t = "rn" if JSIGN[aa] > 0 else "rnn"
                P.do("vector", lambda q, a=a, aa=aa, src_c=src_c, rn_t=rn_t:
                     q.tensor_tensor(a.jlke[:, aa, :], a.lx[:, src_c, :],
                                     getattr(a, rn_t)[:, :], ALU.mult))
    dv_ext = c.s_dv.count

    P.do("vector", lambda q, a=a: q.memset(a.vp[:, :, 64:65], 1.0))

    wvt_prev = a.wv0_tok
    for h in range(NH):
        par = h % 2
        if h + 1 < NH:
            P.wait("gpsimd", c.s_pe, c.s_pe.count)
            wvt_next = emit_wv1_load(cx, a, a.wvh_ap, h + 1)
        P.wait("sync", c.s_pe, c.s_pe.count)
        if h == 0:
            P.wait("sync", c.s_dv, dv_ext)
        rp_toks = []
        for ci in range(6):
            qn = "sync" if ci % 2 == 0 else "scalar"
            rp_toks.append(cx.dma.start(qn, lambda q, a=a, h=h, ci=ci:
                           q.dma_start(a.lqh[ci:ci + 1, :],
                                       a.lqe[h * 8:(h + 1) * 8, ci, :])))
            rp_toks.append(cx.dma.start(qn, lambda q, a=a, h=h, ci=ci:
                           q.dma_start(a.jlkh[ci:ci + 1, :],
                                       a.jlke[h * 8:(h + 1) * 8, ci, :])))
        for t in rp_toks:
            cx.dma.wait("tensor", t)
        cx.gdma.wait("tensor", wvt_prev)
        for tg in range(NKC):
            P.wait("tensor", c.s_dv, c.s_dv.count)
            for ccc in range(CCH):
                P.do("tensor", lambda q, c=c, a=a, tg=tg, ccc=ccc, par=par, v_ps=v_ps:
                     q.matmul(v_ps, c.lnx[:, ccc, tg * 128:(tg + 1) * 128],
                              a.w_vh[:, par, ccc, :],
                              start=(ccc == 0), stop=(ccc == CCH - 1)),
                     c.s_pe, 0 if ccc < CCH - 1 else 1)
            pe = c.s_pe.count
            P.wait("vector", c.s_pe, pe)
            P.do("vector", lambda q, a=a, tg=tg, h=h, v_ps=v_ps:
                 q.tensor_tensor(a.vp[:, tg, 0:64], v_ps,
                                 a.bvb[:, h * 64:(h + 1) * 64], ALU.add))
        for qs in range(NQS):
            nkc = (qs + 1) * (QS // 128)
            exp_done = [None] * nkc
            pool_done = [None] * nkc
            for step in range(nkc + 1):
                if step < nkc:
                    kc = step
                    sp = kc % 2
                    stv = st_pair[sp]
                    if kc >= 2:
                        P.wait("tensor", c.s_act, exp_done[kc - 2])
                    for s2 in range(2):
                        P.do("tensor", lambda q, a=a, kc=kc, qs=qs, s2=s2, stv=stv:
                             q.matmul(stv[:, s2 * 512:(s2 + 1) * 512],
                                      a.jlkh[:, kc * 128:(kc + 1) * 128],
                                      a.lqh[:, qs * QS + s2 * 512:qs * QS + (s2 + 1) * 512],
                                      start=True, stop=True),
                             c.s_pe, 0 if s2 == 0 else 1)
                    pe_st = c.s_pe.count
                    P.wait("scalar", c.s_pe, pe_st)
                    P.do("scalar", lambda q, a=a, sp=sp, stv=stv:
                         q.activation(a.pt[:, sp, :], stv[:, :], AF.Exp), c.s_act, 1)
                    exp_done[kc] = c.s_act.count
                    if kc >= qs * (QS // 128):
                        P.wait("gpsimd", c.s_act, exp_done[kc])
                        P.do("gpsimd", lambda q, a=a, sp=sp, qs=qs, kc=kc:
                             q.affine_select(a.pt[:, sp, :], a.pt[:, sp, :],
                                             [[1, QS]], ALU.is_ge, 0.0,
                                             base=qs * QS - kc * 128,
                                             channel_multiplier=-1), c.s_pool, 1)
                        pool_done[kc] = c.s_pool.count
                if step >= 1:
                    kc = step - 1
                    sp = kc % 2
                    if pool_done[kc] is not None:
                        P.wait("tensor", c.s_pool, pool_done[kc])
                    else:
                        P.wait("tensor", c.s_act, exp_done[kc])
                    if kc == 0:
                        P.wait("tensor", c.s_dv, c.s_dv.count)
                    for s2 in range(2):
                        P.do("tensor", lambda q, a=a, kc=kc, sp=sp, s2=s2, nkc=nkc, o_ps=o_ps:
                             q.matmul(o_ps[:, s2 * 512:(s2 + 1) * 512],
                                      a.vp[:, kc, :], a.pt[:, sp, s2 * 512:(s2 + 1) * 512],
                                      start=(kc == 0), stop=(kc == nkc - 1)),
                             c.s_pe, 0 if (s2 == 0 or kc < nkc - 1) else 1)
            pe_pv = c.s_pe.count
            P.wait("vector", c.s_pe, pe_pv)
            P.do("vector", lambda q, a=a, o_ps=o_ps:
                 q.reciprocal(a.rl[:, :], o_ps[64:65, :]), c.s_dv, 1)
            dv_rl = c.s_dv.count
            P.wait("tensor", c.s_dv, dv_rl)
            for s2 in range(2):
                P.do("tensor", lambda q, c=c, a=a, s2=s2, rb_ps=rb_ps:
                     q.matmul(rb_ps[:, s2 * 512:(s2 + 1) * 512], c.ones1_f[:, 0:64],
                              a.rl[:, s2 * 512:(s2 + 1) * 512], start=True, stop=True),
                     c.s_pe, 0 if s2 == 0 else 1)
            pe_rb = c.s_pe.count
            P.wait("scalar", c.s_pe, pe_rb)
            P.do("scalar", lambda q, a=a, rb_ps=rb_ps:
                 q.copy(a.rb[:, :], rb_ps), c.s_act, 1)
            act_rb = c.s_act.count
            P.wait("vector", c.s_act, act_rb)
            base = (h % 2) * 64
            cc_t = h // 2
            P.do("vector", lambda q, c=c, a=a, base=base, cc_t=cc_t, qs=qs, o_ps=o_ps:
                 q.tensor_tensor(a.AO[base:base + 64, cc_t, qs * QS:(qs + 1) * QS],
                                 o_ps[0:64, :], a.rb[:, :], ALU.mult), c.s_dv, 1)
        if h + 1 < NH:
            wvt_prev = wvt_next
    cx.gdma.wait("tensor", a.wout_tok)
    cx.dma.wait("vector", a.bout_tok)
    dv_ao = c.s_dv.count
    P.wait("tensor", c.s_dv, dv_ao)
    for s in range(NSEG):
        for dg in range(CCH):
            yp = y_ps[dg % 2]
            P.wait("tensor", c.s_dv, c.s_dv.count)
            for ccc in range(CCH):
                P.do("tensor", lambda q, c=c, a=a, s=s, dg=dg, ccc=ccc, yp=yp:
                     q.matmul(yp, a.w_out[:, ccc, dg * 128:(dg + 1) * 128],
                              a.AO[:, ccc, s * SEG:(s + 1) * SEG],
                              start=(ccc == 0), stop=(ccc == CCH - 1)),
                     c.s_pe, 0 if ccc < CCH - 1 else 1)
            pe = c.s_pe.count
            P.wait("vector", c.s_pe, pe)
            P.do("vector", lambda q, a=a, dg=dg, yp=yp:
                 q.tensor_scalar(a.ytmp[:, :], yp, a.bout[:, dg:dg + 1], None,
                                 ALU.add))
            P.do("vector", lambda q, c=c, a=a, dg=dg, s=s:
                 q.tensor_tensor(c.xcur[:, dg, s * SEG:(s + 1) * SEG],
                                 c.xcur[:, dg, s * SEG:(s + 1) * SEG],
                                 a.ytmp[:, :], ALU.add), c.s_dv, 1)
    return c.s_dv.count


def plan_p5(nc, P, cx, A, wpq, wpk, wvh, wout, bv, bout, x_in, x_out):
    if A is not None:
        cx.lnx = A("lnx", [128, CCH, T + 1], BF16)
    a = alloc_attn1(nc, A)
    bind_attn1_aliases(cx, a)
    a.wvh_ap = wvh
    emit_attn1_weights(cx, a, wpq, wpk, wout, bv, bout)
    a.wv0_tok = emit_wv1_load(cx, a, wvh, 0)

    for b in range(2):
        ld_tok = emit_load_xcur(cx, x_in, b)
        cx.dma.wait("vector", ld_tok)
        for s in range(NSEG):
            emit_ln_seg(cx, s * SEG, lambda cc, c=cx, s=s:
                        c.lnx[:, cc, s * SEG:(s + 1) * SEG])
        P.wait("tensor", cx.s_dv, cx.s_dv.count)
        emit_attn1_batch(cx, a)
        if b == 0:
            P.wait("gpsimd", cx.s_pe, cx.s_pe.count)
            a.wv0_tok = emit_wv1_load(cx, a, wvh, 0)
        dv_done = cx.s_dv.count
        P.wait("sync", cx.s_dv, dv_done)
        st_tok = emit_store_xcur(cx, x_out, b)
        note_store(cx, b, st_tok)
        if b == 0:
            for e in ("sync", "vector", "tensor", "scalar"):
                cx.stst.wait(e, st_tok)


def build_p5():
    nc = bass.Bass("TRN2", target_bir_lowering=False, debug=False,
                   detect_race_conditions=False)
    x_in = nc.dram_tensor("x_in", [2, C, T], F32, kind="ExternalInput").ap()
    wpq = nc.dram_tensor("wpq", [C, 128], F32, kind="ExternalInput").ap()
    wpk = nc.dram_tensor("wpk", [C, 128], F32, kind="ExternalInput").ap()
    wvh = nc.dram_tensor("wvh", [NH, C, 64], F32, kind="ExternalInput").ap()
    wout = nc.dram_tensor("wout", [C, C], F32, kind="ExternalInput").ap()
    bv = nc.dram_tensor("bv", [1, C], F32, kind="ExternalInput").ap()
    bout = nc.dram_tensor("bout", [128, CCH], F32, kind="ExternalInput").ap()
    x_out = nc.dram_tensor("x_out", [2, C, T], F32, kind="ExternalOutput").ap()
    P = Prog(nc)
    cx = alloc_common(nc, P)
    cx.lnx = nc.alloc_sbuf_tensor("lnx", [128, CCH, T + 1], BF16)
    emit_consts(cx)
    plan_p5(nc, P, cx, None, wpq, wpk, wvh, wout, bv, bout, x_in, x_out)
    cx.stst.wait_all("sync")
    P.build()
    return nc


def build_fused():
    from contextlib import ExitStack
    nc = bass.Bass("TRN2", target_bir_lowering=False, debug=False,
                   detect_race_conditions=False)
    x_fm = nc.dram_tensor("x_fm", [2, C, T], F32, kind="ExternalInput").ap()
    wr = nc.dram_tensor("wr", [C, C], F32, kind="ExternalInput").ap()
    wk = nc.dram_tensor("wk", [C, C], F32, kind="ExternalInput").ap()
    wv_f0 = nc.dram_tensor("wv_f0", [C, C], F32, kind="ExternalInput").ap()
    mu = nc.dram_tensor("mu", [128, 2, CCH], F32, kind="ExternalInput").ap()
    wqk_h = nc.dram_tensor("wqk_h", [NH, C, 128], F32, kind="ExternalInput").ap()
    wv0 = nc.dram_tensor("wv0", [C, C], F32, kind="ExternalInput").ap()
    bq = nc.dram_tensor("bq", [64, NH], F32, kind="ExternalInput").ap()
    bk = nc.dram_tensor("bk", [64, NH], F32, kind="ExternalInput").ap()
    bv0 = nc.dram_tensor("bv0", [1, C], F32, kind="ExternalInput").ap()
    w1c = nc.dram_tensor("w1c", [C, FF], F32, kind="ExternalInput").ap()
    w2c = nc.dram_tensor("w2c", [FF, C], F32, kind="ExternalInput").ap()
    idx = nc.dram_tensor("idx", [128, 64], I32, kind="ExternalInput").ap()
    ew = nc.dram_tensor("ew", [128, 64], F32, kind="ExternalInput").ap()
    eid = nc.dram_tensor("eid", [128, 1], F32, kind="ExternalInput").ap()
    gate = nc.dram_tensor("gate", [128, 2], F32, kind="ExternalInput").ap()
    bias = nc.dram_tensor("bias", [128, CCH], F32, kind="ExternalInput").ap()
    wpq = nc.dram_tensor("wpq", [C, 128], F32, kind="ExternalInput").ap()
    wpk = nc.dram_tensor("wpk", [C, 128], F32, kind="ExternalInput").ap()
    wvh = nc.dram_tensor("wvh", [NH, C, 64], F32, kind="ExternalInput").ap()
    wout = nc.dram_tensor("wout", [C, C], F32, kind="ExternalInput").ap()
    bv1 = nc.dram_tensor("bv1", [1, C], F32, kind="ExternalInput").ap()
    bout = nc.dram_tensor("bout", [128, CCH], F32, kind="ExternalInput").ap()
    y_fm = nc.dram_tensor("y_fm", [2, C, T], F32, kind="ExternalOutput").ap()

    xstore = nc.dram_tensor("xstore", [2, C, T], F32, kind="Internal").ap()
    dd = moe_dram(nc)

    P = Prog(nc)
    cx = alloc_common(nc, P)
    emit_consts(cx)

    def scoped(tag, stack):
        def A(name, shape, dtype):
            return stack.enter_context(nc.sbuf_tensor(name + tag, shape, dtype))
        return A

    with ExitStack() as es:
        plan_p1(nc, P, cx, scoped("_a", es), wr, wk, wv_f0, mu, x_fm, xstore)
    phase_barrier(P)
    with ExitStack() as es:
        plan_p2(nc, P, cx, scoped("_b", es), wqk_h, wv0, bq, bk, bv0,
                xstore, xstore)
    phase_barrier(P)
    with ExitStack() as es:
        plan_p1(nc, P, cx, scoped("_c", es), wr, wk, wv_f0, mu, xstore, xstore)
    phase_barrier(P)
    with ExitStack() as es:
        plan_p4(nc, P, cx, scoped("_d", es), w1c, w2c, idx, ew, eid, gate, bias,
                dd, xstore, xstore, convert=True)
    phase_barrier(P)
    with ExitStack() as es:
        plan_p5(nc, P, cx, scoped("_e", es), wpq, wpk, wvh, wout, bv1, bout,
                xstore, xstore)
    phase_barrier(P)
    with ExitStack() as es:
        plan_p4(nc, P, cx, scoped("_f", es), w1c, w2c, idx, ew, eid, gate, bias,
                dd, xstore, y_fm, convert=False)
    cx.stst.wait_all("sync")
    P.build()
    return nc



def plan_p1(nc, P, cx, A, wr, wk, wv, mu, x_in, x_out, tag=""):
    if A is not None:
        cx.lnx = A("lnx", [128, CCH, T + 1], BF16)
    f = alloc_ffn0(nc, A)
    f.pslots = [cx.pool[:, 2, 0:SEG], cx.pool[:, 2, SEG:2 * SEG],
                cx.pool[:, 3, 0:SEG], cx.pool[:, 3, SEG:2 * SEG]]
    mu_tok = cx.dma.start("sync", lambda q, f=f, mu=mu: q.dma_start(f.mu[:, :, :], mu))
    cx.dma.wait("vector", mu_tok)
    wdv, wtoks = emit_ffn0_weights(cx, f, wr, wk, wv)
    P.wait("tensor", cx.s_dv, wdv)

    for b in range(2):
        ld_for_batch = emit_load_xcur(cx, x_in, b)
        emit_ffn0_batch(cx, f, wdv, wtoks, ld_for_batch)
        dv_done = cx.s_dv.count
        P.wait("sync", cx.s_dv, dv_done)
        st_tok = emit_store_xcur(cx, x_out, b)
        note_store(cx, b, st_tok)
        if b == 0:
            for e in ("sync", "vector", "tensor", "scalar"):
                cx.stst.wait(e, st_tok)


def build_p1():
    nc = bass.Bass("TRN2", target_bir_lowering=False, debug=False,
                   detect_race_conditions=False)
    x_in = nc.dram_tensor("x_in", [2, C, T], F32, kind="ExternalInput").ap()
    wr = nc.dram_tensor("wr", [C, C], F32, kind="ExternalInput").ap()
    wk = nc.dram_tensor("wk", [C, C], F32, kind="ExternalInput").ap()
    wv = nc.dram_tensor("wv", [C, C], F32, kind="ExternalInput").ap()
    mu = nc.dram_tensor("mu", [128, 2, CCH], F32, kind="ExternalInput").ap()
    x_out = nc.dram_tensor("x_out", [2, C, T], F32, kind="ExternalOutput").ap()
    P = Prog(nc)
    cx = alloc_common(nc, P)
    cx.lnx = nc.alloc_sbuf_tensor("lnx", [128, CCH, T + 1], BF16)
    emit_consts(cx)
    plan_p1(nc, P, cx, None, wr, wk, wv, mu, x_in, x_out)
    cx.stst.wait_all("sync")
    P.build()
    return nc



def _prep_mu(mu_r, mu_k):
    m = np.stack([mu_r.reshape(CCH, 128).T, mu_k.reshape(CCH, 128).T], axis=1)
    return np.ascontiguousarray(m.astype(np.float32))


def kernel(**inputs):
    f32 = lambda k: np.asarray(inputs[k]).astype(np.float32)
    xs = f32("L0_attn_hidden_states")
    wr_t = np.ascontiguousarray(f32("L0_ffn_W_r").T)
    wk_t = np.ascontiguousarray(f32("L0_ffn_W_k").T)
    wv_t = np.ascontiguousarray(f32("L0_ffn_W_v").T)
    mu = _prep_mu(f32("L0_ffn_mu_r"), f32("L0_ffn_mu_k"))
    Wqk = f32("L0_attn_Wqk")
    wqk_h = np.ascontiguousarray(np.stack(
        [np.concatenate([Wqk[:, h * 64:(h + 1) * 64],
                         Wqk[:, C + h * 64:C + (h + 1) * 64]], axis=1)
         for h in range(NH)]))
    bqk = f32("L0_attn_bqk")
    bq = np.ascontiguousarray(bqk[:C].reshape(NH, 64).T)
    bk = np.ascontiguousarray(bqk[C:].reshape(NH, 64).T)
    wv0 = np.ascontiguousarray(f32("L0_attn_Wv"))
    bv0 = np.ascontiguousarray(f32("L0_attn_bv").reshape(1, C))
    w1 = f32("L1_ffn_w1")
    w2 = f32("L1_ffn_w2")
    idx_h = np.ascontiguousarray(
        np.asarray(inputs["L1_ffn_expert_indices"]).astype(np.int32)
        .reshape(128, 32, 2).transpose(0, 2, 1).reshape(128, 64))
    ew_h = np.ascontiguousarray(f32("L1_ffn_expert_weights")
                                .reshape(128, 32, 2).transpose(0, 2, 1).reshape(128, 64))
    l1bias = np.ascontiguousarray(f32("L1_ffn_bias").reshape(CCH, 128).T)
    wpq = np.ascontiguousarray(np.concatenate(
        [f32("L1_attn_W1_q").T, f32("L1_attn_W2_q").T], axis=1))
    wpk = np.ascontiguousarray(np.concatenate(
        [f32("L1_attn_W1_k").T, f32("L1_attn_W2_k").T], axis=1))
    wv1T = f32("L1_attn_W_v").T
    wv1_h = np.ascontiguousarray(np.stack(
        [wv1T[:, h * 64:(h + 1) * 64] for h in range(NH)]))
    wout_t = np.ascontiguousarray(f32("L1_attn_W_out").T)
    bv1 = np.ascontiguousarray(f32("L1_attn_b_v").reshape(1, C))
    bout = np.ascontiguousarray(f32("L1_attn_b_out").reshape(CCH, 128).T)

    ncores = list(range(NCORES))
    xst = [np.ascontiguousarray(np.stack([xs[c].T, xs[c + 8].T]))
           for c in range(NCORES)]

    nc = build_fused()
    in_maps = []
    for c in range(NCORES):
        gate = np.zeros((128, 2), np.float32)
        if c < 2:
            gate[:, c] = 1.0
        in_maps.append({
            "x_fm": xst[c],
            "wr": wr_t, "wk": wk_t, "wv_f0": wv_t, "mu": mu,
            "wqk_h": wqk_h, "wv0": wv0, "bq": bq, "bk": bk, "bv0": bv0,
            "w1c": np.ascontiguousarray(w1[c]),
            "w2c": np.ascontiguousarray(w2[c]),
            "idx": idx_h, "ew": ew_h,
            "eid": np.full((128, 1), float(c), np.float32),
            "gate": gate, "bias": l1bias,
            "wpq": wpq, "wpk": wpk, "wvh": wv1_h, "wout": wout_t,
            "bv1": bv1, "bout": bout,
        })
    res = run_bass_kernel_spmd(nc, in_maps, core_ids=ncores)
    out = np.empty((16, T, C), dtype=np.float32)
    for c in range(NCORES):
        y = res.results[c]["y_fm"]
        out[c] = y[0].T
        out[c + 8] = y[1].T
    return out
